# version 04
# baseline (speedup 1.0000x reference)
import numpy as np
import ml_dtypes
import concourse.bass as bass
import concourse.mybir as mybir
from concourse.bass_utils import run_bass_kernel_spmd
from contextlib import ExitStack

F32 = mybir.dt.float32
BF16 = mybir.dt.bfloat16
AF = mybir.ActivationFunctionType
ALU = mybir.AluOpType
AX = mybir.AxisListType

BF = ml_dtypes.bfloat16

B, S, D = 4, 4096, 1024
NCORE = 8
T = S // 2
NT = T // 128
KD = D // 128

F_HEADS, F_HD = 8, 64
F_INNER = 512
F_WIN = 512
F_HALO = 640
F_HID = 2730
HID_CH = [128] * 21 + [42]
F_QKSCALE = 8.0

A_HEADS, A_HD = 16, 64
A_HALO = 128
LAT_N, LAT_D, LAT_H, LAT_HD = 64, 256, 4, 64
CHUNK = 64

XHB = 136

FF_KT = 512 * F_HALO
FF_V = F_HALO * 512
FF_XB = D * 132
FF_TOT = FF_KT + FF_V + FF_XB
AA_KT = D * A_HALO
AA_V = A_HALO * D
AA_PL = D * 32
AA_TOT = AA_KT + AA_V + AA_PL

RG_PAIRS = [[0, 1], [2, 3], [4, 5], [6, 7]]
NGROUP = 2
STAGES = ["ffn0", "ffn1", "b_ffn", "mla", "ffn0", "ffn1"]
FFN_PARTS = ("attn", "wo", "mlp")
ATTN_CUT = None
SKIP_FINAL = False
LAST_C = None

ENGS = ("sync", "vector", "scalar", "tensor", "gpsimd")


def _rotary_tables(start):
    n = S
    pos = np.arange(start - 128, start + T, dtype=np.float64)
    half = np.arange(0, F_HD, 2, dtype=np.float64)
    inv_freq = 1.0 / (10000.0 ** (half / F_HD))
    freqs = pos[:, None] * inv_freq[None, :]
    base = (half + 0.4 * F_HD) / (1.4 * F_HD)
    power = (pos - n // 2) / float(F_WIN // 2)
    scale = base[None, :] ** power[:, None]
    cos, sin = np.cos(freqs), np.sin(freqs)
    csq = np.concatenate([cos * scale, cos * scale], 1)
    snq = np.concatenate([-sin * scale, sin * scale], 1)
    csk = np.concatenate([cos / scale, cos / scale], 1)
    snk = np.concatenate([-sin / scale, sin / scale], 1)
    bad = pos < 0
    for t in (csq, snq, csk, snk):
        t[bad] = 0.0
    return [t.astype(BF) for t in (csq, snq, csk, snk)]


def _ffn_masks(h):
    i = np.arange(128)[:, None]
    c = np.arange(640)[None, :]
    band = ((c >= i) & (c <= i + 512)).astype(np.float32)
    spec = np.zeros((5, 128, 640), np.float32)
    for idx, t0 in enumerate([-128, 0, 128, 256, 384]):
        m = band.copy()
        if h == 0:
            m = np.where((t0 - 512 + c) < 0, 0.0, m)
        spec[idx] = m
    return band.astype(BF), spec.astype(BF)


def _mla_masks(h):
    i = np.arange(128)[:, None]
    c = np.arange(256)[None, :]
    band = ((c >= i) & (c <= i + 127)).astype(np.float32)
    t0 = band.copy()
    rowkeep = np.ones((T, 1), np.float32)
    if h == 0:
        t0 = np.where((c - 128) < 0, 0.0, t0)
        rowkeep[0, 0] = 0.0
    return band.astype(BF), t0.astype(BF), rowkeep


def _lat_masks(start, h):
    cid = (start + np.arange(T)) // CHUNK
    j = np.arange(LAT_N)[None, :]
    mask = (j < cid[:, None]).astype(np.float32)
    rowkeep = (cid > 0).astype(np.float32)[:, None]
    return mask.astype(BF), rowkeep


class Sem:
    def __init__(self, nc, es, name):
        self.h = es.enter_context(nc.semaphore(name))
        self.n = 0
        self.name = name
        self.last_waited = 0
        self.synced = 0


class Arena:
    def __init__(self, nc, es, name, cols):
        self.t = es.enter_context(nc.sbuf_tensor(name, [128, cols], F32))
        self.cols = cols
        self.top = 0

    def f32(self, cols):
        base = self.top
        assert base + cols <= self.cols, \
            f"arena overflow {base}+{cols}>{self.cols}"
        self.top += cols
        return self.t[:, base:base + cols]

    def bf16(self, cols):
        assert cols % 2 == 0
        return self.f32(cols // 2).bitcast(BF16)

    def mark(self):
        return self.top

    def reset(self, mark):
        self.top = mark


class Prog:

    def __init__(self, nc):
        self.nc = nc
        self.tapes = {e: [] for e in ENGS}
        self.last_dma = {}

    def step(self, eng, fn, waits=(), inc=None, k=1):
        w = []
        for item in waits:
            if isinstance(item, tuple):
                s, v = item
            else:
                s, v = item, None
            v = s.n if v is None else v
            s.last_waited = max(s.last_waited, v)
            w.append((s.h, v))
        if inc is not None and k == 16:
            if inc.last_waited > inc.synced:
                w.insert(0, (inc.h, inc.n))
                inc.synced = inc.n
        self.tapes[eng].append((w, fn, None if inc is None else inc.h, k))
        if inc is not None:
            inc.n += k

    def flush(self):
        tapes = {e: t for e, t in self.tapes.items() if t}
        self.tapes = {e: [] for e in ENGS}
        if not tapes:
            return

        def runner(tape):
            def run(eng):
                for w, fn, semh, k in tape:
                    for h, v in w:
                        eng.wait_ge(h, v)
                    ins = fn(eng)
                    if semh is not None:
                        ins.then_inc(semh, k)
            return run

        with self.nc.Block() as blk:
            for e, tape in tapes.items():
                getattr(blk, e)(runner(tape))

    def barrier(self):
        c = self.ctx
        self.step("sync", lambda e: e.nop(),
                  waits=[(c.sD, None), (c.sD2, None), (c.sO, None)])
        self.step("gpsimd", lambda e: e.nop(), waits=[(c.sW, None)])
        self.flush()
        self.nc.all_engine_barrier()


class Ctx:
    pass


def build_program():
    nc = bass.Bass("TRN2", target_bir_lowering=False)
    c = Ctx()
    c.nc = nc

    def din(name, shape, dt=F32, kind="ExternalInput"):
        return nc.dram_tensor(name, list(shape), dt, kind=kind).ap()

    c.io = io = {}
    io["x_in"] = din("x_in", [T, D])
    for nm, shp in [
        ("aqkv_w", [D, 3 * D]), ("aqkv_b", [3 * D]),
        ("arq_w", [D, LAT_D]), ("arq_b", [LAT_D]),
        ("arkv_w", [D, 2 * LAT_D]), ("arkv_b", [2 * LAT_D]),
        ("arout_w", [LAT_D, D]), ("arout_b", [D]),
        ("aout_w", [D, D]), ("aout_b", [D]),
        ("agl_w", [A_HD, 1]), ("agr_w", [A_HD, 1]),
        ("fwqkv", [2, D, 3 * F_INNER]), ("fq_scale", [2, F_HD]),
        ("fk_scale", [2, F_HD]), ("fwo", [2, F_INNER, D]),
        ("fln_g", [2, D]), ("fln_b", [2, D]),
        ("fw1", [2, D, 2 * F_HID]), ("fb1", [2, 2 * F_HID]),
        ("fw2", [2, F_HID, D]), ("fb2", [2, D]),
    ]:
        io[nm] = din(nm, shp)
    io["csq"] = din("csq", [T + 128, F_HD], BF16)
    io["snq"] = din("snq", [T + 128, F_HD], BF16)
    io["csk"] = din("csk", [T + 128, F_HD], BF16)
    io["snk"] = din("snk", [T + 128, F_HD], BF16)
    io["fband01"] = din("fband01", [128, 640], BF16)
    io["fspec01"] = din("fspec01", [5, 128, 640], BF16)
    io["aband01"] = din("aband01", [128, 256], BF16)
    io["at001"] = din("at001", [128, 256], BF16)
    io["arowkeep"] = din("arowkeep", [T, 1])
    io["lmask01"] = din("lmask01", [T, LAT_N], BF16)
    io["lrowkeep"] = din("lrowkeep", [T, 1])
    io["halo_scale"] = din("halo_scale", [128, 1])
    io["out"] = din("out", [T, D], kind="ExternalOutput")

    c.xmod = din("xmod", [KD, 128, T], kind="Internal")
    c.mixd = din("mixd", [KD, 128, T], BF16, kind="Internal")
    c.ccf_in = din("ccf_in", [1, FF_TOT], BF16, kind="Internal")
    c.ccf_out = din("ccf_out", [NGROUP, FF_TOT], BF16, kind="Internal")
    c.cca_in = din("cca_in", [1, AA_TOT], BF16, kind="Internal")
    c.cca_out = din("cca_out", [NGROUP, AA_TOT], BF16, kind="Internal")

    id_d = nc.inline_tensor(np.eye(128).astype(BF), name="id128").ap()
    onesc_d = nc.inline_tensor(np.ones((128, 1), BF), name="onesc").ap()
    onesr_d = nc.inline_tensor(np.ones((1, 128), BF), name="onesr").ap()
    eps_d = nc.inline_tensor(np.full((1, 512), 1e-5, np.float32),
                             name="epsrow").ap()

    es = ExitStack()
    c.es = es
    c.xm = es.enter_context(nc.sbuf_tensor("xm", [128, KD * T], F32))
    c.xb = es.enter_context(nc.sbuf_tensor("xb", [128, KD * (T + XHB)], BF16))
    c.idt = es.enter_context(nc.sbuf_tensor("idt", [128, 128], BF16))
    c.onc = es.enter_context(nc.sbuf_tensor("onc", [128, 1], BF16))
    c.onr = es.enter_context(nc.sbuf_tensor("onr", [1, 128], BF16))
    c.hsc = es.enter_context(nc.sbuf_tensor("hsc", [128, 1], F32))
    c.epsr_t = es.enter_context(nc.sbuf_tensor("epsr", [1, 512], F32))
    c.epsr = c.epsr_t[0:1, 0:512]
    c.fband = es.enter_context(nc.sbuf_tensor("fband", [128, 640], BF16))
    c.aband = es.enter_context(nc.sbuf_tensor("aband", [128, 256], BF16))
    c.at0 = es.enter_context(nc.sbuf_tensor("at0", [128, 256], BF16))
    c.ar = Arena(nc, es, "arena", 26600)
    c.PS = [es.enter_context(nc.psum_tensor(f"ps{i}", [128, 512], F32))
            for i in range(8)]

    for nm in ["sD", "sW", "sP", "sV", "sA", "sG", "sC", "sO", "sP2", "sV2",
               "sD2"]:
        setattr(c, nm, Sem(nc, es, nm))

    c.p = Prog(nc)
    c.p.ctx = c
    c.consts = dict(id_d=id_d, onesc_d=onesc_d, onesr_d=onesr_d, eps_d=eps_d)

    program(c)
    c.p.flush()
    es.close()
    global LAST_C
    LAST_C = c
    return nc


def xm_v(c, k, t0, tn):
    return c.xm[:, k * T + t0: k * T + t0 + tn]


def xb_v(c, k, t0, tn):
    col = k * (T + XHB) + XHB + t0
    return c.xb[:, col: col + tn]


def emit_prologue(c):
    p = c.p
    io = c.io
    cn = c.consts
    for dst, src in [(c.idt[:], cn["id_d"]), (c.onc[:], cn["onesc_d"]),
                     (c.onr[:], cn["onesr_d"]), (c.epsr, cn["eps_d"]),
                     (c.hsc[:], io["halo_scale"]), (c.fband[:], io["fband01"]),
                     (c.aband[:], io["aband01"]), (c.at0[:], io["at001"])]:
        p.step("sync", lambda e, d=dst, s=src: e.dma_start(out=d, in_=s),
               inc=c.sD, k=16)

    ar = c.ar
    m0 = ar.mark()
    xtok = [ar.f32(1024) for _ in range(2)]
    hi = ar.bf16(1024)
    lo = ar.bf16(1024)
    hswap = ar.bf16(128)
    trb = c.PS[6].bitcast(BF16)
    for tt in range(NT):
        t0 = tt * 128
        xt = xtok[tt % 2]
        p.step("sync", lambda e, xt=xt, t0=t0:
               e.dma_start(out=xt, in_=io["x_in"][t0:t0 + 128, :]),
               waits=[(c.sV, None)], inc=c.sD, k=16)
        p.step("vector", lambda e, xt=xt: e.tensor_copy(hi[:], xt),
               waits=[c.sD, c.sP], inc=c.sV)
        p.step("vector", lambda e, xt=xt:
               e.tensor_tensor(lo[:], xt, hi[:], ALU.subtract),
               waits=[c.sV], inc=c.sV)
        for k in range(KD):
            p.step("tensor", lambda e, k=k: (
                e.transpose(trb[:, 0:128] if False else trb[:, (k % 4) * 256:(k % 4) * 256 + 128],
                            hi[:, k * 128:(k + 1) * 128], c.idt[:]),
                e.transpose(trb[:, (k % 4) * 256 + 128:(k % 4) * 256 + 256],
                            lo[:, k * 128:(k + 1) * 128], c.idt[:]))[-1],
                waits=([c.sV] if k == 0 else [(c.sV2, c.sV2.n - 2)]),
                inc=c.sP)
            p.step("vector", lambda e, k=k:
                   e.tensor_copy(hswap[:], trb[:, (k % 4) * 256:(k % 4) * 256 + 128]),
                   waits=[c.sP, (c.sV2, None)], inc=c.sV2)
            p.step("vector", lambda e, k=k, t0=t0:
                   e.tensor_tensor(xm_v(c, k, t0, 128), hswap[:],
                                   trb[:, (k % 4) * 256 + 128:(k % 4) * 256 + 256],
                                   ALU.add),
                   waits=[c.sV2], inc=c.sV2)
        for k in range(KD):
            p.step("sync", lambda e, k=k, t0=t0:
                   e.dma_start(out=c.xmod[k, :, t0:t0 + 128],
                               in_=xm_v(c, k, t0, 128)),
                   waits=[c.sV2], inc=c.sO, k=16)
    p.step("sync", lambda e: e.nop(), waits=[c.sD, c.sO])
    p.barrier()
    ar.reset(m0)


def emit_ln_chunk(c, src_ap, t0, dst_master=True, write_shadow=True):
    p = c.p
    ar = c.ar
    m0 = ar.mark()
    sq = [ar.bf16(512) for _ in range(2)]
    xcast = [ar.bf16(512) for _ in range(2)]
    rowf = ar.f32(6 * 512)
    R = {nm: rowf[0:1, i * 512:(i + 1) * 512]
         for i, nm in enumerate(["m", "ms", "t", "r1", "r2", "x"])}
    rowb = ar.bf16(4 * 512)
    RB = {nm: rowb[0:1, i * 512:(i + 1) * 512]
          for i, nm in enumerate(["r1h", "r1l", "r2h", "r2l"])}
    b1s = ar.f32(512)
    b2s = ar.f32(512)
    tmp = ar.f32(512)

    ps_sum = c.PS[4][0:1, 0:512]
    ps_ssq = c.PS[5][0:1, 0:512]
    ps_b1 = c.PS[2][:, 0:512]
    ps_b2 = c.PS[3][:, 0:512]

    for k in range(KD):
        p.step("scalar", lambda e, k=k:
               e.activation(sq[k % 2][:], src_ap(k), AF.Square),
               waits=[c.sV] + ([] if k < 2 else [(c.sP, c.sP.n - 2)]),
               inc=c.sA)
        p.step("vector", lambda e, k=k:
               e.tensor_copy(xcast[k % 2][:], src_ap(k)),
               waits=[(c.sV, None)] if k < 2 else [(c.sP, c.sP.n - 1),
                                                   (c.sV, None)],
               inc=c.sV)
        p.step("tensor", lambda e, k=k:
               e.matmul(ps_sum, c.onc[:, 0:1], xcast[k % 2][:],
                        start=(k == 0), stop=(k == KD - 1)),
               waits=[c.sV], inc=c.sP)
        p.step("tensor", lambda e, k=k:
               e.matmul(ps_ssq, c.onc[:, 0:1], sq[k % 2][:],
                        start=(k == 0), stop=(k == KD - 1)),
               waits=[c.sA], inc=c.sP)

    inv = 1.0 / D
    p.step("vector", lambda e: e.tensor_scalar_mul(R["m"], ps_sum, inv),
           waits=[c.sP], inc=c.sV)
    p.step("vector", lambda e: e.tensor_scalar_mul(R["ms"], ps_ssq, inv),
           waits=[c.sV], inc=c.sV)
    p.step("vector", lambda e: e.tensor_tensor(R["t"], R["m"], R["m"], ALU.mult),
           waits=[c.sV], inc=c.sV)
    p.step("vector", lambda e:
           e.tensor_tensor(R["ms"], R["ms"], R["t"], ALU.subtract),
           waits=[c.sV], inc=c.sV)
    p.step("vector", lambda e:
           e.tensor_scalar_add(R["ms"], R["ms"], 1e-5),
           waits=[c.sV], inc=c.sV)
    p.step("scalar", lambda e:
           e.activation(R["t"], R["ms"], AF.Sqrt),
           waits=[c.sV], inc=c.sA)
    p.step("vector", lambda e: e.reciprocal(R["r1"], R["t"]),
           waits=[c.sA], inc=c.sV)
    p.step("vector", lambda e: e.tensor_tensor(R["r2"], R["m"], R["r1"], ALU.mult),
           waits=[c.sV], inc=c.sV)
    p.step("vector", lambda e: e.tensor_scalar_mul(R["r2"], R["r2"], -1.0),
           waits=[c.sV], inc=c.sV)
    for r, rh, rl in [("r1", "r1h", "r1l"), ("r2", "r2h", "r2l")]:
        p.step("vector", lambda e, r=r, rh=rh: e.tensor_copy(RB[rh], R[r]),
               waits=[c.sV], inc=c.sV)
        p.step("vector", lambda e, r=r, rh=rh, rl=rl:
               e.tensor_tensor(R["x"], R[r], RB[rh], ALU.subtract),
               waits=[c.sV], inc=c.sV)
        p.step("vector", lambda e, rl=rl: e.tensor_copy(RB[rl], R["x"]),
               waits=[c.sV], inc=c.sV)

    p.step("tensor", lambda e: (
        e.matmul(ps_b1, c.onr[0:1, :], RB["r1h"], start=True, stop=False),
        e.matmul(ps_b1, c.onr[0:1, :], RB["r1l"], start=False, stop=True))[-1],
        waits=[c.sV, c.sA], inc=c.sP)
    p.step("tensor", lambda e: (
        e.matmul(ps_b2, c.onr[0:1, :], RB["r2h"], start=True, stop=False),
        e.matmul(ps_b2, c.onr[0:1, :], RB["r2l"], start=False, stop=True))[-1],
        waits=[c.sV, c.sA], inc=c.sP)
    p.step("scalar", lambda e: e.activation(b1s, ps_b1, AF.Identity),
           waits=[c.sP], inc=c.sA)
    p.step("scalar", lambda e: e.activation(b2s, ps_b2, AF.Identity),
           waits=[c.sP], inc=c.sA)

    for k in range(KD):
        p.step("vector", lambda e, k=k:
               e.tensor_tensor(tmp, src_ap(k), b1s, ALU.mult),
               waits=[c.sA, c.sV], inc=c.sV)
        p.step("vector", lambda e, k=k, t0=t0:
               e.tensor_tensor(xm_v(c, k, t0, 512), tmp, b2s, ALU.add),
               waits=[c.sV], inc=c.sV)
        if True:
            p.step("vector", lambda e, k=k, t0=t0:
                   e.tensor_copy(xb_v(c, k, t0, 512), xm_v(c, k, t0, 512)),
                   waits=[c.sV], inc=c.sV)
    c.p.flush()
    ar.reset(m0)


def emit_boundary(c, kind):
    p = c.p
    ar = c.ar
    if kind == "init":
        for tc in range(4):
            t0 = tc * 512
            emit_ln_chunk(c, lambda k, t0=t0: xm_v(c, k, t0, 512), t0)
        p.barrier()
        return

    scale = 0.5 if kind in ("final", "mid_ffn") else 1.0
    assert kind in ("final", "mid_ffn", "mid_mla")
    m0 = ar.mark()
    xnew = ar.f32(KD * 512)
    xold = ar.f32(KD * 512)
    for tc in range(4):
        t0 = tc * 512
        for k in range(KD):
            p.step("sync", lambda e, k=k, t0=t0:
                   e.dma_start(out=xold[:, k * 512:(k + 1) * 512],
                               in_=c.xmod[k, :, t0:t0 + 512]),
                   waits=[(c.sV, None)], inc=c.sD, k=16)
        for k in range(KD):
            p.step("vector", lambda e, k=k, t0=t0:
                   e.scalar_tensor_tensor(
                       xnew[:, k * 512:(k + 1) * 512],
                       xm_v(c, k, t0, 512), scale,
                       xold[:, k * 512:(k + 1) * 512], ALU.mult, ALU.add),
                   waits=[c.sD, (c.sO, None)], inc=c.sV)
        if kind != "final":
            for k in range(KD):
                p.step("sync", lambda e, k=k, t0=t0:
                       e.dma_start(out=c.xmod[k, :, t0:t0 + 512],
                                   in_=xnew[:, k * 512:(k + 1) * 512]),
                       waits=[c.sV], inc=c.sO, k=16)
            emit_ln_chunk(c, lambda k: xnew[:, k * 512:(k + 1) * 512], t0)
        else:
            emit_out_chunk(c, xnew, t0)
    p.barrier()
    ar.reset(m0)


def emit_out_chunk(c, xnew, t0):
    p = c.p
    ar = c.ar
    m0 = ar.mark()
    hi = ar.bf16(512)
    lo = ar.bf16(512)
    hswap = ar.f32(128)
    stage = [ar.f32(128) for _ in range(2)]
    trb = c.PS[6].bitcast(BF16)
    for k in range(KD):
        p.step("vector", lambda e, k=k: e.tensor_copy(hi[:], xnew[:, k * 512:(k + 1) * 512]),
               waits=[c.sV, (c.sP, None)], inc=c.sV)
        p.step("vector", lambda e, k=k:
               e.tensor_tensor(lo[:], xnew[:, k * 512:(k + 1) * 512], hi[:],
                               ALU.subtract),
               waits=[c.sV], inc=c.sV)
        for s in range(4):
            p.step("tensor", lambda e, s=s: (
                e.transpose(trb[:, s * 256:s * 256 + 128],
                            hi[:, s * 128:(s + 1) * 128], c.idt[:]),
                e.transpose(trb[:, s * 256 + 128:s * 256 + 256],
                            lo[:, s * 128:(s + 1) * 128], c.idt[:]))[-1],
                waits=[c.sV] + ([] if (k, s) == (0, 0) else [(c.sV2, None)]),
                inc=c.sP)
            p.step("vector", lambda e, s=s:
                   e.tensor_copy(hswap[:], trb[:, s * 256:s * 256 + 128]),
                   waits=[c.sP], inc=c.sV2)
            p.step("vector", lambda e, s=s:
                   e.tensor_tensor(stage[s % 2][:], hswap[:],
                                   trb[:, s * 256 + 128:s * 256 + 256], ALU.add),
                   waits=[c.sV2, (c.sO, None)], inc=c.sV2)
            p.step("sync", lambda e, s=s, k=k, t0=t0:
                   e.dma_start(out=c.io["out"][t0 + s * 128:t0 + (s + 1) * 128,
                                               k * 128:(k + 1) * 128],
                               in_=stage[s % 2][:]),
                   waits=[c.sV2], inc=c.sO, k=16)
    c.p.flush()
    ar.reset(m0)


def rot_view(ap):
    rs = ap.ap[0][0]
    return bass.AP(ap.tensor, ap.offset + 32, [[rs, 128], [64, 8], [-32, 2], [1, 32]])


def head_view(ap):
    rs = ap.ap[0][0]
    return bass.AP(ap.tensor, ap.offset, [[rs, 128], [64, 8], [32, 2], [1, 32]])


def rep_view(ap64):
    rs = ap64.ap[0][0]
    return bass.AP(ap64.tensor, ap64.offset, [[rs, 128], [0, 8], [32, 2], [1, 32]])


def row_swap_view(row64):
    return bass.AP(row64.tensor, row64.offset + 32, [[row64.ap[0][0], 1], [-32, 2], [1, 32]])


def ffn_qkv_mms(c, e, ps, wsl, t0):
    ins = None
    for kk in range(KD):
        src = xb_v(c, kk, t0 if kk < 4 else t0 - 1, 128)
        ins = e.matmul(ps, src, wsl(kk), start=(kk == 0), stop=(kk == KD - 1))
    return ins


def emit_ffn_kv_tile(c, l, tt, A, wait_halo=False):
    p = c.p
    t0 = tt * 128
    psk = c.PS[tt % 2]
    psv = c.PS[2 + tt % 2]
    trb = c.PS[6].bitcast(BF16)

    kw = [c.sW, (c.sV, None), (c.sA, None)]
    if wait_halo:
        kw.append((c.sV, c.haloV))
    p.step("tensor", lambda e: ffn_qkv_mms(
        c, e, psk[:, 0:512], lambda kk: A["wkv"][:, kk * 1024:kk * 1024 + 512],
        t0),
        waits=kw, inc=c.sP)
    p.step("tensor", lambda e: ffn_qkv_mms(
        c, e, psv[:, 0:512], lambda kk: A["wkv"][:, kk * 1024 + 512:kk * 1024 + 1024],
        t0),
        waits=[(c.sA, None)], inc=c.sP)
    vt = tt + 5
    p.step("scalar", lambda e: e.activation(
        A["v"][:, vt * 512:(vt + 1) * 512], psv[:, 0:512], AF.Identity),
        waits=[c.sP], inc=c.sA)

    kW = c.sP.n - 1
    p.step("scalar", lambda e: e.activation(A["sq"][:], psk[:, 0:512], AF.Square),
           waits=[(c.sP, kW), (c.sV, None)], inc=c.sA)
    p.step("vector", lambda e: e.reduce_sum(
        A["ss"][:, 0:8], A["sq"][:].rearrange("p (h d) -> p h d", h=8), axis=AX.X),
        waits=[c.sA, (c.sV, None)], inc=c.sV)
    p.step("scalar", lambda e: e.activation(A["sr"][:, 0:8], A["ss"][:, 0:8], AF.Sqrt),
           waits=[c.sV], inc=c.sA)
    p.step("vector", lambda e: e.tensor_scalar_max(A["sr"][:, 0:8], A["sr"][:, 0:8], 1e-12),
           waits=[c.sA], inc=c.sV)
    p.step("vector", lambda e: e.reciprocal(A["rstd"][:, 0:8], A["sr"][:, 0:8]),
           waits=[c.sV], inc=c.sV)
    for h in range(8):
        p.step("vector", lambda e, h=h: e.tensor_scalar_mul(
            A["kn"][:, h * 64:(h + 1) * 64], psk[:, h * 64:(h + 1) * 64],
            A["rstd"][:, h:h + 1]),
            waits=[c.sV], inc=c.sV)
    r0 = 128 + t0
    p.step("sync", lambda e: e.dma_start(out=A["cst"][:], in_=c.io["csk"][r0:r0 + 128, :]),
           waits=[(c.sV, None)], inc=c.sD, k=16)
    p.step("sync", lambda e: e.dma_start(out=A["snt"][:], in_=c.io["snk"][r0:r0 + 128, :]),
           inc=c.sD, k=16)
    p.step("vector", lambda e: e.tensor_tensor(A["csE"][:], A["cst"][:], A["kscb"][:], ALU.mult),
           waits=[c.sD, c.sV], inc=c.sV)
    p.step("vector", lambda e: e.tensor_tensor(A["snE"][:], A["snt"][:], A["kscsb"][:], ALU.mult),
           waits=[c.sV], inc=c.sV)
    p.step("vector", lambda e: e.tensor_tensor(
        head_view(A["t1"][:]), head_view(A["kn"][:]), rep_view(A["csE"][:]), ALU.mult),
        waits=[c.sV], inc=c.sV)
    p.step("vector", lambda e: e.tensor_tensor(
        head_view(A["t2"][:]), rot_view(A["kn"][:]), rep_view(A["snE"][:]), ALU.mult),
        waits=[c.sV], inc=c.sV)
    p.step("vector", lambda e: e.tensor_tensor(A["kp"][:], A["t1"][:], A["t2"][:], ALU.add),
           waits=[c.sV], inc=c.sV)
    p.step("tensor", lambda e, t0=t0: (
        [e.transpose(trb[:, f * 128:(f + 1) * 128], A["kp"][:, f * 128:(f + 1) * 128],
                     c.idt[:]) for f in range(4)])[-1],
        waits=[c.sV, (c.sV, None)], inc=c.sP)
    for f in range(4):
        p.step("vector", lambda e, f=f, t0=t0: e.tensor_copy(
            c.kT_col(A, f, 640 + t0, 128), trb[:, f * 128:(f + 1) * 128]),
            waits=[c.sP, (c.sV, None)], inc=c.sV)


def emit_ffn_layer(c, l):
    p = c.p
    io = c.io
    ar = c.ar
    mL = ar.mark()

    A = {}
    A["kT"] = ar.bf16(4 * 2688)
    A["v"] = ar.bf16(21 * 512)
    A["att"] = ar.bf16(4 * 2176)
    rows = ar.f32(128)

    def kT_col(A, f, c0, n):
        return A["kT"][:, f * 2688 + c0: f * 2688 + c0 + n]
    c.kT_col = kT_col

    ksrow = rows[0:1, 0:64]
    kbc = ar.f32(64)
    kbcs = ar.f32(64)
    qbc = ar.f32(64)
    qbcs = ar.f32(64)
    A["kscb"] = kbc.bitcast(BF16)[:, 0:64]
    A["kscsb"] = kbcs.bitcast(BF16)[:, 0:64]
    A["qscb"] = qbc.bitcast(BF16)[:, 0:64]
    A["qscsb"] = qbcs.bitcast(BF16)[:, 0:64]

    p.step("sync", lambda e: e.dma_start(out=ksrow, in_=io["fk_scale"][l:l + 1, :]),
           inc=c.sD, k=16)
    emit_bcast_scaled(c, ksrow, A["kscb"], swap=False, scale=1.0)
    emit_bcast_scaled(c, ksrow, A["kscsb"], swap=True, scale=1.0)
    p.step("sync", lambda e: e.dma_start(out=ksrow, in_=io["fq_scale"][l:l + 1, :]),
           waits=[(c.sA, None), (c.sP, None)], inc=c.sD, k=16)
    emit_bcast_scaled(c, ksrow, A["qscb"], swap=False, scale=F_QKSCALE)
    emit_bcast_scaled(c, ksrow, A["qscsb"], swap=True, scale=F_QKSCALE)

    c.p.flush()

    mS = ar.mark()

    A["wkv"] = ar.bf16(8 * 1024)
    A["sq"] = ar.bf16(512)
    A["ss"] = ar.f32(8)
    A["sr"] = ar.f32(8)
    A["rstd"] = ar.f32(8)
    A["kn"] = ar.bf16(512)
    A["t1"] = ar.bf16(512)
    A["t2"] = ar.bf16(512)
    A["kp"] = ar.bf16(512)
    A["cst"] = ar.bf16(64)
    A["snt"] = ar.bf16(64)
    A["csE"] = ar.bf16(64)
    A["snE"] = ar.bf16(64)

    for kk in range(KD):
        p.step("gpsimd", lambda e, kk=kk: e.dma_start(
            out=A["wkv"][:, kk * 1024:(kk + 1) * 1024],
            in_=io["fwqkv"][l, kk * 128:(kk + 1) * 128, 512:1536]),
            inc=c.sW, k=16)

    for tt in [11, 12, 13, 14, 15]:
        emit_ffn_kv_tile(c, l, tt, A)

    offs = 0
    def pack(src_ap, nel):
        nonlocal offs
        dst = c.ccf_in[0, offs:offs + nel].rearrange(
            "(p f) -> p f", p=src_ap.shape[0])
        p.step("sync", lambda e, d=dst, s=src_ap: e.dma_start(out=d, in_=s),
               waits=[(c.sV, None)], inc=c.sD, k=16)
        offs += nel
    for f in range(4):
        pack(kT_col(A, f, 640 + T - 640, 640), 128 * 640)
    for vt in range(16, 21):
        pack(A["v"][:, vt * 512:(vt + 1) * 512], 128 * 512)
    for kk in range(KD):
        pack(xb_v(c, kk, T - 132, 132), 128 * 132)
    assert offs == FF_TOT
    rg = RG_PAIRS if NGROUP == 2 else [[i] for i in range(NCORE)]
    p.step("gpsimd", lambda e, rg=rg: e.collective_compute(
        "AllGather", ALU.bypass, replica_groups=rg,
        ins=[c.ccf_in.opt()], outs=[c.ccf_out.opt()]),
        waits=[c.sD, (c.sC, None)], inc=c.sC)

    for tt in range(1, 11):
        emit_ffn_kv_tile(c, l, tt, A)
    c.p.flush()

    offs = 0
    def unpack(dst_ap, nel):
        nonlocal offs
        src = c.ccf_out[0, offs:offs + nel].rearrange(
            "(p f) -> p f", p=dst_ap.shape[0])
        p.step("sync", lambda e, d=dst_ap, s=src: e.dma_start(out=d, in_=s),
               waits=[(c.sC, None)], inc=c.sD2, k=16)
        offs += nel
    for f in range(4):
        unpack(kT_col(A, f, 0, 640), 128 * 640)
    for vt in range(0, 5):
        unpack(A["v"][:, vt * 512:(vt + 1) * 512], 128 * 512)
    for kk in range(KD):
        unpack(xb_v(c, kk, -132, 132), 128 * 132)
    assert offs == FF_TOT
    for kk in range(KD):
        p.step("vector", lambda e, kk=kk: e.tensor_scalar_mul(
            xb_v(c, kk, -132, 132), xb_v(c, kk, -132, 132), c.hsc[:, 0:1]),
            waits=[(c.sD2, None), (c.sV, None)], inc=c.sV)
    for f in range(4):
        p.step("vector", lambda e, f=f: e.tensor_scalar_mul(
            c.kT_col(A, f, 0, 640), c.kT_col(A, f, 0, 640), c.hsc[:, 0:1]),
            waits=[(c.sV, None)], inc=c.sV)
    p.step("vector", lambda e: e.tensor_scalar_mul(
        A["v"][:, 0:5 * 512], A["v"][:, 0:5 * 512], c.hsc[:, 0:1]),
        waits=[(c.sV, None)], inc=c.sV)
    c.haloV = c.sV.n
    emit_ffn_kv_tile(c, l, 0, A, wait_halo=True)

    p.barrier()
    ar.reset(mS)

    Bw = {}
    Bw["wq"] = ar.bf16(8 * 512)
    for kk in range(KD):
        p.step("gpsimd", lambda e, kk=kk: e.dma_start(
            out=Bw["wq"][:, kk * 512:(kk + 1) * 512],
            in_=io["fwqkv"][l, kk * 128:(kk + 1) * 128, 0:512]),
            inc=c.sW, k=16)
    Bw["sq"] = ar.bf16(512)
    Bw["ss"] = ar.f32(8)
    Bw["sr"] = ar.f32(8)
    Bw["rstd"] = ar.f32(8)
    Bw["qn"] = ar.bf16(512)
    Bw["t1"] = ar.bf16(512)
    Bw["t2"] = ar.bf16(512)
    Bw["qp"] = ar.bf16(512)
    Bw["cst"] = ar.bf16(64)
    Bw["snt"] = ar.bf16(64)
    Bw["csE"] = ar.bf16(64)
    Bw["snE"] = ar.bf16(64)
    Bw["qT"] = [ar.bf16(512), ar.bf16(512)]
    Bw["probs"] = [ar.bf16(640), ar.bf16(640)]
    Bw["probsm"] = [ar.bf16(640), ar.bf16(640)]
    Bw["probsT"] = ar.bf16(640)
    Bw["mask"] = ar.bf16(640)
    Bw["sums"] = ar.f32(2)
    Bw["rr"] = ar.f32(2)

    c.dbg_A, c.dbg_Bw = A, Bw
    if "attn" in FFN_PARTS:
        order = list(range(4, 16)) + [16, 0, 1, 2, 3]
        for ui, q_idx in enumerate(order):
            emit_ffn_q_attn(c, l, q_idx, ui, A, Bw)
    c.p.flush()

    if "wo" not in FFN_PARTS:
        c.p.barrier()
        ar.reset(mL)
        return
    Bw["wo"] = [ar.bf16(4 * 128), ar.bf16(4 * 128)]
    Bw["h5t"] = ar.bf16(128)
    for m in range(KD):
        wob = Bw["wo"][m % 2]
        for kk in range(4):
            p.step("gpsimd", lambda e, m=m, kk=kk, wob=wob: e.dma_start(
                out=wob[:, kk * 128:(kk + 1) * 128],
                in_=io["fwo"][l, kk * 128:(kk + 1) * 128,
                              m * 128:(m + 1) * 128]),
                waits=[(c.sP, None)] if kk == 0 else [],
                inc=c.sW, k=16)
        for tc in range(4):
            ps = c.PS[tc % 2]
            p.step("tensor", lambda e, m=m, tc=tc, wob=wob, ps=ps: (
                [e.matmul(ps[:, 0:512], wob[:, kk * 128:(kk + 1) * 128],
                          A["att"][:, kk * 2176 + 128 + tc * 512:
                                   kk * 2176 + 128 + (tc + 1) * 512],
                          start=(kk == 0), stop=(kk == 3))
                 for kk in range(4)])[-1],
                waits=[c.sW, (c.sV, None)], inc=c.sP)
            p.step("vector", lambda e, m=m, tc=tc, ps=ps: e.tensor_tensor(
                xm_v(c, m, tc * 512, 512), ps[:, 0:512],
                xm_v(c, m, tc * 512, 512), ALU.add),
                waits=[c.sP, (c.sV, None)], inc=c.sV)
        psh = c.PS[2]
        p.step("tensor", lambda e, m=m, wob=wob: (
            [e.matmul(psh[:, 0:128], wob[:, kk * 128:(kk + 1) * 128],
                      A["att"][:, kk * 2176: kk * 2176 + 128],
                      start=(kk == 0), stop=(kk == 3))
             for kk in range(4)])[-1],
            waits=[(c.sV, None)], inc=c.sP)
        p.step("vector", lambda e, m=m: e.tensor_scalar_mul(
            Bw["h5t"][:], psh[:, 0:128], c.hsc[:, 0:1]),
            waits=[c.sP, (c.sV, None)], inc=c.sV)
        p.step("vector", lambda e, m=m: e.tensor_tensor(
            xb_v(c, m, -128, 128), Bw["h5t"][:], xb_v(c, m, -128, 128), ALU.add),
            waits=[c.sV], inc=c.sV)
    vW = c.sV.n
    for m in range(KD):
        for tc in range(4):
            p.step("vector", lambda e, m=m, tc=tc: e.tensor_copy(
                xb_v(c, m, tc * 512, 512), xm_v(c, m, tc * 512, 512)),
                waits=[(c.sV, vW)], inc=c.sV)
    p.barrier()
    if "mlp" not in FFN_PARTS:
        ar.reset(mL)
        return
    ar.reset(mL)
    emit_ffn_mlp(c, l)


def emit_bcast_scaled(c, row_f32, dst_bf16, swap, scale):
    p = c.p
    ar = c.ar
    m0 = ar.mark()
    rs = ar.f32(64)[0:1, :]
    rh = ar.bf16(64)[0:1, :]
    rl = ar.bf16(64)[0:1, :]
    rx = ar.f32(64)[0:1, :]
    ps = c.PS[7][:, 0:64]
    p.step("vector", lambda e: e.tensor_scalar_mul(rs, row_f32, scale),
           waits=[(c.sV, None), c.sD], inc=c.sV)
    p.step("vector", lambda e: e.tensor_copy(rh, rs), waits=[c.sV], inc=c.sV)
    p.step("vector", lambda e: e.tensor_tensor(rx, rs, rh, ALU.subtract),
           waits=[c.sV], inc=c.sV)
    p.step("vector", lambda e: e.tensor_copy(rl, rx), waits=[c.sV], inc=c.sV)
    sh = (lambda r: row_swap_view(r)) if swap else (lambda r: r)
    p.step("tensor", lambda e: (
        e.matmul(ps, c.onr[0:1, :], sh(rh), start=True, stop=False),
        e.matmul(ps, c.onr[0:1, :], sh(rl), start=False, stop=True))[-1],
        waits=[c.sV, (c.sA, None)], inc=c.sP)
    p.step("scalar", lambda e: e.activation(dst_bf16, ps, AF.Identity),
           waits=[c.sP], inc=c.sA)
    c.p.flush()
    ar.reset(m0)


def emit_ffn_q_attn(c, l, q_idx, ui, A, Bw):
    p = c.p
    io = c.io
    tq0 = -128 if q_idx == 16 else q_idx * 128
    special = {16: 0, 0: 1, 1: 2, 2: 3, 3: 4}.get(q_idx)
    trb = c.PS[6].bitcast(BF16)
    qT = Bw["qT"][ui % 2]

    psq = c.PS[7]
    needs_halo = q_idx in (16, 0)
    qw = [c.sW, (c.sV, None), (c.sA, None)]
    if needs_halo:
        qw.append((c.sV, c.haloV))
    p.step("tensor", lambda e: ffn_qkv_mms(
        c, e, psq[:, 0:512], lambda kk: Bw["wq"][:, kk * 512:(kk + 1) * 512],
        tq0),
        waits=qw, inc=c.sP)
    p.step("scalar", lambda e: e.activation(Bw["sq"][:], psq[:, 0:512], AF.Square),
           waits=[c.sP, (c.sV, None)], inc=c.sA)
    p.step("vector", lambda e: e.reduce_sum(
        Bw["ss"][:, 0:8], Bw["sq"][:].rearrange("p (h d) -> p h d", h=8), axis=AX.X),
        waits=[c.sA, (c.sV, None)], inc=c.sV)
    p.step("scalar", lambda e: e.activation(Bw["sr"][:, 0:8], Bw["ss"][:, 0:8], AF.Sqrt),
           waits=[c.sV], inc=c.sA)
    p.step("vector", lambda e: e.tensor_scalar_max(Bw["sr"][:, 0:8], Bw["sr"][:, 0:8], 1e-12),
           waits=[c.sA], inc=c.sV)
    p.step("vector", lambda e: e.reciprocal(Bw["rstd"][:, 0:8], Bw["sr"][:, 0:8]),
           waits=[c.sV], inc=c.sV)
    for h in range(8):
        p.step("vector", lambda e, h=h: e.tensor_scalar_mul(
            Bw["qn"][:, h * 64:(h + 1) * 64], psq[:, h * 64:(h + 1) * 64],
            Bw["rstd"][:, h:h + 1]),
            waits=[c.sV], inc=c.sV)
    r0 = 128 + tq0
    p.step("sync", lambda e: e.dma_start(out=Bw["cst"][:], in_=io["csq"][r0:r0 + 128, :]),
           waits=[(c.sV, None)], inc=c.sD, k=16)
    p.step("sync", lambda e: e.dma_start(out=Bw["snt"][:], in_=io["snq"][r0:r0 + 128, :]),
           inc=c.sD, k=16)
    p.step("vector", lambda e: e.tensor_tensor(Bw["csE"][:], Bw["cst"][:], A["qscb"][:], ALU.mult),
           waits=[c.sD, c.sV], inc=c.sV)
    p.step("vector", lambda e: e.tensor_tensor(Bw["snE"][:], Bw["snt"][:], A["qscsb"][:], ALU.mult),
           waits=[c.sV], inc=c.sV)
    p.step("vector", lambda e: e.tensor_tensor(
        head_view(Bw["t1"][:]), head_view(Bw["qn"][:]), rep_view(Bw["csE"][:]), ALU.mult),
        waits=[c.sV], inc=c.sV)
    p.step("vector", lambda e: e.tensor_tensor(
        head_view(Bw["t2"][:]), rot_view(Bw["qn"][:]), rep_view(Bw["snE"][:]), ALU.mult),
        waits=[c.sV], inc=c.sV)
    p.step("vector", lambda e: e.tensor_tensor(Bw["qp"][:], Bw["t1"][:], Bw["t2"][:], ALU.add),
           waits=[c.sV], inc=c.sV)
    p.step("tensor", lambda e: (
        [e.transpose(trb[:, f * 128:(f + 1) * 128], Bw["qp"][:, f * 128:(f + 1) * 128],
                     c.idt[:]) for f in range(4)])[-1],
        waits=[c.sV], inc=c.sP)
    p.step("vector", lambda e: e.tensor_copy(qT[:], trb[:, 0:512]),
           waits=[c.sP, (c.sV, None)], inc=c.sV)

    if ATTN_CUT == "q":
        return
    if special is not None:
        p.step("sync", lambda e, s=special: e.dma_start(
            out=Bw["mask"][:], in_=io["fspec01"][s, :, :]),
            waits=[(c.sG, None)], inc=c.sD, k=16)
        mask = Bw["mask"]
    else:
        mask = c.fband

    kcol0 = 640 + tq0 - 512
    for hp in range(4):
        st = [c.PS[0], c.PS[1], c.PS[2]] if hp % 2 == 0 else [c.PS[3], c.PS[4], c.PS[5]]
        probs = Bw["probs"][hp % 2]
        probsm = Bw["probsm"][hp % 2]
        f = hp
        sc_w = [c.sV, (c.sA, None), (c.sG, None)]
        if q_idx in (16, 0, 1, 2, 3):
            sc_w.append((c.sD2, None))
        qa = qT[0:64, f * 128:(f + 1) * 128]
        qb = qT[64:128, f * 128:(f + 1) * 128]
        p.step("tensor", lambda e, f=f, st=st, qa=qa, qb=qb: (
            e.matmul(st[0][:, 0:512], qa,
                     c.kT_col(A, f, kcol0, 512)[0:64, :], start=True, stop=True,
                     tile_position=(0, 0)),
            e.matmul(st[1][:, 0:512], qb,
                     c.kT_col(A, f, kcol0, 512)[64:128, :], start=True, stop=True,
                     tile_position=(64, 0)),
            e.matmul(st[2][:, 0:128], qa,
                     c.kT_col(A, f, kcol0 + 512, 128)[0:64, :], start=True,
                     stop=True, tile_position=(0, 0)),
            e.matmul(c.PS[6][:, 384:512], qb,
                     c.kT_col(A, f, kcol0 + 512, 128)[64:128, :], start=True,
                     stop=True, tile_position=(64, 0)))[-1],
            waits=sc_w, inc=c.sP)
        if ATTN_CUT == "scores":
            continue
        p.step("vector", lambda e, st=st: e.reduce_max(
            Bw["rr"][:, 0:1], st[0][:, 0:512], axis=AX.X, negate=True),
            waits=[c.sP, (c.sV, None)], inc=c.sV)
        p.step("vector", lambda e, st=st: e.reduce_max(
            Bw["rr"][:, 1:2], st[1][:, 0:512], axis=AX.X, negate=True),
            waits=[c.sV], inc=c.sV)
        p.step("vector", lambda e, st=st: e.reduce_max(
            Bw["sums"][:, 0:1], st[2][:, 0:128], axis=AX.X, negate=True),
            waits=[c.sV], inc=c.sV)
        p.step("vector", lambda e: e.reduce_max(
            Bw["sums"][:, 1:2], c.PS[6][:, 384:512], axis=AX.X, negate=True),
            waits=[c.sV], inc=c.sV)
        p.step("vector", lambda e: e.tensor_tensor(
            Bw["rr"][:, 0:2], Bw["rr"][:, 0:2], Bw["sums"][:, 0:2], ALU.min),
            waits=[c.sV], inc=c.sV)
        if ATTN_CUT == "max":
            continue
        p.step("scalar", lambda e, st=st, probs=probs: (
            e.activation(probs[:, 0:512], st[0][:, 0:512], AF.Exp,
                         bias=Bw["rr"][:, 0:1]),
            e.activation(probs[:, 512:640], st[2][:, 0:128], AF.Exp,
                         bias=Bw["rr"][:, 0:1]))[-1],
            waits=[c.sV, (c.sG, None)], inc=c.sA)
        if ATTN_CUT == "exp":
            continue
        p.step("gpsimd", lambda e, probsm=probsm, probs=probs, mask=mask:
               e.tensor_tensor(probsm[:], probs[:], mask[:], ALU.mult),
               waits=[c.sA, (c.sG, None), c.sD], inc=c.sG)
        if ATTN_CUT == "mask":
            continue
        p.step("vector", lambda e, probsm=probsm: e.reduce_sum(
            Bw["sums"][:, 0:1], probsm[:], axis=AX.X),
            waits=[c.sG, (c.sV, None)], inc=c.sV)
        p.step("vector", lambda e: e.tensor_scalar_max(
            Bw["sums"][:, 0:1], Bw["sums"][:, 0:1], 1e-30),
            waits=[c.sV], inc=c.sV)
        p.step("vector", lambda e: e.reciprocal(Bw["sums"][:, 1:2], Bw["sums"][:, 0:1]),
               waits=[c.sV], inc=c.sV)
        p.step("vector", lambda e, probsm=probsm: e.tensor_scalar_mul(
            probsm[:], probsm[:], Bw["sums"][:, 1:2]),
            waits=[c.sV, c.sG], inc=c.sV)
        if ATTN_CUT == "norm":
            continue
        p.step("tensor", lambda e, probsm=probsm: (
            [e.transpose(trb[:, i * 128:(i + 1) * 128],
                         probsm[:, i * 128:(i + 1) * 128], c.idt[:])
             for i in range(5)])[-1],
            waits=[c.sV, (c.sA, None)], inc=c.sP)
        p.step("scalar", lambda e: e.activation(Bw["probsT"][:], trb[:, 0:640],
                                                AF.Identity),
               waits=[c.sP], inc=c.sA)
        if ATTN_CUT == "tr":
            continue
        ha = 2 * hp
        p.step("tensor", lambda e, ha=ha: (
            [e.matmul(c.PS[7][0:64, 0:128],
                      A["v"][:, (q_idx + 1 + ct) * 512 + ha * 64:
                             (q_idx + 1 + ct) * 512 + ha * 64 + 64]
                      if False else
                      A["v"][:, ((0 if q_idx == 16 else q_idx + 1) + ct) * 512
                             + ha * 64:
                             ((0 if q_idx == 16 else q_idx + 1) + ct) * 512
                             + ha * 64 + 64],
                      Bw["probsT"][:, ct * 128:(ct + 1) * 128],
                      start=(ct == 0), stop=(ct == 4))
             for ct in range(5)])[-1],
            waits=[c.sA, (c.sV, None)], inc=c.sP)
        p.step("vector", lambda e, f=f, ha=ha: e.tensor_copy(
            A["att"][0:64, f * 2176 + 128 + tq0: f * 2176 + 256 + tq0],
            c.PS[7][0:64, 0:128]),
            waits=[c.sP, (c.sV, None)], inc=c.sV)
        if ATTN_CUT == "av":
            continue
        p.step("scalar", lambda e, st=st, probs=probs: (
            e.activation(probs[:, 0:512], st[1][:, 0:512], AF.Exp,
                         bias=Bw["rr"][:, 1:2]),
            e.activation(probs[:, 512:640], c.PS[6][:, 384:512], AF.Exp,
                         bias=Bw["rr"][:, 1:2]))[-1],
            waits=[(c.sG, None), (c.sV, None)], inc=c.sA)
        p.step("gpsimd", lambda e, probsm=probsm, probs=probs, mask=mask:
               e.tensor_tensor(probsm[:], probs[:], mask[:], ALU.mult),
               waits=[c.sA, (c.sG, None), (c.sP, None)], inc=c.sG)
        p.step("vector", lambda e, probsm=probsm: e.reduce_sum(
            Bw["sums"][:, 0:1], probsm[:], axis=AX.X),
            waits=[c.sG, (c.sV, None)], inc=c.sV)
        p.step("vector", lambda e: e.tensor_scalar_max(
            Bw["sums"][:, 0:1], Bw["sums"][:, 0:1], 1e-30),
            waits=[c.sV], inc=c.sV)
        p.step("vector", lambda e: e.reciprocal(Bw["sums"][:, 1:2], Bw["sums"][:, 0:1]),
               waits=[c.sV], inc=c.sV)
        p.step("vector", lambda e, probsm=probsm: e.tensor_scalar_mul(
            probsm[:], probsm[:], Bw["sums"][:, 1:2]),
            waits=[c.sV, c.sG], inc=c.sV)
        p.step("tensor", lambda e, probsm=probsm: (
            [e.transpose(trb[:, i * 128:(i + 1) * 128],
                         probsm[:, i * 128:(i + 1) * 128], c.idt[:])
             for i in range(5)])[-1],
            waits=[c.sV, (c.sA, None)], inc=c.sP)
        p.step("scalar", lambda e: e.activation(Bw["probsT"][:], trb[:, 0:640],
                                                AF.Identity),
               waits=[c.sP], inc=c.sA)
        hb = 2 * hp + 1
        p.step("tensor", lambda e, hb=hb: (
            [e.matmul(c.PS[7][0:64, 128:256],
                      A["v"][:, ((0 if q_idx == 16 else q_idx + 1) + ct) * 512
                             + hb * 64:
                             ((0 if q_idx == 16 else q_idx + 1) + ct) * 512
                             + hb * 64 + 64],
                      Bw["probsT"][:, ct * 128:(ct + 1) * 128],
                      start=(ct == 0), stop=(ct == 4))
             for ct in range(5)])[-1],
            waits=[c.sA, (c.sV, None)], inc=c.sP)
        p.step("vector", lambda e, f=f: e.tensor_copy(
            A["att"][64:128, f * 2176 + 128 + tq0: f * 2176 + 256 + tq0],
            c.PS[7][0:64, 128:256]),
            waits=[c.sP, (c.sV, None)], inc=c.sV)


def emit_ffn_mlp(c, l):
    p = c.p
    io = c.io
    ar = c.ar
    m0 = ar.mark()
    h2 = ar.bf16(8 * 2048)
    h3 = ar.bf16(22 * 1024)
    wpair = [ar.bf16(2 * 8 * 128) for _ in range(2)]
    w1a = [wp[:, 0:1024] for wp in wpair]
    w1g = [wp[:, 1024:2048] for wp in wpair]
    ag = ar.bf16(512)
    gg = ar.bf16(512)
    sq = ag
    b1c = ar.f32(44)
    ldg = ar.f32(8)
    ldb = ar.f32(8)
    b2c = ar.f32(8)
    rowf = ar.f32(5 * 512)
    R = {nm: rowf[0:1, i * 512:(i + 1) * 512]
         for i, nm in enumerate(["m", "ms", "t", "r1", "r2"])}
    R["x"] = R["t"]
    rowb = ar.bf16(4 * 512)
    RB = {nm: rowb[0:1, i * 512:(i + 1) * 512]
          for i, nm in enumerate(["r1h", "r1l", "r2h", "r2l"])}
    b1s = ar.bf16(512)
    b2s = ar.bf16(512)
    tmp = gg

    for i, n in enumerate(HID_CH):
        o = sum(HID_CH[:i])
        p.step("sync", lambda e, i=i, o=o, n=n: e.dma_start(
            out=b1c[0:n, 2 * i:2 * i + 1],
            in_=io["fb1"][l, o:o + n].unsqueeze(1)),
            inc=c.sD, k=16)
        p.step("sync", lambda e, i=i, o=o, n=n: e.dma_start(
            out=b1c[0:n, 2 * i + 1:2 * i + 2],
            in_=io["fb1"][l, F_HID + o:F_HID + o + n].unsqueeze(1)),
            inc=c.sD, k=16)
    for kk in range(KD):
        p.step("sync", lambda e, kk=kk: e.dma_start(
            out=ldg[:, kk:kk + 1],
            in_=io["fln_g"][l, kk * 128:(kk + 1) * 128].unsqueeze(1)),
            inc=c.sD, k=16)
        p.step("sync", lambda e, kk=kk: e.dma_start(
            out=ldb[:, kk:kk + 1],
            in_=io["fln_b"][l, kk * 128:(kk + 1) * 128].unsqueeze(1)),
            inc=c.sD, k=16)
        p.step("sync", lambda e, kk=kk: e.dma_start(
            out=b2c[:, kk:kk + 1],
            in_=io["fb2"][l, kk * 128:(kk + 1) * 128].unsqueeze(1)),
            inc=c.sD, k=16)
    ldD = c.sD.n

    ps_sum = c.PS[4][0:1, 0:512]
    ps_ssq = c.PS[5][0:1, 0:512]
    ps_b1 = c.PS[6][:, 0:512]
    ps_b2 = c.PS[7][:, 0:512]

    def xa_shift(kk, t0, n):
        return xb_v(c, kk, t0 if kk < 4 else t0 - 1, n)

    for tc in range(2):
        T0 = tc * 1024
        for sub in range(2):
            t0 = T0 + sub * 512
            for kk in range(KD):
                p.step("scalar", lambda e, kk=kk, t0=t0: e.activation(
                    sq[:], xa_shift(kk, t0, 512), AF.Square),
                    waits=[(c.sP, None), (c.sV, None)], inc=c.sA)
                p.step("tensor", lambda e, kk=kk, t0=t0: e.matmul(
                    ps_ssq, c.onc[:, 0:1], sq[:],
                    start=(kk == 0), stop=(kk == KD - 1)),
                    waits=[c.sA], inc=c.sP)
                p.step("tensor", lambda e, kk=kk, t0=t0: e.matmul(
                    ps_sum, c.onc[:, 0:1], xa_shift(kk, t0, 512),
                    start=(kk == 0), stop=(kk == KD - 1)),
                    waits=[(c.sV, None)], inc=c.sP)
            inv = 1.0 / D
            p.step("vector", lambda e: e.tensor_scalar_mul(R["m"], ps_sum, inv),
                   waits=[c.sP, (c.sV, None)], inc=c.sV)
            p.step("vector", lambda e: e.tensor_scalar_mul(R["ms"], ps_ssq, inv),
                   waits=[c.sV], inc=c.sV)
            p.step("vector", lambda e: e.tensor_tensor(R["t"], R["m"], R["m"], ALU.mult),
                   waits=[c.sV], inc=c.sV)
            p.step("vector", lambda e: e.tensor_tensor(R["ms"], R["ms"], R["t"], ALU.subtract),
                   waits=[c.sV], inc=c.sV)
            p.step("vector", lambda e: e.tensor_scalar_add(R["ms"], R["ms"], 1e-5),
                   waits=[c.sV], inc=c.sV)
            p.step("scalar", lambda e: e.activation(R["t"], R["ms"], AF.Sqrt),
                   waits=[c.sV], inc=c.sA)
            p.step("vector", lambda e: e.reciprocal(R["r1"], R["t"]),
                   waits=[c.sA], inc=c.sV)
            p.step("vector", lambda e: e.tensor_tensor(R["r2"], R["m"], R["r1"], ALU.mult),
                   waits=[c.sV], inc=c.sV)
            p.step("vector", lambda e: e.tensor_scalar_mul(R["r2"], R["r2"], -1.0),
                   waits=[c.sV], inc=c.sV)
            for r, rh, rl in [("r1", "r1h", "r1l"), ("r2", "r2h", "r2l")]:
                p.step("vector", lambda e, r=r, rh=rh: e.tensor_copy(RB[rh], R[r]),
                       waits=[c.sV], inc=c.sV)
                p.step("vector", lambda e, r=r, rh=rh: e.tensor_tensor(
                    R["x"], R[r], RB[rh], ALU.subtract),
                    waits=[c.sV], inc=c.sV)
                p.step("vector", lambda e, rl=rl: e.tensor_copy(RB[rl], R["x"]),
                       waits=[c.sV], inc=c.sV)
            p.step("tensor", lambda e: (
                e.matmul(ps_b1, c.onr[0:1, :], RB["r1h"], start=True, stop=False),
                e.matmul(ps_b1, c.onr[0:1, :], RB["r1l"], start=False, stop=True))[-1],
                waits=[c.sV, (c.sA, None)], inc=c.sP)
            p.step("tensor", lambda e: (
                e.matmul(ps_b2, c.onr[0:1, :], RB["r2h"], start=True, stop=False),
                e.matmul(ps_b2, c.onr[0:1, :], RB["r2l"], start=False, stop=True))[-1],
                waits=[c.sV, (c.sA, None)], inc=c.sP)
            p.step("scalar", lambda e: e.activation(b1s, ps_b1, AF.Identity),
                   waits=[c.sP], inc=c.sA)
            p.step("scalar", lambda e: e.activation(b2s, ps_b2, AF.Identity),
                   waits=[c.sP], inc=c.sA)
            for kk in range(KD):
                p.step("vector", lambda e, kk=kk, t0=t0: e.tensor_tensor(
                    tmp[:], xa_shift(kk, t0, 512), b1s[:], ALU.mult),
                    waits=[c.sA, (c.sV, None)], inc=c.sV)
                p.step("vector", lambda e, kk=kk: e.tensor_tensor(
                    tmp[:], tmp[:], b2s[:], ALU.add),
                    waits=[c.sV], inc=c.sV)
                p.step("vector", lambda e, kk=kk, t0=t0: e.tensor_scalar(
                    h2[:, kk * 2048 + t0: kk * 2048 + t0 + 512],
                    tmp[:], ldg[:, kk:kk + 1], ldb[:, kk:kk + 1],
                    ALU.mult, ALU.add),
                    waits=[c.sV, (c.sD, ldD)], inc=c.sV)
    for tc in range(2):
        T0 = tc * 1024
        for i, n in enumerate(HID_CH):
            o = sum(HID_CH[:i])
            wa, wg = w1a[i % 2], w1g[i % 2]
            for kk in range(KD):
                p.step("gpsimd", lambda e, o=o, n=n, kk=kk, wa=wa: e.dma_start(
                    out=wa[0:128, kk * 128:kk * 128 + n],
                    in_=io["fw1"][l, kk * 128:(kk + 1) * 128, o:o + n]),
                    waits=[(c.sP, None)] if kk == 0 else [],
                    inc=c.sW, k=16)
            for kk in range(KD):
                p.step("gpsimd", lambda e, o=o, n=n, kk=kk, wg=wg: e.dma_start(
                    out=wg[0:128, kk * 128:kk * 128 + n],
                    in_=io["fw1"][l, kk * 128:(kk + 1) * 128,
                                  F_HID + o:F_HID + o + n]),
                    inc=c.sW, k=16)
            for sub in range(2):
                t0s = sub * 512
                psa = c.PS[0 + sub]
                psg = c.PS[2 + sub]
                p.step("tensor", lambda e, i=i, n=n, wa=wa, t0s=t0s, psa=psa,
                       T0=T0: (
                    [e.matmul(psa[0:n, 0:512], wa[0:128, kk * 128:kk * 128 + n],
                              h2[:, kk * 2048 + T0 + t0s:
                                 kk * 2048 + T0 + t0s + 512],
                              start=(kk == 0), stop=(kk == KD - 1))
                     for kk in range(KD)])[-1],
                    waits=[c.sW, c.sV, (c.sA, None)], inc=c.sP)
                p.step("tensor", lambda e, i=i, n=n, wg=wg, t0s=t0s, psg=psg,
                       T0=T0: (
                    [e.matmul(psg[0:n, 0:512], wg[0:128, kk * 128:kk * 128 + n],
                              h2[:, kk * 2048 + T0 + t0s:
                                 kk * 2048 + T0 + t0s + 512],
                              start=(kk == 0), stop=(kk == KD - 1))
                     for kk in range(KD)])[-1],
                    inc=c.sP)
                p.step("scalar", lambda e, i=i, n=n, psa=psa: e.activation(
                    ag[0:n, :], psa[0:n, 0:512], AF.Identity,
                    bias=b1c[0:n, 2 * i:2 * i + 1]),
                    waits=[c.sP, (c.sV, None)], inc=c.sA)
                p.step("scalar", lambda e, i=i, n=n, psg=psg: e.activation(
                    gg[0:n, :], psg[0:n, 0:512], AF.Gelu,
                    bias=b1c[0:n, 2 * i + 1:2 * i + 2]),
                    waits=[(c.sV, None)], inc=c.sA)
                p.step("vector", lambda e, i=i, n=n, t0s=t0s: e.tensor_tensor(
                    h3[0:n, i * 1024 + t0s: i * 1024 + t0s + 512],
                    ag[0:n, :], gg[0:n, :], ALU.mult),
                    waits=[c.sA], inc=c.sV)
        emit_ffn_w2(c, l, T0, h3, b2c, wpair)
    c.p.flush()
    ar.reset(m0)
    c.p.barrier()


def emit_ffn_w2(c, l, T0, h3, b2c, wpair):
    p = c.p
    io = c.io
    ar = c.ar
    m0 = ar.mark()
    w2c = wpair
    for sub in range(2):
        t0 = T0 + sub * 512
        ps = c.PS
        for i, n in enumerate(HID_CH):
            o = sum(HID_CH[:i])
            wc = w2c[i % 2]
            p.step("gpsimd", lambda e, i=i, o=o, n=n, wc=wc: e.dma_start(
                out=wc[0:n, 0:1024],
                in_=io["fw2"][l, o:o + n, :]),
                waits=[(c.sP, None)], inc=c.sW, k=16)
            p.step("tensor", lambda e, i=i, n=n, o=o, wc=wc, t0=t0: (
                [e.matmul(ps[m][:, 0:512], wc[0:n, m * 128:(m + 1) * 128],
                          h3[0:n, i * 1024 + (t0 - T0): i * 1024 + (t0 - T0) + 512],
                          start=(i == 0), stop=(i == len(HID_CH) - 1))
                 for m in range(KD)])[-1],
                waits=[c.sW] + ([(c.sV, None)] if i == 0 else []), inc=c.sP)
        for m in range(KD):
            p.step("vector", lambda e, m=m, t0=t0: e.scalar_tensor_tensor(
                xm_v(c, m, t0, 512), ps[m][:, 0:512], b2c[:, m:m + 1],
                xm_v(c, m, t0, 512), ALU.add, ALU.add),
                waits=[c.sP, (c.sV, None)], inc=c.sV)
        vW = c.sV.n
        for m in range(KD):
            p.step("vector", lambda e, m=m, t0=t0: e.tensor_copy(
                xb_v(c, m, t0, 512), xm_v(c, m, t0, 512)),
                waits=[(c.sV, vW)], inc=c.sV)
    ar.reset(m0)


def emit_done(c):
    c.p.step("sync", lambda e: e.engine_nop() if False else e.nop(),
             waits=[c.sO, c.sV, c.sP])
    c.p.flush()


def program(c):
    emit_prologue(c)
    emit_boundary(c, "init")
    if SKIP_FINAL:
        for st in STAGES:
            if st == "ffn0":
                emit_ffn_layer(c, 0)
        emit_done(c)
        return
    for st in STAGES:
        if st == "ffn0":
            emit_ffn_layer(c, 0)
        elif st == "ffn1":
            emit_ffn_layer(c, 1)
        elif st == "b_ffn":
            emit_boundary(c, "mid_ffn")
        elif st == "mla":
            emit_mla_block(c)
        elif st == "b_mla":
            emit_boundary(c, "mid_mla")
        else:
            raise ValueError(st)
    emit_boundary(c, "final")
    emit_done(c)


def mla_qkv_mms(c, e, ps, wsl, t0):
    ins = None
    for kk in range(KD):
        ins = e.matmul(ps, xb_v(c, kk, t0, 128), wsl(kk),
                       start=(kk == 0), stop=False)
    return ins


def emit_mla_block(c):
    p = c.p
    io = c.io
    ar = c.ar
    mL = ar.mark()

    M = {}
    rows = ar.f32(64)
    rb = ar.bf16(3072 + 256 + 512 + 64)
    M["qkvb"] = rb[0:1, 0:3072]
    M["rqb"] = rb[0:1, 3072:3328]
    M["rkvb"] = rb[0:1, 3328:3840]
    M["pooled"] = ar.bf16(8 * 64)
    M["rkv"] = ar.bf16(512)
    M["rkT"] = ar.bf16(2 * 64)
    M["glb"] = ar.bf16(64)
    M["grb"] = ar.bf16(64)
    M["rowk"] = ar.f32(2)
    M["lmrow"] = ar.bf16(64)

    c.wrq = ar.bf16(8 * 256)
    c.wrout = ar.bf16(2 * 1024)
    rb2 = ar.bf16(1024)
    c.routb = rb2[0:1, 0:1024]

    mRows = ar.mark()
    rowscr = ar.f32(3072)
    rfull = rowscr[0:1, 0:3072]
    for nm, src_ap, n in [("qkvb", io["aqkv_b"].unsqueeze(0), 3072),
                          ("rqb", io["arq_b"].unsqueeze(0), 256),
                          ("rkvb", io["arkv_b"].unsqueeze(0), 512)]:
        p.step("sync", lambda e, s=src_ap, n=n: e.dma_start(
            out=rfull[0:1, 0:n], in_=s), waits=[(c.sV, None)], inc=c.sD, k=16)
        p.step("vector", lambda e, nm=nm, n=n: e.tensor_copy(
            M[nm][0:1, 0:n], rfull[0:1, 0:n]),
            waits=[c.sD, (c.sV, None)], inc=c.sV)
    for nm, wn in [("glb", "agl_w"), ("grb", "agr_w")]:
        p.step("sync", lambda e, wn=wn: e.dma_start(
            out=rowscr[0:1, 0:64], in_=io[wn].rearrange("a b -> b a")),
            waits=[(c.sV, None), (c.sP, None)], inc=c.sD, k=16)
        p.step("vector", lambda e: e.tensor_copy(rb[0:1, 3840:3904],
                                                 rowscr[0:1, 0:64]),
               waits=[c.sD, (c.sV, None)], inc=c.sV)
        p.step("tensor", lambda e: e.matmul(c.PS[7][:, 0:64], c.onr[0:1, :],
                                            rb[0:1, 3840:3904],
                                            start=True, stop=True),
               waits=[c.sV, (c.sA, None)], inc=c.sP)
        p.step("scalar", lambda e, nm=nm: e.activation(M[nm][:], c.PS[7][:, 0:64],
                                                       AF.Identity),
               waits=[c.sP], inc=c.sA)

    rs2 = ar.f32(1024)
    for kk in range(KD):
        p.step("gpsimd", lambda e, kk=kk: e.dma_start(
            out=c.wrq[:, kk * 256:(kk + 1) * 256],
            in_=io["arq_w"][kk * 128:(kk + 1) * 128, :]),
            waits=[(c.sP, None)] if kk == 0 else [], inc=c.sW, k=16)
    for ch in range(2):
        p.step("gpsimd", lambda e, ch=ch: e.dma_start(
            out=c.wrout[0:128, ch * 1024:(ch + 1) * 1024],
            in_=io["arout_w"][ch * 128:(ch + 1) * 128, :]),
            inc=c.sW, k=16)
    p.step("sync", lambda e: e.dma_start(
        out=rs2[0:1, 0:1024], in_=io["arout_b"].unsqueeze(0)),
        waits=[(c.sV, None)], inc=c.sD, k=16)
    p.step("vector", lambda e: e.tensor_copy(c.routb[0:1, 0:1024],
                                             rs2[0:1, 0:1024]),
           waits=[c.sD, (c.sV, None)], inc=c.sV)

    c.p.flush()
    ar.reset(mRows)

    pl_own = M["pooled"][:, :].bitcast(BF16) if False else M["pooled"]
    pooled_f = ar.f32(8 * 32)
    for kk in range(KD):
        p.step("vector", lambda e, kk=kk: e.reduce_sum(
            pooled_f[:, kk * 32:(kk + 1) * 32],
            xb_v(c, kk, 0, T).rearrange("p (g c) -> p g c", g=32), axis=AX.X),
            waits=[(c.sV, None)], inc=c.sV)
    p.step("vector", lambda e: e.tensor_scalar_mul(
        pooled_f[:], pooled_f[:], 1.0 / CHUNK),
        waits=[c.sV], inc=c.sV)
    for kk in range(KD):
        p.step("vector", lambda e, kk=kk: e.tensor_copy(
            M["pooled"][:, kk * 64 + 32: kk * 64 + 64],
            pooled_f[:, kk * 32:(kk + 1) * 32]),
            waits=[c.sV], inc=c.sV)

    mS = ar.mark()
    kvlast = ar.bf16(2 * 1024)
    ktl = kvlast[:, 0:1024]
    vtl = kvlast[:, 1024:2048]
    wbuf = [ar.bf16(8 * 512) for _ in range(2)]
    trb = c.PS[6].bitcast(BF16)
    t0 = T - 128
    for ni, n0 in enumerate([2, 3, 4, 5]):
        wb = wbuf[ni % 2]
        for kk in range(KD):
            p.step("gpsimd", lambda e, kk=kk, n0=n0, wb=wb: e.dma_start(
                out=wb[:, kk * 512:(kk + 1) * 512],
                in_=io["aqkv_w"][kk * 128:(kk + 1) * 128,
                                 n0 * 512:(n0 + 1) * 512]),
                waits=[(c.sP, None)] if kk == 0 else [], inc=c.sW, k=16)
        ps = c.PS[ni % 2]
        p.step("tensor", lambda e, wb=wb, ni=ni, n0=n0, ps=ps: (
            mla_qkv_mms(c, e, ps[:, 0:512],
                        lambda kk: wb[:, kk * 512:(kk + 1) * 512], t0),
            e.matmul(ps[:, 0:512], c.onr[0:1, :],
                     M["qkvb"][0:1, n0 * 512:(n0 + 1) * 512],
                     start=False, stop=True))[-1],
            waits=[c.sW, (c.sV, None), (c.sA, None)], inc=c.sP)
        if n0 in (2, 3):
            p.step("scalar", lambda e, ps=ps: e.activation(
                vtl[:, 0:512], ps[:, 0:512], AF.Identity),
                waits=[c.sP], inc=c.sA)
            p.step("tensor", lambda e, n0=n0: (
                [e.transpose(trb[:, f * 128:(f + 1) * 128],
                             vtl[:, f * 128:(f + 1) * 128], c.idt[:])
                 for f in range(4)])[-1],
                waits=[c.sA, (c.sV, None)], inc=c.sP)
            p.step("vector", lambda e, n0=n0: e.tensor_copy(
                ktl[:, (n0 - 2) * 512:(n0 - 1) * 512], trb[:, 0:512]),
                waits=[c.sP, (c.sV, None)], inc=c.sV)
        else:
            p.step("scalar", lambda e, ps=ps, n0=n0: e.activation(
                vtl[:, (n0 - 4) * 512:(n0 - 3) * 512], ps[:, 0:512], AF.Identity),
                waits=[c.sP, (c.sV, None)], inc=c.sA)
    offs = 0
    for src_ap, nel in [(ktl, 128 * 1024), (vtl, 128 * 1024)]:
        dst = c.cca_in[0, offs:offs + nel].rearrange("(p f) -> p f", p=128)
        p.step("sync", lambda e, d=dst, s=src_ap: e.dma_start(out=d, in_=s),
               waits=[(c.sV, None), (c.sA, None)], inc=c.sD, k=16)
        offs += nel
    pl_pack = ar.bf16(8 * 32)
    for kk in range(KD):
        p.step("vector", lambda e, kk=kk: e.tensor_copy(
            pl_pack[:, kk * 32:(kk + 1) * 32],
            M["pooled"][:, kk * 64 + 32:kk * 64 + 64]),
            waits=[(c.sV, None)], inc=c.sV)
    dst = c.cca_in[0, offs:offs + 128 * 256].rearrange("(p f) -> p f", p=128)
    p.step("sync", lambda e, d=dst: e.dma_start(out=d, in_=pl_pack[:]),
           waits=[c.sV], inc=c.sD, k=16)
    offs += 128 * 256
    assert offs == AA_TOT
    rg = RG_PAIRS if NGROUP == 2 else [[i] for i in range(NCORE)]
    p.step("gpsimd", lambda e, rg=rg: e.collective_compute(
        "AllGather", ALU.bypass, replica_groups=rg,
        ins=[c.cca_in.opt()], outs=[c.cca_out.opt()]),
        waits=[c.sD, (c.sC, None)], inc=c.sC)
    c.p.flush()
    ar.reset(mS)

    for half in range(2):
        emit_mla_half(c, M, half)

    emit_mla_out(c, M)
    p.barrier()
    ar.reset(mL)


def emit_mla_half(c, M, half):
    p = c.p
    io = c.io
    ar = c.ar
    mH = ar.mark()
    th0 = half * 8
    t00 = th0 * 128
    Kh = {}
    Kh["kT"] = ar.bf16(8 * 1152)
    Kh["v"] = ar.bf16(9 * 1024)
    Kh["q"] = ar.bf16(8 * 1024)
    wbuf = [ar.bf16(8 * 512) for _ in range(2)]
    trb = c.PS[6].bitcast(BF16)
    evs = ar.bf16(512)

    def kT_col(f, col, n):
        return Kh["kT"][:, f * 1152 + col: f * 1152 + col + n]

    kv_tiles = list(range(max(0, th0 - 1), th0 + 8))
    for ni in range(6):
        wb = wbuf[ni % 2]
        for kk in range(KD):
            p.step("gpsimd", lambda e, kk=kk, ni=ni, wb=wb: e.dma_start(
                out=wb[:, kk * 512:(kk + 1) * 512],
                in_=io["aqkv_w"][kk * 128:(kk + 1) * 128,
                                 ni * 512:(ni + 1) * 512]),
                waits=[(c.sP, None)] if kk == 0 else [], inc=c.sW, k=16)
        tiles = kv_tiles if ni >= 2 else list(range(th0, th0 + 8))
        for tt in tiles:
            ps = c.PS[tt % 2]
            p.step("tensor", lambda e, wb=wb, ni=ni, tt=tt, ps=ps: (
                mla_qkv_mms(c, e, ps[:, 0:512],
                            lambda kk: wb[:, kk * 512:(kk + 1) * 512], tt * 128),
                e.matmul(ps[:, 0:512], c.onr[0:1, :],
                         M["qkvb"][0:1, ni * 512:(ni + 1) * 512],
                         start=False, stop=True))[-1],
                waits=[c.sW, (c.sV, None), (c.sA, None)], inc=c.sP)
            if ni < 2:
                p.step("vector", lambda e, ps=ps, ni=ni, tt=tt: e.tensor_scalar_mul(
                    Kh["q"][:, (tt - th0) * 1024 + ni * 512:
                            (tt - th0) * 1024 + (ni + 1) * 512],
                    ps[:, 0:512], 0.125),
                    waits=[c.sP, (c.sV, None)], inc=c.sV)
            elif ni < 4:
                p.step("scalar", lambda e, ps=ps: e.activation(
                    evs[:], ps[:, 0:512], AF.Identity),
                    waits=[c.sP, (c.sP2, None)], inc=c.sA)
                p.step("tensor", lambda e: (
                    [e.transpose(trb[:, f * 128:(f + 1) * 128],
                                 evs[:, f * 128:(f + 1) * 128], c.idt[:])
                     for f in range(4)])[-1],
                    waits=[c.sA, (c.sV, None)], inc=c.sP2)
                col = 128 * (tt - th0 + 1)
                for f in range(4):
                    fk = (ni - 2) * 4 + f
                    p.step("vector", lambda e, fk=fk, f=f, col=col: e.tensor_copy(
                        kT_col(fk, col, 128), trb[:, f * 128:(f + 1) * 128]),
                        waits=[c.sP2, (c.sV, None)], inc=c.sV)
            else:
                p.step("scalar", lambda e, ps=ps, ni=ni, tt=tt: e.activation(
                    Kh["v"][:, (tt - th0 + 1) * 1024 + (ni - 4) * 512:
                            (tt - th0 + 1) * 1024 + (ni - 3) * 512],
                    ps[:, 0:512], AF.Identity),
                    waits=[c.sP, (c.sV, None)], inc=c.sA)

    if half == 0:
        offs = 0
        for f in range(8):
            src = c.cca_out[0, f * 128 * 128:(f + 1) * 128 * 128].rearrange(
                "(p f2) -> p f2", p=128)
            p.step("sync", lambda e, d=kT_col(f, 0, 128), s=src:
                   e.dma_start(out=d, in_=s),
                   waits=[(c.sC, None)], inc=c.sD2, k=16)
        src = c.cca_out[0, AA_KT:AA_KT + AA_V].rearrange("(p f2) -> p f2", p=128)
        p.step("sync", lambda e, s=src: e.dma_start(out=Kh["v"][:, 0:1024], in_=s),
               waits=[(c.sC, None)], inc=c.sD2, k=16)
        src = c.cca_out[0, AA_KT + AA_V:AA_TOT].rearrange("(p f2) -> p f2", p=128)
        plr = ar.bf16(8 * 32)
        p.step("sync", lambda e, s=src, plr=plr: e.dma_start(out=plr[:], in_=s),
               waits=[(c.sC, None)], inc=c.sD2, k=16)
        for kk in range(KD):
            p.step("vector", lambda e, kk=kk, plr=plr: e.tensor_copy(
                M["pooled"][:, kk * 64:kk * 64 + 32],
                plr[:, kk * 32:(kk + 1) * 32]),
                waits=[(c.sD2, None), (c.sV, None)], inc=c.sV)
        for f in range(8):
            p.step("vector", lambda e, f=f: e.tensor_scalar_mul(
                kT_col(f, 0, 128), kT_col(f, 0, 128), c.hsc[:, 0:1]),
                waits=[(c.sD2, None), (c.sV, None)], inc=c.sV)
        p.step("vector", lambda e: e.tensor_scalar_mul(
            Kh["v"][:, 0:1024], Kh["v"][:, 0:1024], c.hsc[:, 0:1]),
            waits=[(c.sD2, None), (c.sV, None)], inc=c.sV)
        emit_mla_latent_prep(c, M)

    for tt in range(th0, th0 + 8):
        emit_mla_tile(c, M, Kh, half, tt, kT_col)
    c.p.flush()
    ar.reset(mH)


def emit_mla_latent_prep(c, M):
    p = c.p
    io = c.io
    ar = c.ar
    m0 = ar.mark()
    wb = ar.bf16(8 * 512)
    trb = c.PS[6].bitcast(BF16)
    for kk in range(KD):
        p.step("gpsimd", lambda e, kk=kk, wb=wb: e.dma_start(
            out=wb[:, kk * 512:(kk + 1) * 512],
            in_=io["arkv_w"][kk * 128:(kk + 1) * 128, :]),
            waits=[(c.sP, None)] if kk == 0 else [], inc=c.sW, k=16)
    ps = c.PS[7]
    p.step("tensor", lambda e, wb=wb: (
        [e.matmul(ps[0:64, 0:512], M["pooled"][:, kk * 64:(kk + 1) * 64],
                  wb[:, kk * 512:(kk + 1) * 512],
                  start=(kk == 0), stop=False) for kk in range(KD)] +
        [e.matmul(ps[0:64, 0:512], c.onr[0:1, 0:64],
                  M["rkvb"][0:1, 0:512], start=False, stop=True)])[-1],
        waits=[c.sW, (c.sV, None), (c.sA, None)], inc=c.sP)
    p.step("scalar", lambda e: e.activation(M["rkv"][0:64, :], ps[0:64, 0:512],
                                            AF.Identity),
           waits=[c.sP], inc=c.sA)
    p.step("tensor", lambda e: (
        e.transpose(trb[:, 0:64], M["rkv"][0:64, 0:128], c.idt[0:64, 0:64]),
        e.transpose(trb[:, 64:128], M["rkv"][0:64, 128:256], c.idt[0:64, 0:64]))[-1],
        waits=[c.sA, (c.sV, None)], inc=c.sP)
    p.step("vector", lambda e: e.tensor_copy(M["rkT"][:], trb[:, 0:128]),
           waits=[c.sP, (c.sV, None)], inc=c.sV)
    c.p.flush()
    ar.reset(m0)


def emit_mla_tile(c, M, Kh, half, tt, kT_col):
    p = c.p
    io = c.io
    ar = c.ar
    m0 = ar.mark()
    th0 = half * 8
    t0 = tt * 128
    tl = tt - th0
    trb = c.PS[6].bitcast(BF16)

    qTt = ar.bf16(8 * 128)
    probs = ar.bf16(256)
    probsm = ar.bf16(256)
    probsT = ar.bf16(256)
    lpb = ar.bf16(64)
    lpbm = ar.bf16(64)
    lpbT = ar.bf16(128)
    loc = ar.bf16(1024)
    rem = ar.bf16(1024)
    rqs = ar.bf16(256)
    rqT = ar.bf16(256)
    d1 = ar.bf16(1024)
    mstage = loc
    glog = ar.f32(3 * 16)
    gate = ar.f32(16)
    st2 = ar.f32(4)

    p.step("tensor", lambda e: (
        [e.transpose(trb[:, f * 128:(f + 1) * 128],
                     Kh["q"][:, tl * 1024 + f * 128: tl * 1024 + (f + 1) * 128],
                     c.idt[:]) for f in range(8)])[-1],
        waits=[(c.sV, None), (c.sA, None)], inc=c.sP)
    p.step("vector", lambda e: e.tensor_copy(qTt[:], trb[:, 0:1024]),
           waits=[c.sP, (c.sV, None)], inc=c.sV)

    p.step("sync", lambda e: e.dma_start(out=M["rowk"][:, 0:1],
                                         in_=io["arowkeep"][t0:t0 + 128, :]),
           waits=[(c.sV, None)], inc=c.sD, k=16)
    p.step("sync", lambda e: e.dma_start(out=M["rowk"][:, 1:2],
                                         in_=io["lrowkeep"][t0:t0 + 128, :]),
           inc=c.sD, k=16)
    p.step("sync", lambda e: e.dma_start(out=M["lmrow"][:],
                                         in_=io["lmask01"][t0:t0 + 128, :]),
           inc=c.sD, k=16)
    mask = c.at0 if tt == 0 else c.aband

    locps = [c.PS[4], c.PS[5]]
    kcol = 128 * (tl + 1) - 128
    for hp in range(8):
        st = c.PS[hp % 2]
        stb = c.PS[2 + hp % 2]
        qa = qTt[0:64, hp * 128:(hp + 1) * 128]
        qb = qTt[64:128, hp * 128:(hp + 1) * 128]
        p.step("tensor", lambda e, hp=hp, st=st, stb=stb, qa=qa, qb=qb: (
            e.matmul(st[:, 0:256], qa, kT_col(hp, kcol, 256)[0:64, :],
                     start=True, stop=True, tile_position=(0, 0)),
            e.matmul(stb[:, 0:256], qb, kT_col(hp, kcol, 256)[64:128, :],
                     start=True, stop=True, tile_position=(64, 0)))[-1],
            waits=[(c.sV, None), (c.sA, None), (c.sG, None)] +
            ([(c.sD2, None)] if (half == 0 and tt == 0) else []),
            inc=c.sP)
        for hh in range(2):
            h = 2 * hp + hh
            sc = (st if hh == 0 else stb)[:, 0:256]
            p.step("vector", lambda e, sc=sc: e.reduce_max(
                st2[:, 0:1], sc, axis=AX.X, negate=True),
                waits=[c.sP, (c.sV, None)], inc=c.sV)
            p.step("scalar", lambda e, sc=sc: e.activation(
                probs[:], sc, AF.Exp, bias=st2[:, 0:1]),
                waits=[c.sV, (c.sG, None)], inc=c.sA)
            p.step("gpsimd", lambda e, mask=mask: e.tensor_tensor(
                probsm[:], probs[:], mask[:], ALU.mult),
                waits=[c.sA, (c.sG, None), (c.sP, None), c.sD], inc=c.sG)
            p.step("vector", lambda e: e.reduce_sum(st2[:, 1:2], probsm[:],
                                                    axis=AX.X),
                   waits=[c.sG, (c.sV, None)], inc=c.sV)
            p.step("vector", lambda e: e.tensor_scalar_max(
                st2[:, 1:2], st2[:, 1:2], 1e-30), waits=[c.sV], inc=c.sV)
            p.step("vector", lambda e: e.reciprocal(st2[:, 2:3], st2[:, 1:2]),
                   waits=[c.sV], inc=c.sV)
            p.step("vector", lambda e: e.tensor_tensor(
                st2[:, 2:3], st2[:, 2:3], M["rowk"][:, 0:1], ALU.mult),
                waits=[c.sV], inc=c.sV)
            p.step("vector", lambda e: e.tensor_scalar_mul(
                probsm[:], probsm[:], st2[:, 2:3]),
                waits=[c.sV, c.sG], inc=c.sV)
            p.step("tensor", lambda e: (
                e.transpose(trb[:, 0:128], probsm[:, 0:128], c.idt[:]),
                e.transpose(trb[:, 128:256], probsm[:, 128:256], c.idt[:]))[-1],
                waits=[c.sV, (c.sA, None)], inc=c.sP)
            p.step("scalar", lambda e: e.activation(probsT[:], trb[:, 0:256],
                                                    AF.Identity),
                   waits=[c.sP], inc=c.sA)
            lp = locps[h // 8]
            col = (h % 8) * 64
            p.step("tensor", lambda e, h=h, lp=lp, tl=tl, col=col: (
                e.matmul(lp[:, col:col + 64],
                         probsT[:, 0:128],
                         Kh["v"][:, tl * 1024 + h * 64: tl * 1024 + h * 64 + 64],
                         start=True, stop=False),
                e.matmul(lp[:, col:col + 64],
                         probsT[:, 128:256],
                         Kh["v"][:, (tl + 1) * 1024 + h * 64:
                                 (tl + 1) * 1024 + h * 64 + 64],
                         start=False, stop=True))[-1],
                waits=[c.sA, (c.sV, None)], inc=c.sP)
    for b in range(2):
        p.step("vector", lambda e, b=b: e.tensor_copy(
            loc[:, b * 512:(b + 1) * 512], c.PS[4 + b][:, 0:512]),
            waits=[c.sP, (c.sV, None), (c.sO, None)], inc=c.sV)

    psr = c.PS[2]
    p.step("tensor", lambda e: (
        mla_qkv_mms(c, e, psr[:, 0:256],
                    lambda kk: c.wrq[:, kk * 256:(kk + 1) * 256], t0),
        e.matmul(psr[:, 0:256], c.onr[0:1, :], M["rqb"][0:1, 0:256],
                 start=False, stop=True))[-1],
        waits=[(c.sV, None), (c.sA, None)], inc=c.sP)
    p.step("vector", lambda e: e.tensor_scalar_mul(rqs[:], psr[:, 0:256], 0.125),
           waits=[c.sP, (c.sV, None)], inc=c.sV)
    p.step("tensor", lambda e: (
        e.transpose(trb[:, 0:128], rqs[:, 0:128], c.idt[:]),
        e.transpose(trb[:, 128:256], rqs[:, 128:256], c.idt[:]))[-1],
        waits=[c.sV, (c.sA, None)], inc=c.sP)
    p.step("vector", lambda e: e.tensor_copy(rqT[:], trb[:, 0:256]),
           waits=[c.sP, (c.sV, None)], inc=c.sV)
    latps = c.PS[4]
    for lh in range(4):
        qa = rqT[(lh % 2) * 64:(lh % 2) * 64 + 64, (lh // 2) * 128:(lh // 2) * 128 + 128]
        ka = M["rkT"][(lh % 2) * 64:(lh % 2) * 64 + 64, (lh // 2) * 64:(lh // 2) * 64 + 64]
        tp = (64 * (lh % 2), 0)
        lbank = c.PS[2] if lh % 2 == 0 else c.PS[3]
        p.step("tensor", lambda e, qa=qa, ka=ka, tp=tp, lh=lh, lbank=lbank:
               e.matmul(lbank[:, 256 + (lh // 2) * 64: 256 + (lh // 2) * 64 + 64],
                        qa, ka, start=True, stop=True, tile_position=tp),
               waits=[(c.sV, None)], inc=c.sP)
        sc = lbank[:, 256 + (lh // 2) * 64: 256 + (lh // 2) * 64 + 64]
        p.step("vector", lambda e, sc=sc: e.reduce_max(
            st2[:, 0:1], sc, axis=AX.X, negate=True),
            waits=[c.sP, (c.sV, None)], inc=c.sV)
        p.step("scalar", lambda e, sc=sc: e.activation(
            lpb[:], sc, AF.Exp, bias=st2[:, 0:1]),
            waits=[c.sV, (c.sG, None)], inc=c.sA)
        p.step("gpsimd", lambda e: e.tensor_tensor(lpbm[:], lpb[:], M["lmrow"][:],
                                                   ALU.mult),
               waits=[c.sA, (c.sG, None), (c.sP, None), c.sD], inc=c.sG)
        p.step("vector", lambda e: e.reduce_sum(st2[:, 1:2], lpbm[:], axis=AX.X),
               waits=[c.sG, (c.sV, None)], inc=c.sV)
        p.step("vector", lambda e: e.tensor_scalar_max(
            st2[:, 1:2], st2[:, 1:2], 1e-30), waits=[c.sV], inc=c.sV)
        p.step("vector", lambda e: e.reciprocal(st2[:, 2:3], st2[:, 1:2]),
               waits=[c.sV], inc=c.sV)
        p.step("vector", lambda e: e.tensor_tensor(
            st2[:, 2:3], st2[:, 2:3], M["rowk"][:, 1:2], ALU.mult),
            waits=[c.sV], inc=c.sV)
        p.step("vector", lambda e: e.tensor_scalar_mul(
            lpbm[:], lpbm[:], st2[:, 2:3]), waits=[c.sV, c.sG], inc=c.sV)
        p.step("tensor", lambda e: e.transpose(trb[0:64, 0:128], lpbm[:], c.idt[:]),
               waits=[c.sV, (c.sA, None)], inc=c.sP)
        p.step("scalar", lambda e: e.activation(lpbT[0:64, :], trb[0:64, 0:128],
                                                AF.Identity),
               waits=[c.sP], inc=c.sA)
        p.step("tensor", lambda e, lh=lh: e.matmul(
            latps[:, lh * 64:(lh + 1) * 64], lpbT[0:64, :],
            M["rkv"][0:64, 256 + lh * 64:256 + (lh + 1) * 64],
            start=True, stop=True),
            waits=[c.sA, (c.sV, None)], inc=c.sP)
    p.step("vector", lambda e: e.tensor_copy(rqs[:], latps[:, 0:256]),
           waits=[c.sP, (c.sV, None)], inc=c.sV)
    p.step("tensor", lambda e: (
        e.transpose(trb[:, 0:128], rqs[:, 0:128], c.idt[:]),
        e.transpose(trb[:, 128:256], rqs[:, 128:256], c.idt[:]))[-1],
        waits=[c.sV, (c.sA, None)], inc=c.sP)
    p.step("vector", lambda e: e.tensor_copy(rqT[:], trb[:, 0:256]),
           waits=[c.sP, (c.sV, None)], inc=c.sV)
    p.step("tensor", lambda e: (
        [e.matmul((c.PS[2][:, 0:512] if nn == 0 else c.PS[3][:, 0:512]),
                  rqT[:, ch * 128:(ch + 1) * 128],
                  c.wrout[0:128, ch * 1024 + nn * 512: ch * 1024 + (nn + 1) * 512],
                  start=(ch == 0), stop=False)
         for nn in range(2) for ch in range(2)] +
        [e.matmul((c.PS[2][:, 0:512] if nn == 0 else c.PS[3][:, 0:512]),
                  c.onr[0:1, :], c.routb[0:1, nn * 512:(nn + 1) * 512],
                  start=False, stop=True) for nn in range(2)])[-1],
        waits=[(c.sV, None), (c.sA, None)], inc=c.sP)
    p.step("scalar", lambda e: e.activation(rem[:, 0:512], c.PS[2][:, 0:512],
                                            AF.Identity),
           waits=[c.sP], inc=c.sA)
    p.step("scalar", lambda e: e.activation(rem[:, 512:1024], c.PS[3][:, 0:512],
                                            AF.Identity),
           inc=c.sA)

    p.step("vector", lambda e: e.tensor_tensor(
        d1[:].rearrange("p (h d) -> p h d", h=16), loc[:].rearrange("p (h d) -> p h d", h=16),
        bass.AP(M["glb"].tensor, M["glb"].offset,
                [[M["glb"].ap[0][0], 128], [0, 16], [1, 64]]), ALU.mult),
        waits=[c.sA, (c.sV, None)], inc=c.sV)
    p.step("vector", lambda e: e.reduce_sum(
        glog[:, 0:16], d1[:].rearrange("p (h d) -> p h d", h=16), axis=AX.X),
        waits=[c.sV], inc=c.sV)
    p.step("vector", lambda e: e.tensor_tensor(
        d1[:].rearrange("p (h d) -> p h d", h=16), rem[:].rearrange("p (h d) -> p h d", h=16),
        bass.AP(M["grb"].tensor, M["grb"].offset,
                [[M["grb"].ap[0][0], 128], [0, 16], [1, 64]]), ALU.mult),
        waits=[c.sV], inc=c.sV)
    p.step("vector", lambda e: e.reduce_sum(
        glog[:, 16:32], d1[:].rearrange("p (h d) -> p h d", h=16), axis=AX.X),
        waits=[c.sV], inc=c.sV)
    p.step("vector", lambda e: e.tensor_tensor(
        glog[:, 32:48], glog[:, 0:16], glog[:, 16:32], ALU.add),
        waits=[c.sV], inc=c.sV)
    p.step("scalar", lambda e: e.activation(gate[:], glog[:, 32:48], AF.Sigmoid),
           waits=[c.sV], inc=c.sA)
    p.step("vector", lambda e: e.tensor_tensor(d1[:], loc[:], rem[:], ALU.subtract),
           waits=[c.sA, (c.sV, None)], inc=c.sV)
    for h in range(16):
        p.step("vector", lambda e, h=h: e.tensor_scalar_mul(
            d1[:, h * 64:(h + 1) * 64], d1[:, h * 64:(h + 1) * 64],
            gate[:, h:h + 1]),
            waits=[(c.sV, None)], inc=c.sV)
    p.step("vector", lambda e: e.tensor_tensor(d1[:], d1[:], rem[:], ALU.add),
           waits=[c.sV], inc=c.sV)
    p.step("tensor", lambda e: (
        [e.transpose(trb[:, f * 128:(f + 1) * 128],
                     d1[:, f * 128:(f + 1) * 128], c.idt[:]) for f in range(8)])[-1],
        waits=[c.sV, (c.sA, None)], inc=c.sP)
    p.step("vector", lambda e: e.tensor_copy(mstage[:], trb[:, 0:1024]),
           waits=[c.sP, (c.sV, None)], inc=c.sV)
    for kk in range(KD):
        p.step("sync", lambda e, kk=kk, t0=t0: e.dma_start(
            out=c.mixd[kk, :, t0:t0 + 128],
            in_=mstage[:, kk * 128:(kk + 1) * 128]),
            waits=[(c.sV, None)] if kk == 0 else [], inc=c.sO, k=16)
    c.p.flush()
    ar.reset(m0)


def emit_mla_out(c, M):
    p = c.p
    io = c.io
    ar = c.ar
    m0 = ar.mark()
    wout = ar.bf16(8 * 1024)
    xold = ar.f32(KD * 512)
    bcol = ar.f32(8)
    mxb = [ar.bf16(8 * 512) for _ in range(2)]
    for kk in range(KD):
        p.step("gpsimd", lambda e, kk=kk: e.dma_start(
            out=wout[:, kk * 1024:(kk + 1) * 1024],
            in_=io["aout_w"][kk * 128:(kk + 1) * 128, :]),
            waits=[(c.sP, None)] if kk == 0 else [], inc=c.sW, k=16)
    for m in range(KD):
        p.step("sync", lambda e, m=m: e.dma_start(
            out=bcol[:, m:m + 1],
            in_=io["aout_b"][m * 128:(m + 1) * 128].unsqueeze(1)),
            waits=[(c.sV, None)], inc=c.sD, k=16)
    for tc in range(4):
        t0 = tc * 512
        mx = mxb[tc % 2]
        for k in range(KD):
            p.step("sync", lambda e, k=k, t0=t0: e.dma_start(
                out=xold[:, k * 512:(k + 1) * 512],
                in_=c.xmod[k, :, t0:t0 + 512]),
                waits=[(c.sV, None), (c.sO, None)], inc=c.sD, k=16)
        for kk in range(KD):
            p.step("sync", lambda e, kk=kk, t0=t0, mx=mx: e.dma_start(
                out=mx[:, kk * 512:(kk + 1) * 512],
                in_=c.mixd[kk, :, t0:t0 + 512]),
                waits=[(c.sP, None), (c.sO, None)] if kk == 0 else [],
                inc=c.sD, k=16)
        for m in range(KD):
            ps = c.PS[m % 2]
            p.step("tensor", lambda e, m=m, t0=t0, ps=ps, mx=mx: (
                [e.matmul(ps[:, 0:512], wout[:, kk * 1024 + m * 128:
                                             kk * 1024 + (m + 1) * 128],
                          mx[:, kk * 512:(kk + 1) * 512],
                          start=(kk == 0), stop=(kk == KD - 1))
                 for kk in range(KD)])[-1],
                waits=[c.sW, c.sD, (c.sV, None)], inc=c.sP)
            p.step("vector", lambda e, m=m, ps=ps: e.scalar_tensor_tensor(
                xold[:, m * 512:(m + 1) * 512], ps[:, 0:512], bcol[:, m:m + 1],
                xold[:, m * 512:(m + 1) * 512], ALU.add, ALU.add),
                waits=[c.sP, (c.sV, None)], inc=c.sV)
        for k in range(KD):
            p.step("sync", lambda e, k=k, t0=t0: e.dma_start(
                out=c.xmod[k, :, t0:t0 + 512],
                in_=xold[:, k * 512:(k + 1) * 512]),
                waits=[c.sV], inc=c.sO, k=16)
        emit_ln_chunk(c, lambda k: xold[:, k * 512:(k + 1) * 512], t0)
    c.p.flush()
    ar.reset(m0)


def kernel(**inputs):
    x = np.ascontiguousarray(np.asarray(inputs["L0_attn_x"], dtype=np.float32))
    w = {k: np.ascontiguousarray(np.asarray(v, dtype=np.float32))
         for k, v in inputs.items()}

    nc = build_program()

    in_maps = []
    for core in range(NCORE):
        b, h = core // 2, core % 2
        start = h * T
        csq, snq, csk, snk = _rotary_tables(start)
        fband, fspec = _ffn_masks(h)
        aband, at0, arow = _mla_masks(h)
        lmask, lrow = _lat_masks(start, h)
        m = {
            "x_in": x[b, start:start + T, :],
            "aqkv_w": w["L0_attn_qkv_w"], "aqkv_b": w["L0_attn_qkv_b"],
            "arq_w": w["L0_attn_rq_w"], "arq_b": w["L0_attn_rq_b"],
            "arkv_w": w["L0_attn_rkv_w"], "arkv_b": w["L0_attn_rkv_b"],
            "arout_w": w["L0_attn_rout_w"], "arout_b": w["L0_attn_rout_b"],
            "aout_w": w["L0_attn_out_w"], "aout_b": w["L0_attn_out_b"],
            "agl_w": w["L0_attn_gl_w"], "agr_w": w["L0_attn_gr_w"],
            "fwqkv": w["L0_ffn_wqkv"], "fq_scale": w["L0_ffn_q_scale"],
            "fk_scale": w["L0_ffn_k_scale"], "fwo": w["L0_ffn_wo"],
            "fln_g": w["L0_ffn_ln_g"], "fln_b": w["L0_ffn_ln_b"],
            "fw1": w["L0_ffn_w1"], "fb1": w["L0_ffn_b1"],
            "fw2": w["L0_ffn_w2"], "fb2": w["L0_ffn_b2"],
            "csq": csq, "snq": snq, "csk": csk, "snk": snk,
            "fband01": fband, "fspec01": fspec,
            "aband01": aband, "at001": at0, "arowkeep": arow,
            "lmask01": lmask, "lrowkeep": lrow,
            "halo_scale": np.full((128, 1), float(h), np.float32),
        }
        in_maps.append(m)

    res = run_bass_kernel_spmd(nc, in_maps, core_ids=list(range(NCORE)))
    out = np.empty((B, S, D), np.float32)
    for core in range(NCORE):
        b, h = core // 2, core % 2
        out[b, h * T:(h + 1) * T, :] = res.results[core]["out"]
    return out


# version 05
# speedup vs baseline: 1.2049x; 1.2049x over previous
import numpy as np
import ml_dtypes
import concourse.bass as bass
import concourse.mybir as mybir
from concourse.bass_utils import run_bass_kernel_spmd
from contextlib import ExitStack

F32 = mybir.dt.float32
BF16 = mybir.dt.bfloat16
AF = mybir.ActivationFunctionType
ALU = mybir.AluOpType
AX = mybir.AxisListType

BF = ml_dtypes.bfloat16

B, S, D = 4, 4096, 1024
NCORE = 8
T = S // 2
NT = T // 128
KD = D // 128

F_HEADS, F_HD = 8, 64
F_INNER = 512
F_WIN = 512
F_HALO = 640
F_HID = 2730
HID_CH = [128] * 21 + [42]
F_QKSCALE = 8.0

A_HEADS, A_HD = 16, 64
A_HALO = 128
LAT_N, LAT_D, LAT_H, LAT_HD = 64, 256, 4, 64
CHUNK = 64

XHB = 136

FF_KT = 512 * F_HALO
FF_V = F_HALO * 512
FF_XB = D * 132
FF_TOT = FF_KT + FF_V + FF_XB
AA_KT = D * A_HALO
AA_V = A_HALO * D
AA_PL = D * 32
AA_TOT = AA_KT + AA_V + AA_PL

RG_PAIRS = [[0, 1], [2, 3], [4, 5], [6, 7]]
NGROUP = 2
STAGES = ["ffn0", "ffn1", "b_ffn", "mla", "ffn0", "ffn1"]
FFN_PARTS = ("attn", "wo", "mlp")
ATTN_CUT = None
SKIP_FINAL = False
LAST_C = None

ENGS = ("sync", "vector", "scalar", "tensor", "gpsimd")


def _rotary_tables(start):
    n = S
    pos = np.arange(start - 128, start + T, dtype=np.float64)
    half = np.arange(0, F_HD, 2, dtype=np.float64)
    inv_freq = 1.0 / (10000.0 ** (half / F_HD))
    freqs = pos[:, None] * inv_freq[None, :]
    base = (half + 0.4 * F_HD) / (1.4 * F_HD)
    power = (pos - n // 2) / float(F_WIN // 2)
    scale = base[None, :] ** power[:, None]
    cos, sin = np.cos(freqs), np.sin(freqs)
    csq = np.concatenate([cos * scale, cos * scale], 1)
    snq = np.concatenate([-sin * scale, sin * scale], 1)
    csk = np.concatenate([cos / scale, cos / scale], 1)
    snk = np.concatenate([-sin / scale, sin / scale], 1)
    bad = pos < 0
    for t in (csq, snq, csk, snk):
        t[bad] = 0.0
    return [t.astype(BF) for t in (csq, snq, csk, snk)]


def _ffn_masks(h):
    i = np.arange(128)[:, None]
    c = np.arange(640)[None, :]
    band = ((c >= i) & (c <= i + 512)).astype(np.float32)
    spec = np.zeros((5, 128, 640), np.float32)
    for idx, t0 in enumerate([-128, 0, 128, 256, 384]):
        m = band.copy()
        if h == 0:
            m = np.where((t0 - 512 + c) < 0, 0.0, m)
        spec[idx] = m
    return band.astype(BF), spec.astype(BF)


def _mla_masks(h):
    i = np.arange(128)[:, None]
    c = np.arange(256)[None, :]
    band = ((c >= i) & (c <= i + 127)).astype(np.float32)
    t0 = band.copy()
    rowkeep = np.ones((T, 1), np.float32)
    if h == 0:
        t0 = np.where((c - 128) < 0, 0.0, t0)
        rowkeep[0, 0] = 0.0
    return band.astype(BF), t0.astype(BF), rowkeep


def _lat_masks(start, h):
    cid = (start + np.arange(T)) // CHUNK
    j = np.arange(LAT_N)[None, :]
    mask = (j < cid[:, None]).astype(np.float32)
    rowkeep = (cid > 0).astype(np.float32)[:, None]
    return mask.astype(BF), rowkeep


class Sem:
    def __init__(self, nc, es, name):
        self.h = es.enter_context(nc.semaphore(name))
        self.n = 0
        self.name = name
        self.last_waited = 0
        self.synced = 0


class Arena:
    def __init__(self, nc, es, name, cols):
        self.t = es.enter_context(nc.sbuf_tensor(name, [128, cols], F32))
        self.cols = cols
        self.top = 0

    def f32(self, cols):
        base = self.top
        assert base + cols <= self.cols, \
            f"arena overflow {base}+{cols}>{self.cols}"
        self.top += cols
        return self.t[:, base:base + cols]

    def bf16(self, cols):
        assert cols % 2 == 0
        return self.f32(cols // 2).bitcast(BF16)

    def mark(self):
        return self.top

    def reset(self, mark):
        self.top = mark


class Prog:

    def __init__(self, nc):
        self.nc = nc
        self.tapes = {e: [] for e in ENGS}
        self.last_dma = {}

    def step(self, eng, fn, waits=(), inc=None, k=1):
        w = []
        for item in waits:
            if isinstance(item, tuple):
                s, v = item
            else:
                s, v = item, None
            v = s.n if v is None else v
            s.last_waited = max(s.last_waited, v)
            w.append((s.h, v))
        if inc is not None and k == 16:
            if inc.last_waited > inc.synced:
                w.insert(0, (inc.h, inc.n))
                inc.synced = inc.n
        dd = {}
        for h, v in w:
            dd[h] = max(dd.get(h, 0), v)
        w = list(dd.items())
        self.tapes[eng].append((w, fn, None if inc is None else inc.h, k))
        if inc is not None:
            inc.n += k

    def flush(self):
        tapes = {e: t for e, t in self.tapes.items() if t}
        self.tapes = {e: [] for e in ENGS}
        if not tapes:
            return

        def runner(tape):
            def run(eng):
                for w, fn, semh, k in tape:
                    for h, v in w:
                        eng.wait_ge(h, v)
                    ins = fn(eng)
                    if semh is not None:
                        ins.then_inc(semh, k)
            return run

        with self.nc.Block() as blk:
            for e, tape in tapes.items():
                getattr(blk, e)(runner(tape))

    def barrier(self):
        c = self.ctx
        self.step("sync", lambda e: e.nop(),
                  waits=[(c.sD, None), (c.sD2, None), (c.sO, None)])
        self.step("gpsimd", lambda e: e.nop(), waits=[(c.sW, None)])
        self.flush()
        self.nc.all_engine_barrier()
        c.fence_banks()


FENCE_KEYS = ["psk0", "psk1", "psv0", "psv1", "ps7", "set0", "set1", "trb",
              "trbx", "probsT", "probs0", "probs1", "probsm0", "probsm1",
              "Akproc", "Bqproc", "wob0", "wob1", "wops0", "wops1", "wopsh",
              "wp0", "wp1", "psa0", "psa1", "psg0", "psg1", "agbuf", "ggbuf",
              "w2banks"]


class Ctx:
    def fence_banks(self):
        g = [(self.sV, self.sV.n), (self.sA, self.sA.n), (self.sG, self.sG.n),
             (self.sP, self.sP.n)]
        for k in FENCE_KEYS:
            self.bg[k] = list(g)

    def bgate(self, key):
        return list(self.bg.get(key, []))

    def bset(self, key, sem):
        self.bg[key] = [(sem, sem.n)]

    def badd(self, key, sem):
        self.bg.setdefault(key, []).append((sem, sem.n))


def build_program():
    nc = bass.Bass("TRN2", target_bir_lowering=False)
    c = Ctx()
    c.nc = nc

    def din(name, shape, dt=F32, kind="ExternalInput"):
        return nc.dram_tensor(name, list(shape), dt, kind=kind).ap()

    c.io = io = {}
    io["x_in"] = din("x_in", [T, D])
    for nm, shp in [
        ("aqkv_w", [D, 3 * D]), ("aqkv_b", [3 * D]),
        ("arq_w", [D, LAT_D]), ("arq_b", [LAT_D]),
        ("arkv_w", [D, 2 * LAT_D]), ("arkv_b", [2 * LAT_D]),
        ("arout_w", [LAT_D, D]), ("arout_b", [D]),
        ("aout_w", [D, D]), ("aout_b", [D]),
        ("agl_w", [A_HD, 1]), ("agr_w", [A_HD, 1]),
        ("fwqkv", [2, D, 3 * F_INNER]), ("fq_scale", [2, F_HD]),
        ("fk_scale", [2, F_HD]), ("fwo", [2, F_INNER, D]),
        ("fln_g", [2, D]), ("fln_b", [2, D]),
        ("fw1", [2, D, 2 * F_HID]), ("fb1", [2, 2 * F_HID]),
        ("fw2", [2, F_HID, D]), ("fb2", [2, D]),
    ]:
        io[nm] = din(nm, shp)
    io["csq"] = din("csq", [T + 128, F_HD], BF16)
    io["snq"] = din("snq", [T + 128, F_HD], BF16)
    io["csk"] = din("csk", [T + 128, F_HD], BF16)
    io["snk"] = din("snk", [T + 128, F_HD], BF16)
    io["fband01"] = din("fband01", [128, 640], BF16)
    io["fspec01"] = din("fspec01", [5, 128, 640], BF16)
    io["aband01"] = din("aband01", [128, 256], BF16)
    io["at001"] = din("at001", [128, 256], BF16)
    io["arowkeep"] = din("arowkeep", [T, 1])
    io["lmask01"] = din("lmask01", [T, LAT_N], BF16)
    io["lrowkeep"] = din("lrowkeep", [T, 1])
    io["halo_scale"] = din("halo_scale", [128, 1])
    io["out"] = din("out", [T, D], kind="ExternalOutput")

    c.xmod = din("xmod", [KD, 128, T], kind="Internal")
    c.mixd = din("mixd", [KD, 128, T], BF16, kind="Internal")
    c.ccf_in = din("ccf_in", [1, FF_TOT], BF16, kind="Internal")
    c.ccf_out = din("ccf_out", [NGROUP, FF_TOT], BF16, kind="Internal")
    c.cca_in = din("cca_in", [1, AA_TOT], BF16, kind="Internal")
    c.cca_out = din("cca_out", [NGROUP, AA_TOT], BF16, kind="Internal")

    id_d = nc.inline_tensor(np.eye(128).astype(BF), name="id128").ap()
    onesc_d = nc.inline_tensor(np.ones((128, 1), BF), name="onesc").ap()
    onesr_d = nc.inline_tensor(np.ones((1, 128), BF), name="onesr").ap()
    eps_d = nc.inline_tensor(np.full((1, 512), 1e-5, np.float32),
                             name="epsrow").ap()

    es = ExitStack()
    c.es = es
    c.xm = es.enter_context(nc.sbuf_tensor("xm", [128, KD * T], F32))
    c.xb = es.enter_context(nc.sbuf_tensor("xb", [128, KD * (T + XHB)], BF16))
    c.idt = es.enter_context(nc.sbuf_tensor("idt", [128, 128], BF16))
    c.onc = es.enter_context(nc.sbuf_tensor("onc", [128, 1], BF16))
    c.onr = es.enter_context(nc.sbuf_tensor("onr", [1, 128], BF16))
    c.hsc = es.enter_context(nc.sbuf_tensor("hsc", [128, 1], F32))
    c.epsr_t = es.enter_context(nc.sbuf_tensor("epsr", [1, 512], F32))
    c.epsr = c.epsr_t[0:1, 0:512]
    c.fband = es.enter_context(nc.sbuf_tensor("fband", [128, 640], BF16))
    c.aband = es.enter_context(nc.sbuf_tensor("aband", [128, 256], BF16))
    c.at0 = es.enter_context(nc.sbuf_tensor("at0", [128, 256], BF16))
    c.ar = Arena(nc, es, "arena", 26600)
    c.PS = [es.enter_context(nc.psum_tensor(f"ps{i}", [128, 512], F32))
            for i in range(8)]

    for nm in ["sD", "sW", "sP", "sV", "sA", "sG", "sC", "sO", "sP2", "sV2",
               "sD2"]:
        setattr(c, nm, Sem(nc, es, nm))

    c.p = Prog(nc)
    c.p.ctx = c
    c.bg = {}
    c.consts = dict(id_d=id_d, onesc_d=onesc_d, onesr_d=onesr_d, eps_d=eps_d)

    program(c)
    c.p.flush()
    es.close()
    global LAST_C
    LAST_C = c
    return nc


def xm_v(c, k, t0, tn):
    return c.xm[:, k * T + t0: k * T + t0 + tn]


def xb_v(c, k, t0, tn):
    col = k * (T + XHB) + XHB + t0
    return c.xb[:, col: col + tn]


def emit_prologue(c):
    p = c.p
    io = c.io
    cn = c.consts
    for dst, src in [(c.idt[:], cn["id_d"]), (c.onc[:], cn["onesc_d"]),
                     (c.onr[:], cn["onesr_d"]), (c.epsr, cn["eps_d"]),
                     (c.hsc[:], io["halo_scale"]), (c.fband[:], io["fband01"]),
                     (c.aband[:], io["aband01"]), (c.at0[:], io["at001"])]:
        p.step("sync", lambda e, d=dst, s=src: e.dma_start(out=d, in_=s),
               inc=c.sD, k=16)

    ar = c.ar
    m0 = ar.mark()
    xtok = [ar.f32(1024) for _ in range(2)]
    hi = ar.bf16(1024)
    lo = ar.bf16(1024)
    hswap = ar.bf16(128)
    trb = c.PS[6].bitcast(BF16)
    for tt in range(NT):
        t0 = tt * 128
        xt = xtok[tt % 2]
        p.step("sync", lambda e, xt=xt, t0=t0:
               e.dma_start(out=xt, in_=io["x_in"][t0:t0 + 128, :]),
               waits=[(c.sV, None)], inc=c.sD, k=16)
        p.step("vector", lambda e, xt=xt: e.tensor_copy(hi[:], xt),
               waits=[c.sD, c.sP], inc=c.sV)
        p.step("vector", lambda e, xt=xt:
               e.tensor_tensor(lo[:], xt, hi[:], ALU.subtract),
               waits=[c.sV], inc=c.sV)
        for k in range(KD):
            p.step("tensor", lambda e, k=k: (
                e.transpose(trb[:, 0:128] if False else trb[:, (k % 4) * 256:(k % 4) * 256 + 128],
                            hi[:, k * 128:(k + 1) * 128], c.idt[:]),
                e.transpose(trb[:, (k % 4) * 256 + 128:(k % 4) * 256 + 256],
                            lo[:, k * 128:(k + 1) * 128], c.idt[:]))[-1],
                waits=([c.sV] if k == 0 else [(c.sV2, c.sV2.n - 2)]),
                inc=c.sP)
            p.step("vector", lambda e, k=k:
                   e.tensor_copy(hswap[:], trb[:, (k % 4) * 256:(k % 4) * 256 + 128]),
                   waits=[c.sP, (c.sV2, None)], inc=c.sV2)
            p.step("vector", lambda e, k=k, t0=t0:
                   e.tensor_tensor(xm_v(c, k, t0, 128), hswap[:],
                                   trb[:, (k % 4) * 256 + 128:(k % 4) * 256 + 256],
                                   ALU.add),
                   waits=[c.sV2], inc=c.sV2)
        for k in range(KD):
            p.step("sync", lambda e, k=k, t0=t0:
                   e.dma_start(out=c.xmod[k, :, t0:t0 + 128],
                               in_=xm_v(c, k, t0, 128)),
                   waits=[c.sV2], inc=c.sO, k=16)
    p.step("sync", lambda e: e.nop(), waits=[c.sD, c.sO])
    p.barrier()
    ar.reset(m0)


def emit_ln_chunk(c, src_ap, t0, dst_master=True, write_shadow=True):
    p = c.p
    ar = c.ar
    m0 = ar.mark()
    sq = [ar.bf16(512) for _ in range(2)]
    xcast = [ar.bf16(512) for _ in range(2)]
    rowf = ar.f32(6 * 512)
    R = {nm: rowf[0:1, i * 512:(i + 1) * 512]
         for i, nm in enumerate(["m", "ms", "t", "r1", "r2", "x"])}
    rowb = ar.bf16(4 * 512)
    RB = {nm: rowb[0:1, i * 512:(i + 1) * 512]
          for i, nm in enumerate(["r1h", "r1l", "r2h", "r2l"])}
    b1s = ar.f32(512)
    b2s = ar.f32(512)
    tmp = ar.f32(512)

    ps_sum = c.PS[4][0:1, 0:512]
    ps_ssq = c.PS[5][0:1, 0:512]
    ps_b1 = c.PS[2][:, 0:512]
    ps_b2 = c.PS[3][:, 0:512]

    for k in range(KD):
        p.step("scalar", lambda e, k=k:
               e.activation(sq[k % 2][:], src_ap(k), AF.Square),
               waits=[c.sV] + ([] if k < 2 else [(c.sP, c.sP.n - 2)]),
               inc=c.sA)
        p.step("vector", lambda e, k=k:
               e.tensor_copy(xcast[k % 2][:], src_ap(k)),
               waits=[(c.sV, None)] if k < 2 else [(c.sP, c.sP.n - 1),
                                                   (c.sV, None)],
               inc=c.sV)
        p.step("tensor", lambda e, k=k:
               e.matmul(ps_sum, c.onc[:, 0:1], xcast[k % 2][:],
                        start=(k == 0), stop=(k == KD - 1)),
               waits=[c.sV], inc=c.sP)
        p.step("tensor", lambda e, k=k:
               e.matmul(ps_ssq, c.onc[:, 0:1], sq[k % 2][:],
                        start=(k == 0), stop=(k == KD - 1)),
               waits=[c.sA], inc=c.sP)

    inv = 1.0 / D
    p.step("vector", lambda e: e.tensor_scalar_mul(R["m"], ps_sum, inv),
           waits=[c.sP], inc=c.sV)
    p.step("vector", lambda e: e.tensor_scalar_mul(R["ms"], ps_ssq, inv),
           waits=[c.sV], inc=c.sV)
    p.step("vector", lambda e: e.tensor_tensor(R["t"], R["m"], R["m"], ALU.mult),
           waits=[c.sV], inc=c.sV)
    p.step("vector", lambda e:
           e.tensor_tensor(R["ms"], R["ms"], R["t"], ALU.subtract),
           waits=[c.sV], inc=c.sV)
    p.step("vector", lambda e:
           e.tensor_scalar_add(R["ms"], R["ms"], 1e-5),
           waits=[c.sV], inc=c.sV)
    p.step("scalar", lambda e:
           e.activation(R["t"], R["ms"], AF.Sqrt),
           waits=[c.sV], inc=c.sA)
    p.step("vector", lambda e: e.reciprocal(R["r1"], R["t"]),
           waits=[c.sA], inc=c.sV)
    p.step("vector", lambda e: e.tensor_tensor(R["r2"], R["m"], R["r1"], ALU.mult),
           waits=[c.sV], inc=c.sV)
    p.step("vector", lambda e: e.tensor_scalar_mul(R["r2"], R["r2"], -1.0),
           waits=[c.sV], inc=c.sV)
    for r, rh, rl in [("r1", "r1h", "r1l"), ("r2", "r2h", "r2l")]:
        p.step("vector", lambda e, r=r, rh=rh: e.tensor_copy(RB[rh], R[r]),
               waits=[c.sV], inc=c.sV)
        p.step("vector", lambda e, r=r, rh=rh, rl=rl:
               e.tensor_tensor(R["x"], R[r], RB[rh], ALU.subtract),
               waits=[c.sV], inc=c.sV)
        p.step("vector", lambda e, rl=rl: e.tensor_copy(RB[rl], R["x"]),
               waits=[c.sV], inc=c.sV)

    p.step("tensor", lambda e: (
        e.matmul(ps_b1, c.onr[0:1, :], RB["r1h"], start=True, stop=False),
        e.matmul(ps_b1, c.onr[0:1, :], RB["r1l"], start=False, stop=True))[-1],
        waits=[c.sV, c.sA], inc=c.sP)
    p.step("tensor", lambda e: (
        e.matmul(ps_b2, c.onr[0:1, :], RB["r2h"], start=True, stop=False),
        e.matmul(ps_b2, c.onr[0:1, :], RB["r2l"], start=False, stop=True))[-1],
        waits=[c.sV, c.sA], inc=c.sP)
    p.step("scalar", lambda e: e.activation(b1s, ps_b1, AF.Identity),
           waits=[c.sP], inc=c.sA)
    p.step("scalar", lambda e: e.activation(b2s, ps_b2, AF.Identity),
           waits=[c.sP], inc=c.sA)

    for k in range(KD):
        p.step("vector", lambda e, k=k:
               e.tensor_tensor(tmp, src_ap(k), b1s, ALU.mult),
               waits=[c.sA, c.sV], inc=c.sV)
        p.step("vector", lambda e, k=k, t0=t0:
               e.tensor_tensor(xm_v(c, k, t0, 512), tmp, b2s, ALU.add),
               waits=[c.sV], inc=c.sV)
        if True:
            p.step("vector", lambda e, k=k, t0=t0:
                   e.tensor_copy(xb_v(c, k, t0, 512), xm_v(c, k, t0, 512)),
                   waits=[c.sV], inc=c.sV)
    c.p.flush()
    ar.reset(m0)


def emit_boundary(c, kind):
    p = c.p
    ar = c.ar
    if kind == "init":
        for tc in range(4):
            t0 = tc * 512
            emit_ln_chunk(c, lambda k, t0=t0: xm_v(c, k, t0, 512), t0)
        p.barrier()
        return

    scale = 0.5 if kind in ("final", "mid_ffn") else 1.0
    assert kind in ("final", "mid_ffn", "mid_mla")
    m0 = ar.mark()
    xnew = ar.f32(KD * 512)
    xold = ar.f32(KD * 512)
    for tc in range(4):
        t0 = tc * 512
        for k in range(KD):
            p.step("sync", lambda e, k=k, t0=t0:
                   e.dma_start(out=xold[:, k * 512:(k + 1) * 512],
                               in_=c.xmod[k, :, t0:t0 + 512]),
                   waits=[(c.sV, None)], inc=c.sD, k=16)
        for k in range(KD):
            p.step("vector", lambda e, k=k, t0=t0:
                   e.scalar_tensor_tensor(
                       xnew[:, k * 512:(k + 1) * 512],
                       xm_v(c, k, t0, 512), scale,
                       xold[:, k * 512:(k + 1) * 512], ALU.mult, ALU.add),
                   waits=[c.sD, (c.sO, None)], inc=c.sV)
        if kind != "final":
            for k in range(KD):
                p.step("sync", lambda e, k=k, t0=t0:
                       e.dma_start(out=c.xmod[k, :, t0:t0 + 512],
                                   in_=xnew[:, k * 512:(k + 1) * 512]),
                       waits=[c.sV], inc=c.sO, k=16)
            emit_ln_chunk(c, lambda k: xnew[:, k * 512:(k + 1) * 512], t0)
        else:
            emit_out_chunk(c, xnew, t0)
    p.barrier()
    ar.reset(m0)


def emit_out_chunk(c, xnew, t0):
    p = c.p
    ar = c.ar
    m0 = ar.mark()
    hi = ar.bf16(512)
    lo = ar.bf16(512)
    hswap = ar.f32(128)
    stage = [ar.f32(128) for _ in range(2)]
    trb = c.PS[6].bitcast(BF16)
    for k in range(KD):
        p.step("vector", lambda e, k=k: e.tensor_copy(hi[:], xnew[:, k * 512:(k + 1) * 512]),
               waits=[c.sV, (c.sP, None)], inc=c.sV)
        p.step("vector", lambda e, k=k:
               e.tensor_tensor(lo[:], xnew[:, k * 512:(k + 1) * 512], hi[:],
                               ALU.subtract),
               waits=[c.sV], inc=c.sV)
        for s in range(4):
            p.step("tensor", lambda e, s=s: (
                e.transpose(trb[:, s * 256:s * 256 + 128],
                            hi[:, s * 128:(s + 1) * 128], c.idt[:]),
                e.transpose(trb[:, s * 256 + 128:s * 256 + 256],
                            lo[:, s * 128:(s + 1) * 128], c.idt[:]))[-1],
                waits=[c.sV] + ([] if (k, s) == (0, 0) else [(c.sV2, None)]),
                inc=c.sP)
            p.step("vector", lambda e, s=s:
                   e.tensor_copy(hswap[:], trb[:, s * 256:s * 256 + 128]),
                   waits=[c.sP], inc=c.sV2)
            p.step("vector", lambda e, s=s:
                   e.tensor_tensor(stage[s % 2][:], hswap[:],
                                   trb[:, s * 256 + 128:s * 256 + 256], ALU.add),
                   waits=[c.sV2, (c.sO, None)], inc=c.sV2)
            p.step("sync", lambda e, s=s, k=k, t0=t0:
                   e.dma_start(out=c.io["out"][t0 + s * 128:t0 + (s + 1) * 128,
                                               k * 128:(k + 1) * 128],
                               in_=stage[s % 2][:]),
                   waits=[c.sV2], inc=c.sO, k=16)
    c.p.flush()
    ar.reset(m0)


def rot_view(ap):
    rs = ap.ap[0][0]
    return bass.AP(ap.tensor, ap.offset + 32, [[rs, 128], [64, 8], [-32, 2], [1, 32]])


def head_view(ap):
    rs = ap.ap[0][0]
    return bass.AP(ap.tensor, ap.offset, [[rs, 128], [64, 8], [32, 2], [1, 32]])


def rep_view(ap64):
    rs = ap64.ap[0][0]
    return bass.AP(ap64.tensor, ap64.offset, [[rs, 128], [0, 8], [32, 2], [1, 32]])


def row_swap_view(row64):
    return bass.AP(row64.tensor, row64.offset + 32, [[row64.ap[0][0], 1], [-32, 2], [1, 32]])


def ffn_qkv_mms(c, e, ps, wsl, t0):
    ins = None
    for kk in range(KD):
        src = xb_v(c, kk, t0 if kk < 4 else t0 - 1, 128)
        ins = e.matmul(ps, src, wsl(kk), start=(kk == 0), stop=(kk == KD - 1))
    return ins


def emit_ffn_kv_tile(c, l, tt, A, wait_halo=False):
    p = c.p
    t0 = tt * 128
    psk = c.PS[tt % 2]
    psv = c.PS[2 + tt % 2]
    trb = c.PS[6].bitcast(BF16)

    kw = [(c.sW, None)] + c.bg.pop(f"psk{tt % 2}", [])
    if wait_halo:
        kw.append((c.sV, c.haloV))
    p.step("tensor", lambda e: ffn_qkv_mms(
        c, e, psk[:, 0:512], lambda kk: A["wkv"][:, kk * 1024:kk * 1024 + 512],
        t0),
        waits=kw, inc=c.sP)
    p.step("tensor", lambda e: ffn_qkv_mms(
        c, e, psv[:, 0:512], lambda kk: A["wkv"][:, kk * 1024 + 512:kk * 1024 + 1024],
        t0),
        waits=c.bg.pop(f"psv{tt % 2}", []), inc=c.sP)
    vt = tt + 5
    p.step("scalar", lambda e: e.activation(
        A["v"][:, vt * 512:(vt + 1) * 512], psv[:, 0:512], AF.Identity),
        waits=[c.sP], inc=c.sA)
    c.bset(f"psv{tt % 2}", c.sA)

    kW = c.sP.n - 1
    p.step("scalar", lambda e: e.activation(A["sq"][:], psk[:, 0:512], AF.Square),
           waits=[(c.sP, kW)] + c.bg.pop("Akproc", []), inc=c.sA)
    p.step("vector", lambda e: e.reduce_sum(
        A["ss"][:, 0:8], A["sq"][:].rearrange("p (h d) -> p h d", h=8), axis=AX.X),
        waits=[c.sA, (c.sV, None)], inc=c.sV)
    p.step("scalar", lambda e: e.activation(A["sr"][:, 0:8], A["ss"][:, 0:8], AF.Sqrt),
           waits=[c.sV], inc=c.sA)
    p.step("vector", lambda e: e.tensor_scalar_max(A["sr"][:, 0:8], A["sr"][:, 0:8], 1e-12),
           waits=[c.sA], inc=c.sV)
    p.step("vector", lambda e: e.reciprocal(A["rstd"][:, 0:8], A["sr"][:, 0:8]),
           waits=[c.sV], inc=c.sV)
    for h in range(8):
        p.step("vector", lambda e, h=h: e.tensor_scalar_mul(
            A["kn"][:, h * 64:(h + 1) * 64], psk[:, h * 64:(h + 1) * 64],
            A["rstd"][:, h:h + 1]),
            waits=[c.sV], inc=c.sV)
    c.bg[f"psk{tt % 2}"] = [(c.sV, c.sV.n), (c.sA, c.sA.n)]
    r0 = 128 + t0
    p.step("sync", lambda e: e.dma_start(out=A["cst"][:], in_=c.io["csk"][r0:r0 + 128, :]),
           waits=[(c.sV, None)], inc=c.sD, k=16)
    p.step("sync", lambda e: e.dma_start(out=A["snt"][:], in_=c.io["snk"][r0:r0 + 128, :]),
           inc=c.sD, k=16)
    p.step("vector", lambda e: e.tensor_tensor(A["csE"][:], A["cst"][:], A["kscb"][:], ALU.mult),
           waits=[c.sD, c.sV], inc=c.sV)
    p.step("vector", lambda e: e.tensor_tensor(A["snE"][:], A["snt"][:], A["kscsb"][:], ALU.mult),
           waits=[c.sV], inc=c.sV)
    p.step("vector", lambda e: e.tensor_tensor(
        head_view(A["t1"][:]), head_view(A["kn"][:]), rep_view(A["csE"][:]), ALU.mult),
        waits=[c.sV], inc=c.sV)
    p.step("vector", lambda e: e.tensor_tensor(
        head_view(A["t2"][:]), rot_view(A["kn"][:]), rep_view(A["snE"][:]), ALU.mult),
        waits=[c.sV], inc=c.sV)
    p.step("vector", lambda e: e.tensor_tensor(A["kp"][:], A["t1"][:], A["t2"][:], ALU.add),
           waits=[c.sV], inc=c.sV)
    p.step("tensor", lambda e, t0=t0: (
        [e.transpose(trb[:, f * 128:(f + 1) * 128], A["kp"][:, f * 128:(f + 1) * 128],
                     c.idt[:]) for f in range(4)])[-1],
        waits=[c.sV] + c.bg.pop("trb", []), inc=c.sP)
    c.badd("Akproc", c.sP)
    for f in range(4):
        p.step("vector", lambda e, f=f, t0=t0: e.tensor_copy(
            c.kT_col(A, f, 640 + t0, 128), trb[:, f * 128:(f + 1) * 128]),
            waits=[c.sP, (c.sV, None)], inc=c.sV)
    c.badd("trb", c.sV)


def emit_ffn_layer(c, l):
    p = c.p
    io = c.io
    ar = c.ar
    mL = ar.mark()

    A = {}
    A["kT"] = ar.bf16(4 * 2688)
    A["v"] = ar.bf16(21 * 512)
    A["att"] = ar.bf16(4 * 2176)
    rows = ar.f32(128)

    def kT_col(A, f, c0, n):
        return A["kT"][:, f * 2688 + c0: f * 2688 + c0 + n]
    c.kT_col = kT_col

    ksrow = rows[0:1, 0:64]
    kbc = ar.f32(64)
    kbcs = ar.f32(64)
    qbc = ar.f32(64)
    qbcs = ar.f32(64)
    A["kscb"] = kbc.bitcast(BF16)[:, 0:64]
    A["kscsb"] = kbcs.bitcast(BF16)[:, 0:64]
    A["qscb"] = qbc.bitcast(BF16)[:, 0:64]
    A["qscsb"] = qbcs.bitcast(BF16)[:, 0:64]

    p.step("sync", lambda e: e.dma_start(out=ksrow, in_=io["fk_scale"][l:l + 1, :]),
           inc=c.sD, k=16)
    emit_bcast_scaled(c, ksrow, A["kscb"], swap=False, scale=1.0)
    emit_bcast_scaled(c, ksrow, A["kscsb"], swap=True, scale=1.0)
    p.step("sync", lambda e: e.dma_start(out=ksrow, in_=io["fq_scale"][l:l + 1, :]),
           waits=[(c.sA, None), (c.sP, None)], inc=c.sD, k=16)
    emit_bcast_scaled(c, ksrow, A["qscb"], swap=False, scale=F_QKSCALE)
    emit_bcast_scaled(c, ksrow, A["qscsb"], swap=True, scale=F_QKSCALE)

    c.p.flush()

    mS = ar.mark()

    A["wkv"] = ar.bf16(8 * 1024)
    A["sq"] = ar.bf16(512)
    A["ss"] = ar.f32(8)
    A["sr"] = ar.f32(8)
    A["rstd"] = ar.f32(8)
    A["kn"] = ar.bf16(512)
    A["t1"] = ar.bf16(512)
    A["t2"] = ar.bf16(512)
    A["kp"] = ar.bf16(512)
    A["cst"] = ar.bf16(64)
    A["snt"] = ar.bf16(64)
    A["csE"] = ar.bf16(64)
    A["snE"] = ar.bf16(64)

    for kk in range(KD):
        p.step("gpsimd", lambda e, kk=kk: e.dma_start(
            out=A["wkv"][:, kk * 1024:(kk + 1) * 1024],
            in_=io["fwqkv"][l, kk * 128:(kk + 1) * 128, 512:1536]),
            inc=c.sW, k=16)

    for tt in [11, 12, 13, 14, 15]:
        emit_ffn_kv_tile(c, l, tt, A)

    offs = 0
    def pack(src_ap, nel):
        nonlocal offs
        dst = c.ccf_in[0, offs:offs + nel].rearrange(
            "(p f) -> p f", p=src_ap.shape[0])
        p.step("sync", lambda e, d=dst, s=src_ap: e.dma_start(out=d, in_=s),
               waits=[(c.sV, None)], inc=c.sD, k=16)
        offs += nel
    for f in range(4):
        pack(kT_col(A, f, 640 + T - 640, 640), 128 * 640)
    for vt in range(16, 21):
        pack(A["v"][:, vt * 512:(vt + 1) * 512], 128 * 512)
    for kk in range(KD):
        pack(xb_v(c, kk, T - 132, 132), 128 * 132)
    assert offs == FF_TOT
    rg = RG_PAIRS if NGROUP == 2 else [[i] for i in range(NCORE)]
    p.step("gpsimd", lambda e, rg=rg: e.collective_compute(
        "AllGather", ALU.bypass, replica_groups=rg,
        ins=[c.ccf_in.opt()], outs=[c.ccf_out.opt()]),
        waits=[c.sD, (c.sC, None)], inc=c.sC)

    for tt in range(1, 11):
        emit_ffn_kv_tile(c, l, tt, A)
    c.p.flush()

    offs = 0
    def unpack(dst_ap, nel):
        nonlocal offs
        src = c.ccf_out[0, offs:offs + nel].rearrange(
            "(p f) -> p f", p=dst_ap.shape[0])
        p.step("sync", lambda e, d=dst_ap, s=src: e.dma_start(out=d, in_=s),
               waits=[(c.sC, None)], inc=c.sD2, k=16)
        offs += nel
    for f in range(4):
        unpack(kT_col(A, f, 0, 640), 128 * 640)
    for vt in range(0, 5):
        unpack(A["v"][:, vt * 512:(vt + 1) * 512], 128 * 512)
    for kk in range(KD):
        unpack(xb_v(c, kk, -132, 132), 128 * 132)
    assert offs == FF_TOT
    for kk in range(KD):
        p.step("vector", lambda e, kk=kk: e.tensor_scalar_mul(
            xb_v(c, kk, -132, 132), xb_v(c, kk, -132, 132), c.hsc[:, 0:1]),
            waits=[(c.sD2, None), (c.sV, None)], inc=c.sV)
    for f in range(4):
        p.step("vector", lambda e, f=f: e.tensor_scalar_mul(
            c.kT_col(A, f, 0, 640), c.kT_col(A, f, 0, 640), c.hsc[:, 0:1]),
            waits=[(c.sV, None)], inc=c.sV)
    p.step("vector", lambda e: e.tensor_scalar_mul(
        A["v"][:, 0:5 * 512], A["v"][:, 0:5 * 512], c.hsc[:, 0:1]),
        waits=[(c.sV, None)], inc=c.sV)
    c.haloV = c.sV.n
    emit_ffn_kv_tile(c, l, 0, A, wait_halo=True)

    p.barrier()
    ar.reset(mS)

    Bw = {}
    Bw["wq"] = ar.bf16(8 * 512)
    for kk in range(KD):
        p.step("gpsimd", lambda e, kk=kk: e.dma_start(
            out=Bw["wq"][:, kk * 512:(kk + 1) * 512],
            in_=io["fwqkv"][l, kk * 128:(kk + 1) * 128, 0:512]),
            inc=c.sW, k=16)
    Bw["sq"] = ar.bf16(512)
    Bw["ss"] = ar.f32(8)
    Bw["sr"] = ar.f32(8)
    Bw["rstd"] = ar.f32(8)
    Bw["qn"] = ar.bf16(512)
    Bw["t1"] = ar.bf16(512)
    Bw["t2"] = ar.bf16(512)
    Bw["qp"] = ar.bf16(512)
    Bw["cst"] = ar.bf16(64)
    Bw["snt"] = ar.bf16(64)
    Bw["csE"] = ar.bf16(64)
    Bw["snE"] = ar.bf16(64)
    Bw["qT"] = [ar.bf16(512), ar.bf16(512)]
    Bw["probs"] = [ar.bf16(640), ar.bf16(640)]
    Bw["probsm"] = [ar.bf16(640), ar.bf16(640)]
    Bw["probsT"] = ar.bf16(640)
    Bw["mask"] = ar.bf16(640)
    Bw["sums"] = ar.f32(2)
    Bw["rr"] = ar.f32(2)

    c.dbg_A, c.dbg_Bw = A, Bw
    if "attn" in FFN_PARTS:
        order = list(range(4, 16)) + [16, 0, 1, 2, 3]
        for ui, q_idx in enumerate(order):
            emit_ffn_q_attn(c, l, q_idx, ui, A, Bw)
    c.p.flush()

    if "wo" not in FFN_PARTS:
        c.p.barrier()
        ar.reset(mL)
        return
    Bw["wo"] = [ar.bf16(4 * 128), ar.bf16(4 * 128)]
    Bw["h5t"] = ar.bf16(128)
    for m in range(KD):
        wob = Bw["wo"][m % 2]
        for kk in range(4):
            p.step("gpsimd", lambda e, m=m, kk=kk, wob=wob: e.dma_start(
                out=wob[:, kk * 128:(kk + 1) * 128],
                in_=io["fwo"][l, kk * 128:(kk + 1) * 128,
                              m * 128:(m + 1) * 128]),
                waits=[(c.sP, None)] if kk == 0 else [],
                inc=c.sW, k=16)
        for tc in range(4):
            ps = c.PS[tc % 2]
            p.step("tensor", lambda e, m=m, tc=tc, wob=wob, ps=ps: (
                [e.matmul(ps[:, 0:512], wob[:, kk * 128:(kk + 1) * 128],
                          A["att"][:, kk * 2176 + 128 + tc * 512:
                                   kk * 2176 + 128 + (tc + 1) * 512],
                          start=(kk == 0), stop=(kk == 3))
                 for kk in range(4)])[-1],
                waits=[(c.sW, None)] + c.bg.pop(f"wops{tc % 2}", []), inc=c.sP)
            p.step("vector", lambda e, m=m, tc=tc, ps=ps: e.tensor_tensor(
                xm_v(c, m, tc * 512, 512), ps[:, 0:512],
                xm_v(c, m, tc * 512, 512), ALU.add),
                waits=[c.sP, (c.sV, None)], inc=c.sV)
            c.bset(f"wops{tc % 2}", c.sV)
        psh = c.PS[2]
        p.step("tensor", lambda e, m=m, wob=wob: (
            [e.matmul(psh[:, 0:128], wob[:, kk * 128:(kk + 1) * 128],
                      A["att"][:, kk * 2176: kk * 2176 + 128],
                      start=(kk == 0), stop=(kk == 3))
             for kk in range(4)])[-1],
            waits=c.bg.pop("wopsh", []), inc=c.sP)
        c.badd(f"wob{m % 2}", c.sP)
        p.step("vector", lambda e, m=m: e.tensor_scalar_mul(
            Bw["h5t"][:], psh[:, 0:128], c.hsc[:, 0:1]),
            waits=[c.sP, (c.sV, None)], inc=c.sV)
        c.bset("wopsh", c.sV)
        p.step("vector", lambda e, m=m: e.tensor_tensor(
            xb_v(c, m, -128, 128), Bw["h5t"][:], xb_v(c, m, -128, 128), ALU.add),
            waits=[c.sV], inc=c.sV)
    vW = c.sV.n
    for m in range(KD):
        for tc in range(4):
            p.step("vector", lambda e, m=m, tc=tc: e.tensor_copy(
                xb_v(c, m, tc * 512, 512), xm_v(c, m, tc * 512, 512)),
                waits=[(c.sV, vW)], inc=c.sV)
    p.barrier()
    if "mlp" not in FFN_PARTS:
        ar.reset(mL)
        return
    ar.reset(mL)
    emit_ffn_mlp(c, l)


def emit_bcast_scaled(c, row_f32, dst_bf16, swap, scale):
    p = c.p
    ar = c.ar
    m0 = ar.mark()
    rs = ar.f32(64)[0:1, :]
    rh = ar.bf16(64)[0:1, :]
    rl = ar.bf16(64)[0:1, :]
    rx = ar.f32(64)[0:1, :]
    ps = c.PS[7][:, 0:64]
    p.step("vector", lambda e: e.tensor_scalar_mul(rs, row_f32, scale),
           waits=[(c.sV, None), c.sD], inc=c.sV)
    p.step("vector", lambda e: e.tensor_copy(rh, rs), waits=[c.sV], inc=c.sV)
    p.step("vector", lambda e: e.tensor_tensor(rx, rs, rh, ALU.subtract),
           waits=[c.sV], inc=c.sV)
    p.step("vector", lambda e: e.tensor_copy(rl, rx), waits=[c.sV], inc=c.sV)
    sh = (lambda r: row_swap_view(r)) if swap else (lambda r: r)
    p.step("tensor", lambda e: (
        e.matmul(ps, c.onr[0:1, :], sh(rh), start=True, stop=False),
        e.matmul(ps, c.onr[0:1, :], sh(rl), start=False, stop=True))[-1],
        waits=[c.sV, (c.sA, None)], inc=c.sP)
    p.step("scalar", lambda e: e.activation(dst_bf16, ps, AF.Identity),
           waits=[c.sP], inc=c.sA)
    c.p.flush()
    ar.reset(m0)


def emit_ffn_q_attn(c, l, q_idx, ui, A, Bw):
    p = c.p
    io = c.io
    tq0 = -128 if q_idx == 16 else q_idx * 128
    special = {16: 0, 0: 1, 1: 2, 2: 3, 3: 4}.get(q_idx)
    trb = c.PS[6].bitcast(BF16)
    qT = Bw["qT"][ui % 2]

    psq = c.PS[7]
    needs_halo = q_idx in (16, 0)
    qw = [(c.sW, None)] + c.bg.pop("ps7", [])
    if needs_halo:
        qw.append((c.sV, c.haloV))
    p.step("tensor", lambda e: ffn_qkv_mms(
        c, e, psq[:, 0:512], lambda kk: Bw["wq"][:, kk * 512:(kk + 1) * 512],
        tq0),
        waits=qw, inc=c.sP)
    p.step("scalar", lambda e: e.activation(Bw["sq"][:], psq[:, 0:512], AF.Square),
           waits=[c.sP] + c.bg.pop("Bqproc", []), inc=c.sA)
    p.step("vector", lambda e: e.reduce_sum(
        Bw["ss"][:, 0:8], Bw["sq"][:].rearrange("p (h d) -> p h d", h=8), axis=AX.X),
        waits=[c.sA, (c.sV, None)], inc=c.sV)
    p.step("scalar", lambda e: e.activation(Bw["sr"][:, 0:8], Bw["ss"][:, 0:8], AF.Sqrt),
           waits=[c.sV], inc=c.sA)
    p.step("vector", lambda e: e.tensor_scalar_max(Bw["sr"][:, 0:8], Bw["sr"][:, 0:8], 1e-12),
           waits=[c.sA], inc=c.sV)
    p.step("vector", lambda e: e.reciprocal(Bw["rstd"][:, 0:8], Bw["sr"][:, 0:8]),
           waits=[c.sV], inc=c.sV)
    for h in range(8):
        p.step("vector", lambda e, h=h: e.tensor_scalar_mul(
            Bw["qn"][:, h * 64:(h + 1) * 64], psq[:, h * 64:(h + 1) * 64],
            Bw["rstd"][:, h:h + 1]),
            waits=[c.sV], inc=c.sV)
    c.bg["ps7"] = [(c.sV, c.sV.n), (c.sA, c.sA.n)]
    r0 = 128 + tq0
    p.step("sync", lambda e: e.dma_start(out=Bw["cst"][:], in_=io["csq"][r0:r0 + 128, :]),
           waits=[(c.sV, None)], inc=c.sD, k=16)
    p.step("sync", lambda e: e.dma_start(out=Bw["snt"][:], in_=io["snq"][r0:r0 + 128, :]),
           inc=c.sD, k=16)
    p.step("vector", lambda e: e.tensor_tensor(Bw["csE"][:], Bw["cst"][:], A["qscb"][:], ALU.mult),
           waits=[c.sD, c.sV], inc=c.sV)
    p.step("vector", lambda e: e.tensor_tensor(Bw["snE"][:], Bw["snt"][:], A["qscsb"][:], ALU.mult),
           waits=[c.sV], inc=c.sV)
    p.step("vector", lambda e: e.tensor_tensor(
        head_view(Bw["t1"][:]), head_view(Bw["qn"][:]), rep_view(Bw["csE"][:]), ALU.mult),
        waits=[c.sV], inc=c.sV)
    p.step("vector", lambda e: e.tensor_tensor(
        head_view(Bw["t2"][:]), rot_view(Bw["qn"][:]), rep_view(Bw["snE"][:]), ALU.mult),
        waits=[c.sV], inc=c.sV)
    p.step("vector", lambda e: e.tensor_tensor(Bw["qp"][:], Bw["t1"][:], Bw["t2"][:], ALU.add),
           waits=[c.sV], inc=c.sV)
    p.step("tensor", lambda e: (
        [e.transpose(trb[:, f * 128:(f + 1) * 128], Bw["qp"][:, f * 128:(f + 1) * 128],
                     c.idt[:]) for f in range(4)])[-1],
        waits=[c.sV] + c.bg.pop("trb", []) + c.bg.pop(f"qT{ui % 2}", []),
        inc=c.sP)
    c.badd("Bqproc", c.sP)
    p.step("vector", lambda e: e.tensor_copy(qT[:], trb[:, 0:512]),
           waits=[c.sP, (c.sV, None)], inc=c.sV)
    c.badd("trb", c.sV)

    if ATTN_CUT == "q":
        return
    if special is not None:
        p.step("sync", lambda e, s=special: e.dma_start(
            out=Bw["mask"][:], in_=io["fspec01"][s, :, :]),
            waits=[(c.sG, None)], inc=c.sD, k=16)
        mask = Bw["mask"]
    else:
        mask = c.fband

    kcol0 = 640 + tq0 - 512
    for hp in range(4):
        st = [c.PS[0], c.PS[1], c.PS[2]] if hp % 2 == 0 else [c.PS[3], c.PS[4], c.PS[5]]
        probs = Bw["probs"][hp % 2]
        probsm = Bw["probsm"][hp % 2]
        f = hp
        sc_w = [(c.sV, None)] + c.bg.pop(f"set{hp % 2}", []) +             c.bg.pop("trbx", [])
        if q_idx in (16, 0, 1, 2, 3):
            sc_w.append((c.sD2, None))
        qa = qT[0:64, f * 128:(f + 1) * 128]
        qb = qT[64:128, f * 128:(f + 1) * 128]
        p.step("tensor", lambda e, f=f, st=st, qa=qa, qb=qb: (
            e.matmul(st[0][:, 0:512], qa,
                     c.kT_col(A, f, kcol0, 512)[0:64, :], start=True, stop=True,
                     tile_position=(0, 0)),
            e.matmul(st[1][:, 0:512], qb,
                     c.kT_col(A, f, kcol0, 512)[64:128, :], start=True, stop=True,
                     tile_position=(64, 0)),
            e.matmul(st[2][:, 0:128], qa,
                     c.kT_col(A, f, kcol0 + 512, 128)[0:64, :], start=True,
                     stop=True, tile_position=(0, 0)),
            e.matmul(c.PS[6][:, 384:512], qb,
                     c.kT_col(A, f, kcol0 + 512, 128)[64:128, :], start=True,
                     stop=True, tile_position=(64, 0)))[-1],
            waits=sc_w, inc=c.sP)
        if ATTN_CUT == "scores":
            continue
        p.step("vector", lambda e, st=st: e.reduce_max(
            Bw["rr"][:, 0:1], st[0][:, 0:512], axis=AX.X, negate=True),
            waits=[c.sP, (c.sV, None)], inc=c.sV)
        p.step("vector", lambda e, st=st: e.reduce_max(
            Bw["rr"][:, 1:2], st[1][:, 0:512], axis=AX.X, negate=True),
            waits=[c.sV], inc=c.sV)
        p.step("vector", lambda e, st=st: e.reduce_max(
            Bw["sums"][:, 0:1], st[2][:, 0:128], axis=AX.X, negate=True),
            waits=[c.sV], inc=c.sV)
        p.step("vector", lambda e: e.reduce_max(
            Bw["sums"][:, 1:2], c.PS[6][:, 384:512], axis=AX.X, negate=True),
            waits=[c.sV], inc=c.sV)
        p.step("vector", lambda e: e.tensor_tensor(
            Bw["rr"][:, 0:2], Bw["rr"][:, 0:2], Bw["sums"][:, 0:2], ALU.min),
            waits=[c.sV], inc=c.sV)
        c.badd("trbx", c.sV)
        if ATTN_CUT == "max":
            continue
        p.step("scalar", lambda e, st=st, probs=probs: (
            e.activation(probs[:, 0:512], st[0][:, 0:512], AF.Exp,
                         bias=Bw["rr"][:, 0:1]),
            e.activation(probs[:, 512:640], st[2][:, 0:128], AF.Exp,
                         bias=Bw["rr"][:, 0:1]))[-1],
            waits=[c.sV] + c.bg.pop(f"probs{hp % 2}", []), inc=c.sA)
        if ATTN_CUT == "exp":
            continue
        p.step("gpsimd", lambda e, probsm=probsm, probs=probs, mask=mask:
               e.tensor_tensor(probsm[:], probs[:], mask[:], ALU.mult),
               waits=[c.sA, c.sD] + c.bg.pop(f"probsm{hp % 2}", []), inc=c.sG)
        c.badd(f"probs{hp % 2}", c.sG)
        if ATTN_CUT == "mask":
            continue
        p.step("vector", lambda e, probsm=probsm: e.reduce_sum(
            Bw["sums"][:, 0:1], probsm[:], axis=AX.X),
            waits=[c.sG, (c.sV, None)], inc=c.sV)
        p.step("vector", lambda e: e.tensor_scalar_max(
            Bw["sums"][:, 0:1], Bw["sums"][:, 0:1], 1e-30),
            waits=[c.sV], inc=c.sV)
        p.step("vector", lambda e: e.reciprocal(Bw["sums"][:, 1:2], Bw["sums"][:, 0:1]),
               waits=[c.sV], inc=c.sV)
        p.step("vector", lambda e, probsm=probsm: e.tensor_scalar_mul(
            probsm[:], probsm[:], Bw["sums"][:, 1:2]),
            waits=[c.sV, c.sG], inc=c.sV)
        if ATTN_CUT == "norm":
            continue
        p.step("tensor", lambda e, probsm=probsm: (
            [e.transpose(trb[:, i * 128:(i + 1) * 128],
                         probsm[:, i * 128:(i + 1) * 128], c.idt[:])
             for i in range(5)])[-1],
            waits=[c.sV, (c.sA, None)], inc=c.sP)
        p.step("scalar", lambda e: e.activation(Bw["probsT"][:], trb[:, 0:640],
                                                AF.Identity),
               waits=[c.sP], inc=c.sA)
        if ATTN_CUT == "tr":
            continue
        ha = 2 * hp
        p.step("tensor", lambda e, ha=ha: (
            [e.matmul(c.PS[7][0:64, 0:128],
                      A["v"][:, (q_idx + 1 + ct) * 512 + ha * 64:
                             (q_idx + 1 + ct) * 512 + ha * 64 + 64]
                      if False else
                      A["v"][:, ((0 if q_idx == 16 else q_idx + 1) + ct) * 512
                             + ha * 64:
                             ((0 if q_idx == 16 else q_idx + 1) + ct) * 512
                             + ha * 64 + 64],
                      Bw["probsT"][:, ct * 128:(ct + 1) * 128],
                      start=(ct == 0), stop=(ct == 4))
             for ct in range(5)])[-1],
            waits=[c.sA] + c.bg.pop("ps7", []), inc=c.sP)
        c.badd("probsT", c.sP)
        p.step("vector", lambda e, f=f, ha=ha: e.tensor_copy(
            A["att"][0:64, f * 2176 + 128 + tq0: f * 2176 + 256 + tq0],
            c.PS[7][0:64, 0:128]),
            waits=[c.sP, (c.sV, None)], inc=c.sV)
        c.bset("ps7", c.sV)
        if ATTN_CUT == "av":
            continue
        p.step("scalar", lambda e, st=st, probs=probs: (
            e.activation(probs[:, 0:512], st[1][:, 0:512], AF.Exp,
                         bias=Bw["rr"][:, 1:2]),
            e.activation(probs[:, 512:640], c.PS[6][:, 384:512], AF.Exp,
                         bias=Bw["rr"][:, 1:2]))[-1],
            waits=[(c.sV, None)] + c.bg.pop(f"probs{hp % 2}", []), inc=c.sA)
        c.badd("trbx", c.sA)
        p.step("gpsimd", lambda e, probsm=probsm, probs=probs, mask=mask:
               e.tensor_tensor(probsm[:], probs[:], mask[:], ALU.mult),
               waits=[c.sA] + c.bg.pop(f"probsm{hp % 2}", []), inc=c.sG)
        c.badd(f"probs{hp % 2}", c.sG)
        p.step("vector", lambda e, probsm=probsm: e.reduce_sum(
            Bw["sums"][:, 0:1], probsm[:], axis=AX.X),
            waits=[c.sG, (c.sV, None)], inc=c.sV)
        p.step("vector", lambda e: e.tensor_scalar_max(
            Bw["sums"][:, 0:1], Bw["sums"][:, 0:1], 1e-30),
            waits=[c.sV], inc=c.sV)
        p.step("vector", lambda e: e.reciprocal(Bw["sums"][:, 1:2], Bw["sums"][:, 0:1]),
               waits=[c.sV], inc=c.sV)
        p.step("vector", lambda e, probsm=probsm: e.tensor_scalar_mul(
            probsm[:], probsm[:], Bw["sums"][:, 1:2]),
            waits=[c.sV, c.sG], inc=c.sV)
        p.step("tensor", lambda e, probsm=probsm: (
            [e.transpose(trb[:, i * 128:(i + 1) * 128],
                         probsm[:, i * 128:(i + 1) * 128], c.idt[:])
             for i in range(5)])[-1],
            waits=[c.sV] + c.bg.pop("trb", []), inc=c.sP)
        c.badd(f"probsm{hp % 2}", c.sP)
        p.step("scalar", lambda e: e.activation(Bw["probsT"][:], trb[:, 0:640],
                                                AF.Identity),
               waits=[c.sP] + c.bg.pop("probsT", []), inc=c.sA)
        c.badd("trb", c.sA)
        hb = 2 * hp + 1
        p.step("tensor", lambda e, hb=hb: (
            [e.matmul(c.PS[7][0:64, 128:256],
                      A["v"][:, ((0 if q_idx == 16 else q_idx + 1) + ct) * 512
                             + hb * 64:
                             ((0 if q_idx == 16 else q_idx + 1) + ct) * 512
                             + hb * 64 + 64],
                      Bw["probsT"][:, ct * 128:(ct + 1) * 128],
                      start=(ct == 0), stop=(ct == 4))
             for ct in range(5)])[-1],
            waits=[c.sA] + c.bg.pop("ps7", []), inc=c.sP)
        c.badd("probsT", c.sP)
        p.step("vector", lambda e, f=f: e.tensor_copy(
            A["att"][64:128, f * 2176 + 128 + tq0: f * 2176 + 256 + tq0],
            c.PS[7][0:64, 128:256]),
            waits=[c.sP, (c.sV, None)], inc=c.sV)
        c.bset("ps7", c.sV)
        c.bg[f"set{hp % 2}"] = [(c.sV, c.sV.n), (c.sA, c.sA.n)]


def emit_ffn_mlp(c, l):
    p = c.p
    io = c.io
    ar = c.ar
    m0 = ar.mark()
    h2 = ar.bf16(8 * 2048)
    h3 = ar.bf16(22 * 1024)
    wpair = [ar.bf16(2 * 8 * 128) for _ in range(2)]
    w1a = [wp[:, 0:1024] for wp in wpair]
    w1g = [wp[:, 1024:2048] for wp in wpair]
    ag = ar.bf16(512)
    gg = ar.bf16(512)
    sq = ag
    b1c = ar.f32(44)
    ldg = ar.f32(8)
    ldb = ar.f32(8)
    b2c = ar.f32(8)
    rowf = ar.f32(5 * 512)
    R = {nm: rowf[0:1, i * 512:(i + 1) * 512]
         for i, nm in enumerate(["m", "ms", "t", "r1", "r2"])}
    R["x"] = R["t"]
    rowb = ar.bf16(4 * 512)
    RB = {nm: rowb[0:1, i * 512:(i + 1) * 512]
          for i, nm in enumerate(["r1h", "r1l", "r2h", "r2l"])}
    b1s = ar.bf16(512)
    b2s = ar.bf16(512)
    tmp = gg

    for i, n in enumerate(HID_CH):
        o = sum(HID_CH[:i])
        p.step("sync", lambda e, i=i, o=o, n=n: e.dma_start(
            out=b1c[0:n, 2 * i:2 * i + 1],
            in_=io["fb1"][l, o:o + n].unsqueeze(1)),
            inc=c.sD, k=16)
        p.step("sync", lambda e, i=i, o=o, n=n: e.dma_start(
            out=b1c[0:n, 2 * i + 1:2 * i + 2],
            in_=io["fb1"][l, F_HID + o:F_HID + o + n].unsqueeze(1)),
            inc=c.sD, k=16)
    for kk in range(KD):
        p.step("sync", lambda e, kk=kk: e.dma_start(
            out=ldg[:, kk:kk + 1],
            in_=io["fln_g"][l, kk * 128:(kk + 1) * 128].unsqueeze(1)),
            inc=c.sD, k=16)
        p.step("sync", lambda e, kk=kk: e.dma_start(
            out=ldb[:, kk:kk + 1],
            in_=io["fln_b"][l, kk * 128:(kk + 1) * 128].unsqueeze(1)),
            inc=c.sD, k=16)
        p.step("sync", lambda e, kk=kk: e.dma_start(
            out=b2c[:, kk:kk + 1],
            in_=io["fb2"][l, kk * 128:(kk + 1) * 128].unsqueeze(1)),
            inc=c.sD, k=16)
    ldD = c.sD.n

    ps_sum = c.PS[4][0:1, 0:512]
    ps_ssq = c.PS[5][0:1, 0:512]
    ps_b1 = c.PS[6][:, 0:512]
    ps_b2 = c.PS[7][:, 0:512]

    def xa_shift(kk, t0, n):
        return xb_v(c, kk, t0 if kk < 4 else t0 - 1, n)

    for tc in range(2):
        T0 = tc * 1024
        for sub in range(2):
            t0 = T0 + sub * 512
            for kk in range(KD):
                p.step("scalar", lambda e, kk=kk, t0=t0: e.activation(
                    sq[:], xa_shift(kk, t0, 512), AF.Square),
                    waits=[(c.sP, None), (c.sV, None)], inc=c.sA)
                p.step("tensor", lambda e, kk=kk, t0=t0: e.matmul(
                    ps_ssq, c.onc[:, 0:1], sq[:],
                    start=(kk == 0), stop=(kk == KD - 1)),
                    waits=[c.sA], inc=c.sP)
                p.step("tensor", lambda e, kk=kk, t0=t0: e.matmul(
                    ps_sum, c.onc[:, 0:1], xa_shift(kk, t0, 512),
                    start=(kk == 0), stop=(kk == KD - 1)),
                    waits=[(c.sV, None)], inc=c.sP)
            inv = 1.0 / D
            p.step("vector", lambda e: e.tensor_scalar_mul(R["m"], ps_sum, inv),
                   waits=[c.sP, (c.sV, None)], inc=c.sV)
            p.step("vector", lambda e: e.tensor_scalar_mul(R["ms"], ps_ssq, inv),
                   waits=[c.sV], inc=c.sV)
            p.step("vector", lambda e: e.tensor_tensor(R["t"], R["m"], R["m"], ALU.mult),
                   waits=[c.sV], inc=c.sV)
            p.step("vector", lambda e: e.tensor_tensor(R["ms"], R["ms"], R["t"], ALU.subtract),
                   waits=[c.sV], inc=c.sV)
            p.step("vector", lambda e: e.tensor_scalar_add(R["ms"], R["ms"], 1e-5),
                   waits=[c.sV], inc=c.sV)
            p.step("scalar", lambda e: e.activation(R["t"], R["ms"], AF.Sqrt),
                   waits=[c.sV], inc=c.sA)
            p.step("vector", lambda e: e.reciprocal(R["r1"], R["t"]),
                   waits=[c.sA], inc=c.sV)
            p.step("vector", lambda e: e.tensor_tensor(R["r2"], R["m"], R["r1"], ALU.mult),
                   waits=[c.sV], inc=c.sV)
            p.step("vector", lambda e: e.tensor_scalar_mul(R["r2"], R["r2"], -1.0),
                   waits=[c.sV], inc=c.sV)
            for r, rh, rl in [("r1", "r1h", "r1l"), ("r2", "r2h", "r2l")]:
                p.step("vector", lambda e, r=r, rh=rh: e.tensor_copy(RB[rh], R[r]),
                       waits=[c.sV], inc=c.sV)
                p.step("vector", lambda e, r=r, rh=rh: e.tensor_tensor(
                    R["x"], R[r], RB[rh], ALU.subtract),
                    waits=[c.sV], inc=c.sV)
                p.step("vector", lambda e, rl=rl: e.tensor_copy(RB[rl], R["x"]),
                       waits=[c.sV], inc=c.sV)
            p.step("tensor", lambda e: (
                e.matmul(ps_b1, c.onr[0:1, :], RB["r1h"], start=True, stop=False),
                e.matmul(ps_b1, c.onr[0:1, :], RB["r1l"], start=False, stop=True))[-1],
                waits=[c.sV, (c.sA, None)], inc=c.sP)
            p.step("tensor", lambda e: (
                e.matmul(ps_b2, c.onr[0:1, :], RB["r2h"], start=True, stop=False),
                e.matmul(ps_b2, c.onr[0:1, :], RB["r2l"], start=False, stop=True))[-1],
                waits=[c.sV, (c.sA, None)], inc=c.sP)
            p.step("scalar", lambda e: e.activation(b1s, ps_b1, AF.Identity),
                   waits=[c.sP], inc=c.sA)
            p.step("scalar", lambda e: e.activation(b2s, ps_b2, AF.Identity),
                   waits=[c.sP], inc=c.sA)
            for kk in range(KD):
                p.step("vector", lambda e, kk=kk, t0=t0: e.tensor_tensor(
                    tmp[:], xa_shift(kk, t0, 512), b1s[:], ALU.mult),
                    waits=[c.sA, (c.sV, None)], inc=c.sV)
                p.step("vector", lambda e, kk=kk: e.tensor_tensor(
                    tmp[:], tmp[:], b2s[:], ALU.add),
                    waits=[c.sV], inc=c.sV)
                p.step("vector", lambda e, kk=kk, t0=t0: e.tensor_scalar(
                    h2[:, kk * 2048 + t0: kk * 2048 + t0 + 512],
                    tmp[:], ldg[:, kk:kk + 1], ldb[:, kk:kk + 1],
                    ALU.mult, ALU.add),
                    waits=[c.sV, (c.sD, ldD)], inc=c.sV)
    c.h2V = c.sV.n
    for tc in range(2):
        T0 = tc * 1024
        for i, n in enumerate(HID_CH):
            o = sum(HID_CH[:i])
            wa, wg = w1a[i % 2], w1g[i % 2]
            for kk in range(KD):
                p.step("gpsimd", lambda e, o=o, n=n, kk=kk, wa=wa: e.dma_start(
                    out=wa[0:128, kk * 128:kk * 128 + n],
                    in_=io["fw1"][l, kk * 128:(kk + 1) * 128, o:o + n]),
                    waits=c.bg.pop(f"wp{i % 2}", []) if kk == 0 else [],
                    inc=c.sW, k=16)
            for kk in range(KD):
                p.step("gpsimd", lambda e, o=o, n=n, kk=kk, wg=wg: e.dma_start(
                    out=wg[0:128, kk * 128:kk * 128 + n],
                    in_=io["fw1"][l, kk * 128:(kk + 1) * 128,
                                  F_HID + o:F_HID + o + n]),
                    inc=c.sW, k=16)
            for sub in range(2):
                t0s = sub * 512
                psa = c.PS[0 + sub]
                psg = c.PS[2 + sub]
                p.step("tensor", lambda e, i=i, n=n, wa=wa, t0s=t0s, psa=psa,
                       T0=T0: (
                    [e.matmul(psa[0:n, 0:512], wa[0:128, kk * 128:kk * 128 + n],
                              h2[:, kk * 2048 + T0 + t0s:
                                 kk * 2048 + T0 + t0s + 512],
                              start=(kk == 0), stop=(kk == KD - 1))
                     for kk in range(KD)])[-1],
                    waits=[(c.sW, None), (c.sV, c.h2V)] +
                    c.bg.pop(f"psa{sub}", []) +
                    (c.bgate("w2banks") if i == 0 else []), inc=c.sP)
                p.step("tensor", lambda e, i=i, n=n, wg=wg, t0s=t0s, psg=psg,
                       T0=T0: (
                    [e.matmul(psg[0:n, 0:512], wg[0:128, kk * 128:kk * 128 + n],
                              h2[:, kk * 2048 + T0 + t0s:
                                 kk * 2048 + T0 + t0s + 512],
                              start=(kk == 0), stop=(kk == KD - 1))
                     for kk in range(KD)])[-1],
                    waits=c.bg.pop(f"psg{sub}", []) +
                    (c.bgate("w2banks") if i == 0 else []), inc=c.sP)
                if sub == 1:
                    c.badd(f"wp{i % 2}", c.sP)
                p.step("scalar", lambda e, i=i, n=n, psa=psa: e.activation(
                    ag[0:n, :], psa[0:n, 0:512], AF.Identity,
                    bias=b1c[0:n, 2 * i:2 * i + 1]),
                    waits=[c.sP] + c.bg.pop("agbuf", []), inc=c.sA)
                c.bset(f"psa{sub}", c.sA)
                p.step("scalar", lambda e, i=i, n=n, psg=psg: e.activation(
                    gg[0:n, :], psg[0:n, 0:512], AF.Gelu,
                    bias=b1c[0:n, 2 * i + 1:2 * i + 2]),
                    waits=c.bg.pop("ggbuf", []), inc=c.sA)
                c.bset(f"psg{sub}", c.sA)
                p.step("vector", lambda e, i=i, n=n, t0s=t0s: e.tensor_tensor(
                    h3[0:n, i * 1024 + t0s: i * 1024 + t0s + 512],
                    ag[0:n, :], gg[0:n, :], ALU.mult),
                    waits=[c.sA], inc=c.sV)
                c.bset("agbuf", c.sV)
                c.bset("ggbuf", c.sV)
        emit_ffn_w2(c, l, T0, h3, b2c, wpair)
    c.p.flush()
    ar.reset(m0)
    c.p.barrier()


def emit_ffn_w2(c, l, T0, h3, b2c, wpair):
    p = c.p
    io = c.io
    ar = c.ar
    m0 = ar.mark()
    w2c = wpair
    for sub in range(2):
        t0 = T0 + sub * 512
        ps = c.PS
        for i, n in enumerate(HID_CH):
            o = sum(HID_CH[:i])
            wc = w2c[i % 2]
            p.step("gpsimd", lambda e, i=i, o=o, n=n, wc=wc: e.dma_start(
                out=wc[0:n, 0:1024],
                in_=io["fw2"][l, o:o + n, :]),
                waits=c.bg.pop(f"wp{i % 2}", []), inc=c.sW, k=16)
            p.step("tensor", lambda e, i=i, n=n, o=o, wc=wc, t0=t0: (
                [e.matmul(ps[m][:, 0:512], wc[0:n, m * 128:(m + 1) * 128],
                          h3[0:n, i * 1024 + (t0 - T0): i * 1024 + (t0 - T0) + 512],
                          start=(i == 0), stop=(i == len(HID_CH) - 1))
                 for m in range(KD)])[-1],
                waits=[(c.sW, None)] +
                (c.bg.pop("w2banks", []) +
                 c.bg.pop("psa0", []) + c.bg.pop("psa1", []) +
                 c.bg.pop("psg0", []) + c.bg.pop("psg1", [])
                 if i == 0 else []) +
                ([(c.sV, None)] if i == 0 else []), inc=c.sP)
            c.badd(f"wp{i % 2}", c.sP)
        for m in range(KD):
            p.step("vector", lambda e, m=m, t0=t0: e.scalar_tensor_tensor(
                xm_v(c, m, t0, 512), ps[m][:, 0:512], b2c[:, m:m + 1],
                xm_v(c, m, t0, 512), ALU.add, ALU.add),
                waits=[c.sP, (c.sV, None)], inc=c.sV)
        c.bset("w2banks", c.sV)
        vW = c.sV.n
        for m in range(KD):
            p.step("vector", lambda e, m=m, t0=t0: e.tensor_copy(
                xb_v(c, m, t0, 512), xm_v(c, m, t0, 512)),
                waits=[(c.sV, vW)], inc=c.sV)
    ar.reset(m0)


def emit_done(c):
    c.p.step("sync", lambda e: e.engine_nop() if False else e.nop(),
             waits=[c.sO, c.sV, c.sP])
    c.p.flush()


def program(c):
    emit_prologue(c)
    emit_boundary(c, "init")
    if SKIP_FINAL:
        for st in STAGES:
            if st == "ffn0":
                emit_ffn_layer(c, 0)
        emit_done(c)
        return
    for st in STAGES:
        if st == "ffn0":
            emit_ffn_layer(c, 0)
        elif st == "ffn1":
            emit_ffn_layer(c, 1)
        elif st == "b_ffn":
            emit_boundary(c, "mid_ffn")
        elif st == "mla":
            emit_mla_block(c)
        elif st == "b_mla":
            emit_boundary(c, "mid_mla")
        else:
            raise ValueError(st)
    emit_boundary(c, "final")
    emit_done(c)


def mla_qkv_mms(c, e, ps, wsl, t0):
    ins = None
    for kk in range(KD):
        ins = e.matmul(ps, xb_v(c, kk, t0, 128), wsl(kk),
                       start=(kk == 0), stop=False)
    return ins


def emit_mla_block(c):
    p = c.p
    io = c.io
    ar = c.ar
    mL = ar.mark()

    M = {}
    rows = ar.f32(64)
    rb = ar.bf16(3072 + 256 + 512 + 64)
    M["qkvb"] = rb[0:1, 0:3072]
    M["rqb"] = rb[0:1, 3072:3328]
    M["rkvb"] = rb[0:1, 3328:3840]
    M["pooled"] = ar.bf16(8 * 64)
    M["rkv"] = ar.bf16(512)
    M["rkT"] = ar.bf16(2 * 64)
    M["glb"] = ar.bf16(64)
    M["grb"] = ar.bf16(64)
    M["rowk"] = ar.f32(2)
    M["lmrow"] = ar.bf16(64)

    c.wrq = ar.bf16(8 * 256)
    c.wrout = ar.bf16(2 * 1024)
    rb2 = ar.bf16(1024)
    c.routb = rb2[0:1, 0:1024]

    mRows = ar.mark()
    rowscr = ar.f32(3072)
    rfull = rowscr[0:1, 0:3072]
    for nm, src_ap, n in [("qkvb", io["aqkv_b"].unsqueeze(0), 3072),
                          ("rqb", io["arq_b"].unsqueeze(0), 256),
                          ("rkvb", io["arkv_b"].unsqueeze(0), 512)]:
        p.step("sync", lambda e, s=src_ap, n=n: e.dma_start(
            out=rfull[0:1, 0:n], in_=s), waits=[(c.sV, None)], inc=c.sD, k=16)
        p.step("vector", lambda e, nm=nm, n=n: e.tensor_copy(
            M[nm][0:1, 0:n], rfull[0:1, 0:n]),
            waits=[c.sD, (c.sV, None)], inc=c.sV)
    for nm, wn in [("glb", "agl_w"), ("grb", "agr_w")]:
        p.step("sync", lambda e, wn=wn: e.dma_start(
            out=rowscr[0:1, 0:64], in_=io[wn].rearrange("a b -> b a")),
            waits=[(c.sV, None), (c.sP, None)], inc=c.sD, k=16)
        p.step("vector", lambda e: e.tensor_copy(rb[0:1, 3840:3904],
                                                 rowscr[0:1, 0:64]),
               waits=[c.sD, (c.sV, None)], inc=c.sV)
        p.step("tensor", lambda e: e.matmul(c.PS[7][:, 0:64], c.onr[0:1, :],
                                            rb[0:1, 3840:3904],
                                            start=True, stop=True),
               waits=[c.sV, (c.sA, None)], inc=c.sP)
        p.step("scalar", lambda e, nm=nm: e.activation(M[nm][:], c.PS[7][:, 0:64],
                                                       AF.Identity),
               waits=[c.sP], inc=c.sA)

    rs2 = ar.f32(1024)
    for kk in range(KD):
        p.step("gpsimd", lambda e, kk=kk: e.dma_start(
            out=c.wrq[:, kk * 256:(kk + 1) * 256],
            in_=io["arq_w"][kk * 128:(kk + 1) * 128, :]),
            waits=[(c.sP, None)] if kk == 0 else [], inc=c.sW, k=16)
    for ch in range(2):
        p.step("gpsimd", lambda e, ch=ch: e.dma_start(
            out=c.wrout[0:128, ch * 1024:(ch + 1) * 1024],
            in_=io["arout_w"][ch * 128:(ch + 1) * 128, :]),
            inc=c.sW, k=16)
    p.step("sync", lambda e: e.dma_start(
        out=rs2[0:1, 0:1024], in_=io["arout_b"].unsqueeze(0)),
        waits=[(c.sV, None)], inc=c.sD, k=16)
    p.step("vector", lambda e: e.tensor_copy(c.routb[0:1, 0:1024],
                                             rs2[0:1, 0:1024]),
           waits=[c.sD, (c.sV, None)], inc=c.sV)

    c.p.flush()
    ar.reset(mRows)

    pl_own = M["pooled"][:, :].bitcast(BF16) if False else M["pooled"]
    pooled_f = ar.f32(8 * 32)
    for kk in range(KD):
        p.step("vector", lambda e, kk=kk: e.reduce_sum(
            pooled_f[:, kk * 32:(kk + 1) * 32],
            xb_v(c, kk, 0, T).rearrange("p (g c) -> p g c", g=32), axis=AX.X),
            waits=[(c.sV, None)], inc=c.sV)
    p.step("vector", lambda e: e.tensor_scalar_mul(
        pooled_f[:], pooled_f[:], 1.0 / CHUNK),
        waits=[c.sV], inc=c.sV)
    for kk in range(KD):
        p.step("vector", lambda e, kk=kk: e.tensor_copy(
            M["pooled"][:, kk * 64 + 32: kk * 64 + 64],
            pooled_f[:, kk * 32:(kk + 1) * 32]),
            waits=[c.sV], inc=c.sV)

    mS = ar.mark()
    kvlast = ar.bf16(2 * 1024)
    ktl = kvlast[:, 0:1024]
    vtl = kvlast[:, 1024:2048]
    wbuf = [ar.bf16(8 * 512) for _ in range(2)]
    trb = c.PS[6].bitcast(BF16)
    t0 = T - 128
    for ni, n0 in enumerate([2, 3, 4, 5]):
        wb = wbuf[ni % 2]
        for kk in range(KD):
            p.step("gpsimd", lambda e, kk=kk, n0=n0, wb=wb: e.dma_start(
                out=wb[:, kk * 512:(kk + 1) * 512],
                in_=io["aqkv_w"][kk * 128:(kk + 1) * 128,
                                 n0 * 512:(n0 + 1) * 512]),
                waits=[(c.sP, None)] if kk == 0 else [], inc=c.sW, k=16)
        ps = c.PS[ni % 2]
        p.step("tensor", lambda e, wb=wb, ni=ni, n0=n0, ps=ps: (
            mla_qkv_mms(c, e, ps[:, 0:512],
                        lambda kk: wb[:, kk * 512:(kk + 1) * 512], t0),
            e.matmul(ps[:, 0:512], c.onr[0:1, :],
                     M["qkvb"][0:1, n0 * 512:(n0 + 1) * 512],
                     start=False, stop=True))[-1],
            waits=[c.sW, (c.sV, None), (c.sA, None)], inc=c.sP)
        if n0 in (2, 3):
            p.step("scalar", lambda e, ps=ps: e.activation(
                vtl[:, 0:512], ps[:, 0:512], AF.Identity),
                waits=[c.sP], inc=c.sA)
            p.step("tensor", lambda e, n0=n0: (
                [e.transpose(trb[:, f * 128:(f + 1) * 128],
                             vtl[:, f * 128:(f + 1) * 128], c.idt[:])
                 for f in range(4)])[-1],
                waits=[c.sA, (c.sV, None)], inc=c.sP)
            p.step("vector", lambda e, n0=n0: e.tensor_copy(
                ktl[:, (n0 - 2) * 512:(n0 - 1) * 512], trb[:, 0:512]),
                waits=[c.sP, (c.sV, None)], inc=c.sV)
        else:
            p.step("scalar", lambda e, ps=ps, n0=n0: e.activation(
                vtl[:, (n0 - 4) * 512:(n0 - 3) * 512], ps[:, 0:512], AF.Identity),
                waits=[c.sP, (c.sV, None)], inc=c.sA)
    offs = 0
    for src_ap, nel in [(ktl, 128 * 1024), (vtl, 128 * 1024)]:
        dst = c.cca_in[0, offs:offs + nel].rearrange("(p f) -> p f", p=128)
        p.step("sync", lambda e, d=dst, s=src_ap: e.dma_start(out=d, in_=s),
               waits=[(c.sV, None), (c.sA, None)], inc=c.sD, k=16)
        offs += nel
    pl_pack = ar.bf16(8 * 32)
    for kk in range(KD):
        p.step("vector", lambda e, kk=kk: e.tensor_copy(
            pl_pack[:, kk * 32:(kk + 1) * 32],
            M["pooled"][:, kk * 64 + 32:kk * 64 + 64]),
            waits=[(c.sV, None)], inc=c.sV)
    dst = c.cca_in[0, offs:offs + 128 * 256].rearrange("(p f) -> p f", p=128)
    p.step("sync", lambda e, d=dst: e.dma_start(out=d, in_=pl_pack[:]),
           waits=[c.sV], inc=c.sD, k=16)
    offs += 128 * 256
    assert offs == AA_TOT
    rg = RG_PAIRS if NGROUP == 2 else [[i] for i in range(NCORE)]
    p.step("gpsimd", lambda e, rg=rg: e.collective_compute(
        "AllGather", ALU.bypass, replica_groups=rg,
        ins=[c.cca_in.opt()], outs=[c.cca_out.opt()]),
        waits=[c.sD, (c.sC, None)], inc=c.sC)
    c.p.flush()
    ar.reset(mS)

    for half in range(2):
        emit_mla_half(c, M, half)

    emit_mla_out(c, M)
    p.barrier()
    ar.reset(mL)


def emit_mla_half(c, M, half):
    p = c.p
    io = c.io
    ar = c.ar
    mH = ar.mark()
    th0 = half * 8
    t00 = th0 * 128
    Kh = {}
    Kh["kT"] = ar.bf16(8 * 1152)
    Kh["v"] = ar.bf16(9 * 1024)
    Kh["q"] = ar.bf16(8 * 1024)
    wbuf = [ar.bf16(8 * 512) for _ in range(2)]
    trb = c.PS[6].bitcast(BF16)
    evs = ar.bf16(512)

    def kT_col(f, col, n):
        return Kh["kT"][:, f * 1152 + col: f * 1152 + col + n]

    kv_tiles = list(range(max(0, th0 - 1), th0 + 8))
    for ni in range(6):
        wb = wbuf[ni % 2]
        for kk in range(KD):
            p.step("gpsimd", lambda e, kk=kk, ni=ni, wb=wb: e.dma_start(
                out=wb[:, kk * 512:(kk + 1) * 512],
                in_=io["aqkv_w"][kk * 128:(kk + 1) * 128,
                                 ni * 512:(ni + 1) * 512]),
                waits=[(c.sP, None)] if kk == 0 else [], inc=c.sW, k=16)
        tiles = kv_tiles if ni >= 2 else list(range(th0, th0 + 8))
        for tt in tiles:
            ps = c.PS[tt % 2]
            p.step("tensor", lambda e, wb=wb, ni=ni, tt=tt, ps=ps: (
                mla_qkv_mms(c, e, ps[:, 0:512],
                            lambda kk: wb[:, kk * 512:(kk + 1) * 512], tt * 128),
                e.matmul(ps[:, 0:512], c.onr[0:1, :],
                         M["qkvb"][0:1, ni * 512:(ni + 1) * 512],
                         start=False, stop=True))[-1],
                waits=[c.sW, (c.sV, None), (c.sA, None)], inc=c.sP)
            if ni < 2:
                p.step("vector", lambda e, ps=ps, ni=ni, tt=tt: e.tensor_scalar_mul(
                    Kh["q"][:, (tt - th0) * 1024 + ni * 512:
                            (tt - th0) * 1024 + (ni + 1) * 512],
                    ps[:, 0:512], 0.125),
                    waits=[c.sP, (c.sV, None)], inc=c.sV)
            elif ni < 4:
                p.step("scalar", lambda e, ps=ps: e.activation(
                    evs[:], ps[:, 0:512], AF.Identity),
                    waits=[c.sP, (c.sP2, None)], inc=c.sA)
                p.step("tensor", lambda e: (
                    [e.transpose(trb[:, f * 128:(f + 1) * 128],
                                 evs[:, f * 128:(f + 1) * 128], c.idt[:])
                     for f in range(4)])[-1],
                    waits=[c.sA, (c.sV, None)], inc=c.sP2)
                col = 128 * (tt - th0 + 1)
                for f in range(4):
                    fk = (ni - 2) * 4 + f
                    p.step("vector", lambda e, fk=fk, f=f, col=col: e.tensor_copy(
                        kT_col(fk, col, 128), trb[:, f * 128:(f + 1) * 128]),
                        waits=[c.sP2, (c.sV, None)], inc=c.sV)
            else:
                p.step("scalar", lambda e, ps=ps, ni=ni, tt=tt: e.activation(
                    Kh["v"][:, (tt - th0 + 1) * 1024 + (ni - 4) * 512:
                            (tt - th0 + 1) * 1024 + (ni - 3) * 512],
                    ps[:, 0:512], AF.Identity),
                    waits=[c.sP, (c.sV, None)], inc=c.sA)

    if half == 0:
        offs = 0
        for f in range(8):
            src = c.cca_out[0, f * 128 * 128:(f + 1) * 128 * 128].rearrange(
                "(p f2) -> p f2", p=128)
            p.step("sync", lambda e, d=kT_col(f, 0, 128), s=src:
                   e.dma_start(out=d, in_=s),
                   waits=[(c.sC, None)], inc=c.sD2, k=16)
        src = c.cca_out[0, AA_KT:AA_KT + AA_V].rearrange("(p f2) -> p f2", p=128)
        p.step("sync", lambda e, s=src: e.dma_start(out=Kh["v"][:, 0:1024], in_=s),
               waits=[(c.sC, None)], inc=c.sD2, k=16)
        src = c.cca_out[0, AA_KT + AA_V:AA_TOT].rearrange("(p f2) -> p f2", p=128)
        plr = ar.bf16(8 * 32)
        p.step("sync", lambda e, s=src, plr=plr: e.dma_start(out=plr[:], in_=s),
               waits=[(c.sC, None)], inc=c.sD2, k=16)
        for kk in range(KD):
            p.step("vector", lambda e, kk=kk, plr=plr: e.tensor_copy(
                M["pooled"][:, kk * 64:kk * 64 + 32],
                plr[:, kk * 32:(kk + 1) * 32]),
                waits=[(c.sD2, None), (c.sV, None)], inc=c.sV)
        for f in range(8):
            p.step("vector", lambda e, f=f: e.tensor_scalar_mul(
                kT_col(f, 0, 128), kT_col(f, 0, 128), c.hsc[:, 0:1]),
                waits=[(c.sD2, None), (c.sV, None)], inc=c.sV)
        p.step("vector", lambda e: e.tensor_scalar_mul(
            Kh["v"][:, 0:1024], Kh["v"][:, 0:1024], c.hsc[:, 0:1]),
            waits=[(c.sD2, None), (c.sV, None)], inc=c.sV)
        emit_mla_latent_prep(c, M)

    for tt in range(th0, th0 + 8):
        emit_mla_tile(c, M, Kh, half, tt, kT_col)
    c.p.flush()
    ar.reset(mH)


def emit_mla_latent_prep(c, M):
    p = c.p
    io = c.io
    ar = c.ar
    m0 = ar.mark()
    wb = ar.bf16(8 * 512)
    trb = c.PS[6].bitcast(BF16)
    for kk in range(KD):
        p.step("gpsimd", lambda e, kk=kk, wb=wb: e.dma_start(
            out=wb[:, kk * 512:(kk + 1) * 512],
            in_=io["arkv_w"][kk * 128:(kk + 1) * 128, :]),
            waits=[(c.sP, None)] if kk == 0 else [], inc=c.sW, k=16)
    ps = c.PS[7]
    p.step("tensor", lambda e, wb=wb: (
        [e.matmul(ps[0:64, 0:512], M["pooled"][:, kk * 64:(kk + 1) * 64],
                  wb[:, kk * 512:(kk + 1) * 512],
                  start=(kk == 0), stop=False) for kk in range(KD)] +
        [e.matmul(ps[0:64, 0:512], c.onr[0:1, 0:64],
                  M["rkvb"][0:1, 0:512], start=False, stop=True)])[-1],
        waits=[c.sW, (c.sV, None), (c.sA, None)], inc=c.sP)
    p.step("scalar", lambda e: e.activation(M["rkv"][0:64, :], ps[0:64, 0:512],
                                            AF.Identity),
           waits=[c.sP], inc=c.sA)
    p.step("tensor", lambda e: (
        e.transpose(trb[:, 0:64], M["rkv"][0:64, 0:128], c.idt[0:64, 0:64]),
        e.transpose(trb[:, 64:128], M["rkv"][0:64, 128:256], c.idt[0:64, 0:64]))[-1],
        waits=[c.sA, (c.sV, None)], inc=c.sP)
    p.step("vector", lambda e: e.tensor_copy(M["rkT"][:], trb[:, 0:128]),
           waits=[c.sP, (c.sV, None)], inc=c.sV)
    c.p.flush()
    ar.reset(m0)


def emit_mla_tile(c, M, Kh, half, tt, kT_col):
    p = c.p
    io = c.io
    ar = c.ar
    m0 = ar.mark()
    th0 = half * 8
    t0 = tt * 128
    tl = tt - th0
    trb = c.PS[6].bitcast(BF16)

    qTt = ar.bf16(8 * 128)
    probs = ar.bf16(256)
    probsm = ar.bf16(256)
    probsT = ar.bf16(256)
    lpb = ar.bf16(64)
    lpbm = ar.bf16(64)
    lpbT = ar.bf16(128)
    loc = ar.bf16(1024)
    rem = ar.bf16(1024)
    rqs = ar.bf16(256)
    rqT = ar.bf16(256)
    d1 = ar.bf16(1024)
    mstage = loc
    glog = ar.f32(3 * 16)
    gate = ar.f32(16)
    st2 = ar.f32(4)

    p.step("tensor", lambda e: (
        [e.transpose(trb[:, f * 128:(f + 1) * 128],
                     Kh["q"][:, tl * 1024 + f * 128: tl * 1024 + (f + 1) * 128],
                     c.idt[:]) for f in range(8)])[-1],
        waits=[(c.sV, None), (c.sA, None)], inc=c.sP)
    p.step("vector", lambda e: e.tensor_copy(qTt[:], trb[:, 0:1024]),
           waits=[c.sP, (c.sV, None)], inc=c.sV)

    p.step("sync", lambda e: e.dma_start(out=M["rowk"][:, 0:1],
                                         in_=io["arowkeep"][t0:t0 + 128, :]),
           waits=[(c.sV, None)], inc=c.sD, k=16)
    p.step("sync", lambda e: e.dma_start(out=M["rowk"][:, 1:2],
                                         in_=io["lrowkeep"][t0:t0 + 128, :]),
           inc=c.sD, k=16)
    p.step("sync", lambda e: e.dma_start(out=M["lmrow"][:],
                                         in_=io["lmask01"][t0:t0 + 128, :]),
           inc=c.sD, k=16)
    mask = c.at0 if tt == 0 else c.aband

    locps = [c.PS[4], c.PS[5]]
    kcol = 128 * (tl + 1) - 128
    for hp in range(8):
        st = c.PS[hp % 2]
        stb = c.PS[2 + hp % 2]
        qa = qTt[0:64, hp * 128:(hp + 1) * 128]
        qb = qTt[64:128, hp * 128:(hp + 1) * 128]
        p.step("tensor", lambda e, hp=hp, st=st, stb=stb, qa=qa, qb=qb: (
            e.matmul(st[:, 0:256], qa, kT_col(hp, kcol, 256)[0:64, :],
                     start=True, stop=True, tile_position=(0, 0)),
            e.matmul(stb[:, 0:256], qb, kT_col(hp, kcol, 256)[64:128, :],
                     start=True, stop=True, tile_position=(64, 0)))[-1],
            waits=[(c.sV, None), (c.sA, None), (c.sG, None)] +
            ([(c.sD2, None)] if (half == 0 and tt == 0) else []),
            inc=c.sP)
        for hh in range(2):
            h = 2 * hp + hh
            sc = (st if hh == 0 else stb)[:, 0:256]
            p.step("vector", lambda e, sc=sc: e.reduce_max(
                st2[:, 0:1], sc, axis=AX.X, negate=True),
                waits=[c.sP, (c.sV, None)], inc=c.sV)
            p.step("scalar", lambda e, sc=sc: e.activation(
                probs[:], sc, AF.Exp, bias=st2[:, 0:1]),
                waits=[c.sV, (c.sG, None)], inc=c.sA)
            p.step("gpsimd", lambda e, mask=mask: e.tensor_tensor(
                probsm[:], probs[:], mask[:], ALU.mult),
                waits=[c.sA, (c.sG, None), (c.sP, None), c.sD], inc=c.sG)
            p.step("vector", lambda e: e.reduce_sum(st2[:, 1:2], probsm[:],
                                                    axis=AX.X),
                   waits=[c.sG, (c.sV, None)], inc=c.sV)
            p.step("vector", lambda e: e.tensor_scalar_max(
                st2[:, 1:2], st2[:, 1:2], 1e-30), waits=[c.sV], inc=c.sV)
            p.step("vector", lambda e: e.reciprocal(st2[:, 2:3], st2[:, 1:2]),
                   waits=[c.sV], inc=c.sV)
            p.step("vector", lambda e: e.tensor_tensor(
                st2[:, 2:3], st2[:, 2:3], M["rowk"][:, 0:1], ALU.mult),
                waits=[c.sV], inc=c.sV)
            p.step("vector", lambda e: e.tensor_scalar_mul(
                probsm[:], probsm[:], st2[:, 2:3]),
                waits=[c.sV, c.sG], inc=c.sV)
            p.step("tensor", lambda e: (
                e.transpose(trb[:, 0:128], probsm[:, 0:128], c.idt[:]),
                e.transpose(trb[:, 128:256], probsm[:, 128:256], c.idt[:]))[-1],
                waits=[c.sV, (c.sA, None)], inc=c.sP)
            p.step("scalar", lambda e: e.activation(probsT[:], trb[:, 0:256],
                                                    AF.Identity),
                   waits=[c.sP], inc=c.sA)
            lp = locps[h // 8]
            col = (h % 8) * 64
            p.step("tensor", lambda e, h=h, lp=lp, tl=tl, col=col: (
                e.matmul(lp[:, col:col + 64],
                         probsT[:, 0:128],
                         Kh["v"][:, tl * 1024 + h * 64: tl * 1024 + h * 64 + 64],
                         start=True, stop=False),
                e.matmul(lp[:, col:col + 64],
                         probsT[:, 128:256],
                         Kh["v"][:, (tl + 1) * 1024 + h * 64:
                                 (tl + 1) * 1024 + h * 64 + 64],
                         start=False, stop=True))[-1],
                waits=[c.sA, (c.sV, None)], inc=c.sP)
    for b in range(2):
        p.step("vector", lambda e, b=b: e.tensor_copy(
            loc[:, b * 512:(b + 1) * 512], c.PS[4 + b][:, 0:512]),
            waits=[c.sP, (c.sV, None), (c.sO, None)], inc=c.sV)

    psr = c.PS[2]
    p.step("tensor", lambda e: (
        mla_qkv_mms(c, e, psr[:, 0:256],
                    lambda kk: c.wrq[:, kk * 256:(kk + 1) * 256], t0),
        e.matmul(psr[:, 0:256], c.onr[0:1, :], M["rqb"][0:1, 0:256],
                 start=False, stop=True))[-1],
        waits=[(c.sV, None), (c.sA, None)], inc=c.sP)
    p.step("vector", lambda e: e.tensor_scalar_mul(rqs[:], psr[:, 0:256], 0.125),
           waits=[c.sP, (c.sV, None)], inc=c.sV)
    p.step("tensor", lambda e: (
        e.transpose(trb[:, 0:128], rqs[:, 0:128], c.idt[:]),
        e.transpose(trb[:, 128:256], rqs[:, 128:256], c.idt[:]))[-1],
        waits=[c.sV, (c.sA, None)], inc=c.sP)
    p.step("vector", lambda e: e.tensor_copy(rqT[:], trb[:, 0:256]),
           waits=[c.sP, (c.sV, None)], inc=c.sV)
    latps = c.PS[4]
    for lh in range(4):
        qa = rqT[(lh % 2) * 64:(lh % 2) * 64 + 64, (lh // 2) * 128:(lh // 2) * 128 + 128]
        ka = M["rkT"][(lh % 2) * 64:(lh % 2) * 64 + 64, (lh // 2) * 64:(lh // 2) * 64 + 64]
        tp = (64 * (lh % 2), 0)
        lbank = c.PS[2] if lh % 2 == 0 else c.PS[3]
        p.step("tensor", lambda e, qa=qa, ka=ka, tp=tp, lh=lh, lbank=lbank:
               e.matmul(lbank[:, 256 + (lh // 2) * 64: 256 + (lh // 2) * 64 + 64],
                        qa, ka, start=True, stop=True, tile_position=tp),
               waits=[(c.sV, None)], inc=c.sP)
        sc = lbank[:, 256 + (lh // 2) * 64: 256 + (lh // 2) * 64 + 64]
        p.step("vector", lambda e, sc=sc: e.reduce_max(
            st2[:, 0:1], sc, axis=AX.X, negate=True),
            waits=[c.sP, (c.sV, None)], inc=c.sV)
        p.step("scalar", lambda e, sc=sc: e.activation(
            lpb[:], sc, AF.Exp, bias=st2[:, 0:1]),
            waits=[c.sV, (c.sG, None)], inc=c.sA)
        p.step("gpsimd", lambda e: e.tensor_tensor(lpbm[:], lpb[:], M["lmrow"][:],
                                                   ALU.mult),
               waits=[c.sA, (c.sG, None), (c.sP, None), c.sD], inc=c.sG)
        p.step("vector", lambda e: e.reduce_sum(st2[:, 1:2], lpbm[:], axis=AX.X),
               waits=[c.sG, (c.sV, None)], inc=c.sV)
        p.step("vector", lambda e: e.tensor_scalar_max(
            st2[:, 1:2], st2[:, 1:2], 1e-30), waits=[c.sV], inc=c.sV)
        p.step("vector", lambda e: e.reciprocal(st2[:, 2:3], st2[:, 1:2]),
               waits=[c.sV], inc=c.sV)
        p.step("vector", lambda e: e.tensor_tensor(
            st2[:, 2:3], st2[:, 2:3], M["rowk"][:, 1:2], ALU.mult),
            waits=[c.sV], inc=c.sV)
        p.step("vector", lambda e: e.tensor_scalar_mul(
            lpbm[:], lpbm[:], st2[:, 2:3]), waits=[c.sV, c.sG], inc=c.sV)
        p.step("tensor", lambda e: e.transpose(trb[0:64, 0:128], lpbm[:], c.idt[:]),
               waits=[c.sV, (c.sA, None)], inc=c.sP)
        p.step("scalar", lambda e: e.activation(lpbT[0:64, :], trb[0:64, 0:128],
                                                AF.Identity),
               waits=[c.sP], inc=c.sA)
        p.step("tensor", lambda e, lh=lh: e.matmul(
            latps[:, lh * 64:(lh + 1) * 64], lpbT[0:64, :],
            M["rkv"][0:64, 256 + lh * 64:256 + (lh + 1) * 64],
            start=True, stop=True),
            waits=[c.sA, (c.sV, None)], inc=c.sP)
    p.step("vector", lambda e: e.tensor_copy(rqs[:], latps[:, 0:256]),
           waits=[c.sP, (c.sV, None)], inc=c.sV)
    p.step("tensor", lambda e: (
        e.transpose(trb[:, 0:128], rqs[:, 0:128], c.idt[:]),
        e.transpose(trb[:, 128:256], rqs[:, 128:256], c.idt[:]))[-1],
        waits=[c.sV, (c.sA, None)], inc=c.sP)
    p.step("vector", lambda e: e.tensor_copy(rqT[:], trb[:, 0:256]),
           waits=[c.sP, (c.sV, None)], inc=c.sV)
    p.step("tensor", lambda e: (
        [e.matmul((c.PS[2][:, 0:512] if nn == 0 else c.PS[3][:, 0:512]),
                  rqT[:, ch * 128:(ch + 1) * 128],
                  c.wrout[0:128, ch * 1024 + nn * 512: ch * 1024 + (nn + 1) * 512],
                  start=(ch == 0), stop=False)
         for nn in range(2) for ch in range(2)] +
        [e.matmul((c.PS[2][:, 0:512] if nn == 0 else c.PS[3][:, 0:512]),
                  c.onr[0:1, :], c.routb[0:1, nn * 512:(nn + 1) * 512],
                  start=False, stop=True) for nn in range(2)])[-1],
        waits=[(c.sV, None), (c.sA, None)], inc=c.sP)
    p.step("scalar", lambda e: e.activation(rem[:, 0:512], c.PS[2][:, 0:512],
                                            AF.Identity),
           waits=[c.sP], inc=c.sA)
    p.step("scalar", lambda e: e.activation(rem[:, 512:1024], c.PS[3][:, 0:512],
                                            AF.Identity),
           inc=c.sA)

    p.step("vector", lambda e: e.tensor_tensor(
        d1[:].rearrange("p (h d) -> p h d", h=16), loc[:].rearrange("p (h d) -> p h d", h=16),
        bass.AP(M["glb"].tensor, M["glb"].offset,
                [[M["glb"].ap[0][0], 128], [0, 16], [1, 64]]), ALU.mult),
        waits=[c.sA, (c.sV, None)], inc=c.sV)
    p.step("vector", lambda e: e.reduce_sum(
        glog[:, 0:16], d1[:].rearrange("p (h d) -> p h d", h=16), axis=AX.X),
        waits=[c.sV], inc=c.sV)
    p.step("vector", lambda e: e.tensor_tensor(
        d1[:].rearrange("p (h d) -> p h d", h=16), rem[:].rearrange("p (h d) -> p h d", h=16),
        bass.AP(M["grb"].tensor, M["grb"].offset,
                [[M["grb"].ap[0][0], 128], [0, 16], [1, 64]]), ALU.mult),
        waits=[c.sV], inc=c.sV)
    p.step("vector", lambda e: e.reduce_sum(
        glog[:, 16:32], d1[:].rearrange("p (h d) -> p h d", h=16), axis=AX.X),
        waits=[c.sV], inc=c.sV)
    p.step("vector", lambda e: e.tensor_tensor(
        glog[:, 32:48], glog[:, 0:16], glog[:, 16:32], ALU.add),
        waits=[c.sV], inc=c.sV)
    p.step("scalar", lambda e: e.activation(gate[:], glog[:, 32:48], AF.Sigmoid),
           waits=[c.sV], inc=c.sA)
    p.step("vector", lambda e: e.tensor_tensor(d1[:], loc[:], rem[:], ALU.subtract),
           waits=[c.sA, (c.sV, None)], inc=c.sV)
    for h in range(16):
        p.step("vector", lambda e, h=h: e.tensor_scalar_mul(
            d1[:, h * 64:(h + 1) * 64], d1[:, h * 64:(h + 1) * 64],
            gate[:, h:h + 1]),
            waits=[(c.sV, None)], inc=c.sV)
    p.step("vector", lambda e: e.tensor_tensor(d1[:], d1[:], rem[:], ALU.add),
           waits=[c.sV], inc=c.sV)
    p.step("tensor", lambda e: (
        [e.transpose(trb[:, f * 128:(f + 1) * 128],
                     d1[:, f * 128:(f + 1) * 128], c.idt[:]) for f in range(8)])[-1],
        waits=[c.sV, (c.sA, None)], inc=c.sP)
    p.step("vector", lambda e: e.tensor_copy(mstage[:], trb[:, 0:1024]),
           waits=[c.sP, (c.sV, None)], inc=c.sV)
    for kk in range(KD):
        p.step("sync", lambda e, kk=kk, t0=t0: e.dma_start(
            out=c.mixd[kk, :, t0:t0 + 128],
            in_=mstage[:, kk * 128:(kk + 1) * 128]),
            waits=[(c.sV, None)] if kk == 0 else [], inc=c.sO, k=16)
    c.p.flush()
    ar.reset(m0)


def emit_mla_out(c, M):
    p = c.p
    io = c.io
    ar = c.ar
    m0 = ar.mark()
    wout = ar.bf16(8 * 1024)
    xold = ar.f32(KD * 512)
    bcol = ar.f32(8)
    mxb = [ar.bf16(8 * 512) for _ in range(2)]
    for kk in range(KD):
        p.step("gpsimd", lambda e, kk=kk: e.dma_start(
            out=wout[:, kk * 1024:(kk + 1) * 1024],
            in_=io["aout_w"][kk * 128:(kk + 1) * 128, :]),
            waits=[(c.sP, None)] if kk == 0 else [], inc=c.sW, k=16)
    for m in range(KD):
        p.step("sync", lambda e, m=m: e.dma_start(
            out=bcol[:, m:m + 1],
            in_=io["aout_b"][m * 128:(m + 1) * 128].unsqueeze(1)),
            waits=[(c.sV, None)], inc=c.sD, k=16)
    for tc in range(4):
        t0 = tc * 512
        mx = mxb[tc % 2]
        for k in range(KD):
            p.step("sync", lambda e, k=k, t0=t0: e.dma_start(
                out=xold[:, k * 512:(k + 1) * 512],
                in_=c.xmod[k, :, t0:t0 + 512]),
                waits=[(c.sV, None), (c.sO, None)], inc=c.sD, k=16)
        for kk in range(KD):
            p.step("sync", lambda e, kk=kk, t0=t0, mx=mx: e.dma_start(
                out=mx[:, kk * 512:(kk + 1) * 512],
                in_=c.mixd[kk, :, t0:t0 + 512]),
                waits=[(c.sP, None), (c.sO, None)] if kk == 0 else [],
                inc=c.sD, k=16)
        for m in range(KD):
            ps = c.PS[m % 2]
            p.step("tensor", lambda e, m=m, t0=t0, ps=ps, mx=mx: (
                [e.matmul(ps[:, 0:512], wout[:, kk * 1024 + m * 128:
                                             kk * 1024 + (m + 1) * 128],
                          mx[:, kk * 512:(kk + 1) * 512],
                          start=(kk == 0), stop=(kk == KD - 1))
                 for kk in range(KD)])[-1],
                waits=[c.sW, c.sD, (c.sV, None)], inc=c.sP)
            p.step("vector", lambda e, m=m, ps=ps: e.scalar_tensor_tensor(
                xold[:, m * 512:(m + 1) * 512], ps[:, 0:512], bcol[:, m:m + 1],
                xold[:, m * 512:(m + 1) * 512], ALU.add, ALU.add),
                waits=[c.sP, (c.sV, None)], inc=c.sV)
        for k in range(KD):
            p.step("sync", lambda e, k=k, t0=t0: e.dma_start(
                out=c.xmod[k, :, t0:t0 + 512],
                in_=xold[:, k * 512:(k + 1) * 512]),
                waits=[c.sV], inc=c.sO, k=16)
        emit_ln_chunk(c, lambda k: xold[:, k * 512:(k + 1) * 512], t0)
    c.p.flush()
    ar.reset(m0)


def kernel(**inputs):
    x = np.ascontiguousarray(np.asarray(inputs["L0_attn_x"], dtype=np.float32))
    w = {k: np.ascontiguousarray(np.asarray(v, dtype=np.float32))
         for k, v in inputs.items()}

    nc = build_program()

    in_maps = []
    for core in range(NCORE):
        b, h = core // 2, core % 2
        start = h * T
        csq, snq, csk, snk = _rotary_tables(start)
        fband, fspec = _ffn_masks(h)
        aband, at0, arow = _mla_masks(h)
        lmask, lrow = _lat_masks(start, h)
        m = {
            "x_in": x[b, start:start + T, :],
            "aqkv_w": w["L0_attn_qkv_w"], "aqkv_b": w["L0_attn_qkv_b"],
            "arq_w": w["L0_attn_rq_w"], "arq_b": w["L0_attn_rq_b"],
            "arkv_w": w["L0_attn_rkv_w"], "arkv_b": w["L0_attn_rkv_b"],
            "arout_w": w["L0_attn_rout_w"], "arout_b": w["L0_attn_rout_b"],
            "aout_w": w["L0_attn_out_w"], "aout_b": w["L0_attn_out_b"],
            "agl_w": w["L0_attn_gl_w"], "agr_w": w["L0_attn_gr_w"],
            "fwqkv": w["L0_ffn_wqkv"], "fq_scale": w["L0_ffn_q_scale"],
            "fk_scale": w["L0_ffn_k_scale"], "fwo": w["L0_ffn_wo"],
            "fln_g": w["L0_ffn_ln_g"], "fln_b": w["L0_ffn_ln_b"],
            "fw1": w["L0_ffn_w1"], "fb1": w["L0_ffn_b1"],
            "fw2": w["L0_ffn_w2"], "fb2": w["L0_ffn_b2"],
            "csq": csq, "snq": snq, "csk": csk, "snk": snk,
            "fband01": fband, "fspec01": fspec,
            "aband01": aband, "at001": at0, "arowkeep": arow,
            "lmask01": lmask, "lrowkeep": lrow,
            "halo_scale": np.full((128, 1), float(h), np.float32),
        }
        in_maps.append(m)

    res = run_bass_kernel_spmd(nc, in_maps, core_ids=list(range(NCORE)))
    out = np.empty((B, S, D), np.float32)
    for core in range(NCORE):
        b, h = core // 2, core % 2
        out[b, h * T:(h + 1) * T, :] = res.results[core]["out"]
    return out


# version 10
# speedup vs baseline: 1.3153x; 1.0916x over previous
import numpy as np
import ml_dtypes
import concourse.bass as bass
import concourse.mybir as mybir
from concourse.bass_utils import run_bass_kernel_spmd
from contextlib import ExitStack

F32 = mybir.dt.float32
BF16 = mybir.dt.bfloat16
AF = mybir.ActivationFunctionType
ALU = mybir.AluOpType
AX = mybir.AxisListType

BF = ml_dtypes.bfloat16

B, S, D = 4, 4096, 1024
NCORE = 8
T = S // 2
NT = T // 128
KD = D // 128

F_HEADS, F_HD = 8, 64
F_INNER = 512
F_WIN = 512
F_HALO = 640
F_HID = 2730
HID_CH = [128] * 21 + [42]
F_QKSCALE = 8.0

A_HEADS, A_HD = 16, 64
A_HALO = 128
LAT_N, LAT_D, LAT_H, LAT_HD = 64, 256, 4, 64
CHUNK = 64

XHB = 136

FF_KT = 512 * F_HALO
FF_V = F_HALO * 512
FF_XB = D * 132
FF_TOT = FF_KT + FF_V + FF_XB
AA_KT = D * A_HALO
AA_V = A_HALO * D
AA_PL = D * 32
AA_TOT = AA_KT + AA_V + AA_PL

RG_PAIRS = [[0, 1], [2, 3], [4, 5], [6, 7]]
NGROUP = 2
STAGES = ["ffn0", "ffn1", "b_ffn", "mla", "ffn0", "ffn1"]
FFN_PARTS = ("attn", "wo", "mlp")
ATTN_CUT = None
SKIP_FINAL = False
LAST_C = None
SAFE_ACCUM = True
SAFE_NORM = False

ENGS = ("sync", "vector", "scalar", "tensor", "gpsimd")


def _rotary_tables(start):
    n = S
    pos = np.arange(start - 128, start + T, dtype=np.float64)
    half = np.arange(0, F_HD, 2, dtype=np.float64)
    inv_freq = 1.0 / (10000.0 ** (half / F_HD))
    freqs = pos[:, None] * inv_freq[None, :]
    base = (half + 0.4 * F_HD) / (1.4 * F_HD)
    power = (pos - n // 2) / float(F_WIN // 2)
    scale = base[None, :] ** power[:, None]
    cos, sin = np.cos(freqs), np.sin(freqs)
    csq = np.concatenate([cos * scale, cos * scale], 1)
    snq = np.concatenate([-sin * scale, sin * scale], 1)
    csk = np.concatenate([cos / scale, cos / scale], 1)
    snk = np.concatenate([-sin / scale, sin / scale], 1)
    bad = pos < 0
    for t in (csq, snq, csk, snk):
        t[bad] = 0.0
    return [t.astype(BF) for t in (csq, snq, csk, snk)]


def _ffn_masks(h):
    i = np.arange(128)[:, None]
    c = np.arange(640)[None, :]
    band = ((c >= i) & (c <= i + 512)).astype(np.float32)
    spec = np.zeros((5, 128, 640), np.float32)
    for idx, t0 in enumerate([-128, 0, 128, 256, 384]):
        m = band.copy()
        if h == 0:
            m = np.where((t0 - 512 + c) < 0, 0.0, m)
        spec[idx] = m
    band = (band - 1.0) * 30000.0
    spec = (spec - 1.0) * 30000.0
    return band.astype(BF), spec.astype(BF)


def _mla_masks(h):
    i = np.arange(128)[:, None]
    c = np.arange(256)[None, :]
    band = ((c >= i) & (c <= i + 127)).astype(np.float32)
    t0 = band.copy()
    rowkeep = np.ones((T, 1), np.float32)
    if h == 0:
        t0 = np.where((c - 128) < 0, 0.0, t0)
        rowkeep[0, 0] = 0.0
    return band.astype(BF), t0.astype(BF), rowkeep


def _lat_masks(start, h):
    cid = (start + np.arange(T)) // CHUNK
    j = np.arange(LAT_N)[None, :]
    mask = (j < cid[:, None]).astype(np.float32)
    rowkeep = (cid > 0).astype(np.float32)[:, None]
    return mask.astype(BF), rowkeep


class Sem:
    def __init__(self, nc, es, name):
        self.h = es.enter_context(nc.semaphore(name))
        self.n = 0
        self.name = name
        self.last_waited = 0
        self.synced = 0


class Arena:
    def __init__(self, nc, es, name, cols):
        self.t = es.enter_context(nc.sbuf_tensor(name, [128, cols], F32))
        self.cols = cols
        self.top = 0

    def f32(self, cols):
        base = self.top
        assert base + cols <= self.cols, \
            f"arena overflow {base}+{cols}>{self.cols}"
        self.top += cols
        return self.t[:, base:base + cols]

    def bf16(self, cols):
        assert cols % 2 == 0
        return self.f32(cols // 2).bitcast(BF16)

    def mark(self):
        return self.top

    def reset(self, mark):
        self.top = mark


class Prog:

    def __init__(self, nc):
        self.nc = nc
        self.tapes = {e: [] for e in ENGS}
        self.last_dma = {}

    def step(self, eng, fn, waits=(), inc=None, k=1):
        w = []
        for item in waits:
            if isinstance(item, tuple):
                s, v = item
            else:
                s, v = item, None
            v = s.n if v is None else v
            s.last_waited = max(s.last_waited, v)
            w.append((s.h, v))
        if inc is not None and k == 16:
            if inc.last_waited > inc.synced:
                w.insert(0, (inc.h, inc.n))
                inc.synced = inc.n
        dd = {}
        for h, v in w:
            dd[h] = max(dd.get(h, 0), v)
        w = list(dd.items())
        self.tapes[eng].append((w, fn, None if inc is None else inc.h, k))
        if inc is not None:
            inc.n += k

    def flush(self):
        tapes = {e: t for e, t in self.tapes.items() if t}
        self.tapes = {e: [] for e in ENGS}
        if not tapes:
            return

        def runner(tape):
            def run(eng):
                for w, fn, semh, k in tape:
                    for h, v in w:
                        eng.wait_ge(h, v)
                    ins = fn(eng)
                    if semh is not None:
                        ins.then_inc(semh, k)
            return run

        with self.nc.Block() as blk:
            for e, tape in tapes.items():
                getattr(blk, e)(runner(tape))

    def barrier(self):
        c = self.ctx
        self.step("sync", lambda e: e.nop(),
                  waits=[(c.sD, None), (c.sD2, None), (c.sO, None)])
        self.step("gpsimd", lambda e: e.nop(), waits=[(c.sW, None)])
        self.flush()
        self.nc.all_engine_barrier()
        c.fence_banks()


FENCE_KEYS = ["psk0", "psk1", "psv0", "psv1", "ps7", "set0", "set1", "trb",
              "bx0", "bx1", "pT0", "pT1", "avE0", "avE1",
              "gmsk0", "gmsk1", "gmsk2", "gmsk3",
              "gmask0", "gmask1", "qT0", "qT1",
              "grr0", "grr1", "grr2", "grr3",
              "pm0", "pm1", "pm2", "pm3", "pm4", "pm5", "pm6", "pm7",
              "Akproc", "Bqproc", "wob0", "wob1", "wops0", "wops1", "wopsh",
              "wp0", "wp1", "psa0", "psa1", "psg0", "psg1", "agbuf", "ggbuf",
              "w2banks"]


class Ctx:
    def fence_banks(self):
        g = [(self.sV, self.sV.n), (self.sA, self.sA.n), (self.sG, self.sG.n),
             (self.sP, self.sP.n)]
        for k in FENCE_KEYS:
            self.bg[k] = list(g)

    def bgate(self, key):
        return list(self.bg.get(key, []))

    def bset(self, key, sem):
        self.bg[key] = [(sem, sem.n)]

    def badd(self, key, sem):
        self.bg.setdefault(key, []).append((sem, sem.n))


def build_program():
    nc = bass.Bass("TRN2", target_bir_lowering=False)
    c = Ctx()
    c.nc = nc

    def din(name, shape, dt=F32, kind="ExternalInput"):
        return nc.dram_tensor(name, list(shape), dt, kind=kind).ap()

    c.io = io = {}
    io["x_in"] = din("x_in", [T, D])
    for nm, shp in [
        ("aqkv_w", [D, 3 * D]), ("aqkv_b", [3 * D]),
        ("arq_w", [D, LAT_D]), ("arq_b", [LAT_D]),
        ("arkv_w", [D, 2 * LAT_D]), ("arkv_b", [2 * LAT_D]),
        ("arout_w", [LAT_D, D]), ("arout_b", [D]),
        ("aout_w", [D, D]), ("aout_b", [D]),
        ("agl_w", [A_HD, 1]), ("agr_w", [A_HD, 1]),
        ("fwqkv", [2, D, 3 * F_INNER]), ("fq_scale", [2, F_HD]),
        ("fk_scale", [2, F_HD]), ("fwo", [2, F_INNER, D]),
        ("fln_g", [2, D]), ("fln_b", [2, D]),
        ("fw1", [2, D, 2 * F_HID]), ("fb1", [2, 2 * F_HID]),
        ("fw2", [2, F_HID, D]), ("fb2", [2, D]),
    ]:
        io[nm] = din(nm, shp)
    io["csq"] = din("csq", [T + 128, F_HD], BF16)
    io["snq"] = din("snq", [T + 128, F_HD], BF16)
    io["csk"] = din("csk", [T + 128, F_HD], BF16)
    io["snk"] = din("snk", [T + 128, F_HD], BF16)
    io["fband01"] = din("fband01", [128, 640], BF16)
    io["fspec01"] = din("fspec01", [5, 128, 640], BF16)
    io["aband01"] = din("aband01", [128, 256], BF16)
    io["at001"] = din("at001", [128, 256], BF16)
    io["arowkeep"] = din("arowkeep", [T, 1])
    io["lmask01"] = din("lmask01", [T, LAT_N], BF16)
    io["lrowkeep"] = din("lrowkeep", [T, 1])
    io["halo_scale"] = din("halo_scale", [128, 1])
    io["out"] = din("out", [T, D], kind="ExternalOutput")

    c.xmod = din("xmod", [KD, 128, T], kind="Internal")
    c.mixd = din("mixd", [KD, 128, T], BF16, kind="Internal")
    c.ccf_in = din("ccf_in", [1, FF_TOT], BF16, kind="Internal")
    c.ccf_out = din("ccf_out", [NGROUP, FF_TOT], BF16, kind="Internal")
    c.cca_in = din("cca_in", [1, AA_TOT], BF16, kind="Internal")
    c.cca_out = din("cca_out", [NGROUP, AA_TOT], BF16, kind="Internal")

    id_d = nc.inline_tensor(np.eye(128).astype(BF), name="id128").ap()
    onesc_d = nc.inline_tensor(np.ones((128, 1), BF), name="onesc").ap()
    onesr_d = nc.inline_tensor(np.ones((1, 128), BF), name="onesr").ap()
    eps_d = nc.inline_tensor(np.full((1, 512), 1e-5, np.float32),
                             name="epsrow").ap()

    es = ExitStack()
    c.es = es
    c.xm = es.enter_context(nc.sbuf_tensor("xm", [128, KD * T], F32))
    c.xb = es.enter_context(nc.sbuf_tensor("xb", [128, KD * (T + XHB)], BF16))
    c.idt = es.enter_context(nc.sbuf_tensor("idt", [128, 128], BF16))
    c.onc = es.enter_context(nc.sbuf_tensor("onc", [128, 1], BF16))
    c.onr = es.enter_context(nc.sbuf_tensor("onr", [1, 128], BF16))
    c.hsc = es.enter_context(nc.sbuf_tensor("hsc", [128, 1], F32))
    c.epsr_t = es.enter_context(nc.sbuf_tensor("epsr", [1, 512], F32))
    c.epsr = c.epsr_t[0:1, 0:512]
    c.fband = es.enter_context(nc.sbuf_tensor("fband", [128, 640], BF16))
    c.aband = es.enter_context(nc.sbuf_tensor("aband", [128, 256], BF16))
    c.at0 = es.enter_context(nc.sbuf_tensor("at0", [128, 256], BF16))
    c.ar = Arena(nc, es, "arena", 26600)
    c.PS = [es.enter_context(nc.psum_tensor(f"ps{i}", [128, 512], F32))
            for i in range(8)]

    for nm in ["sD", "sW", "sP", "sV", "sA", "sG", "sC", "sO", "sP2", "sV2",
               "sD2"]:
        setattr(c, nm, Sem(nc, es, nm))

    c.p = Prog(nc)
    c.p.ctx = c
    c.bg = {}
    c.consts = dict(id_d=id_d, onesc_d=onesc_d, onesr_d=onesr_d, eps_d=eps_d)

    program(c)
    c.p.flush()
    es.close()
    global LAST_C
    LAST_C = c
    return nc


def xm_v(c, k, t0, tn):
    return c.xm[:, k * T + t0: k * T + t0 + tn]


def xb_v(c, k, t0, tn):
    col = k * (T + XHB) + XHB + t0
    return c.xb[:, col: col + tn]


def emit_prologue(c):
    p = c.p
    io = c.io
    cn = c.consts
    for dst, src in [(c.idt[:], cn["id_d"]), (c.onc[:], cn["onesc_d"]),
                     (c.onr[:], cn["onesr_d"]), (c.epsr, cn["eps_d"]),
                     (c.hsc[:], io["halo_scale"]), (c.fband[:], io["fband01"]),
                     (c.aband[:], io["aband01"]), (c.at0[:], io["at001"])]:
        p.step("sync", lambda e, d=dst, s=src: e.dma_start(out=d, in_=s),
               inc=c.sD, k=16)

    ar = c.ar
    m0 = ar.mark()
    xtok = [ar.f32(1024) for _ in range(2)]
    hi = ar.bf16(1024)
    lo = ar.bf16(1024)
    hswap = ar.bf16(128)
    trb = c.PS[6].bitcast(BF16)
    for tt in range(NT):
        t0 = tt * 128
        xt = xtok[tt % 2]
        p.step("sync", lambda e, xt=xt, t0=t0:
               e.dma_start(out=xt, in_=io["x_in"][t0:t0 + 128, :]),
               waits=[(c.sV, None)], inc=c.sD, k=16)
        p.step("vector", lambda e, xt=xt: e.tensor_copy(hi[:], xt),
               waits=[c.sD, c.sP], inc=c.sV)
        p.step("vector", lambda e, xt=xt:
               e.tensor_tensor(lo[:], xt, hi[:], ALU.subtract),
               waits=[c.sV], inc=c.sV)
        for k in range(KD):
            p.step("tensor", lambda e, k=k: (
                e.transpose(trb[:, 0:128] if False else trb[:, (k % 4) * 256:(k % 4) * 256 + 128],
                            hi[:, k * 128:(k + 1) * 128], c.idt[:]),
                e.transpose(trb[:, (k % 4) * 256 + 128:(k % 4) * 256 + 256],
                            lo[:, k * 128:(k + 1) * 128], c.idt[:]))[-1],
                waits=([c.sV] if k == 0 else [(c.sV2, c.sV2.n - 2)]),
                inc=c.sP)
            p.step("vector", lambda e, k=k:
                   e.tensor_copy(hswap[:], trb[:, (k % 4) * 256:(k % 4) * 256 + 128]),
                   waits=[c.sP, (c.sV2, None)], inc=c.sV2)
            p.step("vector", lambda e, k=k, t0=t0:
                   e.tensor_tensor(xm_v(c, k, t0, 128), hswap[:],
                                   trb[:, (k % 4) * 256 + 128:(k % 4) * 256 + 256],
                                   ALU.add),
                   waits=[c.sV2], inc=c.sV2)
        for k in range(KD):
            p.step("sync", lambda e, k=k, t0=t0:
                   e.dma_start(out=c.xmod[k, :, t0:t0 + 128],
                               in_=xm_v(c, k, t0, 128)),
                   waits=[c.sV2], inc=c.sO, k=16)
    p.step("sync", lambda e: e.nop(), waits=[c.sD, c.sO])
    p.barrier()
    ar.reset(m0)


def emit_ln_chunk(c, src_ap, t0, dst_master=True, write_shadow=True):
    p = c.p
    ar = c.ar
    m0 = ar.mark()
    sq = [ar.bf16(512) for _ in range(2)]
    xcast = [ar.bf16(512) for _ in range(2)]
    rowf = ar.f32(6 * 512)
    R = {nm: rowf[0:1, i * 512:(i + 1) * 512]
         for i, nm in enumerate(["m", "ms", "t", "r1", "r2", "x"])}
    rowb = ar.bf16(4 * 512)
    RB = {nm: rowb[0:1, i * 512:(i + 1) * 512]
          for i, nm in enumerate(["r1h", "r1l", "r2h", "r2l"])}
    b1s = ar.f32(512)
    b2s = ar.f32(512)
    tmp = ar.f32(512)

    ps_sum = c.PS[4][0:1, 0:512]
    ps_ssq = c.PS[5][0:1, 0:512]
    ps_b1 = c.PS[2][:, 0:512]
    ps_b2 = c.PS[3][:, 0:512]

    for k in range(KD):
        p.step("scalar", lambda e, k=k:
               e.activation(sq[k % 2][:], src_ap(k), AF.Square),
               waits=[c.sV] + ([] if k < 2 else [(c.sP, c.sP.n - 2)]),
               inc=c.sA)
        p.step("vector", lambda e, k=k:
               e.tensor_copy(xcast[k % 2][:], src_ap(k)),
               waits=[(c.sV, None)] if k < 2 else [(c.sP, c.sP.n - 1),
                                                   (c.sV, None)],
               inc=c.sV)
        p.step("tensor", lambda e, k=k:
               e.matmul(ps_sum, c.onc[:, 0:1], xcast[k % 2][:],
                        start=(k == 0), stop=(k == KD - 1)),
               waits=[c.sV], inc=c.sP)
        p.step("tensor", lambda e, k=k:
               e.matmul(ps_ssq, c.onc[:, 0:1], sq[k % 2][:],
                        start=(k == 0), stop=(k == KD - 1)),
               waits=[c.sA], inc=c.sP)

    inv = 1.0 / D
    p.step("vector", lambda e: e.tensor_scalar_mul(R["m"], ps_sum, inv),
           waits=[c.sP], inc=c.sV)
    p.step("vector", lambda e: e.tensor_scalar_mul(R["ms"], ps_ssq, inv),
           waits=[c.sV], inc=c.sV)
    p.step("vector", lambda e: e.tensor_tensor(R["t"], R["m"], R["m"], ALU.mult),
           waits=[c.sV], inc=c.sV)
    p.step("vector", lambda e:
           e.tensor_tensor(R["ms"], R["ms"], R["t"], ALU.subtract),
           waits=[c.sV], inc=c.sV)
    p.step("vector", lambda e:
           e.tensor_scalar_add(R["ms"], R["ms"], 1e-5),
           waits=[c.sV], inc=c.sV)
    p.step("scalar", lambda e:
           e.activation(R["t"], R["ms"], AF.Sqrt),
           waits=[c.sV], inc=c.sA)
    p.step("vector", lambda e: e.reciprocal(R["r1"], R["t"]),
           waits=[c.sA], inc=c.sV)
    p.step("vector", lambda e: e.tensor_tensor(R["r2"], R["m"], R["r1"], ALU.mult),
           waits=[c.sV], inc=c.sV)
    p.step("vector", lambda e: e.tensor_scalar_mul(R["r2"], R["r2"], -1.0),
           waits=[c.sV], inc=c.sV)
    for r, rh, rl in [("r1", "r1h", "r1l"), ("r2", "r2h", "r2l")]:
        p.step("vector", lambda e, r=r, rh=rh: e.tensor_copy(RB[rh], R[r]),
               waits=[c.sV], inc=c.sV)
        p.step("vector", lambda e, r=r, rh=rh, rl=rl:
               e.tensor_tensor(R["x"], R[r], RB[rh], ALU.subtract),
               waits=[c.sV], inc=c.sV)
        p.step("vector", lambda e, rl=rl: e.tensor_copy(RB[rl], R["x"]),
               waits=[c.sV], inc=c.sV)

    p.step("tensor", lambda e: (
        e.matmul(ps_b1, c.onr[0:1, :], RB["r1h"], start=True, stop=False),
        e.matmul(ps_b1, c.onr[0:1, :], RB["r1l"], start=False, stop=True))[-1],
        waits=[c.sV, c.sA], inc=c.sP)
    p.step("tensor", lambda e: (
        e.matmul(ps_b2, c.onr[0:1, :], RB["r2h"], start=True, stop=False),
        e.matmul(ps_b2, c.onr[0:1, :], RB["r2l"], start=False, stop=True))[-1],
        waits=[c.sV, c.sA], inc=c.sP)
    p.step("scalar", lambda e: e.activation(b1s, ps_b1, AF.Identity),
           waits=[c.sP], inc=c.sA)
    p.step("scalar", lambda e: e.activation(b2s, ps_b2, AF.Identity),
           waits=[c.sP], inc=c.sA)

    for k in range(KD):
        p.step("vector", lambda e, k=k:
               e.tensor_tensor(tmp, src_ap(k), b1s, ALU.mult),
               waits=[c.sA, c.sV], inc=c.sV)
        p.step("vector", lambda e, k=k, t0=t0:
               e.tensor_tensor(xm_v(c, k, t0, 512), tmp, b2s, ALU.add),
               waits=[c.sV], inc=c.sV)
        if True:
            p.step("vector", lambda e, k=k, t0=t0:
                   e.tensor_copy(xb_v(c, k, t0, 512), xm_v(c, k, t0, 512)),
                   waits=[c.sV], inc=c.sV)
    c.p.flush()
    ar.reset(m0)


def emit_boundary(c, kind):
    p = c.p
    ar = c.ar
    if kind == "init":
        for tc in range(4):
            t0 = tc * 512
            emit_ln_chunk(c, lambda k, t0=t0: xm_v(c, k, t0, 512), t0)
        p.barrier()
        return

    scale = 0.5 if kind in ("final", "mid_ffn") else 1.0
    assert kind in ("final", "mid_ffn", "mid_mla")
    m0 = ar.mark()
    xnew = ar.f32(KD * 512)
    xold = ar.f32(KD * 512)
    for tc in range(4):
        t0 = tc * 512
        for k in range(KD):
            p.step("sync", lambda e, k=k, t0=t0:
                   e.dma_start(out=xold[:, k * 512:(k + 1) * 512],
                               in_=c.xmod[k, :, t0:t0 + 512]),
                   waits=[(c.sV, None)], inc=c.sD, k=16)
        for k in range(KD):
            p.step("vector", lambda e, k=k, t0=t0:
                   e.scalar_tensor_tensor(
                       xnew[:, k * 512:(k + 1) * 512],
                       xm_v(c, k, t0, 512), scale,
                       xold[:, k * 512:(k + 1) * 512], ALU.mult, ALU.add),
                   waits=[c.sD, (c.sO, None)], inc=c.sV)
        if kind != "final":
            for k in range(KD):
                p.step("sync", lambda e, k=k, t0=t0:
                       e.dma_start(out=c.xmod[k, :, t0:t0 + 512],
                                   in_=xnew[:, k * 512:(k + 1) * 512]),
                       waits=[c.sV], inc=c.sO, k=16)
            emit_ln_chunk(c, lambda k: xnew[:, k * 512:(k + 1) * 512], t0)
        else:
            emit_out_chunk(c, xnew, t0)
    p.barrier()
    ar.reset(m0)


def emit_out_chunk(c, xnew, t0):
    p = c.p
    ar = c.ar
    m0 = ar.mark()
    hi = ar.bf16(512)
    lo = ar.bf16(512)
    hswap = ar.f32(128)
    stage = [ar.f32(128) for _ in range(2)]
    trb = c.PS[6].bitcast(BF16)
    for k in range(KD):
        p.step("vector", lambda e, k=k: e.tensor_copy(hi[:], xnew[:, k * 512:(k + 1) * 512]),
               waits=[c.sV, (c.sP, None)], inc=c.sV)
        p.step("vector", lambda e, k=k:
               e.tensor_tensor(lo[:], xnew[:, k * 512:(k + 1) * 512], hi[:],
                               ALU.subtract),
               waits=[c.sV], inc=c.sV)
        for s in range(4):
            p.step("tensor", lambda e, s=s: (
                e.transpose(trb[:, s * 256:s * 256 + 128],
                            hi[:, s * 128:(s + 1) * 128], c.idt[:]),
                e.transpose(trb[:, s * 256 + 128:s * 256 + 256],
                            lo[:, s * 128:(s + 1) * 128], c.idt[:]))[-1],
                waits=[c.sV] + ([] if (k, s) == (0, 0) else [(c.sV2, None)]),
                inc=c.sP)
            p.step("vector", lambda e, s=s:
                   e.tensor_copy(hswap[:], trb[:, s * 256:s * 256 + 128]),
                   waits=[c.sP], inc=c.sV2)
            p.step("vector", lambda e, s=s:
                   e.tensor_tensor(stage[s % 2][:], hswap[:],
                                   trb[:, s * 256 + 128:s * 256 + 256], ALU.add),
                   waits=[c.sV2, (c.sO, None)], inc=c.sV2)
            p.step("sync", lambda e, s=s, k=k, t0=t0:
                   e.dma_start(out=c.io["out"][t0 + s * 128:t0 + (s + 1) * 128,
                                               k * 128:(k + 1) * 128],
                               in_=stage[s % 2][:]),
                   waits=[c.sV2], inc=c.sO, k=16)
    c.p.flush()
    ar.reset(m0)


def rot_view(ap):
    rs = ap.ap[0][0]
    return bass.AP(ap.tensor, ap.offset + 32, [[rs, 128], [64, 8], [-32, 2], [1, 32]])


def head_view(ap):
    rs = ap.ap[0][0]
    return bass.AP(ap.tensor, ap.offset, [[rs, 128], [64, 8], [32, 2], [1, 32]])


def rep_view(ap64):
    rs = ap64.ap[0][0]
    return bass.AP(ap64.tensor, ap64.offset, [[rs, 128], [0, 8], [32, 2], [1, 32]])


def row_swap_view(row64):
    return bass.AP(row64.tensor, row64.offset + 32, [[row64.ap[0][0], 1], [-32, 2], [1, 32]])


def ffn_qkv_mms(c, e, ps, wsl, t0):
    ins = None
    for kk in range(KD):
        src = xb_v(c, kk, t0 if kk < 4 else t0 - 1, 128)
        ins = e.matmul(ps, src, wsl(kk), start=(kk == 0), stop=(kk == KD - 1))
    return ins


def emit_ffn_kv_tile(c, l, tt, A, wait_halo=False):
    p = c.p
    t0 = tt * 128
    psk = c.PS[tt % 2]
    psv = c.PS[2 + tt % 2]
    trb = c.PS[6].bitcast(BF16)

    kw = [(c.sW, None)] + c.bg.pop(f"psk{tt % 2}", [])
    if wait_halo:
        kw.append((c.sV, c.haloV))
    p.step("tensor", lambda e: ffn_qkv_mms(
        c, e, psk[:, 0:512], lambda kk: A["wkv"][:, kk * 1024:kk * 1024 + 512],
        t0),
        waits=kw, inc=c.sP)
    p.step("tensor", lambda e: ffn_qkv_mms(
        c, e, psv[:, 0:512], lambda kk: A["wkv"][:, kk * 1024 + 512:kk * 1024 + 1024],
        t0),
        waits=c.bg.pop(f"psv{tt % 2}", []), inc=c.sP)
    vt = tt + 5
    p.step("scalar", lambda e: e.activation(
        A["v"][:, vt * 512:(vt + 1) * 512], psv[:, 0:512], AF.Identity),
        waits=[c.sP], inc=c.sA)
    c.bset(f"psv{tt % 2}", c.sA)

    kW = c.sP.n - 1
    p.step("scalar", lambda e: e.activation(A["sq"][:], psk[:, 0:512], AF.Square),
           waits=[(c.sP, kW)] + c.bg.pop("Akproc", []), inc=c.sA)
    p.step("vector", lambda e: e.reduce_sum(
        A["ss"][:, 0:8], A["sq"][:].rearrange("p (h d) -> p h d", h=8), axis=AX.X),
        waits=[c.sA, (c.sV, None)], inc=c.sV)
    p.step("scalar", lambda e: e.activation(A["sr"][:, 0:8], A["ss"][:, 0:8], AF.Sqrt),
           waits=[c.sV], inc=c.sA)
    p.step("vector", lambda e: e.tensor_scalar_max(A["sr"][:, 0:8], A["sr"][:, 0:8], 1e-12),
           waits=[c.sA], inc=c.sV)
    p.step("vector", lambda e: e.reciprocal(A["rstd"][:, 0:8], A["sr"][:, 0:8]),
           waits=[c.sV], inc=c.sV)
    for h in range(8):
        p.step("vector", lambda e, h=h: e.tensor_scalar_mul(
            A["kn"][:, h * 64:(h + 1) * 64], psk[:, h * 64:(h + 1) * 64],
            A["rstd"][:, h:h + 1]),
            waits=[c.sV], inc=c.sV)
    c.bg[f"psk{tt % 2}"] = [(c.sV, c.sV.n), (c.sA, c.sA.n)]
    r0 = 128 + t0
    p.step("sync", lambda e: e.dma_start(out=A["cst"][:], in_=c.io["csk"][r0:r0 + 128, :]),
           waits=[(c.sV, None)], inc=c.sD, k=16)
    p.step("sync", lambda e: e.dma_start(out=A["snt"][:], in_=c.io["snk"][r0:r0 + 128, :]),
           inc=c.sD, k=16)
    p.step("vector", lambda e: e.tensor_tensor(A["csE"][:], A["cst"][:], A["kscb"][:], ALU.mult),
           waits=[c.sD, c.sV], inc=c.sV)
    p.step("vector", lambda e: e.tensor_tensor(A["snE"][:], A["snt"][:], A["kscsb"][:], ALU.mult),
           waits=[c.sV], inc=c.sV)
    p.step("vector", lambda e: e.tensor_tensor(
        head_view(A["t1"][:]), head_view(A["kn"][:]), rep_view(A["csE"][:]), ALU.mult),
        waits=[c.sV], inc=c.sV)
    p.step("vector", lambda e: e.tensor_tensor(
        head_view(A["t2"][:]), rot_view(A["kn"][:]), rep_view(A["snE"][:]), ALU.mult),
        waits=[c.sV], inc=c.sV)
    p.step("vector", lambda e: e.tensor_tensor(A["kp"][:], A["t1"][:], A["t2"][:], ALU.add),
           waits=[c.sV], inc=c.sV)
    p.step("tensor", lambda e, t0=t0: (
        [e.transpose(trb[:, f * 128:(f + 1) * 128], A["kp"][:, f * 128:(f + 1) * 128],
                     c.idt[:]) for f in range(4)])[-1],
        waits=[c.sV] + c.bg.pop("trb", []), inc=c.sP)
    c.badd("Akproc", c.sP)
    for f in range(4):
        p.step("vector", lambda e, f=f, t0=t0: e.tensor_copy(
            c.kT_col(A, f, 640 + t0, 128), trb[:, f * 128:(f + 1) * 128]),
            waits=[c.sP, (c.sV, None)], inc=c.sV)
    c.badd("trb", c.sV)


def emit_ffn_layer(c, l):
    p = c.p
    io = c.io
    ar = c.ar
    mL = ar.mark()

    A = {}
    A["kT"] = ar.bf16(4 * 2688)
    A["v"] = ar.bf16(21 * 512)
    A["att"] = ar.bf16(4 * 2176)
    rows = ar.f32(128)

    def kT_col(A, f, c0, n):
        return A["kT"][:, f * 2688 + c0: f * 2688 + c0 + n]
    c.kT_col = kT_col

    ksrow = rows[0:1, 0:64]
    kbc = ar.f32(64)
    kbcs = ar.f32(64)
    qbc = ar.f32(64)
    qbcs = ar.f32(64)
    A["kscb"] = kbc.bitcast(BF16)[:, 0:64]
    A["kscsb"] = kbcs.bitcast(BF16)[:, 0:64]
    A["qscb"] = qbc.bitcast(BF16)[:, 0:64]
    A["qscsb"] = qbcs.bitcast(BF16)[:, 0:64]

    p.step("sync", lambda e: e.dma_start(out=ksrow, in_=io["fk_scale"][l:l + 1, :]),
           inc=c.sD, k=16)
    emit_bcast_scaled(c, ksrow, A["kscb"], swap=False, scale=1.0)
    emit_bcast_scaled(c, ksrow, A["kscsb"], swap=True, scale=1.0)
    p.step("sync", lambda e: e.dma_start(out=ksrow, in_=io["fq_scale"][l:l + 1, :]),
           waits=[(c.sA, None), (c.sP, None)], inc=c.sD, k=16)
    emit_bcast_scaled(c, ksrow, A["qscb"], swap=False, scale=F_QKSCALE)
    emit_bcast_scaled(c, ksrow, A["qscsb"], swap=True, scale=F_QKSCALE)

    c.p.flush()

    mS = ar.mark()

    A["wkv"] = ar.bf16(8 * 1024)
    A["sq"] = ar.bf16(512)
    A["ss"] = ar.f32(8)
    A["sr"] = ar.f32(8)
    A["rstd"] = ar.f32(8)
    A["kn"] = ar.bf16(512)
    A["t1"] = ar.bf16(512)
    A["t2"] = ar.bf16(512)
    A["kp"] = ar.bf16(512)
    A["cst"] = ar.bf16(64)
    A["snt"] = ar.bf16(64)
    A["csE"] = ar.bf16(64)
    A["snE"] = ar.bf16(64)

    for kk in range(KD):
        p.step("gpsimd", lambda e, kk=kk: e.dma_start(
            out=A["wkv"][:, kk * 1024:(kk + 1) * 1024],
            in_=io["fwqkv"][l, kk * 128:(kk + 1) * 128, 512:1536]),
            inc=c.sW, k=16)

    for tt in [11, 12, 13, 14, 15]:
        emit_ffn_kv_tile(c, l, tt, A)

    offs = 0
    def pack(src_ap, nel):
        nonlocal offs
        dst = c.ccf_in[0, offs:offs + nel].rearrange(
            "(p f) -> p f", p=src_ap.shape[0])
        p.step("sync", lambda e, d=dst, s=src_ap: e.dma_start(out=d, in_=s),
               waits=[(c.sV, None)], inc=c.sD, k=16)
        offs += nel
    for f in range(4):
        pack(kT_col(A, f, 640 + T - 640, 640), 128 * 640)
    for vt in range(16, 21):
        pack(A["v"][:, vt * 512:(vt + 1) * 512], 128 * 512)
    for kk in range(KD):
        pack(xb_v(c, kk, T - 132, 132), 128 * 132)
    assert offs == FF_TOT
    rg = RG_PAIRS if NGROUP == 2 else [[i] for i in range(NCORE)]
    p.step("gpsimd", lambda e, rg=rg: e.collective_compute(
        "AllGather", ALU.bypass, replica_groups=rg,
        ins=[c.ccf_in.opt()], outs=[c.ccf_out.opt()]),
        waits=[c.sD, (c.sC, None)], inc=c.sC)

    for tt in range(1, 11):
        emit_ffn_kv_tile(c, l, tt, A)
    c.p.flush()

    offs = 0
    def unpack(dst_ap, nel):
        nonlocal offs
        src = c.ccf_out[0, offs:offs + nel].rearrange(
            "(p f) -> p f", p=dst_ap.shape[0])
        p.step("sync", lambda e, d=dst_ap, s=src: e.dma_start(out=d, in_=s),
               waits=[(c.sC, None)], inc=c.sD2, k=16)
        offs += nel
    for f in range(4):
        unpack(kT_col(A, f, 0, 640), 128 * 640)
    for vt in range(0, 5):
        unpack(A["v"][:, vt * 512:(vt + 1) * 512], 128 * 512)
    for kk in range(KD):
        unpack(xb_v(c, kk, -132, 132), 128 * 132)
    assert offs == FF_TOT
    for kk in range(KD):
        p.step("vector", lambda e, kk=kk: e.tensor_scalar_mul(
            xb_v(c, kk, -132, 132), xb_v(c, kk, -132, 132), c.hsc[:, 0:1]),
            waits=[(c.sD2, None), (c.sV, None)], inc=c.sV)
    for f in range(4):
        p.step("vector", lambda e, f=f: e.tensor_scalar_mul(
            c.kT_col(A, f, 0, 640), c.kT_col(A, f, 0, 640), c.hsc[:, 0:1]),
            waits=[(c.sV, None)], inc=c.sV)
    p.step("vector", lambda e: e.tensor_scalar_mul(
        A["v"][:, 0:5 * 512], A["v"][:, 0:5 * 512], c.hsc[:, 0:1]),
        waits=[(c.sV, None)], inc=c.sV)
    c.haloV = c.sV.n
    emit_ffn_kv_tile(c, l, 0, A, wait_halo=True)

    p.barrier()
    ar.reset(mS)

    Bw = {}
    Bw["wq"] = ar.bf16(8 * 512)
    for kk in range(KD):
        p.step("gpsimd", lambda e, kk=kk: e.dma_start(
            out=Bw["wq"][:, kk * 512:(kk + 1) * 512],
            in_=io["fwqkv"][l, kk * 128:(kk + 1) * 128, 0:512]),
            inc=c.sW, k=16)
    Bw["sq"] = ar.bf16(512)
    Bw["ss"] = ar.f32(8)
    Bw["sr"] = ar.f32(8)
    Bw["rstd"] = ar.f32(8)
    Bw["qn"] = ar.bf16(512)
    Bw["t1"] = ar.bf16(512)
    Bw["t2"] = ar.bf16(512)
    Bw["qp"] = ar.bf16(512)
    Bw["cst"] = ar.bf16(64)
    Bw["snt"] = ar.bf16(64)
    Bw["csE"] = ar.bf16(64)
    Bw["snE"] = ar.bf16(64)
    Bw["qT"] = [ar.bf16(512), ar.bf16(512)]
    Bw["msk"] = [ar.f32(640) for _ in range(4)]
    Bw["probsm"] = [ar.bf16(640) for _ in range(8)]
    Bw["probsT"] = [ar.bf16(640), ar.bf16(640)]
    Bw["mask"] = [ar.bf16(640), ar.bf16(640)]
    Bw["sums"] = ar.f32(16)
    Bw["rr"] = ar.f32(8)

    c.dbg_A, c.dbg_Bw = A, Bw
    if "attn" in FFN_PARTS:
        order = list(range(4, 16)) + [16, 0, 1, 2, 3]
        pend = None
        for ui, q_idx in enumerate(order):
            pend = emit_ffn_unit(c, l, q_idx, ui, A, Bw, pend)
        for fn in pend:
            fn()
    c.attn_done = c.sV.n
    c.p.flush()

    if "wo" not in FFN_PARTS:
        c.p.barrier()
        ar.reset(mL)
        return
    Bw["wo"] = [ar.bf16(4 * 128), ar.bf16(4 * 128)]
    Bw["h5t"] = ar.bf16(128)
    for m in range(KD):
        wob = Bw["wo"][m % 2]
        for kk in range(4):
            p.step("gpsimd", lambda e, m=m, kk=kk, wob=wob: e.dma_start(
                out=wob[:, kk * 128:(kk + 1) * 128],
                in_=io["fwo"][l, kk * 128:(kk + 1) * 128,
                              m * 128:(m + 1) * 128]),
                waits=[(c.sP, None)] if kk == 0 else [],
                inc=c.sW, k=16)
        for tc in range(4):
            ps = c.PS[tc % 2]
            p.step("tensor", lambda e, m=m, tc=tc, wob=wob, ps=ps: (
                [e.matmul(ps[:, 0:512], wob[:, kk * 128:(kk + 1) * 128],
                          A["att"][:, kk * 2176 + 128 + tc * 512:
                                   kk * 2176 + 128 + (tc + 1) * 512],
                          start=(kk == 0), stop=(kk == 3))
                 for kk in range(4)])[-1],
                waits=[(c.sW, None), (c.sV, c.attn_done)]
                + c.bg.pop(f"wops{tc % 2}", []), inc=c.sP)
            p.step("vector", lambda e, m=m, tc=tc, ps=ps: e.tensor_tensor(
                xm_v(c, m, tc * 512, 512), ps[:, 0:512],
                xm_v(c, m, tc * 512, 512), ALU.add),
                waits=[c.sP, (c.sV, None)], inc=c.sV)
            c.bset(f"wops{tc % 2}", c.sV)
        psh = c.PS[2]
        p.step("tensor", lambda e, m=m, wob=wob: (
            [e.matmul(psh[:, 0:128], wob[:, kk * 128:(kk + 1) * 128],
                      A["att"][:, kk * 2176: kk * 2176 + 128],
                      start=(kk == 0), stop=(kk == 3))
             for kk in range(4)])[-1],
            waits=[(c.sV, c.attn_done)] + c.bg.pop("wopsh", []), inc=c.sP)
        c.badd(f"wob{m % 2}", c.sP)
        p.step("vector", lambda e, m=m: e.tensor_scalar_mul(
            Bw["h5t"][:], psh[:, 0:128], c.hsc[:, 0:1]),
            waits=[c.sP, (c.sV, None)], inc=c.sV)
        c.bset("wopsh", c.sV)
        p.step("vector", lambda e, m=m: e.tensor_tensor(
            xb_v(c, m, -128, 128), Bw["h5t"][:], xb_v(c, m, -128, 128), ALU.add),
            waits=[c.sV], inc=c.sV)
    vW = c.sV.n
    for m in range(KD):
        for tc in range(4):
            p.step("vector", lambda e, m=m, tc=tc: e.tensor_copy(
                xb_v(c, m, tc * 512, 512), xm_v(c, m, tc * 512, 512)),
                waits=[(c.sV, vW)], inc=c.sV)
    p.barrier()
    if "mlp" not in FFN_PARTS:
        ar.reset(mL)
        return
    ar.reset(mL)
    emit_ffn_mlp(c, l)


def emit_bcast_scaled(c, row_f32, dst_bf16, swap, scale):
    p = c.p
    ar = c.ar
    m0 = ar.mark()
    rs = ar.f32(64)[0:1, :]
    rh = ar.bf16(64)[0:1, :]
    rl = ar.bf16(64)[0:1, :]
    rx = ar.f32(64)[0:1, :]
    ps = c.PS[7][:, 0:64]
    p.step("vector", lambda e: e.tensor_scalar_mul(rs, row_f32, scale),
           waits=[(c.sV, None), c.sD], inc=c.sV)
    p.step("vector", lambda e: e.tensor_copy(rh, rs), waits=[c.sV], inc=c.sV)
    p.step("vector", lambda e: e.tensor_tensor(rx, rs, rh, ALU.subtract),
           waits=[c.sV], inc=c.sV)
    p.step("vector", lambda e: e.tensor_copy(rl, rx), waits=[c.sV], inc=c.sV)
    sh = (lambda r: row_swap_view(r)) if swap else (lambda r: r)
    p.step("tensor", lambda e: (
        e.matmul(ps, c.onr[0:1, :], sh(rh), start=True, stop=False),
        e.matmul(ps, c.onr[0:1, :], sh(rl), start=False, stop=True))[-1],
        waits=[c.sV, (c.sA, None)], inc=c.sP)
    p.step("scalar", lambda e: e.activation(dst_bf16, ps, AF.Identity),
           waits=[c.sP], inc=c.sA)
    c.p.flush()
    ar.reset(m0)


def emit_ffn_unit(c, l, q_idx, ui, A, Bw, pend):
    p = c.p
    io = c.io
    tq0 = -128 if q_idx == 16 else q_idx * 128
    special = {16: 0, 0: 1, 1: 2, 2: 3, 3: 4}.get(q_idx)
    trb = c.PS[6].bitcast(BF16)
    qT = Bw["qT"][ui % 2]

    psq = c.PS[7]
    qw = [(c.sW, None)] + c.bg.pop("ps7", [])
    p.step("tensor", lambda e: ffn_qkv_mms(
        c, e, psq[:, 0:512], lambda kk: Bw["wq"][:, kk * 512:(kk + 1) * 512],
        tq0),
        waits=qw, inc=c.sP)
    qmm_mark = c.sP.n
    p.step("scalar", lambda e: e.activation(Bw["sq"][:], psq[:, 0:512], AF.Square),
           waits=[(c.sP, qmm_mark)] + c.bg.pop("Bqproc", []), inc=c.sA)
    p.step("vector", lambda e: e.reduce_sum(
        Bw["ss"][:, 0:8], Bw["sq"][:].rearrange("p (h d) -> p h d", h=8), axis=AX.X),
        waits=[c.sA, (c.sV, None)], inc=c.sV)
    p.step("scalar", lambda e: e.activation(Bw["sr"][:, 0:8], Bw["ss"][:, 0:8], AF.Sqrt),
           waits=[c.sV], inc=c.sA)
    p.step("vector", lambda e: e.tensor_scalar_max(Bw["sr"][:, 0:8], Bw["sr"][:, 0:8], 1e-12),
           waits=[c.sA], inc=c.sV)
    p.step("vector", lambda e: e.reciprocal(Bw["rstd"][:, 0:8], Bw["sr"][:, 0:8]),
           waits=[c.sV], inc=c.sV)
    for h in range(8):
        p.step("vector", lambda e, h=h: e.tensor_scalar_mul(
            Bw["qn"][:, h * 64:(h + 1) * 64], psq[:, h * 64:(h + 1) * 64],
            Bw["rstd"][:, h:h + 1]),
            waits=[c.sV], inc=c.sV)
    c.bg["ps7"] = [(c.sV, c.sV.n), (c.sA, c.sA.n)]
    r0 = 128 + tq0
    p.step("sync", lambda e: e.dma_start(out=Bw["cst"][:], in_=io["csq"][r0:r0 + 128, :]),
           waits=[(c.sV, None)], inc=c.sD, k=16)
    p.step("sync", lambda e: e.dma_start(out=Bw["snt"][:], in_=io["snq"][r0:r0 + 128, :]),
           inc=c.sD, k=16)
    p.step("vector", lambda e: e.tensor_tensor(Bw["csE"][:], Bw["cst"][:], A["qscb"][:], ALU.mult),
           waits=[c.sD, c.sV], inc=c.sV)
    p.step("vector", lambda e: e.tensor_tensor(Bw["snE"][:], Bw["snt"][:], A["qscsb"][:], ALU.mult),
           waits=[c.sV], inc=c.sV)
    p.step("vector", lambda e: e.tensor_tensor(
        head_view(Bw["t1"][:]), head_view(Bw["qn"][:]), rep_view(Bw["csE"][:]), ALU.mult),
        waits=[c.sV], inc=c.sV)
    p.step("vector", lambda e: e.tensor_tensor(
        head_view(Bw["t2"][:]), rot_view(Bw["qn"][:]), rep_view(Bw["snE"][:]), ALU.mult),
        waits=[c.sV], inc=c.sV)
    p.step("vector", lambda e: e.tensor_tensor(Bw["qp"][:], Bw["t1"][:], Bw["t2"][:], ALU.add),
           waits=[c.sV], inc=c.sV)
    qp_mark = c.sV.n

    if pend:
        pend[0]()

    p.step("tensor", lambda e: (
        [e.transpose(trb[:, f * 128:(f + 1) * 128], Bw["qp"][:, f * 128:(f + 1) * 128],
                     c.idt[:]) for f in range(4)])[-1],
        waits=[(c.sV, qp_mark)] + c.bg.pop("trb", []) + c.bg.pop(f"qT{ui % 2}", []),
        inc=c.sP)
    c.badd("Bqproc", c.sP)
    p.step("vector", lambda e: e.tensor_copy(qT[:], trb[:, 0:512]),
           waits=[c.sP, (c.sV, None)], inc=c.sV)
    c.badd("trb", c.sV)
    qT_mark = c.sV.n

    if pend:
        pend[1]()

    if ATTN_CUT == "q":
        return [lambda: None] * 4

    if special is not None:
        msl = special % 2
        p.step("sync", lambda e, s=special, msl=msl: e.dma_start(
            out=Bw["mask"][msl][:], in_=io["fspec01"][s, :, :]),
            waits=c.bg.pop(f"gmask{msl}", []), inc=c.sD, k=16)
        mask = Bw["mask"][msl]
        mask_dma_mark = c.sD.n
    else:
        msl = None
        mask = c.fband
        mask_dma_mark = 0

    kcol0 = 640 + tq0 - 512
    vt0 = 0 if q_idx == 16 else q_idx + 1
    newpend = []
    for hp in range(4):
        st = [c.PS[0], c.PS[1], c.PS[2]] if hp % 2 == 0 else [c.PS[3], c.PS[4], c.PS[5]]
        s = hp % 2
        bslot = c.PS[6]
        f = hp
        qa = qT[0:64, f * 128:(f + 1) * 128]
        qb = qT[64:128, f * 128:(f + 1) * 128]
        p.step("tensor", lambda e, f=f, st=st, qa=qa, qb=qb, bslot=bslot: (
            e.matmul(st[0][:, 0:512], qa,
                     c.kT_col(A, f, kcol0, 512)[0:64, :], start=True, stop=True,
                     tile_position=(0, 0)),
            e.matmul(st[1][:, 0:512], qb,
                     c.kT_col(A, f, kcol0, 512)[64:128, :], start=True, stop=True,
                     tile_position=(64, 0)),
            e.matmul(st[2][:, 0:128], qa,
                     c.kT_col(A, f, kcol0 + 512, 128)[0:64, :], start=True,
                     stop=True, tile_position=(0, 0)),
            e.matmul(bslot[:, 384:512], qb,
                     c.kT_col(A, f, kcol0 + 512, 128)[64:128, :], start=True,
                     stop=True, tile_position=(64, 0)))[-1],
            waits=[(c.sV, qT_mark)] + c.bg.pop(f"set{s}", [])
            + c.bg.pop("bx0", []), inc=c.sP)
        sc_mark = c.sP.n
        ha, hb = 2 * hp, 2 * hp + 1
        chain_mark = {}
        for j, (h, stm) in enumerate(((ha, st[0]), (hb, st[1]))):
            extc = st[2][:, 0:128] if j == 0 else bslot[:, 384:512]
            mslot = Bw["msk"][2 * s + j]
            pm = Bw["probsm"][h]
            mw = [(c.sP, sc_mark), (c.sV, None)] + c.bg.pop(f"gmsk{2 * s + j}", [])
            if special is not None:
                mw.append((c.sD, mask_dma_mark))
            p.step("vector", lambda e, mslot=mslot, stm=stm, extc=extc, mask=mask: (
                e.tensor_tensor(mslot[:, 0:512], stm[:, 0:512],
                                mask[:, 0:512], ALU.add),
                e.tensor_tensor(mslot[:, 512:640], extc,
                                mask[:, 512:640], ALU.add))[-1],
                waits=mw, inc=c.sV)
            madd_mark = c.sV.n
            p.step("vector", lambda e, mslot=mslot, h=h: e.reduce_max(
                Bw["rr"][:, h:h + 1], mslot[:], axis=AX.X, negate=True),
                waits=[c.sV] + c.bg.pop(f"grr{h}", []), inc=c.sV)
            max_mark = c.sV.n
            if SAFE_ACCUM:
                p.step("scalar", lambda e, mslot=mslot, pm=pm, h=h: e.activation(
                    pm[:], mslot[:], AF.Exp, bias=Bw["rr"][:, h:h + 1]),
                    waits=[(c.sV, max_mark)] + c.bg.pop(f"pm{h}", []), inc=c.sA)
                exp_mark = c.sA.n
                p.step("vector", lambda e, pm=pm, h=h: e.reduce_sum(
                    Bw["sums"][:, 2 * h:2 * h + 1], pm[:], axis=AX.X),
                    waits=[(c.sA, exp_mark), (c.sV, None)], inc=c.sV)
            else:
                p.step("scalar", lambda e, mslot=mslot, pm=pm, h=h: e.activation(
                    pm[:], mslot[:], AF.Exp, bias=Bw["rr"][:, h:h + 1],
                    accum_out=Bw["sums"][:, 2 * h:2 * h + 1]),
                    waits=[(c.sV, max_mark)] + c.bg.pop(f"pm{h}", []), inc=c.sA)
                exp_mark = c.sA.n
            c.bg[f"gmsk{2 * s + j}"] = [(c.sA, exp_mark)]
            c.bg[f"grr{h}"] = [(c.sA, exp_mark)]
            p.step("vector", lambda e, h=h: e.reciprocal(
                Bw["sums"][:, 2 * h + 1:2 * h + 2], Bw["sums"][:, 2 * h:2 * h + 1]),
                waits=[(c.sA, exp_mark), (c.sV, None)], inc=c.sV)
            recip_mark = c.sV.n
            if SAFE_NORM:
                p.step("vector", lambda e, pm=pm, h=h: e.tensor_scalar_mul(
                    pm[:], pm[:], Bw["sums"][:, 2 * h + 1:2 * h + 2]),
                    waits=[(c.sV, recip_mark)], inc=c.sV)
                chain_mark[h] = (c.sV, c.sV.n)
            else:
                p.step("gpsimd", lambda e, pm=pm, h=h: e.tensor_scalar_mul(
                    pm[:], pm[:], Bw["sums"][:, 2 * h + 1:2 * h + 2]),
                    waits=[(c.sV, recip_mark), (c.sG, None)], inc=c.sG)
                chain_mark[h] = (c.sG, c.sG.n)
            if j == 1:
                c.bg[f"set{s}"] = [(c.sV, madd_mark)]
                c.bg["bx0"] = [(c.sV, madd_mark)]

        def BP(hp=hp, s=s, f=f, Es=st[2], cma=chain_mark[ha], cmb=chain_mark[hb],
               ha=ha, hb=hb, vt0=vt0, tq0=tq0):
            attc = f * 2176 + 128 + tq0
            pT0 = Bw["probsT"][0]
            pT1 = Bw["probsT"][1]
            p.step("tensor", lambda e, h=ha: (
                [e.transpose(trb[:, i * 128:(i + 1) * 128],
                             Bw["probsm"][h][:, i * 128:(i + 1) * 128], c.idt[:])
                 for i in range(5)])[-1],
                waits=[cma] + c.bg.pop("trb", []), inc=c.sP)
            c.bg[f"pm{ha}"] = [(c.sP, c.sP.n)]
            p.step("scalar", lambda e, pT0=pT0: e.activation(
                pT0[:], trb[:, 0:640], AF.Identity),
                waits=[c.sP] + c.bg.pop("pT0", []), inc=c.sA)
            evaca_mark = c.sA.n
            c.badd("trb", c.sA)
            p.step("tensor", lambda e, h=hb: (
                [e.transpose(trb[:, i * 128:(i + 1) * 128],
                             Bw["probsm"][h][:, i * 128:(i + 1) * 128], c.idt[:])
                 for i in range(5)])[-1],
                waits=[cmb] + c.bg.pop("trb", []), inc=c.sP)
            c.bg[f"pm{hb}"] = [(c.sP, c.sP.n)]
            trpb_mark = c.sP.n
            p.step("tensor", lambda e, Es=Es, pT0=pT0, ha=ha, vt0=vt0: (
                [e.matmul(Es[0:64, 128:256],
                          A["v"][:, (vt0 + ct) * 512 + ha * 64:
                                 (vt0 + ct) * 512 + ha * 64 + 64],
                          pT0[:, ct * 128:(ct + 1) * 128],
                          start=(ct == 0), stop=(ct == 4)) for ct in range(5)])[-1],
                waits=[(c.sA, evaca_mark)] + c.bg.pop(f"avE{s}", []), inc=c.sP)
            c.bg["pT0"] = [(c.sP, c.sP.n)]
            p.step("scalar", lambda e, pT1=pT1: e.activation(
                pT1[:], trb[:, 0:640], AF.Identity),
                waits=[(c.sP, trpb_mark)] + c.bg.pop("pT1", []), inc=c.sA)
            evacb_mark = c.sA.n
            c.badd("trb", c.sA)
            p.step("tensor", lambda e, Es=Es, pT1=pT1, hb=hb, vt0=vt0: (
                [e.matmul(Es[0:64, 256:384],
                          A["v"][:, (vt0 + ct) * 512 + hb * 64:
                                 (vt0 + ct) * 512 + hb * 64 + 64],
                          pT1[:, ct * 128:(ct + 1) * 128],
                          start=(ct == 0), stop=(ct == 4)) for ct in range(5)])[-1],
                waits=[(c.sA, evacb_mark)], inc=c.sP)
            c.bg["pT1"] = [(c.sP, c.sP.n)]
            avb_mark = c.sP.n
            p.step("vector", lambda e, Es=Es, attc=attc: e.tensor_copy(
                A["att"][0:64, attc:attc + 128], Es[0:64, 128:256]),
                waits=[(c.sP, avb_mark), (c.sV, None)], inc=c.sV)
            p.step("vector", lambda e, Es=Es, attc=attc: e.tensor_copy(
                A["att"][64:128, attc:attc + 128], Es[0:64, 256:384]),
                waits=[c.sV], inc=c.sV)
            c.bg[f"avE{s}"] = [(c.sV, c.sV.n)]
        newpend.append(BP)

        if pend and hp < 2:
            pend[hp + 2]()

    if special is not None:
        c.bg[f"gmask{msl}"] = [(c.sG, c.sG.n)]
    return newpend


def emit_ffn_mlp(c, l):
    p = c.p
    io = c.io
    ar = c.ar
    m0 = ar.mark()
    h2 = ar.bf16(8 * 2048)
    h3 = ar.bf16(22 * 1024)
    wpair = [ar.bf16(2 * 8 * 128) for _ in range(2)]
    w1a = [wp[:, 0:1024] for wp in wpair]
    w1g = [wp[:, 1024:2048] for wp in wpair]
    ag = ar.bf16(512)
    gg = ar.bf16(512)
    sq = ag
    b1c = ar.f32(44)
    ldg = ar.f32(8)
    ldb = ar.f32(8)
    b2c = ar.f32(8)
    rowf = ar.f32(5 * 512)
    R = {nm: rowf[0:1, i * 512:(i + 1) * 512]
         for i, nm in enumerate(["m", "ms", "t", "r1", "r2"])}
    R["x"] = R["t"]
    rowb = ar.bf16(4 * 512)
    RB = {nm: rowb[0:1, i * 512:(i + 1) * 512]
          for i, nm in enumerate(["r1h", "r1l", "r2h", "r2l"])}
    b1s = ar.bf16(512)
    b2s = ar.bf16(512)
    tmp = gg

    for i, n in enumerate(HID_CH):
        o = sum(HID_CH[:i])
        p.step("sync", lambda e, i=i, o=o, n=n: e.dma_start(
            out=b1c[0:n, 2 * i:2 * i + 1],
            in_=io["fb1"][l, o:o + n].unsqueeze(1)),
            inc=c.sD, k=16)
        p.step("sync", lambda e, i=i, o=o, n=n: e.dma_start(
            out=b1c[0:n, 2 * i + 1:2 * i + 2],
            in_=io["fb1"][l, F_HID + o:F_HID + o + n].unsqueeze(1)),
            inc=c.sD, k=16)
    for kk in range(KD):
        p.step("sync", lambda e, kk=kk: e.dma_start(
            out=ldg[:, kk:kk + 1],
            in_=io["fln_g"][l, kk * 128:(kk + 1) * 128].unsqueeze(1)),
            inc=c.sD, k=16)
        p.step("sync", lambda e, kk=kk: e.dma_start(
            out=ldb[:, kk:kk + 1],
            in_=io["fln_b"][l, kk * 128:(kk + 1) * 128].unsqueeze(1)),
            inc=c.sD, k=16)
        p.step("sync", lambda e, kk=kk: e.dma_start(
            out=b2c[:, kk:kk + 1],
            in_=io["fb2"][l, kk * 128:(kk + 1) * 128].unsqueeze(1)),
            inc=c.sD, k=16)
    ldD = c.sD.n

    ps_sum = c.PS[4][0:1, 0:512]
    ps_ssq = c.PS[5][0:1, 0:512]
    ps_b1 = c.PS[6][:, 0:512]
    ps_b2 = c.PS[7][:, 0:512]

    def xa_shift(kk, t0, n):
        return xb_v(c, kk, t0 if kk < 4 else t0 - 1, n)

    for tc in range(2):
        T0 = tc * 1024
        for sub in range(2):
            t0 = T0 + sub * 512
            for kk in range(KD):
                p.step("scalar", lambda e, kk=kk, t0=t0: e.activation(
                    sq[:], xa_shift(kk, t0, 512), AF.Square),
                    waits=[(c.sP, None), (c.sV, None)], inc=c.sA)
                p.step("tensor", lambda e, kk=kk, t0=t0: e.matmul(
                    ps_ssq, c.onc[:, 0:1], sq[:],
                    start=(kk == 0), stop=(kk == KD - 1)),
                    waits=[c.sA], inc=c.sP)
                p.step("tensor", lambda e, kk=kk, t0=t0: e.matmul(
                    ps_sum, c.onc[:, 0:1], xa_shift(kk, t0, 512),
                    start=(kk == 0), stop=(kk == KD - 1)),
                    waits=[(c.sV, None)], inc=c.sP)
            inv = 1.0 / D
            p.step("vector", lambda e: e.tensor_scalar_mul(R["m"], ps_sum, inv),
                   waits=[c.sP, (c.sV, None)], inc=c.sV)
            p.step("vector", lambda e: e.tensor_scalar_mul(R["ms"], ps_ssq, inv),
                   waits=[c.sV], inc=c.sV)
            p.step("vector", lambda e: e.tensor_tensor(R["t"], R["m"], R["m"], ALU.mult),
                   waits=[c.sV], inc=c.sV)
            p.step("vector", lambda e: e.tensor_tensor(R["ms"], R["ms"], R["t"], ALU.subtract),
                   waits=[c.sV], inc=c.sV)
            p.step("vector", lambda e: e.tensor_scalar_add(R["ms"], R["ms"], 1e-5),
                   waits=[c.sV], inc=c.sV)
            p.step("scalar", lambda e: e.activation(R["t"], R["ms"], AF.Sqrt),
                   waits=[c.sV], inc=c.sA)
            p.step("vector", lambda e: e.reciprocal(R["r1"], R["t"]),
                   waits=[c.sA], inc=c.sV)
            p.step("vector", lambda e: e.tensor_tensor(R["r2"], R["m"], R["r1"], ALU.mult),
                   waits=[c.sV], inc=c.sV)
            p.step("vector", lambda e: e.tensor_scalar_mul(R["r2"], R["r2"], -1.0),
                   waits=[c.sV], inc=c.sV)
            for r, rh, rl in [("r1", "r1h", "r1l"), ("r2", "r2h", "r2l")]:
                p.step("vector", lambda e, r=r, rh=rh: e.tensor_copy(RB[rh], R[r]),
                       waits=[c.sV], inc=c.sV)
                p.step("vector", lambda e, r=r, rh=rh: e.tensor_tensor(
                    R["x"], R[r], RB[rh], ALU.subtract),
                    waits=[c.sV], inc=c.sV)
                p.step("vector", lambda e, rl=rl: e.tensor_copy(RB[rl], R["x"]),
                       waits=[c.sV], inc=c.sV)
            p.step("tensor", lambda e: (
                e.matmul(ps_b1, c.onr[0:1, :], RB["r1h"], start=True, stop=False),
                e.matmul(ps_b1, c.onr[0:1, :], RB["r1l"], start=False, stop=True))[-1],
                waits=[c.sV, (c.sA, None)], inc=c.sP)
            p.step("tensor", lambda e: (
                e.matmul(ps_b2, c.onr[0:1, :], RB["r2h"], start=True, stop=False),
                e.matmul(ps_b2, c.onr[0:1, :], RB["r2l"], start=False, stop=True))[-1],
                waits=[c.sV, (c.sA, None)], inc=c.sP)
            p.step("scalar", lambda e: e.activation(b1s, ps_b1, AF.Identity),
                   waits=[c.sP], inc=c.sA)
            p.step("scalar", lambda e: e.activation(b2s, ps_b2, AF.Identity),
                   waits=[c.sP], inc=c.sA)
            for kk in range(KD):
                p.step("vector", lambda e, kk=kk, t0=t0: e.tensor_tensor(
                    tmp[:], xa_shift(kk, t0, 512), b1s[:], ALU.mult),
                    waits=[c.sA, (c.sV, None)], inc=c.sV)
                p.step("vector", lambda e, kk=kk: e.tensor_tensor(
                    tmp[:], tmp[:], b2s[:], ALU.add),
                    waits=[c.sV], inc=c.sV)
                p.step("vector", lambda e, kk=kk, t0=t0: e.tensor_scalar(
                    h2[:, kk * 2048 + t0: kk * 2048 + t0 + 512],
                    tmp[:], ldg[:, kk:kk + 1], ldb[:, kk:kk + 1],
                    ALU.mult, ALU.add),
                    waits=[c.sV, (c.sD, ldD)], inc=c.sV)
    c.h2V = c.sV.n
    for tc in range(2):
        T0 = tc * 1024
        for i, n in enumerate(HID_CH):
            o = sum(HID_CH[:i])
            wa, wg = w1a[i % 2], w1g[i % 2]
            for kk in range(KD):
                p.step("gpsimd", lambda e, o=o, n=n, kk=kk, wa=wa: e.dma_start(
                    out=wa[0:128, kk * 128:kk * 128 + n],
                    in_=io["fw1"][l, kk * 128:(kk + 1) * 128, o:o + n]),
                    waits=c.bg.pop(f"wp{i % 2}", []) if kk == 0 else [],
                    inc=c.sW, k=16)
            for kk in range(KD):
                p.step("gpsimd", lambda e, o=o, n=n, kk=kk, wg=wg: e.dma_start(
                    out=wg[0:128, kk * 128:kk * 128 + n],
                    in_=io["fw1"][l, kk * 128:(kk + 1) * 128,
                                  F_HID + o:F_HID + o + n]),
                    inc=c.sW, k=16)
            for sub in range(2):
                t0s = sub * 512
                psa = c.PS[0 + sub]
                psg = c.PS[2 + sub]
                p.step("tensor", lambda e, i=i, n=n, wa=wa, t0s=t0s, psa=psa,
                       T0=T0: (
                    [e.matmul(psa[0:n, 0:512], wa[0:128, kk * 128:kk * 128 + n],
                              h2[:, kk * 2048 + T0 + t0s:
                                 kk * 2048 + T0 + t0s + 512],
                              start=(kk == 0), stop=(kk == KD - 1))
                     for kk in range(KD)])[-1],
                    waits=[(c.sW, None), (c.sV, c.h2V)] +
                    c.bg.pop(f"psa{sub}", []) +
                    (c.bgate("w2banks") if i == 0 else []), inc=c.sP)
                p.step("tensor", lambda e, i=i, n=n, wg=wg, t0s=t0s, psg=psg,
                       T0=T0: (
                    [e.matmul(psg[0:n, 0:512], wg[0:128, kk * 128:kk * 128 + n],
                              h2[:, kk * 2048 + T0 + t0s:
                                 kk * 2048 + T0 + t0s + 512],
                              start=(kk == 0), stop=(kk == KD - 1))
                     for kk in range(KD)])[-1],
                    waits=c.bg.pop(f"psg{sub}", []) +
                    (c.bgate("w2banks") if i == 0 else []), inc=c.sP)
                if sub == 1:
                    c.badd(f"wp{i % 2}", c.sP)
                p.step("scalar", lambda e, i=i, n=n, psa=psa: e.activation(
                    ag[0:n, :], psa[0:n, 0:512], AF.Identity,
                    bias=b1c[0:n, 2 * i:2 * i + 1]),
                    waits=[c.sP] + c.bg.pop("agbuf", []), inc=c.sA)
                c.bset(f"psa{sub}", c.sA)
                p.step("scalar", lambda e, i=i, n=n, psg=psg: e.activation(
                    gg[0:n, :], psg[0:n, 0:512], AF.Gelu,
                    bias=b1c[0:n, 2 * i + 1:2 * i + 2]),
                    waits=c.bg.pop("ggbuf", []), inc=c.sA)
                c.bset(f"psg{sub}", c.sA)
                p.step("vector", lambda e, i=i, n=n, t0s=t0s: e.tensor_tensor(
                    h3[0:n, i * 1024 + t0s: i * 1024 + t0s + 512],
                    ag[0:n, :], gg[0:n, :], ALU.mult),
                    waits=[c.sA], inc=c.sV)
                c.bset("agbuf", c.sV)
                c.bset("ggbuf", c.sV)
        emit_ffn_w2(c, l, T0, h3, b2c, wpair)
    c.p.flush()
    ar.reset(m0)
    c.p.barrier()


def emit_ffn_w2(c, l, T0, h3, b2c, wpair):
    p = c.p
    io = c.io
    ar = c.ar
    m0 = ar.mark()
    w2c = wpair
    for sub in range(2):
        t0 = T0 + sub * 512
        ps = c.PS
        for i, n in enumerate(HID_CH):
            o = sum(HID_CH[:i])
            wc = w2c[i % 2]
            p.step("gpsimd", lambda e, i=i, o=o, n=n, wc=wc: e.dma_start(
                out=wc[0:n, 0:1024],
                in_=io["fw2"][l, o:o + n, :]),
                waits=c.bg.pop(f"wp{i % 2}", []), inc=c.sW, k=16)
            p.step("tensor", lambda e, i=i, n=n, o=o, wc=wc, t0=t0: (
                [e.matmul(ps[m][:, 0:512], wc[0:n, m * 128:(m + 1) * 128],
                          h3[0:n, i * 1024 + (t0 - T0): i * 1024 + (t0 - T0) + 512],
                          start=(i == 0), stop=(i == len(HID_CH) - 1))
                 for m in range(KD)])[-1],
                waits=[(c.sW, None)] +
                (c.bg.pop("w2banks", []) +
                 c.bg.pop("psa0", []) + c.bg.pop("psa1", []) +
                 c.bg.pop("psg0", []) + c.bg.pop("psg1", [])
                 if i == 0 else []) +
                ([(c.sV, None)] if i == 0 else []), inc=c.sP)
            c.badd(f"wp{i % 2}", c.sP)
        for m in range(KD):
            p.step("vector", lambda e, m=m, t0=t0: e.scalar_tensor_tensor(
                xm_v(c, m, t0, 512), ps[m][:, 0:512], b2c[:, m:m + 1],
                xm_v(c, m, t0, 512), ALU.add, ALU.add),
                waits=[c.sP, (c.sV, None)], inc=c.sV)
        c.bset("w2banks", c.sV)
        vW = c.sV.n
        for m in range(KD):
            p.step("vector", lambda e, m=m, t0=t0: e.tensor_copy(
                xb_v(c, m, t0, 512), xm_v(c, m, t0, 512)),
                waits=[(c.sV, vW)], inc=c.sV)
    ar.reset(m0)


def emit_done(c):
    c.p.step("sync", lambda e: e.engine_nop() if False else e.nop(),
             waits=[c.sO, c.sV, c.sP])
    c.p.flush()


def program(c):
    emit_prologue(c)
    emit_boundary(c, "init")
    if SKIP_FINAL:
        for st in STAGES:
            if st == "ffn0":
                emit_ffn_layer(c, 0)
        emit_done(c)
        return
    for st in STAGES:
        if st == "ffn0":
            emit_ffn_layer(c, 0)
        elif st == "ffn1":
            emit_ffn_layer(c, 1)
        elif st == "b_ffn":
            emit_boundary(c, "mid_ffn")
        elif st == "mla":
            emit_mla_block(c)
        elif st == "b_mla":
            emit_boundary(c, "mid_mla")
        else:
            raise ValueError(st)
    emit_boundary(c, "final")
    emit_done(c)


def mla_qkv_mms(c, e, ps, wsl, t0):
    ins = None
    for kk in range(KD):
        ins = e.matmul(ps, xb_v(c, kk, t0, 128), wsl(kk),
                       start=(kk == 0), stop=False)
    return ins


def emit_mla_block(c):
    p = c.p
    io = c.io
    ar = c.ar
    mL = ar.mark()

    M = {}
    rows = ar.f32(64)
    rb = ar.bf16(3072 + 256 + 512 + 64)
    M["qkvb"] = rb[0:1, 0:3072]
    M["rqb"] = rb[0:1, 3072:3328]
    M["rkvb"] = rb[0:1, 3328:3840]
    M["pooled"] = ar.bf16(8 * 64)
    M["rkv"] = ar.bf16(512)
    M["rkT"] = ar.bf16(2 * 64)
    M["glb"] = ar.bf16(64)
    M["grb"] = ar.bf16(64)
    M["rowk"] = ar.f32(2)
    M["lmrow"] = ar.bf16(64)

    c.wrq = ar.bf16(8 * 256)
    c.wrout = ar.bf16(2 * 1024)
    rb2 = ar.bf16(1024)
    c.routb = rb2[0:1, 0:1024]

    mRows = ar.mark()
    rowscr = ar.f32(3072)
    rfull = rowscr[0:1, 0:3072]
    for nm, src_ap, n in [("qkvb", io["aqkv_b"].unsqueeze(0), 3072),
                          ("rqb", io["arq_b"].unsqueeze(0), 256),
                          ("rkvb", io["arkv_b"].unsqueeze(0), 512)]:
        p.step("sync", lambda e, s=src_ap, n=n: e.dma_start(
            out=rfull[0:1, 0:n], in_=s), waits=[(c.sV, None)], inc=c.sD, k=16)
        p.step("vector", lambda e, nm=nm, n=n: e.tensor_copy(
            M[nm][0:1, 0:n], rfull[0:1, 0:n]),
            waits=[c.sD, (c.sV, None)], inc=c.sV)
    for nm, wn in [("glb", "agl_w"), ("grb", "agr_w")]:
        p.step("sync", lambda e, wn=wn: e.dma_start(
            out=rowscr[0:1, 0:64], in_=io[wn].rearrange("a b -> b a")),
            waits=[(c.sV, None), (c.sP, None)], inc=c.sD, k=16)
        p.step("vector", lambda e: e.tensor_copy(rb[0:1, 3840:3904],
                                                 rowscr[0:1, 0:64]),
               waits=[c.sD, (c.sV, None)], inc=c.sV)
        p.step("tensor", lambda e: e.matmul(c.PS[7][:, 0:64], c.onr[0:1, :],
                                            rb[0:1, 3840:3904],
                                            start=True, stop=True),
               waits=[c.sV, (c.sA, None)], inc=c.sP)
        p.step("scalar", lambda e, nm=nm: e.activation(M[nm][:], c.PS[7][:, 0:64],
                                                       AF.Identity),
               waits=[c.sP], inc=c.sA)

    rs2 = ar.f32(1024)
    for kk in range(KD):
        p.step("gpsimd", lambda e, kk=kk: e.dma_start(
            out=c.wrq[:, kk * 256:(kk + 1) * 256],
            in_=io["arq_w"][kk * 128:(kk + 1) * 128, :]),
            waits=[(c.sP, None)] if kk == 0 else [], inc=c.sW, k=16)
    for ch in range(2):
        p.step("gpsimd", lambda e, ch=ch: e.dma_start(
            out=c.wrout[0:128, ch * 1024:(ch + 1) * 1024],
            in_=io["arout_w"][ch * 128:(ch + 1) * 128, :]),
            inc=c.sW, k=16)
    p.step("sync", lambda e: e.dma_start(
        out=rs2[0:1, 0:1024], in_=io["arout_b"].unsqueeze(0)),
        waits=[(c.sV, None)], inc=c.sD, k=16)
    p.step("vector", lambda e: e.tensor_copy(c.routb[0:1, 0:1024],
                                             rs2[0:1, 0:1024]),
           waits=[c.sD, (c.sV, None)], inc=c.sV)

    c.p.flush()
    ar.reset(mRows)

    pl_own = M["pooled"][:, :].bitcast(BF16) if False else M["pooled"]
    pooled_f = ar.f32(8 * 32)
    for kk in range(KD):
        p.step("vector", lambda e, kk=kk: e.reduce_sum(
            pooled_f[:, kk * 32:(kk + 1) * 32],
            xb_v(c, kk, 0, T).rearrange("p (g c) -> p g c", g=32), axis=AX.X),
            waits=[(c.sV, None)], inc=c.sV)
    p.step("vector", lambda e: e.tensor_scalar_mul(
        pooled_f[:], pooled_f[:], 1.0 / CHUNK),
        waits=[c.sV], inc=c.sV)
    for kk in range(KD):
        p.step("vector", lambda e, kk=kk: e.tensor_copy(
            M["pooled"][:, kk * 64 + 32: kk * 64 + 64],
            pooled_f[:, kk * 32:(kk + 1) * 32]),
            waits=[c.sV], inc=c.sV)

    mS = ar.mark()
    kvlast = ar.bf16(2 * 1024)
    ktl = kvlast[:, 0:1024]
    vtl = kvlast[:, 1024:2048]
    wbuf = [ar.bf16(8 * 512) for _ in range(2)]
    trb = c.PS[6].bitcast(BF16)
    t0 = T - 128
    for ni, n0 in enumerate([2, 3, 4, 5]):
        wb = wbuf[ni % 2]
        for kk in range(KD):
            p.step("gpsimd", lambda e, kk=kk, n0=n0, wb=wb: e.dma_start(
                out=wb[:, kk * 512:(kk + 1) * 512],
                in_=io["aqkv_w"][kk * 128:(kk + 1) * 128,
                                 n0 * 512:(n0 + 1) * 512]),
                waits=[(c.sP, None)] if kk == 0 else [], inc=c.sW, k=16)
        ps = c.PS[ni % 2]
        p.step("tensor", lambda e, wb=wb, ni=ni, n0=n0, ps=ps: (
            mla_qkv_mms(c, e, ps[:, 0:512],
                        lambda kk: wb[:, kk * 512:(kk + 1) * 512], t0),
            e.matmul(ps[:, 0:512], c.onr[0:1, :],
                     M["qkvb"][0:1, n0 * 512:(n0 + 1) * 512],
                     start=False, stop=True))[-1],
            waits=[c.sW, (c.sV, None), (c.sA, None)], inc=c.sP)
        if n0 in (2, 3):
            p.step("scalar", lambda e, ps=ps: e.activation(
                vtl[:, 0:512], ps[:, 0:512], AF.Identity),
                waits=[c.sP], inc=c.sA)
            p.step("tensor", lambda e, n0=n0: (
                [e.transpose(trb[:, f * 128:(f + 1) * 128],
                             vtl[:, f * 128:(f + 1) * 128], c.idt[:])
                 for f in range(4)])[-1],
                waits=[c.sA, (c.sV, None)], inc=c.sP)
            p.step("vector", lambda e, n0=n0: e.tensor_copy(
                ktl[:, (n0 - 2) * 512:(n0 - 1) * 512], trb[:, 0:512]),
                waits=[c.sP, (c.sV, None)], inc=c.sV)
        else:
            p.step("scalar", lambda e, ps=ps, n0=n0: e.activation(
                vtl[:, (n0 - 4) * 512:(n0 - 3) * 512], ps[:, 0:512], AF.Identity),
                waits=[c.sP, (c.sV, None)], inc=c.sA)
    offs = 0
    for src_ap, nel in [(ktl, 128 * 1024), (vtl, 128 * 1024)]:
        dst = c.cca_in[0, offs:offs + nel].rearrange("(p f) -> p f", p=128)
        p.step("sync", lambda e, d=dst, s=src_ap: e.dma_start(out=d, in_=s),
               waits=[(c.sV, None), (c.sA, None)], inc=c.sD, k=16)
        offs += nel
    pl_pack = ar.bf16(8 * 32)
    for kk in range(KD):
        p.step("vector", lambda e, kk=kk: e.tensor_copy(
            pl_pack[:, kk * 32:(kk + 1) * 32],
            M["pooled"][:, kk * 64 + 32:kk * 64 + 64]),
            waits=[(c.sV, None)], inc=c.sV)
    dst = c.cca_in[0, offs:offs + 128 * 256].rearrange("(p f) -> p f", p=128)
    p.step("sync", lambda e, d=dst: e.dma_start(out=d, in_=pl_pack[:]),
           waits=[c.sV], inc=c.sD, k=16)
    offs += 128 * 256
    assert offs == AA_TOT
    rg = RG_PAIRS if NGROUP == 2 else [[i] for i in range(NCORE)]
    p.step("gpsimd", lambda e, rg=rg: e.collective_compute(
        "AllGather", ALU.bypass, replica_groups=rg,
        ins=[c.cca_in.opt()], outs=[c.cca_out.opt()]),
        waits=[c.sD, (c.sC, None)], inc=c.sC)
    c.p.flush()
    ar.reset(mS)

    for half in range(2):
        emit_mla_half(c, M, half)

    emit_mla_out(c, M)
    p.barrier()
    ar.reset(mL)


def emit_mla_half(c, M, half):
    p = c.p
    io = c.io
    ar = c.ar
    mH = ar.mark()
    th0 = half * 8
    t00 = th0 * 128
    Kh = {}
    Kh["kT"] = ar.bf16(8 * 1152)
    Kh["v"] = ar.bf16(9 * 1024)
    Kh["q"] = ar.bf16(8 * 1024)
    wbuf = [ar.bf16(8 * 512) for _ in range(2)]
    trb = c.PS[6].bitcast(BF16)
    evs = ar.bf16(512)

    def kT_col(f, col, n):
        return Kh["kT"][:, f * 1152 + col: f * 1152 + col + n]

    kv_tiles = list(range(max(0, th0 - 1), th0 + 8))
    for ni in range(6):
        wb = wbuf[ni % 2]
        for kk in range(KD):
            p.step("gpsimd", lambda e, kk=kk, ni=ni, wb=wb: e.dma_start(
                out=wb[:, kk * 512:(kk + 1) * 512],
                in_=io["aqkv_w"][kk * 128:(kk + 1) * 128,
                                 ni * 512:(ni + 1) * 512]),
                waits=[(c.sP, None)] if kk == 0 else [], inc=c.sW, k=16)
        tiles = kv_tiles if ni >= 2 else list(range(th0, th0 + 8))
        for tt in tiles:
            ps = c.PS[tt % 2]
            p.step("tensor", lambda e, wb=wb, ni=ni, tt=tt, ps=ps: (
                mla_qkv_mms(c, e, ps[:, 0:512],
                            lambda kk: wb[:, kk * 512:(kk + 1) * 512], tt * 128),
                e.matmul(ps[:, 0:512], c.onr[0:1, :],
                         M["qkvb"][0:1, ni * 512:(ni + 1) * 512],
                         start=False, stop=True))[-1],
                waits=[c.sW, (c.sV, None), (c.sA, None)], inc=c.sP)
            if ni < 2:
                p.step("vector", lambda e, ps=ps, ni=ni, tt=tt: e.tensor_scalar_mul(
                    Kh["q"][:, (tt - th0) * 1024 + ni * 512:
                            (tt - th0) * 1024 + (ni + 1) * 512],
                    ps[:, 0:512], 0.125),
                    waits=[c.sP, (c.sV, None)], inc=c.sV)
            elif ni < 4:
                p.step("scalar", lambda e, ps=ps: e.activation(
                    evs[:], ps[:, 0:512], AF.Identity),
                    waits=[c.sP, (c.sP2, None)], inc=c.sA)
                p.step("tensor", lambda e: (
                    [e.transpose(trb[:, f * 128:(f + 1) * 128],
                                 evs[:, f * 128:(f + 1) * 128], c.idt[:])
                     for f in range(4)])[-1],
                    waits=[c.sA, (c.sV, None)], inc=c.sP2)
                col = 128 * (tt - th0 + 1)
                for f in range(4):
                    fk = (ni - 2) * 4 + f
                    p.step("vector", lambda e, fk=fk, f=f, col=col: e.tensor_copy(
                        kT_col(fk, col, 128), trb[:, f * 128:(f + 1) * 128]),
                        waits=[c.sP2, (c.sV, None)], inc=c.sV)
            else:
                p.step("scalar", lambda e, ps=ps, ni=ni, tt=tt: e.activation(
                    Kh["v"][:, (tt - th0 + 1) * 1024 + (ni - 4) * 512:
                            (tt - th0 + 1) * 1024 + (ni - 3) * 512],
                    ps[:, 0:512], AF.Identity),
                    waits=[c.sP, (c.sV, None)], inc=c.sA)

    if half == 0:
        offs = 0
        for f in range(8):
            src = c.cca_out[0, f * 128 * 128:(f + 1) * 128 * 128].rearrange(
                "(p f2) -> p f2", p=128)
            p.step("sync", lambda e, d=kT_col(f, 0, 128), s=src:
                   e.dma_start(out=d, in_=s),
                   waits=[(c.sC, None)], inc=c.sD2, k=16)
        src = c.cca_out[0, AA_KT:AA_KT + AA_V].rearrange("(p f2) -> p f2", p=128)
        p.step("sync", lambda e, s=src: e.dma_start(out=Kh["v"][:, 0:1024], in_=s),
               waits=[(c.sC, None)], inc=c.sD2, k=16)
        src = c.cca_out[0, AA_KT + AA_V:AA_TOT].rearrange("(p f2) -> p f2", p=128)
        plr = ar.bf16(8 * 32)
        p.step("sync", lambda e, s=src, plr=plr: e.dma_start(out=plr[:], in_=s),
               waits=[(c.sC, None)], inc=c.sD2, k=16)
        for kk in range(KD):
            p.step("vector", lambda e, kk=kk, plr=plr: e.tensor_copy(
                M["pooled"][:, kk * 64:kk * 64 + 32],
                plr[:, kk * 32:(kk + 1) * 32]),
                waits=[(c.sD2, None), (c.sV, None)], inc=c.sV)
        for f in range(8):
            p.step("vector", lambda e, f=f: e.tensor_scalar_mul(
                kT_col(f, 0, 128), kT_col(f, 0, 128), c.hsc[:, 0:1]),
                waits=[(c.sD2, None), (c.sV, None)], inc=c.sV)
        p.step("vector", lambda e: e.tensor_scalar_mul(
            Kh["v"][:, 0:1024], Kh["v"][:, 0:1024], c.hsc[:, 0:1]),
            waits=[(c.sD2, None), (c.sV, None)], inc=c.sV)
        emit_mla_latent_prep(c, M)

    for tt in range(th0, th0 + 8):
        emit_mla_tile(c, M, Kh, half, tt, kT_col)
    c.p.flush()
    ar.reset(mH)


def emit_mla_latent_prep(c, M):
    p = c.p
    io = c.io
    ar = c.ar
    m0 = ar.mark()
    wb = ar.bf16(8 * 512)
    trb = c.PS[6].bitcast(BF16)
    for kk in range(KD):
        p.step("gpsimd", lambda e, kk=kk, wb=wb: e.dma_start(
            out=wb[:, kk * 512:(kk + 1) * 512],
            in_=io["arkv_w"][kk * 128:(kk + 1) * 128, :]),
            waits=[(c.sP, None)] if kk == 0 else [], inc=c.sW, k=16)
    ps = c.PS[7]
    p.step("tensor", lambda e, wb=wb: (
        [e.matmul(ps[0:64, 0:512], M["pooled"][:, kk * 64:(kk + 1) * 64],
                  wb[:, kk * 512:(kk + 1) * 512],
                  start=(kk == 0), stop=False) for kk in range(KD)] +
        [e.matmul(ps[0:64, 0:512], c.onr[0:1, 0:64],
                  M["rkvb"][0:1, 0:512], start=False, stop=True)])[-1],
        waits=[c.sW, (c.sV, None), (c.sA, None)], inc=c.sP)
    p.step("scalar", lambda e: e.activation(M["rkv"][0:64, :], ps[0:64, 0:512],
                                            AF.Identity),
           waits=[c.sP], inc=c.sA)
    p.step("tensor", lambda e: (
        e.transpose(trb[:, 0:64], M["rkv"][0:64, 0:128], c.idt[0:64, 0:64]),
        e.transpose(trb[:, 64:128], M["rkv"][0:64, 128:256], c.idt[0:64, 0:64]))[-1],
        waits=[c.sA, (c.sV, None)], inc=c.sP)
    p.step("vector", lambda e: e.tensor_copy(M["rkT"][:], trb[:, 0:128]),
           waits=[c.sP, (c.sV, None)], inc=c.sV)
    c.p.flush()
    ar.reset(m0)


def emit_mla_tile(c, M, Kh, half, tt, kT_col):
    p = c.p
    io = c.io
    ar = c.ar
    m0 = ar.mark()
    th0 = half * 8
    t0 = tt * 128
    tl = tt - th0
    trb = c.PS[6].bitcast(BF16)

    qTt = ar.bf16(8 * 128)
    probs = ar.bf16(256)
    probsm = ar.bf16(256)
    probsT = ar.bf16(256)
    lpb = ar.bf16(64)
    lpbm = ar.bf16(64)
    lpbT = ar.bf16(128)
    loc = ar.bf16(1024)
    rem = ar.bf16(1024)
    rqs = ar.bf16(256)
    rqT = ar.bf16(256)
    d1 = ar.bf16(1024)
    mstage = loc
    glog = ar.f32(3 * 16)
    gate = ar.f32(16)
    st2 = ar.f32(4)

    p.step("tensor", lambda e: (
        [e.transpose(trb[:, f * 128:(f + 1) * 128],
                     Kh["q"][:, tl * 1024 + f * 128: tl * 1024 + (f + 1) * 128],
                     c.idt[:]) for f in range(8)])[-1],
        waits=[(c.sV, None), (c.sA, None)], inc=c.sP)
    p.step("vector", lambda e: e.tensor_copy(qTt[:], trb[:, 0:1024]),
           waits=[c.sP, (c.sV, None)], inc=c.sV)

    p.step("sync", lambda e: e.dma_start(out=M["rowk"][:, 0:1],
                                         in_=io["arowkeep"][t0:t0 + 128, :]),
           waits=[(c.sV, None)], inc=c.sD, k=16)
    p.step("sync", lambda e: e.dma_start(out=M["rowk"][:, 1:2],
                                         in_=io["lrowkeep"][t0:t0 + 128, :]),
           inc=c.sD, k=16)
    p.step("sync", lambda e: e.dma_start(out=M["lmrow"][:],
                                         in_=io["lmask01"][t0:t0 + 128, :]),
           inc=c.sD, k=16)
    mask = c.at0 if tt == 0 else c.aband

    locps = [c.PS[4], c.PS[5]]
    kcol = 128 * (tl + 1) - 128
    for hp in range(8):
        st = c.PS[hp % 2]
        stb = c.PS[2 + hp % 2]
        qa = qTt[0:64, hp * 128:(hp + 1) * 128]
        qb = qTt[64:128, hp * 128:(hp + 1) * 128]
        p.step("tensor", lambda e, hp=hp, st=st, stb=stb, qa=qa, qb=qb: (
            e.matmul(st[:, 0:256], qa, kT_col(hp, kcol, 256)[0:64, :],
                     start=True, stop=True, tile_position=(0, 0)),
            e.matmul(stb[:, 0:256], qb, kT_col(hp, kcol, 256)[64:128, :],
                     start=True, stop=True, tile_position=(64, 0)))[-1],
            waits=[(c.sV, None), (c.sA, None), (c.sG, None)] +
            ([(c.sD2, None)] if (half == 0 and tt == 0) else []),
            inc=c.sP)
        for hh in range(2):
            h = 2 * hp + hh
            sc = (st if hh == 0 else stb)[:, 0:256]
            p.step("vector", lambda e, sc=sc: e.reduce_max(
                st2[:, 0:1], sc, axis=AX.X, negate=True),
                waits=[c.sP, (c.sV, None)], inc=c.sV)
            p.step("scalar", lambda e, sc=sc: e.activation(
                probs[:], sc, AF.Exp, bias=st2[:, 0:1]),
                waits=[c.sV, (c.sG, None)], inc=c.sA)
            p.step("gpsimd", lambda e, mask=mask: e.tensor_tensor(
                probsm[:], probs[:], mask[:], ALU.mult),
                waits=[c.sA, (c.sG, None), (c.sP, None), c.sD], inc=c.sG)
            p.step("vector", lambda e: e.reduce_sum(st2[:, 1:2], probsm[:],
                                                    axis=AX.X),
                   waits=[c.sG, (c.sV, None)], inc=c.sV)
            p.step("vector", lambda e: e.tensor_scalar_max(
                st2[:, 1:2], st2[:, 1:2], 1e-30), waits=[c.sV], inc=c.sV)
            p.step("vector", lambda e: e.reciprocal(st2[:, 2:3], st2[:, 1:2]),
                   waits=[c.sV], inc=c.sV)
            p.step("vector", lambda e: e.tensor_tensor(
                st2[:, 2:3], st2[:, 2:3], M["rowk"][:, 0:1], ALU.mult),
                waits=[c.sV], inc=c.sV)
            p.step("vector", lambda e: e.tensor_scalar_mul(
                probsm[:], probsm[:], st2[:, 2:3]),
                waits=[c.sV, c.sG], inc=c.sV)
            p.step("tensor", lambda e: (
                e.transpose(trb[:, 0:128], probsm[:, 0:128], c.idt[:]),
                e.transpose(trb[:, 128:256], probsm[:, 128:256], c.idt[:]))[-1],
                waits=[c.sV, (c.sA, None)], inc=c.sP)
            p.step("scalar", lambda e: e.activation(probsT[:], trb[:, 0:256],
                                                    AF.Identity),
                   waits=[c.sP], inc=c.sA)
            lp = locps[h // 8]
            col = (h % 8) * 64
            p.step("tensor", lambda e, h=h, lp=lp, tl=tl, col=col: (
                e.matmul(lp[:, col:col + 64],
                         probsT[:, 0:128],
                         Kh["v"][:, tl * 1024 + h * 64: tl * 1024 + h * 64 + 64],
                         start=True, stop=False),
                e.matmul(lp[:, col:col + 64],
                         probsT[:, 128:256],
                         Kh["v"][:, (tl + 1) * 1024 + h * 64:
                                 (tl + 1) * 1024 + h * 64 + 64],
                         start=False, stop=True))[-1],
                waits=[c.sA, (c.sV, None)], inc=c.sP)
    for b in range(2):
        p.step("vector", lambda e, b=b: e.tensor_copy(
            loc[:, b * 512:(b + 1) * 512], c.PS[4 + b][:, 0:512]),
            waits=[c.sP, (c.sV, None), (c.sO, None)], inc=c.sV)

    psr = c.PS[2]
    p.step("tensor", lambda e: (
        mla_qkv_mms(c, e, psr[:, 0:256],
                    lambda kk: c.wrq[:, kk * 256:(kk + 1) * 256], t0),
        e.matmul(psr[:, 0:256], c.onr[0:1, :], M["rqb"][0:1, 0:256],
                 start=False, stop=True))[-1],
        waits=[(c.sV, None), (c.sA, None)], inc=c.sP)
    p.step("vector", lambda e: e.tensor_scalar_mul(rqs[:], psr[:, 0:256], 0.125),
           waits=[c.sP, (c.sV, None)], inc=c.sV)
    p.step("tensor", lambda e: (
        e.transpose(trb[:, 0:128], rqs[:, 0:128], c.idt[:]),
        e.transpose(trb[:, 128:256], rqs[:, 128:256], c.idt[:]))[-1],
        waits=[c.sV, (c.sA, None)], inc=c.sP)
    p.step("vector", lambda e: e.tensor_copy(rqT[:], trb[:, 0:256]),
           waits=[c.sP, (c.sV, None)], inc=c.sV)
    latps = c.PS[4]
    for lh in range(4):
        qa = rqT[(lh % 2) * 64:(lh % 2) * 64 + 64, (lh // 2) * 128:(lh // 2) * 128 + 128]
        ka = M["rkT"][(lh % 2) * 64:(lh % 2) * 64 + 64, (lh // 2) * 64:(lh // 2) * 64 + 64]
        tp = (64 * (lh % 2), 0)
        lbank = c.PS[2] if lh % 2 == 0 else c.PS[3]
        p.step("tensor", lambda e, qa=qa, ka=ka, tp=tp, lh=lh, lbank=lbank:
               e.matmul(lbank[:, 256 + (lh // 2) * 64: 256 + (lh // 2) * 64 + 64],
                        qa, ka, start=True, stop=True, tile_position=tp),
               waits=[(c.sV, None)], inc=c.sP)
        sc = lbank[:, 256 + (lh // 2) * 64: 256 + (lh // 2) * 64 + 64]
        p.step("vector", lambda e, sc=sc: e.reduce_max(
            st2[:, 0:1], sc, axis=AX.X, negate=True),
            waits=[c.sP, (c.sV, None)], inc=c.sV)
        p.step("scalar", lambda e, sc=sc: e.activation(
            lpb[:], sc, AF.Exp, bias=st2[:, 0:1]),
            waits=[c.sV, (c.sG, None)], inc=c.sA)
        p.step("gpsimd", lambda e: e.tensor_tensor(lpbm[:], lpb[:], M["lmrow"][:],
                                                   ALU.mult),
               waits=[c.sA, (c.sG, None), (c.sP, None), c.sD], inc=c.sG)
        p.step("vector", lambda e: e.reduce_sum(st2[:, 1:2], lpbm[:], axis=AX.X),
               waits=[c.sG, (c.sV, None)], inc=c.sV)
        p.step("vector", lambda e: e.tensor_scalar_max(
            st2[:, 1:2], st2[:, 1:2], 1e-30), waits=[c.sV], inc=c.sV)
        p.step("vector", lambda e: e.reciprocal(st2[:, 2:3], st2[:, 1:2]),
               waits=[c.sV], inc=c.sV)
        p.step("vector", lambda e: e.tensor_tensor(
            st2[:, 2:3], st2[:, 2:3], M["rowk"][:, 1:2], ALU.mult),
            waits=[c.sV], inc=c.sV)
        p.step("vector", lambda e: e.tensor_scalar_mul(
            lpbm[:], lpbm[:], st2[:, 2:3]), waits=[c.sV, c.sG], inc=c.sV)
        p.step("tensor", lambda e: e.transpose(trb[0:64, 0:128], lpbm[:], c.idt[:]),
               waits=[c.sV, (c.sA, None)], inc=c.sP)
        p.step("scalar", lambda e: e.activation(lpbT[0:64, :], trb[0:64, 0:128],
                                                AF.Identity),
               waits=[c.sP], inc=c.sA)
        p.step("tensor", lambda e, lh=lh: e.matmul(
            latps[:, lh * 64:(lh + 1) * 64], lpbT[0:64, :],
            M["rkv"][0:64, 256 + lh * 64:256 + (lh + 1) * 64],
            start=True, stop=True),
            waits=[c.sA, (c.sV, None)], inc=c.sP)
    p.step("vector", lambda e: e.tensor_copy(rqs[:], latps[:, 0:256]),
           waits=[c.sP, (c.sV, None)], inc=c.sV)
    p.step("tensor", lambda e: (
        e.transpose(trb[:, 0:128], rqs[:, 0:128], c.idt[:]),
        e.transpose(trb[:, 128:256], rqs[:, 128:256], c.idt[:]))[-1],
        waits=[c.sV, (c.sA, None)], inc=c.sP)
    p.step("vector", lambda e: e.tensor_copy(rqT[:], trb[:, 0:256]),
           waits=[c.sP, (c.sV, None)], inc=c.sV)
    p.step("tensor", lambda e: (
        [e.matmul((c.PS[2][:, 0:512] if nn == 0 else c.PS[3][:, 0:512]),
                  rqT[:, ch * 128:(ch + 1) * 128],
                  c.wrout[0:128, ch * 1024 + nn * 512: ch * 1024 + (nn + 1) * 512],
                  start=(ch == 0), stop=False)
         for nn in range(2) for ch in range(2)] +
        [e.matmul((c.PS[2][:, 0:512] if nn == 0 else c.PS[3][:, 0:512]),
                  c.onr[0:1, :], c.routb[0:1, nn * 512:(nn + 1) * 512],
                  start=False, stop=True) for nn in range(2)])[-1],
        waits=[(c.sV, None), (c.sA, None)], inc=c.sP)
    p.step("scalar", lambda e: e.activation(rem[:, 0:512], c.PS[2][:, 0:512],
                                            AF.Identity),
           waits=[c.sP], inc=c.sA)
    p.step("scalar", lambda e: e.activation(rem[:, 512:1024], c.PS[3][:, 0:512],
                                            AF.Identity),
           inc=c.sA)

    p.step("vector", lambda e: e.tensor_tensor(
        d1[:].rearrange("p (h d) -> p h d", h=16), loc[:].rearrange("p (h d) -> p h d", h=16),
        bass.AP(M["glb"].tensor, M["glb"].offset,
                [[M["glb"].ap[0][0], 128], [0, 16], [1, 64]]), ALU.mult),
        waits=[c.sA, (c.sV, None)], inc=c.sV)
    p.step("vector", lambda e: e.reduce_sum(
        glog[:, 0:16], d1[:].rearrange("p (h d) -> p h d", h=16), axis=AX.X),
        waits=[c.sV], inc=c.sV)
    p.step("vector", lambda e: e.tensor_tensor(
        d1[:].rearrange("p (h d) -> p h d", h=16), rem[:].rearrange("p (h d) -> p h d", h=16),
        bass.AP(M["grb"].tensor, M["grb"].offset,
                [[M["grb"].ap[0][0], 128], [0, 16], [1, 64]]), ALU.mult),
        waits=[c.sV], inc=c.sV)
    p.step("vector", lambda e: e.reduce_sum(
        glog[:, 16:32], d1[:].rearrange("p (h d) -> p h d", h=16), axis=AX.X),
        waits=[c.sV], inc=c.sV)
    p.step("vector", lambda e: e.tensor_tensor(
        glog[:, 32:48], glog[:, 0:16], glog[:, 16:32], ALU.add),
        waits=[c.sV], inc=c.sV)
    p.step("scalar", lambda e: e.activation(gate[:], glog[:, 32:48], AF.Sigmoid),
           waits=[c.sV], inc=c.sA)
    p.step("vector", lambda e: e.tensor_tensor(d1[:], loc[:], rem[:], ALU.subtract),
           waits=[c.sA, (c.sV, None)], inc=c.sV)
    for h in range(16):
        p.step("vector", lambda e, h=h: e.tensor_scalar_mul(
            d1[:, h * 64:(h + 1) * 64], d1[:, h * 64:(h + 1) * 64],
            gate[:, h:h + 1]),
            waits=[(c.sV, None)], inc=c.sV)
    p.step("vector", lambda e: e.tensor_tensor(d1[:], d1[:], rem[:], ALU.add),
           waits=[c.sV], inc=c.sV)
    p.step("tensor", lambda e: (
        [e.transpose(trb[:, f * 128:(f + 1) * 128],
                     d1[:, f * 128:(f + 1) * 128], c.idt[:]) for f in range(8)])[-1],
        waits=[c.sV, (c.sA, None)], inc=c.sP)
    p.step("vector", lambda e: e.tensor_copy(mstage[:], trb[:, 0:1024]),
           waits=[c.sP, (c.sV, None)], inc=c.sV)
    for kk in range(KD):
        p.step("sync", lambda e, kk=kk, t0=t0: e.dma_start(
            out=c.mixd[kk, :, t0:t0 + 128],
            in_=mstage[:, kk * 128:(kk + 1) * 128]),
            waits=[(c.sV, None)] if kk == 0 else [], inc=c.sO, k=16)
    c.p.flush()
    ar.reset(m0)


def emit_mla_out(c, M):
    p = c.p
    io = c.io
    ar = c.ar
    m0 = ar.mark()
    wout = ar.bf16(8 * 1024)
    xold = ar.f32(KD * 512)
    bcol = ar.f32(8)
    mxb = [ar.bf16(8 * 512) for _ in range(2)]
    for kk in range(KD):
        p.step("gpsimd", lambda e, kk=kk: e.dma_start(
            out=wout[:, kk * 1024:(kk + 1) * 1024],
            in_=io["aout_w"][kk * 128:(kk + 1) * 128, :]),
            waits=[(c.sP, None)] if kk == 0 else [], inc=c.sW, k=16)
    for m in range(KD):
        p.step("sync", lambda e, m=m: e.dma_start(
            out=bcol[:, m:m + 1],
            in_=io["aout_b"][m * 128:(m + 1) * 128].unsqueeze(1)),
            waits=[(c.sV, None)], inc=c.sD, k=16)
    for tc in range(4):
        t0 = tc * 512
        mx = mxb[tc % 2]
        for k in range(KD):
            p.step("sync", lambda e, k=k, t0=t0: e.dma_start(
                out=xold[:, k * 512:(k + 1) * 512],
                in_=c.xmod[k, :, t0:t0 + 512]),
                waits=[(c.sV, None), (c.sO, None)], inc=c.sD, k=16)
        for kk in range(KD):
            p.step("sync", lambda e, kk=kk, t0=t0, mx=mx: e.dma_start(
                out=mx[:, kk * 512:(kk + 1) * 512],
                in_=c.mixd[kk, :, t0:t0 + 512]),
                waits=[(c.sP, None), (c.sO, None)] if kk == 0 else [],
                inc=c.sD, k=16)
        for m in range(KD):
            ps = c.PS[m % 2]
            p.step("tensor", lambda e, m=m, t0=t0, ps=ps, mx=mx: (
                [e.matmul(ps[:, 0:512], wout[:, kk * 1024 + m * 128:
                                             kk * 1024 + (m + 1) * 128],
                          mx[:, kk * 512:(kk + 1) * 512],
                          start=(kk == 0), stop=(kk == KD - 1))
                 for kk in range(KD)])[-1],
                waits=[c.sW, c.sD, (c.sV, None)], inc=c.sP)
            p.step("vector", lambda e, m=m, ps=ps: e.scalar_tensor_tensor(
                xold[:, m * 512:(m + 1) * 512], ps[:, 0:512], bcol[:, m:m + 1],
                xold[:, m * 512:(m + 1) * 512], ALU.add, ALU.add),
                waits=[c.sP, (c.sV, None)], inc=c.sV)
        for k in range(KD):
            p.step("sync", lambda e, k=k, t0=t0: e.dma_start(
                out=c.xmod[k, :, t0:t0 + 512],
                in_=xold[:, k * 512:(k + 1) * 512]),
                waits=[c.sV], inc=c.sO, k=16)
        emit_ln_chunk(c, lambda k: xold[:, k * 512:(k + 1) * 512], t0)
    c.p.flush()
    ar.reset(m0)


def kernel(**inputs):
    x = np.ascontiguousarray(np.asarray(inputs["L0_attn_x"], dtype=np.float32))
    w = {k: np.ascontiguousarray(np.asarray(v, dtype=np.float32))
         for k, v in inputs.items()}

    nc = build_program()

    in_maps = []
    for core in range(NCORE):
        b, h = core // 2, core % 2
        start = h * T
        csq, snq, csk, snk = _rotary_tables(start)
        fband, fspec = _ffn_masks(h)
        aband, at0, arow = _mla_masks(h)
        lmask, lrow = _lat_masks(start, h)
        m = {
            "x_in": x[b, start:start + T, :],
            "aqkv_w": w["L0_attn_qkv_w"], "aqkv_b": w["L0_attn_qkv_b"],
            "arq_w": w["L0_attn_rq_w"], "arq_b": w["L0_attn_rq_b"],
            "arkv_w": w["L0_attn_rkv_w"], "arkv_b": w["L0_attn_rkv_b"],
            "arout_w": w["L0_attn_rout_w"], "arout_b": w["L0_attn_rout_b"],
            "aout_w": w["L0_attn_out_w"], "aout_b": w["L0_attn_out_b"],
            "agl_w": w["L0_attn_gl_w"], "agr_w": w["L0_attn_gr_w"],
            "fwqkv": w["L0_ffn_wqkv"], "fq_scale": w["L0_ffn_q_scale"],
            "fk_scale": w["L0_ffn_k_scale"], "fwo": w["L0_ffn_wo"],
            "fln_g": w["L0_ffn_ln_g"], "fln_b": w["L0_ffn_ln_b"],
            "fw1": w["L0_ffn_w1"], "fb1": w["L0_ffn_b1"],
            "fw2": w["L0_ffn_w2"], "fb2": w["L0_ffn_b2"],
            "csq": csq, "snq": snq, "csk": csk, "snk": snk,
            "fband01": fband, "fspec01": fspec,
            "aband01": aband, "at001": at0, "arowkeep": arow,
            "lmask01": lmask, "lrowkeep": lrow,
            "halo_scale": np.full((128, 1), float(h), np.float32),
        }
        in_maps.append(m)

    res = run_bass_kernel_spmd(nc, in_maps, core_ids=list(range(NCORE)))
    out = np.empty((B, S, D), np.float32)
    for core in range(NCORE):
        b, h = core // 2, core % 2
        out[b, h * T:(h + 1) * T, :] = res.results[core]["out"]
    return out


# version 11
# speedup vs baseline: 1.3711x; 1.0424x over previous
import numpy as np
import ml_dtypes
import concourse.bass as bass
import concourse.mybir as mybir
from concourse.bass_utils import run_bass_kernel_spmd
from contextlib import ExitStack

F32 = mybir.dt.float32
BF16 = mybir.dt.bfloat16
AF = mybir.ActivationFunctionType
ALU = mybir.AluOpType
AX = mybir.AxisListType

BF = ml_dtypes.bfloat16

B, S, D = 4, 4096, 1024
NCORE = 8
T = S // 2
NT = T // 128
KD = D // 128

F_HEADS, F_HD = 8, 64
F_INNER = 512
F_WIN = 512
F_HALO = 640
F_HID = 2730
HID_CH = [128] * 21 + [42]
F_QKSCALE = 8.0

A_HEADS, A_HD = 16, 64
A_HALO = 128
LAT_N, LAT_D, LAT_H, LAT_HD = 64, 256, 4, 64
CHUNK = 64

XHB = 136

FF_KT = 512 * F_HALO
FF_V = F_HALO * 512
FF_XB = D * 132
FF_TOT = FF_KT + FF_V + FF_XB
AA_KT = D * A_HALO
AA_V = A_HALO * D
AA_PL = D * 32
AA_TOT = AA_KT + AA_V + AA_PL

RG_PAIRS = [[0, 1], [2, 3], [4, 5], [6, 7]]
NGROUP = 2
STAGES = ["ffn0", "ffn1", "b_ffn", "mla", "ffn0", "ffn1"]
FFN_PARTS = ("attn", "wo", "mlp")
ATTN_CUT = None
SKIP_FINAL = False
LAST_C = None

ENGS = ("sync", "vector", "scalar", "tensor", "gpsimd")


def _rotary_tables(start):
    n = S
    pos = np.arange(start - 128, start + T, dtype=np.float64)
    half = np.arange(0, F_HD, 2, dtype=np.float64)
    inv_freq = 1.0 / (10000.0 ** (half / F_HD))
    freqs = pos[:, None] * inv_freq[None, :]
    base = (half + 0.4 * F_HD) / (1.4 * F_HD)
    power = (pos - n // 2) / float(F_WIN // 2)
    scale = base[None, :] ** power[:, None]
    cos, sin = np.cos(freqs), np.sin(freqs)
    csq = np.concatenate([cos * scale, cos * scale], 1)
    snq = np.concatenate([-sin * scale, sin * scale], 1)
    csk = np.concatenate([cos / scale, cos / scale], 1)
    snk = np.concatenate([-sin / scale, sin / scale], 1)
    bad = pos < 0
    for t in (csq, snq, csk, snk):
        t[bad] = 0.0
    return [t.astype(BF) for t in (csq, snq, csk, snk)]


def _ffn_masks(h):
    i = np.arange(128)[:, None]
    c = np.arange(640)[None, :]
    band = ((c >= i) & (c <= i + 512)).astype(np.float32)
    spec = np.zeros((5, 128, 640), np.float32)
    for idx, t0 in enumerate([-128, 0, 128, 256, 384]):
        m = band.copy()
        if h == 0:
            m = np.where((t0 - 512 + c) < 0, 0.0, m)
        spec[idx] = m
    return band.astype(BF), spec.astype(BF)


def _mla_masks(h):
    i = np.arange(128)[:, None]
    c = np.arange(256)[None, :]
    band = ((c >= i) & (c <= i + 127)).astype(np.float32)
    t0 = band.copy()
    rowkeep = np.ones((T, 1), np.float32)
    if h == 0:
        t0 = np.where((c - 128) < 0, 0.0, t0)
        rowkeep[0, 0] = 0.0
    return band.astype(BF), t0.astype(BF), rowkeep


def _lat_masks(start, h):
    cid = (start + np.arange(T)) // CHUNK
    j = np.arange(LAT_N)[None, :]
    mask = (j < cid[:, None]).astype(np.float32)
    rowkeep = (cid > 0).astype(np.float32)[:, None]
    return mask.astype(BF), rowkeep


class Sem:
    def __init__(self, nc, es, name):
        self.h = es.enter_context(nc.semaphore(name))
        self.n = 0
        self.name = name
        self.last_waited = 0
        self.synced = 0


class Arena:
    def __init__(self, nc, es, name, cols):
        self.t = es.enter_context(nc.sbuf_tensor(name, [128, cols], F32))
        self.cols = cols
        self.top = 0

    def f32(self, cols):
        base = self.top
        assert base + cols <= self.cols, \
            f"arena overflow {base}+{cols}>{self.cols}"
        self.top += cols
        return self.t[:, base:base + cols]

    def bf16(self, cols):
        assert cols % 2 == 0
        return self.f32(cols // 2).bitcast(BF16)

    def mark(self):
        return self.top

    def reset(self, mark):
        self.top = mark


class Prog:

    def __init__(self, nc):
        self.nc = nc
        self.tapes = {e: [] for e in ENGS}
        self.last_dma = {}

    def step(self, eng, fn, waits=(), inc=None, k=1):
        w = []
        for item in waits:
            if isinstance(item, tuple):
                s, v = item
            else:
                s, v = item, None
            v = s.n if v is None else v
            s.last_waited = max(s.last_waited, v)
            w.append((s.h, v))
        if inc is not None and k == 16:
            if inc.last_waited > inc.synced:
                w.insert(0, (inc.h, inc.n))
                inc.synced = inc.n
        dd = {}
        for h, v in w:
            dd[h] = max(dd.get(h, 0), v)
        w = list(dd.items())
        self.tapes[eng].append((w, fn, None if inc is None else inc.h, k))
        if inc is not None:
            inc.n += k

    def flush(self):
        tapes = {e: t for e, t in self.tapes.items() if t}
        self.tapes = {e: [] for e in ENGS}
        if not tapes:
            return

        def runner(tape):
            def run(eng):
                for w, fn, semh, k in tape:
                    for h, v in w:
                        eng.wait_ge(h, v)
                    ins = fn(eng)
                    if semh is not None:
                        ins.then_inc(semh, k)
            return run

        with self.nc.Block() as blk:
            for e, tape in tapes.items():
                getattr(blk, e)(runner(tape))

    def barrier(self):
        c = self.ctx
        self.step("sync", lambda e: e.nop(),
                  waits=[(c.sD, None), (c.sD2, None), (c.sO, None)])
        self.step("gpsimd", lambda e: e.nop(), waits=[(c.sW, None)])
        self.flush()
        self.nc.all_engine_barrier()
        c.fence_banks()


FENCE_KEYS = ["psk0", "psk1", "psv0", "psv1", "ps7", "set0", "set1", "trb",
              "bx0", "bx1", "pT0", "pT1", "avE0", "avE1", "gpb0", "gpb1",
              "gmask0", "gmask1", "qT0", "qT1",
              "grr0", "grr1", "grr2", "grr3",
              "pm0", "pm1", "pm2", "pm3", "pm4", "pm5", "pm6", "pm7",
              "Akproc", "Bqproc", "wob0", "wob1", "wops0", "wops1", "wopsh",
              "wp0", "wp1", "psa0", "psa1", "psg0", "psg1", "agbuf", "ggbuf",
              "w2banks"]


class Ctx:
    def fence_banks(self):
        g = [(self.sV, self.sV.n), (self.sA, self.sA.n), (self.sG, self.sG.n),
             (self.sP, self.sP.n)]
        for k in FENCE_KEYS:
            self.bg[k] = list(g)

    def bgate(self, key):
        return list(self.bg.get(key, []))

    def bset(self, key, sem):
        self.bg[key] = [(sem, sem.n)]

    def badd(self, key, sem):
        self.bg.setdefault(key, []).append((sem, sem.n))


def build_program():
    nc = bass.Bass("TRN2", target_bir_lowering=False)
    c = Ctx()
    c.nc = nc

    def din(name, shape, dt=F32, kind="ExternalInput"):
        return nc.dram_tensor(name, list(shape), dt, kind=kind).ap()

    c.io = io = {}
    io["x_in"] = din("x_in", [T, D])
    for nm, shp in [
        ("aqkv_w", [D, 3 * D]), ("aqkv_b", [3 * D]),
        ("arq_w", [D, LAT_D]), ("arq_b", [LAT_D]),
        ("arkv_w", [D, 2 * LAT_D]), ("arkv_b", [2 * LAT_D]),
        ("arout_w", [LAT_D, D]), ("arout_b", [D]),
        ("aout_w", [D, D]), ("aout_b", [D]),
        ("agl_w", [A_HD, 1]), ("agr_w", [A_HD, 1]),
        ("fwqkv", [2, D, 3 * F_INNER]), ("fq_scale", [2, F_HD]),
        ("fk_scale", [2, F_HD]), ("fwo", [2, F_INNER, D]),
        ("fln_g", [2, D]), ("fln_b", [2, D]),
        ("fw1", [2, D, 2 * F_HID]), ("fb1", [2, 2 * F_HID]),
        ("fw2", [2, F_HID, D]), ("fb2", [2, D]),
    ]:
        io[nm] = din(nm, shp)
    io["csq"] = din("csq", [T + 128, F_HD], BF16)
    io["snq"] = din("snq", [T + 128, F_HD], BF16)
    io["csk"] = din("csk", [T + 128, F_HD], BF16)
    io["snk"] = din("snk", [T + 128, F_HD], BF16)
    io["fband01"] = din("fband01", [128, 640], BF16)
    io["fspec01"] = din("fspec01", [5, 128, 640], BF16)
    io["aband01"] = din("aband01", [128, 256], BF16)
    io["at001"] = din("at001", [128, 256], BF16)
    io["arowkeep"] = din("arowkeep", [T, 1])
    io["lmask01"] = din("lmask01", [T, LAT_N], BF16)
    io["lrowkeep"] = din("lrowkeep", [T, 1])
    io["halo_scale"] = din("halo_scale", [128, 1])
    io["out"] = din("out", [T, D], kind="ExternalOutput")

    c.xmod = din("xmod", [KD, 128, T], kind="Internal")
    c.mixd = din("mixd", [KD, 128, T], BF16, kind="Internal")
    c.ccf_in = din("ccf_in", [1, FF_TOT], BF16, kind="Internal")
    c.ccf_out = din("ccf_out", [NGROUP, FF_TOT], BF16, kind="Internal")
    c.cca_in = din("cca_in", [1, AA_TOT], BF16, kind="Internal")
    c.cca_out = din("cca_out", [NGROUP, AA_TOT], BF16, kind="Internal")

    id_d = nc.inline_tensor(np.eye(128).astype(BF), name="id128").ap()
    onesc_d = nc.inline_tensor(np.ones((128, 1), BF), name="onesc").ap()
    onesr_d = nc.inline_tensor(np.ones((1, 128), BF), name="onesr").ap()
    eps_d = nc.inline_tensor(np.full((1, 512), 1e-5, np.float32),
                             name="epsrow").ap()

    es = ExitStack()
    c.es = es
    c.xm = es.enter_context(nc.sbuf_tensor("xm", [128, KD * T], F32))
    c.xb = es.enter_context(nc.sbuf_tensor("xb", [128, KD * (T + XHB)], BF16))
    c.idt = es.enter_context(nc.sbuf_tensor("idt", [128, 128], BF16))
    c.onc = es.enter_context(nc.sbuf_tensor("onc", [128, 1], BF16))
    c.onr = es.enter_context(nc.sbuf_tensor("onr", [1, 128], BF16))
    c.hsc = es.enter_context(nc.sbuf_tensor("hsc", [128, 1], F32))
    c.epsr_t = es.enter_context(nc.sbuf_tensor("epsr", [1, 512], F32))
    c.epsr = c.epsr_t[0:1, 0:512]
    c.fband = es.enter_context(nc.sbuf_tensor("fband", [128, 640], BF16))
    c.aband = es.enter_context(nc.sbuf_tensor("aband", [128, 256], BF16))
    c.at0 = es.enter_context(nc.sbuf_tensor("at0", [128, 256], BF16))
    c.ar = Arena(nc, es, "arena", 26600)
    c.PS = [es.enter_context(nc.psum_tensor(f"ps{i}", [128, 512], F32))
            for i in range(8)]

    for nm in ["sD", "sW", "sP", "sV", "sA", "sG", "sC", "sO", "sP2", "sV2",
               "sD2"]:
        setattr(c, nm, Sem(nc, es, nm))

    c.p = Prog(nc)
    c.p.ctx = c
    c.bg = {}
    c.consts = dict(id_d=id_d, onesc_d=onesc_d, onesr_d=onesr_d, eps_d=eps_d)

    program(c)
    c.p.flush()
    es.close()
    global LAST_C
    LAST_C = c
    return nc


def xm_v(c, k, t0, tn):
    return c.xm[:, k * T + t0: k * T + t0 + tn]


def xb_v(c, k, t0, tn):
    col = k * (T + XHB) + XHB + t0
    return c.xb[:, col: col + tn]


def emit_prologue(c):
    p = c.p
    io = c.io
    cn = c.consts
    for dst, src in [(c.idt[:], cn["id_d"]), (c.onc[:], cn["onesc_d"]),
                     (c.onr[:], cn["onesr_d"]), (c.epsr, cn["eps_d"]),
                     (c.hsc[:], io["halo_scale"]), (c.fband[:], io["fband01"]),
                     (c.aband[:], io["aband01"]), (c.at0[:], io["at001"])]:
        p.step("sync", lambda e, d=dst, s=src: e.dma_start(out=d, in_=s),
               inc=c.sD, k=16)

    ar = c.ar
    m0 = ar.mark()
    xtok = [ar.f32(1024) for _ in range(2)]
    hi = ar.bf16(1024)
    lo = ar.bf16(1024)
    hswap = ar.bf16(128)
    trb = c.PS[6].bitcast(BF16)
    for tt in range(NT):
        t0 = tt * 128
        xt = xtok[tt % 2]
        p.step("sync", lambda e, xt=xt, t0=t0:
               e.dma_start(out=xt, in_=io["x_in"][t0:t0 + 128, :]),
               waits=[(c.sV, None)], inc=c.sD, k=16)
        p.step("vector", lambda e, xt=xt: e.tensor_copy(hi[:], xt),
               waits=[c.sD, c.sP], inc=c.sV)
        p.step("vector", lambda e, xt=xt:
               e.tensor_tensor(lo[:], xt, hi[:], ALU.subtract),
               waits=[c.sV], inc=c.sV)
        for k in range(KD):
            p.step("tensor", lambda e, k=k: (
                e.transpose(trb[:, 0:128] if False else trb[:, (k % 4) * 256:(k % 4) * 256 + 128],
                            hi[:, k * 128:(k + 1) * 128], c.idt[:]),
                e.transpose(trb[:, (k % 4) * 256 + 128:(k % 4) * 256 + 256],
                            lo[:, k * 128:(k + 1) * 128], c.idt[:]))[-1],
                waits=([c.sV] if k == 0 else [(c.sV2, c.sV2.n - 2)]),
                inc=c.sP)
            p.step("vector", lambda e, k=k:
                   e.tensor_copy(hswap[:], trb[:, (k % 4) * 256:(k % 4) * 256 + 128]),
                   waits=[c.sP, (c.sV2, None)], inc=c.sV2)
            p.step("vector", lambda e, k=k, t0=t0:
                   e.tensor_tensor(xm_v(c, k, t0, 128), hswap[:],
                                   trb[:, (k % 4) * 256 + 128:(k % 4) * 256 + 256],
                                   ALU.add),
                   waits=[c.sV2], inc=c.sV2)
        for k in range(KD):
            p.step("sync", lambda e, k=k, t0=t0:
                   e.dma_start(out=c.xmod[k, :, t0:t0 + 128],
                               in_=xm_v(c, k, t0, 128)),
                   waits=[c.sV2], inc=c.sO, k=16)
    p.step("sync", lambda e: e.nop(), waits=[c.sD, c.sO])
    p.barrier()
    ar.reset(m0)


def emit_ln_chunk(c, src_ap, t0, dst_master=True, write_shadow=True):
    p = c.p
    ar = c.ar
    m0 = ar.mark()
    sq = [ar.bf16(512) for _ in range(2)]
    xcast = [ar.bf16(512) for _ in range(2)]
    rowf = ar.f32(6 * 512)
    R = {nm: rowf[0:1, i * 512:(i + 1) * 512]
         for i, nm in enumerate(["m", "ms", "t", "r1", "r2", "x"])}
    rowb = ar.bf16(4 * 512)
    RB = {nm: rowb[0:1, i * 512:(i + 1) * 512]
          for i, nm in enumerate(["r1h", "r1l", "r2h", "r2l"])}
    b1s = ar.f32(512)
    b2s = ar.f32(512)
    tmp = ar.f32(512)

    ps_sum = c.PS[4][0:1, 0:512]
    ps_ssq = c.PS[5][0:1, 0:512]
    ps_b1 = c.PS[2][:, 0:512]
    ps_b2 = c.PS[3][:, 0:512]

    for k in range(KD):
        p.step("scalar", lambda e, k=k:
               e.activation(sq[k % 2][:], src_ap(k), AF.Square),
               waits=[c.sV] + ([] if k < 2 else [(c.sP, c.sP.n - 2)]),
               inc=c.sA)
        p.step("vector", lambda e, k=k:
               e.tensor_copy(xcast[k % 2][:], src_ap(k)),
               waits=[(c.sV, None)] if k < 2 else [(c.sP, c.sP.n - 1),
                                                   (c.sV, None)],
               inc=c.sV)
        p.step("tensor", lambda e, k=k:
               e.matmul(ps_sum, c.onc[:, 0:1], xcast[k % 2][:],
                        start=(k == 0), stop=(k == KD - 1)),
               waits=[c.sV], inc=c.sP)
        p.step("tensor", lambda e, k=k:
               e.matmul(ps_ssq, c.onc[:, 0:1], sq[k % 2][:],
                        start=(k == 0), stop=(k == KD - 1)),
               waits=[c.sA], inc=c.sP)

    inv = 1.0 / D
    p.step("vector", lambda e: e.tensor_scalar_mul(R["m"], ps_sum, inv),
           waits=[c.sP], inc=c.sV)
    p.step("vector", lambda e: e.tensor_scalar_mul(R["ms"], ps_ssq, inv),
           waits=[c.sV], inc=c.sV)
    p.step("vector", lambda e: e.tensor_tensor(R["t"], R["m"], R["m"], ALU.mult),
           waits=[c.sV], inc=c.sV)
    p.step("vector", lambda e:
           e.tensor_tensor(R["ms"], R["ms"], R["t"], ALU.subtract),
           waits=[c.sV], inc=c.sV)
    p.step("vector", lambda e:
           e.tensor_scalar_add(R["ms"], R["ms"], 1e-5),
           waits=[c.sV], inc=c.sV)
    p.step("scalar", lambda e:
           e.activation(R["t"], R["ms"], AF.Sqrt),
           waits=[c.sV], inc=c.sA)
    p.step("vector", lambda e: e.reciprocal(R["r1"], R["t"]),
           waits=[c.sA], inc=c.sV)
    p.step("vector", lambda e: e.tensor_tensor(R["r2"], R["m"], R["r1"], ALU.mult),
           waits=[c.sV], inc=c.sV)
    p.step("vector", lambda e: e.tensor_scalar_mul(R["r2"], R["r2"], -1.0),
           waits=[c.sV], inc=c.sV)
    for r, rh, rl in [("r1", "r1h", "r1l"), ("r2", "r2h", "r2l")]:
        p.step("vector", lambda e, r=r, rh=rh: e.tensor_copy(RB[rh], R[r]),
               waits=[c.sV], inc=c.sV)
        p.step("vector", lambda e, r=r, rh=rh, rl=rl:
               e.tensor_tensor(R["x"], R[r], RB[rh], ALU.subtract),
               waits=[c.sV], inc=c.sV)
        p.step("vector", lambda e, rl=rl: e.tensor_copy(RB[rl], R["x"]),
               waits=[c.sV], inc=c.sV)

    p.step("tensor", lambda e: (
        e.matmul(ps_b1, c.onr[0:1, :], RB["r1h"], start=True, stop=False),
        e.matmul(ps_b1, c.onr[0:1, :], RB["r1l"], start=False, stop=True))[-1],
        waits=[c.sV, c.sA], inc=c.sP)
    p.step("tensor", lambda e: (
        e.matmul(ps_b2, c.onr[0:1, :], RB["r2h"], start=True, stop=False),
        e.matmul(ps_b2, c.onr[0:1, :], RB["r2l"], start=False, stop=True))[-1],
        waits=[c.sV, c.sA], inc=c.sP)
    p.step("scalar", lambda e: e.activation(b1s, ps_b1, AF.Identity),
           waits=[c.sP], inc=c.sA)
    p.step("scalar", lambda e: e.activation(b2s, ps_b2, AF.Identity),
           waits=[c.sP], inc=c.sA)

    for k in range(KD):
        p.step("vector", lambda e, k=k:
               e.tensor_tensor(tmp, src_ap(k), b1s, ALU.mult),
               waits=[c.sA, c.sV], inc=c.sV)
        p.step("vector", lambda e, k=k, t0=t0:
               e.tensor_tensor(xm_v(c, k, t0, 512), tmp, b2s, ALU.add),
               waits=[c.sV], inc=c.sV)
        if True:
            p.step("vector", lambda e, k=k, t0=t0:
                   e.tensor_copy(xb_v(c, k, t0, 512), xm_v(c, k, t0, 512)),
                   waits=[c.sV], inc=c.sV)
    c.p.flush()
    ar.reset(m0)


def emit_boundary(c, kind):
    p = c.p
    ar = c.ar
    if kind == "init":
        for tc in range(4):
            t0 = tc * 512
            emit_ln_chunk(c, lambda k, t0=t0: xm_v(c, k, t0, 512), t0)
        p.barrier()
        return

    scale = 0.5 if kind in ("final", "mid_ffn") else 1.0
    assert kind in ("final", "mid_ffn", "mid_mla")
    m0 = ar.mark()
    xnew = ar.f32(KD * 512)
    xold = ar.f32(KD * 512)
    for tc in range(4):
        t0 = tc * 512
        for k in range(KD):
            p.step("sync", lambda e, k=k, t0=t0:
                   e.dma_start(out=xold[:, k * 512:(k + 1) * 512],
                               in_=c.xmod[k, :, t0:t0 + 512]),
                   waits=[(c.sV, None)], inc=c.sD, k=16)
        for k in range(KD):
            p.step("vector", lambda e, k=k, t0=t0:
                   e.scalar_tensor_tensor(
                       xnew[:, k * 512:(k + 1) * 512],
                       xm_v(c, k, t0, 512), scale,
                       xold[:, k * 512:(k + 1) * 512], ALU.mult, ALU.add),
                   waits=[c.sD, (c.sO, None)], inc=c.sV)
        if kind != "final":
            for k in range(KD):
                p.step("sync", lambda e, k=k, t0=t0:
                       e.dma_start(out=c.xmod[k, :, t0:t0 + 512],
                                   in_=xnew[:, k * 512:(k + 1) * 512]),
                       waits=[c.sV], inc=c.sO, k=16)
            emit_ln_chunk(c, lambda k: xnew[:, k * 512:(k + 1) * 512], t0)
        else:
            emit_out_chunk(c, xnew, t0)
    p.barrier()
    ar.reset(m0)


def emit_out_chunk(c, xnew, t0):
    p = c.p
    ar = c.ar
    m0 = ar.mark()
    hi = ar.bf16(512)
    lo = ar.bf16(512)
    hswap = ar.f32(128)
    stage = [ar.f32(128) for _ in range(2)]
    trb = c.PS[6].bitcast(BF16)
    for k in range(KD):
        p.step("vector", lambda e, k=k: e.tensor_copy(hi[:], xnew[:, k * 512:(k + 1) * 512]),
               waits=[c.sV, (c.sP, None)], inc=c.sV)
        p.step("vector", lambda e, k=k:
               e.tensor_tensor(lo[:], xnew[:, k * 512:(k + 1) * 512], hi[:],
                               ALU.subtract),
               waits=[c.sV], inc=c.sV)
        for s in range(4):
            p.step("tensor", lambda e, s=s: (
                e.transpose(trb[:, s * 256:s * 256 + 128],
                            hi[:, s * 128:(s + 1) * 128], c.idt[:]),
                e.transpose(trb[:, s * 256 + 128:s * 256 + 256],
                            lo[:, s * 128:(s + 1) * 128], c.idt[:]))[-1],
                waits=[c.sV] + ([] if (k, s) == (0, 0) else [(c.sV2, None)]),
                inc=c.sP)
            p.step("vector", lambda e, s=s:
                   e.tensor_copy(hswap[:], trb[:, s * 256:s * 256 + 128]),
                   waits=[c.sP], inc=c.sV2)
            p.step("vector", lambda e, s=s:
                   e.tensor_tensor(stage[s % 2][:], hswap[:],
                                   trb[:, s * 256 + 128:s * 256 + 256], ALU.add),
                   waits=[c.sV2, (c.sO, None)], inc=c.sV2)
            p.step("sync", lambda e, s=s, k=k, t0=t0:
                   e.dma_start(out=c.io["out"][t0 + s * 128:t0 + (s + 1) * 128,
                                               k * 128:(k + 1) * 128],
                               in_=stage[s % 2][:]),
                   waits=[c.sV2], inc=c.sO, k=16)
    c.p.flush()
    ar.reset(m0)


def rot_view(ap):
    rs = ap.ap[0][0]
    return bass.AP(ap.tensor, ap.offset + 32, [[rs, 128], [64, 8], [-32, 2], [1, 32]])


def head_view(ap):
    rs = ap.ap[0][0]
    return bass.AP(ap.tensor, ap.offset, [[rs, 128], [64, 8], [32, 2], [1, 32]])


def rep_view(ap64):
    rs = ap64.ap[0][0]
    return bass.AP(ap64.tensor, ap64.offset, [[rs, 128], [0, 8], [32, 2], [1, 32]])


def row_swap_view(row64):
    return bass.AP(row64.tensor, row64.offset + 32, [[row64.ap[0][0], 1], [-32, 2], [1, 32]])


def ffn_qkv_mms(c, e, ps, wsl, t0):
    ins = None
    for kk in range(KD):
        src = xb_v(c, kk, t0 if kk < 4 else t0 - 1, 128)
        ins = e.matmul(ps, src, wsl(kk), start=(kk == 0), stop=(kk == KD - 1))
    return ins


def emit_ffn_kv_tile(c, l, tt, A, wait_halo=False):
    p = c.p
    t0 = tt * 128
    psk = c.PS[tt % 2]
    psv = c.PS[2 + tt % 2]
    trb = c.PS[6].bitcast(BF16)

    kw = [(c.sW, None)] + c.bg.pop(f"psk{tt % 2}", [])
    if wait_halo:
        kw.append((c.sV, c.haloV))
    p.step("tensor", lambda e: ffn_qkv_mms(
        c, e, psk[:, 0:512], lambda kk: A["wkv"][:, kk * 1024:kk * 1024 + 512],
        t0),
        waits=kw, inc=c.sP)
    p.step("tensor", lambda e: ffn_qkv_mms(
        c, e, psv[:, 0:512], lambda kk: A["wkv"][:, kk * 1024 + 512:kk * 1024 + 1024],
        t0),
        waits=c.bg.pop(f"psv{tt % 2}", []), inc=c.sP)
    vt = tt + 5
    p.step("scalar", lambda e: e.activation(
        A["v"][:, vt * 512:(vt + 1) * 512], psv[:, 0:512], AF.Identity),
        waits=[c.sP], inc=c.sA)
    c.bset(f"psv{tt % 2}", c.sA)

    kW = c.sP.n - 1
    p.step("scalar", lambda e: e.activation(A["sq"][:], psk[:, 0:512], AF.Square),
           waits=[(c.sP, kW)] + c.bg.pop("Akproc", []), inc=c.sA)
    p.step("vector", lambda e: e.reduce_sum(
        A["ss"][:, 0:8], A["sq"][:].rearrange("p (h d) -> p h d", h=8), axis=AX.X),
        waits=[c.sA, (c.sV, None)], inc=c.sV)
    p.step("scalar", lambda e: e.activation(A["sr"][:, 0:8], A["ss"][:, 0:8], AF.Sqrt),
           waits=[c.sV], inc=c.sA)
    p.step("vector", lambda e: e.tensor_scalar_max(A["sr"][:, 0:8], A["sr"][:, 0:8], 1e-12),
           waits=[c.sA], inc=c.sV)
    p.step("vector", lambda e: e.reciprocal(A["rstd"][:, 0:8], A["sr"][:, 0:8]),
           waits=[c.sV], inc=c.sV)
    for h in range(8):
        p.step("vector", lambda e, h=h: e.tensor_scalar_mul(
            A["kn"][:, h * 64:(h + 1) * 64], psk[:, h * 64:(h + 1) * 64],
            A["rstd"][:, h:h + 1]),
            waits=[c.sV], inc=c.sV)
    c.bg[f"psk{tt % 2}"] = [(c.sV, c.sV.n), (c.sA, c.sA.n)]
    r0 = 128 + t0
    p.step("sync", lambda e: e.dma_start(out=A["cst"][:], in_=c.io["csk"][r0:r0 + 128, :]),
           waits=[(c.sV, None)], inc=c.sD, k=16)
    p.step("sync", lambda e: e.dma_start(out=A["snt"][:], in_=c.io["snk"][r0:r0 + 128, :]),
           inc=c.sD, k=16)
    p.step("vector", lambda e: e.tensor_tensor(A["csE"][:], A["cst"][:], A["kscb"][:], ALU.mult),
           waits=[c.sD, c.sV], inc=c.sV)
    p.step("vector", lambda e: e.tensor_tensor(A["snE"][:], A["snt"][:], A["kscsb"][:], ALU.mult),
           waits=[c.sV], inc=c.sV)
    p.step("vector", lambda e: e.tensor_tensor(
        head_view(A["t1"][:]), head_view(A["kn"][:]), rep_view(A["csE"][:]), ALU.mult),
        waits=[c.sV], inc=c.sV)
    p.step("vector", lambda e: e.tensor_tensor(
        head_view(A["t2"][:]), rot_view(A["kn"][:]), rep_view(A["snE"][:]), ALU.mult),
        waits=[c.sV], inc=c.sV)
    p.step("vector", lambda e: e.tensor_tensor(A["kp"][:], A["t1"][:], A["t2"][:], ALU.add),
           waits=[c.sV], inc=c.sV)
    p.step("tensor", lambda e, t0=t0: (
        [e.transpose(trb[:, f * 128:(f + 1) * 128], A["kp"][:, f * 128:(f + 1) * 128],
                     c.idt[:]) for f in range(4)])[-1],
        waits=[c.sV] + c.bg.pop("trb", []), inc=c.sP)
    c.badd("Akproc", c.sP)
    for f in range(4):
        p.step("vector", lambda e, f=f, t0=t0: e.tensor_copy(
            c.kT_col(A, f, 640 + t0, 128), trb[:, f * 128:(f + 1) * 128]),
            waits=[c.sP, (c.sV, None)], inc=c.sV)
    c.badd("trb", c.sV)


def emit_ffn_layer(c, l):
    p = c.p
    io = c.io
    ar = c.ar
    mL = ar.mark()

    A = {}
    A["kT"] = ar.bf16(4 * 2688)
    A["v"] = ar.bf16(21 * 512)
    A["att"] = ar.bf16(4 * 2176)
    rows = ar.f32(128)

    def kT_col(A, f, c0, n):
        return A["kT"][:, f * 2688 + c0: f * 2688 + c0 + n]
    c.kT_col = kT_col

    ksrow = rows[0:1, 0:64]
    kbc = ar.f32(64)
    kbcs = ar.f32(64)
    qbc = ar.f32(64)
    qbcs = ar.f32(64)
    A["kscb"] = kbc.bitcast(BF16)[:, 0:64]
    A["kscsb"] = kbcs.bitcast(BF16)[:, 0:64]
    A["qscb"] = qbc.bitcast(BF16)[:, 0:64]
    A["qscsb"] = qbcs.bitcast(BF16)[:, 0:64]

    p.step("sync", lambda e: e.dma_start(out=ksrow, in_=io["fk_scale"][l:l + 1, :]),
           inc=c.sD, k=16)
    emit_bcast_scaled(c, ksrow, A["kscb"], swap=False, scale=1.0)
    emit_bcast_scaled(c, ksrow, A["kscsb"], swap=True, scale=1.0)
    p.step("sync", lambda e: e.dma_start(out=ksrow, in_=io["fq_scale"][l:l + 1, :]),
           waits=[(c.sA, None), (c.sP, None)], inc=c.sD, k=16)
    emit_bcast_scaled(c, ksrow, A["qscb"], swap=False, scale=F_QKSCALE)
    emit_bcast_scaled(c, ksrow, A["qscsb"], swap=True, scale=F_QKSCALE)

    c.p.flush()

    mS = ar.mark()

    A["wkv"] = ar.bf16(8 * 1024)
    A["sq"] = ar.bf16(512)
    A["ss"] = ar.f32(8)
    A["sr"] = ar.f32(8)
    A["rstd"] = ar.f32(8)
    A["kn"] = ar.bf16(512)
    A["t1"] = ar.bf16(512)
    A["t2"] = ar.bf16(512)
    A["kp"] = ar.bf16(512)
    A["cst"] = ar.bf16(64)
    A["snt"] = ar.bf16(64)
    A["csE"] = ar.bf16(64)
    A["snE"] = ar.bf16(64)

    for kk in range(KD):
        p.step("gpsimd", lambda e, kk=kk: e.dma_start(
            out=A["wkv"][:, kk * 1024:(kk + 1) * 1024],
            in_=io["fwqkv"][l, kk * 128:(kk + 1) * 128, 512:1536]),
            inc=c.sW, k=16)

    for tt in [11, 12, 13, 14, 15]:
        emit_ffn_kv_tile(c, l, tt, A)

    offs = 0
    def pack(src_ap, nel):
        nonlocal offs
        dst = c.ccf_in[0, offs:offs + nel].rearrange(
            "(p f) -> p f", p=src_ap.shape[0])
        p.step("sync", lambda e, d=dst, s=src_ap: e.dma_start(out=d, in_=s),
               waits=[(c.sV, None)], inc=c.sD, k=16)
        offs += nel
    for f in range(4):
        pack(kT_col(A, f, 640 + T - 640, 640), 128 * 640)
    for vt in range(16, 21):
        pack(A["v"][:, vt * 512:(vt + 1) * 512], 128 * 512)
    for kk in range(KD):
        pack(xb_v(c, kk, T - 132, 132), 128 * 132)
    assert offs == FF_TOT
    rg = RG_PAIRS if NGROUP == 2 else [[i] for i in range(NCORE)]
    p.step("gpsimd", lambda e, rg=rg: e.collective_compute(
        "AllGather", ALU.bypass, replica_groups=rg,
        ins=[c.ccf_in.opt()], outs=[c.ccf_out.opt()]),
        waits=[c.sD, (c.sC, None)], inc=c.sC)

    for tt in range(1, 11):
        emit_ffn_kv_tile(c, l, tt, A)
    c.p.flush()

    offs = 0
    def unpack(dst_ap, nel):
        nonlocal offs
        src = c.ccf_out[0, offs:offs + nel].rearrange(
            "(p f) -> p f", p=dst_ap.shape[0])
        p.step("sync", lambda e, d=dst_ap, s=src: e.dma_start(out=d, in_=s),
               waits=[(c.sC, None)], inc=c.sD2, k=16)
        offs += nel
    for f in range(4):
        unpack(kT_col(A, f, 0, 640), 128 * 640)
    for vt in range(0, 5):
        unpack(A["v"][:, vt * 512:(vt + 1) * 512], 128 * 512)
    for kk in range(KD):
        unpack(xb_v(c, kk, -132, 132), 128 * 132)
    assert offs == FF_TOT
    for kk in range(KD):
        p.step("vector", lambda e, kk=kk: e.tensor_scalar_mul(
            xb_v(c, kk, -132, 132), xb_v(c, kk, -132, 132), c.hsc[:, 0:1]),
            waits=[(c.sD2, None), (c.sV, None)], inc=c.sV)
    for f in range(4):
        p.step("vector", lambda e, f=f: e.tensor_scalar_mul(
            c.kT_col(A, f, 0, 640), c.kT_col(A, f, 0, 640), c.hsc[:, 0:1]),
            waits=[(c.sV, None)], inc=c.sV)
    p.step("vector", lambda e: e.tensor_scalar_mul(
        A["v"][:, 0:5 * 512], A["v"][:, 0:5 * 512], c.hsc[:, 0:1]),
        waits=[(c.sV, None)], inc=c.sV)
    c.haloV = c.sV.n
    emit_ffn_kv_tile(c, l, 0, A, wait_halo=True)

    p.barrier()
    ar.reset(mS)

    Bw = {}
    Bw["wq"] = ar.bf16(8 * 512)
    for kk in range(KD):
        p.step("gpsimd", lambda e, kk=kk: e.dma_start(
            out=Bw["wq"][:, kk * 512:(kk + 1) * 512],
            in_=io["fwqkv"][l, kk * 128:(kk + 1) * 128, 0:512]),
            inc=c.sW, k=16)
    Bw["sq"] = ar.bf16(512)
    Bw["ss"] = ar.f32(8)
    Bw["sr"] = ar.f32(8)
    Bw["rstd"] = ar.f32(8)
    Bw["qn"] = ar.bf16(512)
    Bw["t1"] = ar.bf16(512)
    Bw["t2"] = ar.bf16(512)
    Bw["qp"] = ar.bf16(512)
    Bw["cst"] = ar.bf16(64)
    Bw["snt"] = ar.bf16(64)
    Bw["csE"] = ar.bf16(64)
    Bw["snE"] = ar.bf16(64)
    Bw["qT"] = [ar.bf16(512), ar.bf16(512)]
    Bw["probs"] = [ar.bf16(640), ar.bf16(640)]
    Bw["probsm"] = [ar.bf16(640) for _ in range(8)]
    Bw["probsT"] = [ar.bf16(640), ar.bf16(640)]
    Bw["mask"] = [ar.bf16(640), ar.bf16(640)]
    Bw["sums"] = ar.f32(6)
    Bw["rr"] = ar.f32(8)

    c.dbg_A, c.dbg_Bw = A, Bw
    if "attn" in FFN_PARTS:
        order = list(range(4, 16)) + [16, 0, 1, 2, 3]
        pend = None
        for ui, q_idx in enumerate(order):
            pend = emit_ffn_unit(c, l, q_idx, ui, A, Bw, pend)
        for fn in pend:
            fn()
    c.attn_done = c.sV.n
    c.p.flush()

    if "wo" not in FFN_PARTS:
        c.p.barrier()
        ar.reset(mL)
        return
    Bw["wo"] = [ar.bf16(4 * 128), ar.bf16(4 * 128)]
    Bw["h5t"] = ar.bf16(128)
    for m in range(KD):
        wob = Bw["wo"][m % 2]
        for kk in range(4):
            p.step("gpsimd", lambda e, m=m, kk=kk, wob=wob: e.dma_start(
                out=wob[:, kk * 128:(kk + 1) * 128],
                in_=io["fwo"][l, kk * 128:(kk + 1) * 128,
                              m * 128:(m + 1) * 128]),
                waits=[(c.sP, None)] if kk == 0 else [],
                inc=c.sW, k=16)
        for tc in range(4):
            ps = c.PS[tc % 2]
            p.step("tensor", lambda e, m=m, tc=tc, wob=wob, ps=ps: (
                [e.matmul(ps[:, 0:512], wob[:, kk * 128:(kk + 1) * 128],
                          A["att"][:, kk * 2176 + 128 + tc * 512:
                                   kk * 2176 + 128 + (tc + 1) * 512],
                          start=(kk == 0), stop=(kk == 3))
                 for kk in range(4)])[-1],
                waits=[(c.sW, None), (c.sV, c.attn_done)]
                + c.bg.pop(f"wops{tc % 2}", []), inc=c.sP)
            p.step("vector", lambda e, m=m, tc=tc, ps=ps: e.tensor_tensor(
                xm_v(c, m, tc * 512, 512), ps[:, 0:512],
                xm_v(c, m, tc * 512, 512), ALU.add),
                waits=[c.sP, (c.sV, None)], inc=c.sV)
            c.bset(f"wops{tc % 2}", c.sV)
        psh = c.PS[2]
        p.step("tensor", lambda e, m=m, wob=wob: (
            [e.matmul(psh[:, 0:128], wob[:, kk * 128:(kk + 1) * 128],
                      A["att"][:, kk * 2176: kk * 2176 + 128],
                      start=(kk == 0), stop=(kk == 3))
             for kk in range(4)])[-1],
            waits=[(c.sV, c.attn_done)] + c.bg.pop("wopsh", []), inc=c.sP)
        c.badd(f"wob{m % 2}", c.sP)
        p.step("vector", lambda e, m=m: e.tensor_scalar_mul(
            Bw["h5t"][:], psh[:, 0:128], c.hsc[:, 0:1]),
            waits=[c.sP, (c.sV, None)], inc=c.sV)
        c.bset("wopsh", c.sV)
        p.step("vector", lambda e, m=m: e.tensor_tensor(
            xb_v(c, m, -128, 128), Bw["h5t"][:], xb_v(c, m, -128, 128), ALU.add),
            waits=[c.sV], inc=c.sV)
    vW = c.sV.n
    for m in range(KD):
        for tc in range(4):
            p.step("vector", lambda e, m=m, tc=tc: e.tensor_copy(
                xb_v(c, m, tc * 512, 512), xm_v(c, m, tc * 512, 512)),
                waits=[(c.sV, vW)], inc=c.sV)
    p.barrier()
    if "mlp" not in FFN_PARTS:
        ar.reset(mL)
        return
    ar.reset(mL)
    emit_ffn_mlp(c, l)


def emit_bcast_scaled(c, row_f32, dst_bf16, swap, scale):
    p = c.p
    ar = c.ar
    m0 = ar.mark()
    rs = ar.f32(64)[0:1, :]
    rh = ar.bf16(64)[0:1, :]
    rl = ar.bf16(64)[0:1, :]
    rx = ar.f32(64)[0:1, :]
    ps = c.PS[7][:, 0:64]
    p.step("vector", lambda e: e.tensor_scalar_mul(rs, row_f32, scale),
           waits=[(c.sV, None), c.sD], inc=c.sV)
    p.step("vector", lambda e: e.tensor_copy(rh, rs), waits=[c.sV], inc=c.sV)
    p.step("vector", lambda e: e.tensor_tensor(rx, rs, rh, ALU.subtract),
           waits=[c.sV], inc=c.sV)
    p.step("vector", lambda e: e.tensor_copy(rl, rx), waits=[c.sV], inc=c.sV)
    sh = (lambda r: row_swap_view(r)) if swap else (lambda r: r)
    p.step("tensor", lambda e: (
        e.matmul(ps, c.onr[0:1, :], sh(rh), start=True, stop=False),
        e.matmul(ps, c.onr[0:1, :], sh(rl), start=False, stop=True))[-1],
        waits=[c.sV, (c.sA, None)], inc=c.sP)
    p.step("scalar", lambda e: e.activation(dst_bf16, ps, AF.Identity),
           waits=[c.sP], inc=c.sA)
    c.p.flush()
    ar.reset(m0)


def emit_ffn_unit(c, l, q_idx, ui, A, Bw, pend):
    p = c.p
    io = c.io
    tq0 = -128 if q_idx == 16 else q_idx * 128
    special = {16: 0, 0: 1, 1: 2, 2: 3, 3: 4}.get(q_idx)
    trb = c.PS[6].bitcast(BF16)
    qT = Bw["qT"][ui % 2]

    psq = c.PS[7]
    qw = [(c.sW, None)] + c.bg.pop("ps7", []) + c.bg.pop("bx1", [])
    p.step("tensor", lambda e: ffn_qkv_mms(
        c, e, psq[:, 0:512], lambda kk: Bw["wq"][:, kk * 512:(kk + 1) * 512],
        tq0),
        waits=qw, inc=c.sP)
    qmm_mark = c.sP.n
    p.step("scalar", lambda e: e.activation(Bw["sq"][:], psq[:, 0:512], AF.Square),
           waits=[(c.sP, qmm_mark)] + c.bg.pop("Bqproc", []), inc=c.sA)
    p.step("vector", lambda e: e.reduce_sum(
        Bw["ss"][:, 0:8], Bw["sq"][:].rearrange("p (h d) -> p h d", h=8), axis=AX.X),
        waits=[c.sA, (c.sV, None)], inc=c.sV)
    p.step("scalar", lambda e: e.activation(Bw["sr"][:, 0:8], Bw["ss"][:, 0:8], AF.Sqrt),
           waits=[c.sV], inc=c.sA)
    p.step("vector", lambda e: e.tensor_scalar_max(Bw["sr"][:, 0:8], Bw["sr"][:, 0:8], 1e-12),
           waits=[c.sA], inc=c.sV)
    p.step("vector", lambda e: e.reciprocal(Bw["rstd"][:, 0:8], Bw["sr"][:, 0:8]),
           waits=[c.sV], inc=c.sV)
    for h in range(8):
        p.step("vector", lambda e, h=h: e.tensor_scalar_mul(
            Bw["qn"][:, h * 64:(h + 1) * 64], psq[:, h * 64:(h + 1) * 64],
            Bw["rstd"][:, h:h + 1]),
            waits=[c.sV], inc=c.sV)
    c.bg["ps7"] = [(c.sV, c.sV.n), (c.sA, c.sA.n)]
    r0 = 128 + tq0
    p.step("sync", lambda e: e.dma_start(out=Bw["cst"][:], in_=io["csq"][r0:r0 + 128, :]),
           waits=[(c.sV, None)], inc=c.sD, k=16)
    p.step("sync", lambda e: e.dma_start(out=Bw["snt"][:], in_=io["snq"][r0:r0 + 128, :]),
           inc=c.sD, k=16)
    p.step("vector", lambda e: e.tensor_tensor(Bw["csE"][:], Bw["cst"][:], A["qscb"][:], ALU.mult),
           waits=[c.sD, c.sV], inc=c.sV)
    p.step("vector", lambda e: e.tensor_tensor(Bw["snE"][:], Bw["snt"][:], A["qscsb"][:], ALU.mult),
           waits=[c.sV], inc=c.sV)
    p.step("vector", lambda e: e.tensor_tensor(
        head_view(Bw["t1"][:]), head_view(Bw["qn"][:]), rep_view(Bw["csE"][:]), ALU.mult),
        waits=[c.sV], inc=c.sV)
    p.step("vector", lambda e: e.tensor_tensor(
        head_view(Bw["t2"][:]), rot_view(Bw["qn"][:]), rep_view(Bw["snE"][:]), ALU.mult),
        waits=[c.sV], inc=c.sV)
    p.step("vector", lambda e: e.tensor_tensor(Bw["qp"][:], Bw["t1"][:], Bw["t2"][:], ALU.add),
           waits=[c.sV], inc=c.sV)
    qp_mark = c.sV.n

    if pend:
        pend[0]()

    p.step("tensor", lambda e: (
        [e.transpose(trb[:, f * 128:(f + 1) * 128], Bw["qp"][:, f * 128:(f + 1) * 128],
                     c.idt[:]) for f in range(4)])[-1],
        waits=[(c.sV, qp_mark)] + c.bg.pop("trb", []) + c.bg.pop(f"qT{ui % 2}", []),
        inc=c.sP)
    c.badd("Bqproc", c.sP)
    p.step("vector", lambda e: e.tensor_copy(qT[:], trb[:, 0:512]),
           waits=[c.sP, (c.sV, None)], inc=c.sV)
    c.badd("trb", c.sV)
    qT_mark = c.sV.n

    if pend:
        pend[1]()
        pend[2]()
        pend[3]()

    if ATTN_CUT == "q":
        return [lambda: None] * 4

    if special is not None:
        msl = special % 2
        p.step("sync", lambda e, s=special, msl=msl: e.dma_start(
            out=Bw["mask"][msl][:], in_=io["fspec01"][s, :, :]),
            waits=c.bg.pop(f"gmask{msl}", []), inc=c.sD, k=16)
        mask = Bw["mask"][msl]
        mask_dma_mark = c.sD.n
    else:
        msl = None
        mask = c.fband
        mask_dma_mark = 0

    kcol0 = 640 + tq0 - 512
    vt0 = 0 if q_idx == 16 else q_idx + 1
    newpend = []
    for hp in range(4):
        st = [c.PS[0], c.PS[1], c.PS[2]] if hp % 2 == 0 else [c.PS[3], c.PS[4], c.PS[5]]
        s = hp % 2
        bslot = c.PS[6] if s == 0 else c.PS[7]
        f = hp
        qa = qT[0:64, f * 128:(f + 1) * 128]
        qb = qT[64:128, f * 128:(f + 1) * 128]
        p.step("tensor", lambda e, f=f, st=st, qa=qa, qb=qb, bslot=bslot: (
            e.matmul(st[0][:, 0:512], qa,
                     c.kT_col(A, f, kcol0, 512)[0:64, :], start=True, stop=True,
                     tile_position=(0, 0)),
            e.matmul(st[1][:, 0:512], qb,
                     c.kT_col(A, f, kcol0, 512)[64:128, :], start=True, stop=True,
                     tile_position=(64, 0)),
            e.matmul(st[2][:, 0:128], qa,
                     c.kT_col(A, f, kcol0 + 512, 128)[0:64, :], start=True,
                     stop=True, tile_position=(0, 0)),
            e.matmul(bslot[:, 384:512], qb,
                     c.kT_col(A, f, kcol0 + 512, 128)[64:128, :], start=True,
                     stop=True, tile_position=(64, 0)))[-1],
            waits=[(c.sV, qT_mark)] + c.bg.pop(f"set{s}", [])
            + c.bg.pop(f"bx{s}", []), inc=c.sP)
        sc_mark = c.sP.n
        p.step("vector", lambda e, st=st, hp=hp: e.reduce_max(
            Bw["rr"][:, 2 * hp:2 * hp + 1], st[0][:, 0:512], axis=AX.X, negate=True),
            waits=[(c.sP, sc_mark), (c.sV, None)] + c.bg.pop(f"grr{hp}", []),
            inc=c.sV)
        p.step("vector", lambda e, st=st, hp=hp: e.reduce_max(
            Bw["rr"][:, 2 * hp + 1:2 * hp + 2], st[1][:, 0:512], axis=AX.X, negate=True),
            waits=[c.sV], inc=c.sV)
        p.step("vector", lambda e, st=st: e.reduce_max(
            Bw["sums"][:, 0:1], st[2][:, 0:128], axis=AX.X, negate=True),
            waits=[c.sV], inc=c.sV)
        p.step("vector", lambda e, bslot=bslot: e.reduce_max(
            Bw["sums"][:, 1:2], bslot[:, 384:512], axis=AX.X, negate=True),
            waits=[c.sV], inc=c.sV)
        p.step("vector", lambda e, hp=hp: e.tensor_tensor(
            Bw["rr"][:, 2 * hp:2 * hp + 2], Bw["rr"][:, 2 * hp:2 * hp + 2],
            Bw["sums"][:, 0:2], ALU.min),
            waits=[c.sV], inc=c.sV)
        min_mark = c.sV.n
        p.step("scalar", lambda e, st=st, hp=hp: (
            e.activation(Bw["probs"][0][:, 0:512], st[0][:, 0:512], AF.Exp,
                         bias=Bw["rr"][:, 2 * hp:2 * hp + 1]),
            e.activation(Bw["probs"][0][:, 512:640], st[2][:, 0:128], AF.Exp,
                         bias=Bw["rr"][:, 2 * hp:2 * hp + 1]))[-1],
            waits=[(c.sV, min_mark)] + c.bg.pop("gpb0", []), inc=c.sA)
        expa_mark = c.sA.n
        p.step("scalar", lambda e, st=st, hp=hp, bslot=bslot: (
            e.activation(Bw["probs"][1][:, 0:512], st[1][:, 0:512], AF.Exp,
                         bias=Bw["rr"][:, 2 * hp + 1:2 * hp + 2]),
            e.activation(Bw["probs"][1][:, 512:640], bslot[:, 384:512], AF.Exp,
                         bias=Bw["rr"][:, 2 * hp + 1:2 * hp + 2]))[-1],
            waits=[(c.sV, min_mark)] + c.bg.pop("gpb1", []), inc=c.sA)
        expb_mark = c.sA.n
        c.bg[f"set{s}"] = [(c.sV, min_mark), (c.sA, expb_mark)]
        c.bg[f"bx{s}"] = [(c.sV, min_mark), (c.sA, expb_mark)]
        c.bg[f"grr{hp}"] = [(c.sA, expb_mark)]
        ha, hb = 2 * hp, 2 * hp + 1
        chain_mark = {}
        for j, (h, em) in enumerate(((ha, expa_mark), (hb, expb_mark))):
            pm = Bw["probsm"][h]
            mw = [(c.sA, em)] + c.bg.pop(f"pm{h}", [])
            if special is not None:
                mw.append((c.sD, mask_dma_mark))
            p.step("gpsimd", lambda e, pm=pm, j=j, mask=mask:
                   e.tensor_tensor(pm[:], Bw["probs"][j][:], mask[:], ALU.mult),
                   waits=mw, inc=c.sG)
            c.bg[f"gpb{j}"] = [(c.sG, c.sG.n)]
            mask_mark = c.sG.n
            p.step("vector", lambda e, pm=pm, j=j: e.reduce_sum(
                Bw["sums"][:, 2 + 2 * j:3 + 2 * j], pm[:], axis=AX.X),
                waits=[(c.sG, mask_mark), (c.sV, None)], inc=c.sV)
            p.step("vector", lambda e, j=j: e.tensor_scalar_max(
                Bw["sums"][:, 2 + 2 * j:3 + 2 * j],
                Bw["sums"][:, 2 + 2 * j:3 + 2 * j], 1e-30),
                waits=[c.sV], inc=c.sV)
            p.step("vector", lambda e, j=j: e.reciprocal(
                Bw["sums"][:, 3 + 2 * j:4 + 2 * j],
                Bw["sums"][:, 2 + 2 * j:3 + 2 * j]),
                waits=[c.sV], inc=c.sV)
            p.step("vector", lambda e, pm=pm, j=j: e.tensor_scalar_mul(
                pm[:], pm[:], Bw["sums"][:, 3 + 2 * j:4 + 2 * j]),
                waits=[c.sV], inc=c.sV)
            chain_mark[h] = c.sV.n

        def BP(hp=hp, s=s, f=f, Es=st[2], cma=chain_mark[ha], cmb=chain_mark[hb],
               ha=ha, hb=hb, vt0=vt0, tq0=tq0):
            attc = f * 2176 + 128 + tq0
            pT0 = Bw["probsT"][0]
            pT1 = Bw["probsT"][1]
            p.step("tensor", lambda e, h=ha: (
                [e.transpose(trb[:, i * 128:(i + 1) * 128],
                             Bw["probsm"][h][:, i * 128:(i + 1) * 128], c.idt[:])
                 for i in range(5)])[-1],
                waits=[(c.sV, cma)] + c.bg.pop("trb", []), inc=c.sP)
            c.bg[f"pm{ha}"] = [(c.sP, c.sP.n)]
            p.step("scalar", lambda e, pT0=pT0: e.activation(
                pT0[:], trb[:, 0:640], AF.Identity),
                waits=[c.sP] + c.bg.pop("pT0", []), inc=c.sA)
            evaca_mark = c.sA.n
            c.badd("trb", c.sA)
            p.step("tensor", lambda e, h=hb: (
                [e.transpose(trb[:, i * 128:(i + 1) * 128],
                             Bw["probsm"][h][:, i * 128:(i + 1) * 128], c.idt[:])
                 for i in range(5)])[-1],
                waits=[(c.sV, cmb)] + c.bg.pop("trb", []), inc=c.sP)
            c.bg[f"pm{hb}"] = [(c.sP, c.sP.n)]
            trpb_mark = c.sP.n
            p.step("tensor", lambda e, Es=Es, pT0=pT0, ha=ha, vt0=vt0: (
                [e.matmul(Es[0:64, 128:256],
                          A["v"][:, (vt0 + ct) * 512 + ha * 64:
                                 (vt0 + ct) * 512 + ha * 64 + 64],
                          pT0[:, ct * 128:(ct + 1) * 128],
                          start=(ct == 0), stop=(ct == 4)) for ct in range(5)])[-1],
                waits=[(c.sA, evaca_mark)] + c.bg.pop(f"avE{s}", []), inc=c.sP)
            c.bg["pT0"] = [(c.sP, c.sP.n)]
            p.step("scalar", lambda e, pT1=pT1: e.activation(
                pT1[:], trb[:, 0:640], AF.Identity),
                waits=[(c.sP, trpb_mark)] + c.bg.pop("pT1", []), inc=c.sA)
            evacb_mark = c.sA.n
            c.badd("trb", c.sA)
            p.step("tensor", lambda e, Es=Es, pT1=pT1, hb=hb, vt0=vt0: (
                [e.matmul(Es[0:64, 256:384],
                          A["v"][:, (vt0 + ct) * 512 + hb * 64:
                                 (vt0 + ct) * 512 + hb * 64 + 64],
                          pT1[:, ct * 128:(ct + 1) * 128],
                          start=(ct == 0), stop=(ct == 4)) for ct in range(5)])[-1],
                waits=[(c.sA, evacb_mark)], inc=c.sP)
            c.bg["pT1"] = [(c.sP, c.sP.n)]
            avb_mark = c.sP.n
            p.step("vector", lambda e, Es=Es, attc=attc: e.tensor_copy(
                A["att"][0:64, attc:attc + 128], Es[0:64, 128:256]),
                waits=[(c.sP, avb_mark), (c.sV, None)], inc=c.sV)
            p.step("vector", lambda e, Es=Es, attc=attc: e.tensor_copy(
                A["att"][64:128, attc:attc + 128], Es[0:64, 256:384]),
                waits=[c.sV], inc=c.sV)
            c.bg[f"avE{s}"] = [(c.sV, c.sV.n)]
        newpend.append(BP)

    if special is not None:
        c.bg[f"gmask{msl}"] = [(c.sG, c.sG.n)]
    return newpend


def emit_ffn_mlp(c, l):
    p = c.p
    io = c.io
    ar = c.ar
    m0 = ar.mark()
    h2 = ar.bf16(8 * 2048)
    h3 = ar.bf16(22 * 1024)
    wpair = [ar.bf16(2 * 8 * 128) for _ in range(2)]
    w1a = [wp[:, 0:1024] for wp in wpair]
    w1g = [wp[:, 1024:2048] for wp in wpair]
    ag = ar.bf16(512)
    gg = ar.bf16(512)
    sq = ag
    b1c = ar.f32(44)
    ldg = ar.f32(8)
    ldb = ar.f32(8)
    b2c = ar.f32(8)
    rowf = ar.f32(5 * 512)
    R = {nm: rowf[0:1, i * 512:(i + 1) * 512]
         for i, nm in enumerate(["m", "ms", "t", "r1", "r2"])}
    R["x"] = R["t"]
    rowb = ar.bf16(4 * 512)
    RB = {nm: rowb[0:1, i * 512:(i + 1) * 512]
          for i, nm in enumerate(["r1h", "r1l", "r2h", "r2l"])}
    b1s = ar.bf16(512)
    b2s = ar.bf16(512)
    tmp = gg

    for i, n in enumerate(HID_CH):
        o = sum(HID_CH[:i])
        p.step("sync", lambda e, i=i, o=o, n=n: e.dma_start(
            out=b1c[0:n, 2 * i:2 * i + 1],
            in_=io["fb1"][l, o:o + n].unsqueeze(1)),
            inc=c.sD, k=16)
        p.step("sync", lambda e, i=i, o=o, n=n: e.dma_start(
            out=b1c[0:n, 2 * i + 1:2 * i + 2],
            in_=io["fb1"][l, F_HID + o:F_HID + o + n].unsqueeze(1)),
            inc=c.sD, k=16)
    for kk in range(KD):
        p.step("sync", lambda e, kk=kk: e.dma_start(
            out=ldg[:, kk:kk + 1],
            in_=io["fln_g"][l, kk * 128:(kk + 1) * 128].unsqueeze(1)),
            inc=c.sD, k=16)
        p.step("sync", lambda e, kk=kk: e.dma_start(
            out=ldb[:, kk:kk + 1],
            in_=io["fln_b"][l, kk * 128:(kk + 1) * 128].unsqueeze(1)),
            inc=c.sD, k=16)
        p.step("sync", lambda e, kk=kk: e.dma_start(
            out=b2c[:, kk:kk + 1],
            in_=io["fb2"][l, kk * 128:(kk + 1) * 128].unsqueeze(1)),
            inc=c.sD, k=16)
    ldD = c.sD.n

    ps_sum = c.PS[4][0:1, 0:512]
    ps_ssq = c.PS[5][0:1, 0:512]
    ps_b1 = c.PS[6][:, 0:512]
    ps_b2 = c.PS[7][:, 0:512]

    def xa_shift(kk, t0, n):
        return xb_v(c, kk, t0 if kk < 4 else t0 - 1, n)

    for tc in range(2):
        T0 = tc * 1024
        for sub in range(2):
            t0 = T0 + sub * 512
            for kk in range(KD):
                p.step("scalar", lambda e, kk=kk, t0=t0: e.activation(
                    sq[:], xa_shift(kk, t0, 512), AF.Square),
                    waits=[(c.sP, None), (c.sV, None)], inc=c.sA)
                p.step("tensor", lambda e, kk=kk, t0=t0: e.matmul(
                    ps_ssq, c.onc[:, 0:1], sq[:],
                    start=(kk == 0), stop=(kk == KD - 1)),
                    waits=[c.sA], inc=c.sP)
                p.step("tensor", lambda e, kk=kk, t0=t0: e.matmul(
                    ps_sum, c.onc[:, 0:1], xa_shift(kk, t0, 512),
                    start=(kk == 0), stop=(kk == KD - 1)),
                    waits=[(c.sV, None)], inc=c.sP)
            inv = 1.0 / D
            p.step("vector", lambda e: e.tensor_scalar_mul(R["m"], ps_sum, inv),
                   waits=[c.sP, (c.sV, None)], inc=c.sV)
            p.step("vector", lambda e: e.tensor_scalar_mul(R["ms"], ps_ssq, inv),
                   waits=[c.sV], inc=c.sV)
            p.step("vector", lambda e: e.tensor_tensor(R["t"], R["m"], R["m"], ALU.mult),
                   waits=[c.sV], inc=c.sV)
            p.step("vector", lambda e: e.tensor_tensor(R["ms"], R["ms"], R["t"], ALU.subtract),
                   waits=[c.sV], inc=c.sV)
            p.step("vector", lambda e: e.tensor_scalar_add(R["ms"], R["ms"], 1e-5),
                   waits=[c.sV], inc=c.sV)
            p.step("scalar", lambda e: e.activation(R["t"], R["ms"], AF.Sqrt),
                   waits=[c.sV], inc=c.sA)
            p.step("vector", lambda e: e.reciprocal(R["r1"], R["t"]),
                   waits=[c.sA], inc=c.sV)
            p.step("vector", lambda e: e.tensor_tensor(R["r2"], R["m"], R["r1"], ALU.mult),
                   waits=[c.sV], inc=c.sV)
            p.step("vector", lambda e: e.tensor_scalar_mul(R["r2"], R["r2"], -1.0),
                   waits=[c.sV], inc=c.sV)
            for r, rh, rl in [("r1", "r1h", "r1l"), ("r2", "r2h", "r2l")]:
                p.step("vector", lambda e, r=r, rh=rh: e.tensor_copy(RB[rh], R[r]),
                       waits=[c.sV], inc=c.sV)
                p.step("vector", lambda e, r=r, rh=rh: e.tensor_tensor(
                    R["x"], R[r], RB[rh], ALU.subtract),
                    waits=[c.sV], inc=c.sV)
                p.step("vector", lambda e, rl=rl: e.tensor_copy(RB[rl], R["x"]),
                       waits=[c.sV], inc=c.sV)
            p.step("tensor", lambda e: (
                e.matmul(ps_b1, c.onr[0:1, :], RB["r1h"], start=True, stop=False),
                e.matmul(ps_b1, c.onr[0:1, :], RB["r1l"], start=False, stop=True))[-1],
                waits=[c.sV, (c.sA, None)], inc=c.sP)
            p.step("tensor", lambda e: (
                e.matmul(ps_b2, c.onr[0:1, :], RB["r2h"], start=True, stop=False),
                e.matmul(ps_b2, c.onr[0:1, :], RB["r2l"], start=False, stop=True))[-1],
                waits=[c.sV, (c.sA, None)], inc=c.sP)
            p.step("scalar", lambda e: e.activation(b1s, ps_b1, AF.Identity),
                   waits=[c.sP], inc=c.sA)
            p.step("scalar", lambda e: e.activation(b2s, ps_b2, AF.Identity),
                   waits=[c.sP], inc=c.sA)
            for kk in range(KD):
                p.step("vector", lambda e, kk=kk, t0=t0: e.tensor_tensor(
                    tmp[:], xa_shift(kk, t0, 512), b1s[:], ALU.mult),
                    waits=[c.sA, (c.sV, None)], inc=c.sV)
                p.step("vector", lambda e, kk=kk: e.tensor_tensor(
                    tmp[:], tmp[:], b2s[:], ALU.add),
                    waits=[c.sV], inc=c.sV)
                p.step("vector", lambda e, kk=kk, t0=t0: e.tensor_scalar(
                    h2[:, kk * 2048 + t0: kk * 2048 + t0 + 512],
                    tmp[:], ldg[:, kk:kk + 1], ldb[:, kk:kk + 1],
                    ALU.mult, ALU.add),
                    waits=[c.sV, (c.sD, ldD)], inc=c.sV)
    c.h2V = c.sV.n
    for tc in range(2):
        T0 = tc * 1024
        for i, n in enumerate(HID_CH):
            o = sum(HID_CH[:i])
            wa, wg = w1a[i % 2], w1g[i % 2]
            for kk in range(KD):
                p.step("gpsimd", lambda e, o=o, n=n, kk=kk, wa=wa: e.dma_start(
                    out=wa[0:128, kk * 128:kk * 128 + n],
                    in_=io["fw1"][l, kk * 128:(kk + 1) * 128, o:o + n]),
                    waits=c.bg.pop(f"wp{i % 2}", []) if kk == 0 else [],
                    inc=c.sW, k=16)
            for kk in range(KD):
                p.step("gpsimd", lambda e, o=o, n=n, kk=kk, wg=wg: e.dma_start(
                    out=wg[0:128, kk * 128:kk * 128 + n],
                    in_=io["fw1"][l, kk * 128:(kk + 1) * 128,
                                  F_HID + o:F_HID + o + n]),
                    inc=c.sW, k=16)
            for sub in range(2):
                t0s = sub * 512
                psa = c.PS[0 + sub]
                psg = c.PS[2 + sub]
                p.step("tensor", lambda e, i=i, n=n, wa=wa, t0s=t0s, psa=psa,
                       T0=T0: (
                    [e.matmul(psa[0:n, 0:512], wa[0:128, kk * 128:kk * 128 + n],
                              h2[:, kk * 2048 + T0 + t0s:
                                 kk * 2048 + T0 + t0s + 512],
                              start=(kk == 0), stop=(kk == KD - 1))
                     for kk in range(KD)])[-1],
                    waits=[(c.sW, None), (c.sV, c.h2V)] +
                    c.bg.pop(f"psa{sub}", []) +
                    (c.bgate("w2banks") if i == 0 else []), inc=c.sP)
                p.step("tensor", lambda e, i=i, n=n, wg=wg, t0s=t0s, psg=psg,
                       T0=T0: (
                    [e.matmul(psg[0:n, 0:512], wg[0:128, kk * 128:kk * 128 + n],
                              h2[:, kk * 2048 + T0 + t0s:
                                 kk * 2048 + T0 + t0s + 512],
                              start=(kk == 0), stop=(kk == KD - 1))
                     for kk in range(KD)])[-1],
                    waits=c.bg.pop(f"psg{sub}", []) +
                    (c.bgate("w2banks") if i == 0 else []), inc=c.sP)
                if sub == 1:
                    c.badd(f"wp{i % 2}", c.sP)
                p.step("scalar", lambda e, i=i, n=n, psa=psa: e.activation(
                    ag[0:n, :], psa[0:n, 0:512], AF.Identity,
                    bias=b1c[0:n, 2 * i:2 * i + 1]),
                    waits=[c.sP] + c.bg.pop("agbuf", []), inc=c.sA)
                c.bset(f"psa{sub}", c.sA)
                p.step("scalar", lambda e, i=i, n=n, psg=psg: e.activation(
                    gg[0:n, :], psg[0:n, 0:512], AF.Gelu,
                    bias=b1c[0:n, 2 * i + 1:2 * i + 2]),
                    waits=c.bg.pop("ggbuf", []), inc=c.sA)
                c.bset(f"psg{sub}", c.sA)
                p.step("vector", lambda e, i=i, n=n, t0s=t0s: e.tensor_tensor(
                    h3[0:n, i * 1024 + t0s: i * 1024 + t0s + 512],
                    ag[0:n, :], gg[0:n, :], ALU.mult),
                    waits=[c.sA], inc=c.sV)
                c.bset("agbuf", c.sV)
                c.bset("ggbuf", c.sV)
        emit_ffn_w2(c, l, T0, h3, b2c, wpair)
    c.p.flush()
    ar.reset(m0)
    c.p.barrier()


def emit_ffn_w2(c, l, T0, h3, b2c, wpair):
    p = c.p
    io = c.io
    ar = c.ar
    m0 = ar.mark()
    w2c = wpair
    for sub in range(2):
        t0 = T0 + sub * 512
        ps = c.PS
        for i, n in enumerate(HID_CH):
            o = sum(HID_CH[:i])
            wc = w2c[i % 2]
            p.step("gpsimd", lambda e, i=i, o=o, n=n, wc=wc: e.dma_start(
                out=wc[0:n, 0:1024],
                in_=io["fw2"][l, o:o + n, :]),
                waits=c.bg.pop(f"wp{i % 2}", []), inc=c.sW, k=16)
            p.step("tensor", lambda e, i=i, n=n, o=o, wc=wc, t0=t0: (
                [e.matmul(ps[m][:, 0:512], wc[0:n, m * 128:(m + 1) * 128],
                          h3[0:n, i * 1024 + (t0 - T0): i * 1024 + (t0 - T0) + 512],
                          start=(i == 0), stop=(i == len(HID_CH) - 1))
                 for m in range(KD)])[-1],
                waits=[(c.sW, None)] +
                (c.bg.pop("w2banks", []) +
                 c.bg.pop("psa0", []) + c.bg.pop("psa1", []) +
                 c.bg.pop("psg0", []) + c.bg.pop("psg1", [])
                 if i == 0 else []) +
                ([(c.sV, None)] if i == 0 else []), inc=c.sP)
            c.badd(f"wp{i % 2}", c.sP)
        for m in range(KD):
            p.step("vector", lambda e, m=m, t0=t0: e.scalar_tensor_tensor(
                xm_v(c, m, t0, 512), ps[m][:, 0:512], b2c[:, m:m + 1],
                xm_v(c, m, t0, 512), ALU.add, ALU.add),
                waits=[c.sP, (c.sV, None)], inc=c.sV)
        c.bset("w2banks", c.sV)
        vW = c.sV.n
        for m in range(KD):
            p.step("vector", lambda e, m=m, t0=t0: e.tensor_copy(
                xb_v(c, m, t0, 512), xm_v(c, m, t0, 512)),
                waits=[(c.sV, vW)], inc=c.sV)
    ar.reset(m0)


def emit_done(c):
    c.p.step("sync", lambda e: e.engine_nop() if False else e.nop(),
             waits=[c.sO, c.sV, c.sP])
    c.p.flush()


def program(c):
    emit_prologue(c)
    emit_boundary(c, "init")
    if SKIP_FINAL:
        for st in STAGES:
            if st == "ffn0":
                emit_ffn_layer(c, 0)
        emit_done(c)
        return
    for st in STAGES:
        if st == "ffn0":
            emit_ffn_layer(c, 0)
        elif st == "ffn1":
            emit_ffn_layer(c, 1)
        elif st == "b_ffn":
            emit_boundary(c, "mid_ffn")
        elif st == "mla":
            emit_mla_block(c)
        elif st == "b_mla":
            emit_boundary(c, "mid_mla")
        else:
            raise ValueError(st)
    emit_boundary(c, "final")
    emit_done(c)


def mla_qkv_mms(c, e, ps, wsl, t0):
    ins = None
    for kk in range(KD):
        ins = e.matmul(ps, xb_v(c, kk, t0, 128), wsl(kk),
                       start=(kk == 0), stop=False)
    return ins


def emit_mla_block(c):
    p = c.p
    io = c.io
    ar = c.ar
    mL = ar.mark()

    M = {}
    rows = ar.f32(64)
    rb = ar.bf16(3072 + 256 + 512 + 64)
    M["qkvb"] = rb[0:1, 0:3072]
    M["rqb"] = rb[0:1, 3072:3328]
    M["rkvb"] = rb[0:1, 3328:3840]
    M["pooled"] = ar.bf16(8 * 64)
    M["rkv"] = ar.bf16(512)
    M["rkT"] = ar.bf16(2 * 64)
    M["glb"] = ar.bf16(64)
    M["grb"] = ar.bf16(64)
    M["rowk"] = ar.f32(2)
    M["lmrow"] = ar.bf16(64)

    c.wrq = ar.bf16(8 * 256)
    c.wrout = ar.bf16(2 * 1024)
    rb2 = ar.bf16(1024)
    c.routb = rb2[0:1, 0:1024]

    mRows = ar.mark()
    rowscr = ar.f32(3072)
    rfull = rowscr[0:1, 0:3072]
    for nm, src_ap, n in [("qkvb", io["aqkv_b"].unsqueeze(0), 3072),
                          ("rqb", io["arq_b"].unsqueeze(0), 256),
                          ("rkvb", io["arkv_b"].unsqueeze(0), 512)]:
        p.step("sync", lambda e, s=src_ap, n=n: e.dma_start(
            out=rfull[0:1, 0:n], in_=s), waits=[(c.sV, None)], inc=c.sD, k=16)
        p.step("vector", lambda e, nm=nm, n=n: e.tensor_copy(
            M[nm][0:1, 0:n], rfull[0:1, 0:n]),
            waits=[c.sD, (c.sV, None)], inc=c.sV)
    for nm, wn in [("glb", "agl_w"), ("grb", "agr_w")]:
        p.step("sync", lambda e, wn=wn: e.dma_start(
            out=rowscr[0:1, 0:64], in_=io[wn].rearrange("a b -> b a")),
            waits=[(c.sV, None), (c.sP, None)], inc=c.sD, k=16)
        p.step("vector", lambda e: e.tensor_copy(rb[0:1, 3840:3904],
                                                 rowscr[0:1, 0:64]),
               waits=[c.sD, (c.sV, None)], inc=c.sV)
        p.step("tensor", lambda e: e.matmul(c.PS[7][:, 0:64], c.onr[0:1, :],
                                            rb[0:1, 3840:3904],
                                            start=True, stop=True),
               waits=[c.sV, (c.sA, None)], inc=c.sP)
        p.step("scalar", lambda e, nm=nm: e.activation(M[nm][:], c.PS[7][:, 0:64],
                                                       AF.Identity),
               waits=[c.sP], inc=c.sA)

    rs2 = ar.f32(1024)
    for kk in range(KD):
        p.step("gpsimd", lambda e, kk=kk: e.dma_start(
            out=c.wrq[:, kk * 256:(kk + 1) * 256],
            in_=io["arq_w"][kk * 128:(kk + 1) * 128, :]),
            waits=[(c.sP, None)] if kk == 0 else [], inc=c.sW, k=16)
    for ch in range(2):
        p.step("gpsimd", lambda e, ch=ch: e.dma_start(
            out=c.wrout[0:128, ch * 1024:(ch + 1) * 1024],
            in_=io["arout_w"][ch * 128:(ch + 1) * 128, :]),
            inc=c.sW, k=16)
    p.step("sync", lambda e: e.dma_start(
        out=rs2[0:1, 0:1024], in_=io["arout_b"].unsqueeze(0)),
        waits=[(c.sV, None)], inc=c.sD, k=16)
    p.step("vector", lambda e: e.tensor_copy(c.routb[0:1, 0:1024],
                                             rs2[0:1, 0:1024]),
           waits=[c.sD, (c.sV, None)], inc=c.sV)

    c.p.flush()
    ar.reset(mRows)

    pl_own = M["pooled"][:, :].bitcast(BF16) if False else M["pooled"]
    pooled_f = ar.f32(8 * 32)
    for kk in range(KD):
        p.step("vector", lambda e, kk=kk: e.reduce_sum(
            pooled_f[:, kk * 32:(kk + 1) * 32],
            xb_v(c, kk, 0, T).rearrange("p (g c) -> p g c", g=32), axis=AX.X),
            waits=[(c.sV, None)], inc=c.sV)
    p.step("vector", lambda e: e.tensor_scalar_mul(
        pooled_f[:], pooled_f[:], 1.0 / CHUNK),
        waits=[c.sV], inc=c.sV)
    for kk in range(KD):
        p.step("vector", lambda e, kk=kk: e.tensor_copy(
            M["pooled"][:, kk * 64 + 32: kk * 64 + 64],
            pooled_f[:, kk * 32:(kk + 1) * 32]),
            waits=[c.sV], inc=c.sV)

    mS = ar.mark()
    kvlast = ar.bf16(2 * 1024)
    ktl = kvlast[:, 0:1024]
    vtl = kvlast[:, 1024:2048]
    wbuf = [ar.bf16(8 * 512) for _ in range(2)]
    trb = c.PS[6].bitcast(BF16)
    t0 = T - 128
    for ni, n0 in enumerate([2, 3, 4, 5]):
        wb = wbuf[ni % 2]
        for kk in range(KD):
            p.step("gpsimd", lambda e, kk=kk, n0=n0, wb=wb: e.dma_start(
                out=wb[:, kk * 512:(kk + 1) * 512],
                in_=io["aqkv_w"][kk * 128:(kk + 1) * 128,
                                 n0 * 512:(n0 + 1) * 512]),
                waits=[(c.sP, None)] if kk == 0 else [], inc=c.sW, k=16)
        ps = c.PS[ni % 2]
        p.step("tensor", lambda e, wb=wb, ni=ni, n0=n0, ps=ps: (
            mla_qkv_mms(c, e, ps[:, 0:512],
                        lambda kk: wb[:, kk * 512:(kk + 1) * 512], t0),
            e.matmul(ps[:, 0:512], c.onr[0:1, :],
                     M["qkvb"][0:1, n0 * 512:(n0 + 1) * 512],
                     start=False, stop=True))[-1],
            waits=[c.sW, (c.sV, None), (c.sA, None)], inc=c.sP)
        if n0 in (2, 3):
            p.step("scalar", lambda e, ps=ps: e.activation(
                vtl[:, 0:512], ps[:, 0:512], AF.Identity),
                waits=[c.sP], inc=c.sA)
            p.step("tensor", lambda e, n0=n0: (
                [e.transpose(trb[:, f * 128:(f + 1) * 128],
                             vtl[:, f * 128:(f + 1) * 128], c.idt[:])
                 for f in range(4)])[-1],
                waits=[c.sA, (c.sV, None)], inc=c.sP)
            p.step("vector", lambda e, n0=n0: e.tensor_copy(
                ktl[:, (n0 - 2) * 512:(n0 - 1) * 512], trb[:, 0:512]),
                waits=[c.sP, (c.sV, None)], inc=c.sV)
        else:
            p.step("scalar", lambda e, ps=ps, n0=n0: e.activation(
                vtl[:, (n0 - 4) * 512:(n0 - 3) * 512], ps[:, 0:512], AF.Identity),
                waits=[c.sP, (c.sV, None)], inc=c.sA)
    offs = 0
    for src_ap, nel in [(ktl, 128 * 1024), (vtl, 128 * 1024)]:
        dst = c.cca_in[0, offs:offs + nel].rearrange("(p f) -> p f", p=128)
        p.step("sync", lambda e, d=dst, s=src_ap: e.dma_start(out=d, in_=s),
               waits=[(c.sV, None), (c.sA, None)], inc=c.sD, k=16)
        offs += nel
    pl_pack = ar.bf16(8 * 32)
    for kk in range(KD):
        p.step("vector", lambda e, kk=kk: e.tensor_copy(
            pl_pack[:, kk * 32:(kk + 1) * 32],
            M["pooled"][:, kk * 64 + 32:kk * 64 + 64]),
            waits=[(c.sV, None)], inc=c.sV)
    dst = c.cca_in[0, offs:offs + 128 * 256].rearrange("(p f) -> p f", p=128)
    p.step("sync", lambda e, d=dst: e.dma_start(out=d, in_=pl_pack[:]),
           waits=[c.sV], inc=c.sD, k=16)
    offs += 128 * 256
    assert offs == AA_TOT
    rg = RG_PAIRS if NGROUP == 2 else [[i] for i in range(NCORE)]
    p.step("gpsimd", lambda e, rg=rg: e.collective_compute(
        "AllGather", ALU.bypass, replica_groups=rg,
        ins=[c.cca_in.opt()], outs=[c.cca_out.opt()]),
        waits=[c.sD, (c.sC, None)], inc=c.sC)
    c.p.flush()
    ar.reset(mS)

    for half in range(2):
        emit_mla_half(c, M, half)

    emit_mla_out(c, M)
    p.barrier()
    ar.reset(mL)


def emit_mla_half(c, M, half):
    p = c.p
    io = c.io
    ar = c.ar
    mH = ar.mark()
    th0 = half * 8
    t00 = th0 * 128
    Kh = {}
    Kh["kT"] = ar.bf16(8 * 1152)
    Kh["v"] = ar.bf16(9 * 1024)
    Kh["q"] = ar.bf16(8 * 1024)
    wbuf = [ar.bf16(8 * 512) for _ in range(2)]
    trb = c.PS[6].bitcast(BF16)
    evs = ar.bf16(512)

    def kT_col(f, col, n):
        return Kh["kT"][:, f * 1152 + col: f * 1152 + col + n]

    kv_tiles = list(range(max(0, th0 - 1), th0 + 8))
    for ni in range(6):
        wb = wbuf[ni % 2]
        for kk in range(KD):
            p.step("gpsimd", lambda e, kk=kk, ni=ni, wb=wb: e.dma_start(
                out=wb[:, kk * 512:(kk + 1) * 512],
                in_=io["aqkv_w"][kk * 128:(kk + 1) * 128,
                                 ni * 512:(ni + 1) * 512]),
                waits=[(c.sP, None)] if kk == 0 else [], inc=c.sW, k=16)
        tiles = kv_tiles if ni >= 2 else list(range(th0, th0 + 8))
        for tt in tiles:
            ps = c.PS[tt % 2]
            p.step("tensor", lambda e, wb=wb, ni=ni, tt=tt, ps=ps: (
                mla_qkv_mms(c, e, ps[:, 0:512],
                            lambda kk: wb[:, kk * 512:(kk + 1) * 512], tt * 128),
                e.matmul(ps[:, 0:512], c.onr[0:1, :],
                         M["qkvb"][0:1, ni * 512:(ni + 1) * 512],
                         start=False, stop=True))[-1],
                waits=[c.sW, (c.sV, None), (c.sA, None)], inc=c.sP)
            if ni < 2:
                p.step("vector", lambda e, ps=ps, ni=ni, tt=tt: e.tensor_scalar_mul(
                    Kh["q"][:, (tt - th0) * 1024 + ni * 512:
                            (tt - th0) * 1024 + (ni + 1) * 512],
                    ps[:, 0:512], 0.125),
                    waits=[c.sP, (c.sV, None)], inc=c.sV)
            elif ni < 4:
                p.step("scalar", lambda e, ps=ps: e.activation(
                    evs[:], ps[:, 0:512], AF.Identity),
                    waits=[c.sP, (c.sP2, None)], inc=c.sA)
                p.step("tensor", lambda e: (
                    [e.transpose(trb[:, f * 128:(f + 1) * 128],
                                 evs[:, f * 128:(f + 1) * 128], c.idt[:])
                     for f in range(4)])[-1],
                    waits=[c.sA, (c.sV, None)], inc=c.sP2)
                col = 128 * (tt - th0 + 1)
                for f in range(4):
                    fk = (ni - 2) * 4 + f
                    p.step("vector", lambda e, fk=fk, f=f, col=col: e.tensor_copy(
                        kT_col(fk, col, 128), trb[:, f * 128:(f + 1) * 128]),
                        waits=[c.sP2, (c.sV, None)], inc=c.sV)
            else:
                p.step("scalar", lambda e, ps=ps, ni=ni, tt=tt: e.activation(
                    Kh["v"][:, (tt - th0 + 1) * 1024 + (ni - 4) * 512:
                            (tt - th0 + 1) * 1024 + (ni - 3) * 512],
                    ps[:, 0:512], AF.Identity),
                    waits=[c.sP, (c.sV, None)], inc=c.sA)

    if half == 0:
        offs = 0
        for f in range(8):
            src = c.cca_out[0, f * 128 * 128:(f + 1) * 128 * 128].rearrange(
                "(p f2) -> p f2", p=128)
            p.step("sync", lambda e, d=kT_col(f, 0, 128), s=src:
                   e.dma_start(out=d, in_=s),
                   waits=[(c.sC, None)], inc=c.sD2, k=16)
        src = c.cca_out[0, AA_KT:AA_KT + AA_V].rearrange("(p f2) -> p f2", p=128)
        p.step("sync", lambda e, s=src: e.dma_start(out=Kh["v"][:, 0:1024], in_=s),
               waits=[(c.sC, None)], inc=c.sD2, k=16)
        src = c.cca_out[0, AA_KT + AA_V:AA_TOT].rearrange("(p f2) -> p f2", p=128)
        plr = ar.bf16(8 * 32)
        p.step("sync", lambda e, s=src, plr=plr: e.dma_start(out=plr[:], in_=s),
               waits=[(c.sC, None)], inc=c.sD2, k=16)
        for kk in range(KD):
            p.step("vector", lambda e, kk=kk, plr=plr: e.tensor_copy(
                M["pooled"][:, kk * 64:kk * 64 + 32],
                plr[:, kk * 32:(kk + 1) * 32]),
                waits=[(c.sD2, None), (c.sV, None)], inc=c.sV)
        for f in range(8):
            p.step("vector", lambda e, f=f: e.tensor_scalar_mul(
                kT_col(f, 0, 128), kT_col(f, 0, 128), c.hsc[:, 0:1]),
                waits=[(c.sD2, None), (c.sV, None)], inc=c.sV)
        p.step("vector", lambda e: e.tensor_scalar_mul(
            Kh["v"][:, 0:1024], Kh["v"][:, 0:1024], c.hsc[:, 0:1]),
            waits=[(c.sD2, None), (c.sV, None)], inc=c.sV)
        emit_mla_latent_prep(c, M)

    for tt in range(th0, th0 + 8):
        emit_mla_tile(c, M, Kh, half, tt, kT_col)
    c.p.flush()
    ar.reset(mH)


def emit_mla_latent_prep(c, M):
    p = c.p
    io = c.io
    ar = c.ar
    m0 = ar.mark()
    wb = ar.bf16(8 * 512)
    trb = c.PS[6].bitcast(BF16)
    for kk in range(KD):
        p.step("gpsimd", lambda e, kk=kk, wb=wb: e.dma_start(
            out=wb[:, kk * 512:(kk + 1) * 512],
            in_=io["arkv_w"][kk * 128:(kk + 1) * 128, :]),
            waits=[(c.sP, None)] if kk == 0 else [], inc=c.sW, k=16)
    ps = c.PS[7]
    p.step("tensor", lambda e, wb=wb: (
        [e.matmul(ps[0:64, 0:512], M["pooled"][:, kk * 64:(kk + 1) * 64],
                  wb[:, kk * 512:(kk + 1) * 512],
                  start=(kk == 0), stop=False) for kk in range(KD)] +
        [e.matmul(ps[0:64, 0:512], c.onr[0:1, 0:64],
                  M["rkvb"][0:1, 0:512], start=False, stop=True)])[-1],
        waits=[c.sW, (c.sV, None), (c.sA, None)], inc=c.sP)
    p.step("scalar", lambda e: e.activation(M["rkv"][0:64, :], ps[0:64, 0:512],
                                            AF.Identity),
           waits=[c.sP], inc=c.sA)
    p.step("tensor", lambda e: (
        e.transpose(trb[:, 0:64], M["rkv"][0:64, 0:128], c.idt[0:64, 0:64]),
        e.transpose(trb[:, 64:128], M["rkv"][0:64, 128:256], c.idt[0:64, 0:64]))[-1],
        waits=[c.sA, (c.sV, None)], inc=c.sP)
    p.step("vector", lambda e: e.tensor_copy(M["rkT"][:], trb[:, 0:128]),
           waits=[c.sP, (c.sV, None)], inc=c.sV)
    c.p.flush()
    ar.reset(m0)


def emit_mla_tile(c, M, Kh, half, tt, kT_col):
    p = c.p
    io = c.io
    ar = c.ar
    m0 = ar.mark()
    th0 = half * 8
    t0 = tt * 128
    tl = tt - th0
    trb = c.PS[6].bitcast(BF16)

    qTt = ar.bf16(8 * 128)
    probs = ar.bf16(256)
    probsm = ar.bf16(256)
    probsT = ar.bf16(256)
    lpb = ar.bf16(64)
    lpbm = ar.bf16(64)
    lpbT = ar.bf16(128)
    loc = ar.bf16(1024)
    rem = ar.bf16(1024)
    rqs = ar.bf16(256)
    rqT = ar.bf16(256)
    d1 = ar.bf16(1024)
    mstage = loc
    glog = ar.f32(3 * 16)
    gate = ar.f32(16)
    st2 = ar.f32(4)

    p.step("tensor", lambda e: (
        [e.transpose(trb[:, f * 128:(f + 1) * 128],
                     Kh["q"][:, tl * 1024 + f * 128: tl * 1024 + (f + 1) * 128],
                     c.idt[:]) for f in range(8)])[-1],
        waits=[(c.sV, None), (c.sA, None)], inc=c.sP)
    p.step("vector", lambda e: e.tensor_copy(qTt[:], trb[:, 0:1024]),
           waits=[c.sP, (c.sV, None)], inc=c.sV)

    p.step("sync", lambda e: e.dma_start(out=M["rowk"][:, 0:1],
                                         in_=io["arowkeep"][t0:t0 + 128, :]),
           waits=[(c.sV, None)], inc=c.sD, k=16)
    p.step("sync", lambda e: e.dma_start(out=M["rowk"][:, 1:2],
                                         in_=io["lrowkeep"][t0:t0 + 128, :]),
           inc=c.sD, k=16)
    p.step("sync", lambda e: e.dma_start(out=M["lmrow"][:],
                                         in_=io["lmask01"][t0:t0 + 128, :]),
           inc=c.sD, k=16)
    mask = c.at0 if tt == 0 else c.aband

    locps = [c.PS[4], c.PS[5]]
    kcol = 128 * (tl + 1) - 128
    for hp in range(8):
        st = c.PS[hp % 2]
        stb = c.PS[2 + hp % 2]
        qa = qTt[0:64, hp * 128:(hp + 1) * 128]
        qb = qTt[64:128, hp * 128:(hp + 1) * 128]
        p.step("tensor", lambda e, hp=hp, st=st, stb=stb, qa=qa, qb=qb: (
            e.matmul(st[:, 0:256], qa, kT_col(hp, kcol, 256)[0:64, :],
                     start=True, stop=True, tile_position=(0, 0)),
            e.matmul(stb[:, 0:256], qb, kT_col(hp, kcol, 256)[64:128, :],
                     start=True, stop=True, tile_position=(64, 0)))[-1],
            waits=[(c.sV, None), (c.sA, None), (c.sG, None)] +
            ([(c.sD2, None)] if (half == 0 and tt == 0) else []),
            inc=c.sP)
        for hh in range(2):
            h = 2 * hp + hh
            sc = (st if hh == 0 else stb)[:, 0:256]
            p.step("vector", lambda e, sc=sc: e.reduce_max(
                st2[:, 0:1], sc, axis=AX.X, negate=True),
                waits=[c.sP, (c.sV, None)], inc=c.sV)
            p.step("scalar", lambda e, sc=sc: e.activation(
                probs[:], sc, AF.Exp, bias=st2[:, 0:1]),
                waits=[c.sV, (c.sG, None)], inc=c.sA)
            p.step("gpsimd", lambda e, mask=mask: e.tensor_tensor(
                probsm[:], probs[:], mask[:], ALU.mult),
                waits=[c.sA, (c.sG, None), (c.sP, None), c.sD], inc=c.sG)
            p.step("vector", lambda e: e.reduce_sum(st2[:, 1:2], probsm[:],
                                                    axis=AX.X),
                   waits=[c.sG, (c.sV, None)], inc=c.sV)
            p.step("vector", lambda e: e.tensor_scalar_max(
                st2[:, 1:2], st2[:, 1:2], 1e-30), waits=[c.sV], inc=c.sV)
            p.step("vector", lambda e: e.reciprocal(st2[:, 2:3], st2[:, 1:2]),
                   waits=[c.sV], inc=c.sV)
            p.step("vector", lambda e: e.tensor_tensor(
                st2[:, 2:3], st2[:, 2:3], M["rowk"][:, 0:1], ALU.mult),
                waits=[c.sV], inc=c.sV)
            p.step("vector", lambda e: e.tensor_scalar_mul(
                probsm[:], probsm[:], st2[:, 2:3]),
                waits=[c.sV, c.sG], inc=c.sV)
            p.step("tensor", lambda e: (
                e.transpose(trb[:, 0:128], probsm[:, 0:128], c.idt[:]),
                e.transpose(trb[:, 128:256], probsm[:, 128:256], c.idt[:]))[-1],
                waits=[c.sV, (c.sA, None)], inc=c.sP)
            p.step("scalar", lambda e: e.activation(probsT[:], trb[:, 0:256],
                                                    AF.Identity),
                   waits=[c.sP], inc=c.sA)
            lp = locps[h // 8]
            col = (h % 8) * 64
            p.step("tensor", lambda e, h=h, lp=lp, tl=tl, col=col: (
                e.matmul(lp[:, col:col + 64],
                         probsT[:, 0:128],
                         Kh["v"][:, tl * 1024 + h * 64: tl * 1024 + h * 64 + 64],
                         start=True, stop=False),
                e.matmul(lp[:, col:col + 64],
                         probsT[:, 128:256],
                         Kh["v"][:, (tl + 1) * 1024 + h * 64:
                                 (tl + 1) * 1024 + h * 64 + 64],
                         start=False, stop=True))[-1],
                waits=[c.sA, (c.sV, None)], inc=c.sP)
    for b in range(2):
        p.step("vector", lambda e, b=b: e.tensor_copy(
            loc[:, b * 512:(b + 1) * 512], c.PS[4 + b][:, 0:512]),
            waits=[c.sP, (c.sV, None), (c.sO, None)], inc=c.sV)

    psr = c.PS[2]
    p.step("tensor", lambda e: (
        mla_qkv_mms(c, e, psr[:, 0:256],
                    lambda kk: c.wrq[:, kk * 256:(kk + 1) * 256], t0),
        e.matmul(psr[:, 0:256], c.onr[0:1, :], M["rqb"][0:1, 0:256],
                 start=False, stop=True))[-1],
        waits=[(c.sV, None), (c.sA, None)], inc=c.sP)
    p.step("vector", lambda e: e.tensor_scalar_mul(rqs[:], psr[:, 0:256], 0.125),
           waits=[c.sP, (c.sV, None)], inc=c.sV)
    p.step("tensor", lambda e: (
        e.transpose(trb[:, 0:128], rqs[:, 0:128], c.idt[:]),
        e.transpose(trb[:, 128:256], rqs[:, 128:256], c.idt[:]))[-1],
        waits=[c.sV, (c.sA, None)], inc=c.sP)
    p.step("vector", lambda e: e.tensor_copy(rqT[:], trb[:, 0:256]),
           waits=[c.sP, (c.sV, None)], inc=c.sV)
    latps = c.PS[4]
    for lh in range(4):
        qa = rqT[(lh % 2) * 64:(lh % 2) * 64 + 64, (lh // 2) * 128:(lh // 2) * 128 + 128]
        ka = M["rkT"][(lh % 2) * 64:(lh % 2) * 64 + 64, (lh // 2) * 64:(lh // 2) * 64 + 64]
        tp = (64 * (lh % 2), 0)
        lbank = c.PS[2] if lh % 2 == 0 else c.PS[3]
        p.step("tensor", lambda e, qa=qa, ka=ka, tp=tp, lh=lh, lbank=lbank:
               e.matmul(lbank[:, 256 + (lh // 2) * 64: 256 + (lh // 2) * 64 + 64],
                        qa, ka, start=True, stop=True, tile_position=tp),
               waits=[(c.sV, None)], inc=c.sP)
        sc = lbank[:, 256 + (lh // 2) * 64: 256 + (lh // 2) * 64 + 64]
        p.step("vector", lambda e, sc=sc: e.reduce_max(
            st2[:, 0:1], sc, axis=AX.X, negate=True),
            waits=[c.sP, (c.sV, None)], inc=c.sV)
        p.step("scalar", lambda e, sc=sc: e.activation(
            lpb[:], sc, AF.Exp, bias=st2[:, 0:1]),
            waits=[c.sV, (c.sG, None)], inc=c.sA)
        p.step("gpsimd", lambda e: e.tensor_tensor(lpbm[:], lpb[:], M["lmrow"][:],
                                                   ALU.mult),
               waits=[c.sA, (c.sG, None), (c.sP, None), c.sD], inc=c.sG)
        p.step("vector", lambda e: e.reduce_sum(st2[:, 1:2], lpbm[:], axis=AX.X),
               waits=[c.sG, (c.sV, None)], inc=c.sV)
        p.step("vector", lambda e: e.tensor_scalar_max(
            st2[:, 1:2], st2[:, 1:2], 1e-30), waits=[c.sV], inc=c.sV)
        p.step("vector", lambda e: e.reciprocal(st2[:, 2:3], st2[:, 1:2]),
               waits=[c.sV], inc=c.sV)
        p.step("vector", lambda e: e.tensor_tensor(
            st2[:, 2:3], st2[:, 2:3], M["rowk"][:, 1:2], ALU.mult),
            waits=[c.sV], inc=c.sV)
        p.step("vector", lambda e: e.tensor_scalar_mul(
            lpbm[:], lpbm[:], st2[:, 2:3]), waits=[c.sV, c.sG], inc=c.sV)
        p.step("tensor", lambda e: e.transpose(trb[0:64, 0:128], lpbm[:], c.idt[:]),
               waits=[c.sV, (c.sA, None)], inc=c.sP)
        p.step("scalar", lambda e: e.activation(lpbT[0:64, :], trb[0:64, 0:128],
                                                AF.Identity),
               waits=[c.sP], inc=c.sA)
        p.step("tensor", lambda e, lh=lh: e.matmul(
            latps[:, lh * 64:(lh + 1) * 64], lpbT[0:64, :],
            M["rkv"][0:64, 256 + lh * 64:256 + (lh + 1) * 64],
            start=True, stop=True),
            waits=[c.sA, (c.sV, None)], inc=c.sP)
    p.step("vector", lambda e: e.tensor_copy(rqs[:], latps[:, 0:256]),
           waits=[c.sP, (c.sV, None)], inc=c.sV)
    p.step("tensor", lambda e: (
        e.transpose(trb[:, 0:128], rqs[:, 0:128], c.idt[:]),
        e.transpose(trb[:, 128:256], rqs[:, 128:256], c.idt[:]))[-1],
        waits=[c.sV, (c.sA, None)], inc=c.sP)
    p.step("vector", lambda e: e.tensor_copy(rqT[:], trb[:, 0:256]),
           waits=[c.sP, (c.sV, None)], inc=c.sV)
    p.step("tensor", lambda e: (
        [e.matmul((c.PS[2][:, 0:512] if nn == 0 else c.PS[3][:, 0:512]),
                  rqT[:, ch * 128:(ch + 1) * 128],
                  c.wrout[0:128, ch * 1024 + nn * 512: ch * 1024 + (nn + 1) * 512],
                  start=(ch == 0), stop=False)
         for nn in range(2) for ch in range(2)] +
        [e.matmul((c.PS[2][:, 0:512] if nn == 0 else c.PS[3][:, 0:512]),
                  c.onr[0:1, :], c.routb[0:1, nn * 512:(nn + 1) * 512],
                  start=False, stop=True) for nn in range(2)])[-1],
        waits=[(c.sV, None), (c.sA, None)], inc=c.sP)
    p.step("scalar", lambda e: e.activation(rem[:, 0:512], c.PS[2][:, 0:512],
                                            AF.Identity),
           waits=[c.sP], inc=c.sA)
    p.step("scalar", lambda e: e.activation(rem[:, 512:1024], c.PS[3][:, 0:512],
                                            AF.Identity),
           inc=c.sA)

    p.step("vector", lambda e: e.tensor_tensor(
        d1[:].rearrange("p (h d) -> p h d", h=16), loc[:].rearrange("p (h d) -> p h d", h=16),
        bass.AP(M["glb"].tensor, M["glb"].offset,
                [[M["glb"].ap[0][0], 128], [0, 16], [1, 64]]), ALU.mult),
        waits=[c.sA, (c.sV, None)], inc=c.sV)
    p.step("vector", lambda e: e.reduce_sum(
        glog[:, 0:16], d1[:].rearrange("p (h d) -> p h d", h=16), axis=AX.X),
        waits=[c.sV], inc=c.sV)
    p.step("vector", lambda e: e.tensor_tensor(
        d1[:].rearrange("p (h d) -> p h d", h=16), rem[:].rearrange("p (h d) -> p h d", h=16),
        bass.AP(M["grb"].tensor, M["grb"].offset,
                [[M["grb"].ap[0][0], 128], [0, 16], [1, 64]]), ALU.mult),
        waits=[c.sV], inc=c.sV)
    p.step("vector", lambda e: e.reduce_sum(
        glog[:, 16:32], d1[:].rearrange("p (h d) -> p h d", h=16), axis=AX.X),
        waits=[c.sV], inc=c.sV)
    p.step("vector", lambda e: e.tensor_tensor(
        glog[:, 32:48], glog[:, 0:16], glog[:, 16:32], ALU.add),
        waits=[c.sV], inc=c.sV)
    p.step("scalar", lambda e: e.activation(gate[:], glog[:, 32:48], AF.Sigmoid),
           waits=[c.sV], inc=c.sA)
    p.step("vector", lambda e: e.tensor_tensor(d1[:], loc[:], rem[:], ALU.subtract),
           waits=[c.sA, (c.sV, None)], inc=c.sV)
    for h in range(16):
        p.step("vector", lambda e, h=h: e.tensor_scalar_mul(
            d1[:, h * 64:(h + 1) * 64], d1[:, h * 64:(h + 1) * 64],
            gate[:, h:h + 1]),
            waits=[(c.sV, None)], inc=c.sV)
    p.step("vector", lambda e: e.tensor_tensor(d1[:], d1[:], rem[:], ALU.add),
           waits=[c.sV], inc=c.sV)
    p.step("tensor", lambda e: (
        [e.transpose(trb[:, f * 128:(f + 1) * 128],
                     d1[:, f * 128:(f + 1) * 128], c.idt[:]) for f in range(8)])[-1],
        waits=[c.sV, (c.sA, None)], inc=c.sP)
    p.step("vector", lambda e: e.tensor_copy(mstage[:], trb[:, 0:1024]),
           waits=[c.sP, (c.sV, None)], inc=c.sV)
    for kk in range(KD):
        p.step("sync", lambda e, kk=kk, t0=t0: e.dma_start(
            out=c.mixd[kk, :, t0:t0 + 128],
            in_=mstage[:, kk * 128:(kk + 1) * 128]),
            waits=[(c.sV, None)] if kk == 0 else [], inc=c.sO, k=16)
    c.p.flush()
    ar.reset(m0)


def emit_mla_out(c, M):
    p = c.p
    io = c.io
    ar = c.ar
    m0 = ar.mark()
    wout = ar.bf16(8 * 1024)
    xold = ar.f32(KD * 512)
    bcol = ar.f32(8)
    mxb = [ar.bf16(8 * 512) for _ in range(2)]
    for kk in range(KD):
        p.step("gpsimd", lambda e, kk=kk: e.dma_start(
            out=wout[:, kk * 1024:(kk + 1) * 1024],
            in_=io["aout_w"][kk * 128:(kk + 1) * 128, :]),
            waits=[(c.sP, None)] if kk == 0 else [], inc=c.sW, k=16)
    for m in range(KD):
        p.step("sync", lambda e, m=m: e.dma_start(
            out=bcol[:, m:m + 1],
            in_=io["aout_b"][m * 128:(m + 1) * 128].unsqueeze(1)),
            waits=[(c.sV, None)], inc=c.sD, k=16)
    for tc in range(4):
        t0 = tc * 512
        mx = mxb[tc % 2]
        for k in range(KD):
            p.step("sync", lambda e, k=k, t0=t0: e.dma_start(
                out=xold[:, k * 512:(k + 1) * 512],
                in_=c.xmod[k, :, t0:t0 + 512]),
                waits=[(c.sV, None), (c.sO, None)], inc=c.sD, k=16)
        for kk in range(KD):
            p.step("sync", lambda e, kk=kk, t0=t0, mx=mx: e.dma_start(
                out=mx[:, kk * 512:(kk + 1) * 512],
                in_=c.mixd[kk, :, t0:t0 + 512]),
                waits=[(c.sP, None), (c.sO, None)] if kk == 0 else [],
                inc=c.sD, k=16)
        for m in range(KD):
            ps = c.PS[m % 2]
            p.step("tensor", lambda e, m=m, t0=t0, ps=ps, mx=mx: (
                [e.matmul(ps[:, 0:512], wout[:, kk * 1024 + m * 128:
                                             kk * 1024 + (m + 1) * 128],
                          mx[:, kk * 512:(kk + 1) * 512],
                          start=(kk == 0), stop=(kk == KD - 1))
                 for kk in range(KD)])[-1],
                waits=[c.sW, c.sD, (c.sV, None)], inc=c.sP)
            p.step("vector", lambda e, m=m, ps=ps: e.scalar_tensor_tensor(
                xold[:, m * 512:(m + 1) * 512], ps[:, 0:512], bcol[:, m:m + 1],
                xold[:, m * 512:(m + 1) * 512], ALU.add, ALU.add),
                waits=[c.sP, (c.sV, None)], inc=c.sV)
        for k in range(KD):
            p.step("sync", lambda e, k=k, t0=t0: e.dma_start(
                out=c.xmod[k, :, t0:t0 + 512],
                in_=xold[:, k * 512:(k + 1) * 512]),
                waits=[c.sV], inc=c.sO, k=16)
        emit_ln_chunk(c, lambda k: xold[:, k * 512:(k + 1) * 512], t0)
    c.p.flush()
    ar.reset(m0)


def kernel(**inputs):
    x = np.ascontiguousarray(np.asarray(inputs["L0_attn_x"], dtype=np.float32))
    w = {k: np.ascontiguousarray(np.asarray(v, dtype=np.float32))
         for k, v in inputs.items()}

    nc = build_program()

    in_maps = []
    for core in range(NCORE):
        b, h = core // 2, core % 2
        start = h * T
        csq, snq, csk, snk = _rotary_tables(start)
        fband, fspec = _ffn_masks(h)
        aband, at0, arow = _mla_masks(h)
        lmask, lrow = _lat_masks(start, h)
        m = {
            "x_in": x[b, start:start + T, :],
            "aqkv_w": w["L0_attn_qkv_w"], "aqkv_b": w["L0_attn_qkv_b"],
            "arq_w": w["L0_attn_rq_w"], "arq_b": w["L0_attn_rq_b"],
            "arkv_w": w["L0_attn_rkv_w"], "arkv_b": w["L0_attn_rkv_b"],
            "arout_w": w["L0_attn_rout_w"], "arout_b": w["L0_attn_rout_b"],
            "aout_w": w["L0_attn_out_w"], "aout_b": w["L0_attn_out_b"],
            "agl_w": w["L0_attn_gl_w"], "agr_w": w["L0_attn_gr_w"],
            "fwqkv": w["L0_ffn_wqkv"], "fq_scale": w["L0_ffn_q_scale"],
            "fk_scale": w["L0_ffn_k_scale"], "fwo": w["L0_ffn_wo"],
            "fln_g": w["L0_ffn_ln_g"], "fln_b": w["L0_ffn_ln_b"],
            "fw1": w["L0_ffn_w1"], "fb1": w["L0_ffn_b1"],
            "fw2": w["L0_ffn_w2"], "fb2": w["L0_ffn_b2"],
            "csq": csq, "snq": snq, "csk": csk, "snk": snk,
            "fband01": fband, "fspec01": fspec,
            "aband01": aband, "at001": at0, "arowkeep": arow,
            "lmask01": lmask, "lrowkeep": lrow,
            "halo_scale": np.full((128, 1), float(h), np.float32),
        }
        in_maps.append(m)

    res = run_bass_kernel_spmd(nc, in_maps, core_ids=list(range(NCORE)))
    out = np.empty((B, S, D), np.float32)
    for core in range(NCORE):
        b, h = core // 2, core % 2
        out[b, h * T:(h + 1) * T, :] = res.results[core]["out"]
    return out
